# version 37
# baseline (speedup 1.0000x reference)
import numpy as np
import concourse.bass as bass
import concourse.mybir as mybir
from concourse.bass_utils import run_bass_kernel_spmd
from contextlib import ExitStack

F32 = mybir.dt.float32
BF16 = mybir.dt.bfloat16
I32 = mybir.dt.int32
ALU = mybir.AluOpType
ACT = mybir.ActivationFunctionType
AX = mybir.AxisListType

L = 2048
DM = 1024
DI = 2048
DS = 16
NTT = 16
NDI = 16
NCH = 2
TC = 1024
NSUB = 4
SC = 256
INNER = 4096

ENGINES = ("sync", "scalar", "vector", "tensor", "gpsimd")


class Step:
    __slots__ = ("engine", "emit", "inc", "count_after", "idx", "deps",
                 "fdeps")


class Buf:
    __slots__ = ("writers", "readers")

    def __init__(self, init=()):
        self.writers = list(init)
        self.readers = []


class Prog:
    def __init__(self, nc):
        self.nc = nc
        self.steps = []
        self._counts = {e: 0 for e in ENGINES}
        self._last = {e: None for e in ENGINES}

    def add(self, engine, emit, reads=(), writes=(), extra=(), dma=False,
            ptr_reads=(), force=False):
        deps = set()
        for b in reads:
            deps.update(b.writers)
        for b in writes:
            deps.update(b.writers)
            deps.update(b.readers)
        deps.update(extra)
        fdeps = set()
        for b in ptr_reads:
            fdeps.update(b.writers)
        if force:
            fdeps.update(deps)
        st = Step()
        st.engine = engine
        st.emit = emit
        st.inc = 16 if dma else 1
        st.deps = list(deps - fdeps)
        st.fdeps = list(fdeps)
        st.idx = len(self.steps)
        self._counts[engine] += st.inc
        st.count_after = self._counts[engine]
        self.steps.append(st)
        self._last[engine] = st
        for b in reads:
            b.readers.append(st)
        for b in ptr_reads:
            b.readers.append(st)
        for b in writes:
            b.writers = [st]
            b.readers = []
        return st

    def dma(self, engine, out, in_, reads=(), writes=(), extra=(), slow=False):
        def emit(eng, out=out, in_=in_, slow=slow):
            return eng.dma_start(out, in_, allow_slow_non_contiguous=slow)
        return self.add(engine, emit, reads, writes, extra, dma=True)

    def barrier(self):
        return [s for s in self._last.values() if s is not None]

    def finalize(self, es):
        nc = self.nc
        sems = {e: es.enter_context(nc.semaphore("pg_" + e)) for e in ENGINES}
        by_engine = {e: [] for e in ENGINES}
        for st in self.steps:
            by_engine[st.engine].append(st)
        final = dict(self._counts)
        blk = es.enter_context(nc.Block())

        def mk(ename):
            def fn(eng):
                waited = {e: 0 for e in ENGINES}
                for st in by_engine[ename]:
                    need = {}
                    if st.inc == 16:
                        need[ename] = st.count_after - 16
                    for d in st.deps:
                        if d.engine == ename and d.inc == 1:
                            continue
                        if d.count_after > need.get(d.engine, 0):
                            need[d.engine] = d.count_after
                    for d in st.fdeps:
                        if d.count_after > need.get(d.engine, 0):
                            need[d.engine] = d.count_after
                    for e in sorted(need):
                        if need[e] > waited[e]:
                            eng.wait_ge(sems[e], need[e])
                            waited[e] = need[e]
                    st.emit(eng).then_inc(sems[ename], st.inc)
                if ename == "sync":
                    for e in ENGINES:
                        if final[e] > 0:
                            eng.wait_ge(sems[e], final[e])
            return fn

        for e in ENGINES:
            if by_engine[e] or e == "sync":
                getattr(blk, e)(mk(e))


def build_nc():
    nc = bass.Bass("TRN2", target_bir_lowering=False, debug=False)

    i_x = nc.dram_tensor("xb", [L, DM], F32, kind="ExternalInput")
    i_nw = nc.dram_tensor("nw", [1, DM], F32, kind="ExternalInput")
    i_nb = nc.dram_tensor("nb", [1, DM], F32, kind="ExternalInput")
    i_w1 = nc.dram_tensor("w_in", [DM, 2 * DI], F32, kind="ExternalInput")
    i_cw = nc.dram_tensor("conv_w", [DI, 4], F32, kind="ExternalInput")
    i_cb = nc.dram_tensor("conv_b", [1, DI], F32, kind="ExternalInput")
    i_wx = nc.dram_tensor("w_xp", [DI, 2080], F32, kind="ExternalInput")
    i_wdt = nc.dram_tensor("w_dt", [DI, DI], F32, kind="ExternalInput")
    i_dtb = nc.dram_tensor("dt_b", [1, DI], F32, kind="ExternalInput")
    i_alog = nc.dram_tensor("a_log", [DI, DS], F32, kind="ExternalInput")
    i_dp = nc.dram_tensor("d_par", [1, DI], F32, kind="ExternalInput")
    i_wo = nc.dram_tensor("w_out", [DI, DM], F32, kind="ExternalInput")
    i_wg = nc.dram_tensor("w_g", [INNER, DM], F32, kind="ExternalInput")
    i_wu = nc.dram_tensor("w_u", [INNER, DM], F32, kind="ExternalInput")
    i_wd = nc.dram_tensor("w_d", [DM, INNER], F32, kind="ExternalInput")
    o_y = nc.dram_tensor("y_out", [L, DM], F32, kind="ExternalOutput")
    h_base = nc.dram_tensor("s_base", [DM, L], F32, kind="Internal")
    h_xn2 = nc.dram_tensor("s_xn2", [DM, L], BF16, kind="Internal")
    h_xq = nc.dram_tensor("s_xq", [DM, L], BF16, kind="Internal")
    h_xb = nc.dram_tensor("s_xb", [DI, L], BF16, kind="Internal")
    h_z = nc.dram_tensor("s_z", [DI, L], BF16, kind="Internal")
    h_y = nc.dram_tensor("s_y", [DI, L], BF16, kind="Internal")
    h_wqg = nc.dram_tensor("s_wqg", [INNER // 128 * DM, 128], BF16,
                           kind="Internal")
    h_wqu = nc.dram_tensor("s_wqu", [INNER // 128 * DM, 128], BF16,
                           kind="Internal")
    h_wqd = nc.dram_tensor("s_wqd", [DM // 128 * INNER, 128], BF16,
                           kind="Internal")

    es = ExitStack()
    with es:
        def sbt(name, shape, dt=F32):
            return es.enter_context(nc.sbuf_tensor(name, shape, dt))

        MEGA = sbt("mega", [128, 45056], F32)
        Mf = MEGA[:]
        Mb = MEGA[:].bitcast(BF16)
        Mi = MEGA[:].bitcast(I32)

        def mf(off, n, p=128):
            return Mf[0:p, off // 4: off // 4 + n]

        def mb(off, n, p=128):
            return Mb[0:p, off // 2: off // 2 + n]

        def mi(off, n, p=128):
            return Mi[0:p, off // 4: off // 4 + n]

        sb_fx = sbt("fx", [128, L], F32)
        sb_A = sbt("a_sb", [128, NDI * DS], F32)
        sb_D = sbt("d_sb", [128, NDI], F32)
        sb_cb = sbt("cb_sb", [128, NDI], F32)
        sb_dtb = sbt("dtb_sb", [128, NDI], F32)
        sb_cw = sbt("cw_sb", [128, NDI * 4], F32)
        sb_pidx = sbt("pidx", [128, 1], F32)
        sb_srec = sbt("srec", [128, NTT], F32)
        sb_cols = sbt("cols", [128, 16], F32)
        sb_idf = sbt("idf", [128, 128], F32)
        sb_idb = sbt("idb", [128, 128], BF16)
        sb_hcar = sbt("hcar", [128, NDI * DS], F32)
        sb_halo = sbt("halo", [128, NDI * 4], F32)
        sb_Bbf = sbt("bbf", [16, TC], BF16)
        sb_Cbf = sbt("cbf", [16, TC], BF16)
        sb_u = sbt("u_sb", [128, 256], BF16)
        sb_u16 = sbt("u16_sb", [128, 16], F32)
        sb_s2b = sbt("s2b", [128, 512], F32)
        sb_fdb = sbt("fdb", [128, 512], F32)
        sb_wo4 = sbt("wo4", [128, 4096], BF16)

        psum = [es.enter_context(nc.psum_tensor("pb%d" % i, [128, 512], F32))
                for i in range(8)]

        p = Prog(nc)
        bufs = {}

        def B(name):
            if name not in bufs:
                bufs[name] = Buf()
            return bufs[name]

        def newB(name, init=()):
            bufs[name] = Buf(init)
            return bufs[name]

        pbuf = [B("psum%d" % i) for i in range(8)]

        C_SWG, C_SWU, C_SWD, C_THR, C_NTHR, C_T0, C_T1, C_T2 = range(8)

        def col(i):
            return sb_cols[:, i:i + 1]

        p.add("gpsimd", lambda g: g.iota(
            sb_idf[:], [[1, 128]], channel_multiplier=0,
            allow_small_or_imprecise_dtypes=True), writes=[B("idf")])
        p.add("gpsimd", lambda g: g.iota(
            sb_pidx[:], [[0, 1]], channel_multiplier=1,
            allow_small_or_imprecise_dtypes=True), writes=[B("pidx")])
        p.add("vector", lambda v: v.tensor_scalar(
            sb_idf[:], sb_idf[:], sb_pidx[:], None, ALU.is_equal),
            reads=[B("pidx")], writes=[B("idf")])
        p.add("vector", lambda v: v.tensor_copy(sb_idb[:], sb_idf[:]),
              reads=[B("idf")], writes=[B("idb")])

        p.dma("sync", sb_A[:], i_alog.ap().rearrange("(k p) n -> p k n", p=128),
              writes=[B("A")])
        p.add("scalar", lambda s: s.activation(sb_A[:], sb_A[:], ACT.Exp),
              writes=[B("A")])
        p.add("vector", lambda v: v.tensor_scalar_mul(sb_A[:], sb_A[:], -1.0),
              writes=[B("A")])
        p.dma("sync", sb_D[:], i_dp.ap().rearrange("o (k p) -> (o p) k", p=128), slow=True,
              writes=[B("D")])
        p.dma("sync", sb_cb[:], i_cb.ap().rearrange("o (k p) -> (o p) k", p=128), slow=True,
              writes=[B("cb")])
        p.dma("sync", sb_dtb[:], i_dtb.ap().rearrange("o (k p) -> (o p) k", p=128), slow=True,
              writes=[B("dtb")])
        p.dma("sync", sb_cw[:], i_cw.ap().rearrange("(k p) t -> p k t", p=128),
              writes=[B("cw")])
        p.add("vector", lambda v: v.memset(sb_hcar[:], 0.0), writes=[B("hcar")])
        p.add("vector", lambda v: v.memset(sb_halo[:], 0.0), writes=[B("halo")])

        O_BASE = 0
        O_XN2F = 65536
        O_XQF = 98304
        O_RA = 131072
        O_RB = 135168
        O_RC = 139264
        O_RD = 143360
        O_WQA = 147456
        O_WQB = 151552
        O_WQQ = 155648
        O_WQT = 157696
        O_NWB = 167936
        O_NBB = 172032

        base_fm = mf(O_BASE, 8 * 2048)
        xn2_fm = mb(O_XN2F, 8 * 2048)
        xq_fm = mb(O_XQF, 8 * 2048)
        nwb = mf(O_NWB, 1024)
        nbb = mf(O_NBB, 1024)
        b_base = newB("r_base")
        b_xn2f = newB("r_xn2f")
        b_xqf = newB("r_xqf")
        b_nwb = newB("r_nwb")
        b_nbb = newB("r_nbb")

        def bcast_rows(dst_row_ap_fn, first_in, n_parts, reads, bname):
            st = p.dma("sync", dst_row_ap_fn(0, 1), first_in,
                       reads=reads, writes=[B(bname)])
            pc = 1
            while pc < n_parts:
                st = p.dma("sync", dst_row_ap_fn(pc, min(pc, n_parts - pc)),
                           dst_row_ap_fn(0, min(pc, n_parts - pc)),
                           extra=[st])
                pc *= 2
            B(bname).writers.append(st)
            return st

        bcast_rows(lambda p0, n: Mf[p0:p0 + n, O_NWB // 4:O_NWB // 4 + 1024],
                   i_nw.ap(), 128, [], "r_nwb")
        bcast_rows(lambda p0, n: Mf[p0:p0 + n, O_NBB // 4:O_NBB // 4 + 1024],
                   i_nb.ap(), 128, [], "r_nbb")

        def transpose_to_fm(src_ap, src_buf, ident_ap, ident_buf,
                            dst_half_off, dst_is_bf16, dst_buf, tt):
            for half in range(2):
                bank = 6 + half
                pview = psum[bank][:].bitcast(BF16) if dst_is_bf16 \
                    else psum[bank][:]

                def em_t(t, src_ap=src_ap, half=half, pview=pview,
                         ident_ap=ident_ap):
                    r = None
                    for q in range(4):
                        j = half * 4 + q
                        r = t.transpose(pview[:, q * 128:(q + 1) * 128],
                                        src_ap[:, j * 128:(j + 1) * 128],
                                        ident_ap)
                    return r
                p.add("tensor", em_t, reads=[src_buf, ident_buf],
                      writes=[pbuf[bank]])
                if dst_is_bf16:
                    dstv = Mb[:, dst_half_off // 2:
                              dst_half_off // 2 + 16384].rearrange(
                        "p (j t) -> p j t", j=8)
                else:
                    dstv = Mf[:, dst_half_off // 4:
                              dst_half_off // 4 + 16384].rearrange(
                        "p (j t) -> p j t", j=8)
                dst = dstv[:, half * 4:half * 4 + 4, tt * 128:tt * 128 + 128]
                psv = pview[:, 0:512].rearrange("p (q t) -> p q t", q=4)
                p.add("scalar", lambda sc, dst=dst, psv=psv: sc.copy(dst, psv),
                      reads=[pbuf[bank]], writes=[dst_buf])

        XO = [131072, 135168, 139264, 143360]
        XNO = [147456, 151552, 155648, 159744]
        JNK = mf(163840, 1024)
        QT = mf(176128, 1024)
        QTb = mb(176128, 1024)
        QTi = mi(176128, 1024)
        bX = [newB("x%d" % i) for i in range(4)]
        bXN = [newB("xn%d" % i) for i in range(4)]
        bJ = newB("jnk")
        bQ = newB("qt")
        CS0, CS1, CS2 = 4, 8, 12
        bc0, bc1, bc2 = newB("cc0"), newB("cc1"), newB("cc2")

        def stats_chain():
            c0 = sb_cols[:, CS0:CS0 + 4]
            c1 = sb_cols[:, CS1:CS1 + 4]
            c2 = sb_cols[:, CS2:CS2 + 4]
            p.add("vector", lambda v: v.tensor_scalar_mul(c0, c0, 1.0 / DM),
                  writes=[bc0], force=True)
            p.add("vector", lambda v: v.tensor_tensor(c2, c0, c0, ALU.mult),
                  reads=[bc0], writes=[bc2], force=True)
            p.add("vector", lambda v: v.tensor_scalar_mul(c1, c1, 1.0 / DM),
                  writes=[bc1], force=True)
            p.add("vector", lambda v: v.tensor_tensor(c1, c1, c2, ALU.subtract),
                  reads=[bc2], writes=[bc1], force=True)
            p.add("vector", lambda v: v.tensor_scalar_add(c1, c1, 1e-5),
                  writes=[bc1], force=True)
            p.add("scalar", lambda s: s.activation(c1, c1, ACT.Sqrt),
                  writes=[bc1], force=True)
            p.add("vector", lambda v: v.reciprocal(c1, c1),
                  writes=[bc1], force=True)

        for rnd in range(4):
            tts = [rnd * 4 + i for i in range(4)]
            p.dma("sync" if rnd % 2 else "gpsimd",
                  Mf[:, XO[0] // 4:XO[0] // 4 + 4096].rearrange(
                      "p (r c) -> p r c", r=4),
                  i_x.ap()[rnd * 512:(rnd + 1) * 512, :].rearrange(
                      "(r p) c -> p r c", p=128),
                  writes=[bX[0], bX[1], bX[2], bX[3]])
            for i, tt in enumerate(tts):
                X = mf(XO[i], 1024)
                p.add("vector", lambda v, X=X, i=i: v.reduce_sum(
                    sb_cols[:, CS0 + i:CS0 + i + 1], X, AX.X),
                    reads=[bX[i]], writes=[bc0])
                p.add("scalar", lambda s, X=X, i=i: s.activation(
                    JNK, X, ACT.Square,
                    accum_out=sb_cols[:, CS1 + i:CS1 + i + 1]),
                    reads=[bX[i]], writes=[bJ, bc1])
            stats_chain()
            for i, tt in enumerate(tts):
                X = mf(XO[i], 1024)
                XN = mf(XNO[i], 1024)
                p.add("vector", lambda v, X=X, XN=XN, i=i: v.tensor_scalar(
                    XN, X, sb_cols[:, CS0 + i:CS0 + i + 1],
                    sb_cols[:, CS1 + i:CS1 + i + 1], ALU.subtract, ALU.mult),
                    reads=[bX[i]], writes=[bXN[i]], ptr_reads=[bc0, bc1])
                p.add("vector", lambda v, X=X, XN=XN: v.tensor_tensor(
                    X, X, XN, ALU.add), reads=[bXN[i]], writes=[bX[i]])
                transpose_to_fm(X, bX[i], sb_idf[:], B("idf"), O_BASE, False,
                                b_base, tt)
                p.add("vector", lambda v, XN=XN, i=i: v.reduce_sum(
                    sb_cols[:, CS0 + i:CS0 + i + 1], XN, AX.X),
                    reads=[bXN[i]], writes=[bc0])
                p.add("scalar", lambda s, XN=XN, i=i: s.activation(
                    JNK, XN, ACT.Square,
                    accum_out=sb_cols[:, CS1 + i:CS1 + i + 1]),
                    reads=[bXN[i]], writes=[bJ, bc1])
            stats_chain()
            for i, tt in enumerate(tts):
                X = mf(XO[i], 1024)
                Xb = mb(XO[i], 1024)
                XN = mf(XNO[i], 1024)
                p.add("vector", lambda v, X=X, XN=XN, i=i: v.tensor_scalar(
                    X, XN, sb_cols[:, CS0 + i:CS0 + i + 1],
                    sb_cols[:, CS1 + i:CS1 + i + 1], ALU.subtract, ALU.mult),
                    reads=[bXN[i]], writes=[bX[i]], ptr_reads=[bc0, bc1])
                p.add("vector", lambda v, X=X: v.tensor_tensor(
                    X, X, nwb, ALU.mult), reads=[b_nwb], writes=[bX[i]])
                p.add("vector", lambda v, X=X, Xb=Xb: v.tensor_tensor(
                    Xb, X, nbb, ALU.add), reads=[b_nbb], writes=[bX[i]])
                transpose_to_fm(Xb, bX[i], sb_idb[:], B("idb"), O_XN2F, True,
                                b_xn2f, tt)
                p.add("vector", lambda v, XN=XN, i=i: v.tensor_reduce(
                    sb_cols[:, CS0 + i:CS0 + i + 1], XN, AX.X, ALU.max,
                    apply_absolute_value=True),
                    reads=[bXN[i]], writes=[bc0])
            c0 = sb_cols[:, CS0:CS0 + 4]
            c1 = sb_cols[:, CS1:CS1 + 4]
            p.add("vector", lambda v: v.tensor_scalar_max(c0, c0, 1e-5),
                  writes=[bc0], force=True)
            p.add("vector", lambda v, rnd=rnd: v.tensor_scalar_mul(
                sb_srec[:, rnd * 4:rnd * 4 + 4], c0, 1.0 / 127.0),
                reads=[bc0], writes=[B("srec")], force=True)
            p.add("vector", lambda v: v.reciprocal(c1, c0),
                  reads=[bc0], writes=[bc1], force=True)
            p.add("vector", lambda v: v.tensor_scalar_mul(c1, c1, 127.0),
                  writes=[bc1], force=True)
            for i, tt in enumerate(tts):
                XN = mf(XNO[i], 1024)
                XNi = mi(XNO[i], 1024)
                Xb2 = mb(XO[i], 1024)
                p.add("vector", lambda v, XN=XN, XNi=XNi, i=i: v.tensor_scalar(
                    XNi, XN, sb_cols[:, CS1 + i:CS1 + i + 1], None, ALU.mult),
                    reads=[bXN[i]], writes=[bXN[i]], ptr_reads=[bc1])
                p.add("vector", lambda v, XNi=XNi, Xb2=Xb2: v.tensor_copy(
                    Xb2, XNi), reads=[bXN[i]], writes=[bX[i]])
                transpose_to_fm(Xb2, bX[i], sb_idb[:], B("idb"), O_XQF, True,
                                b_xqf, tt)

        p.add("tensor", lambda t: t.transpose(
            psum[6][0:16, 0:128], sb_srec[:], sb_idf[:]),
            reads=[B("srec"), B("idf")], writes=[pbuf[6]])
        p.add("scalar", lambda sc: sc.copy(
            Mf[0:16, O_RA // 4:O_RA // 4 + 128], psum[6][0:16, 0:128]),
            reads=[pbuf[6]], writes=[bX[0]])
        bcast_rows(lambda p0, n: sb_fx[p0:p0 + n, :],
                   Mf[0:16, O_RA // 4:O_RA // 4 + 128], 128, [bX[0]],
                   "fx")
        b_fx = B("fx")

        st_pb = p.dma("sync", h_base.ap().rearrange("(j p) t -> p j t", p=128),
                      base_fm.rearrange("p (j t) -> p j t", j=8),
                      reads=[b_base], writes=[B("h_base")])
        st_p2 = p.dma("gpsimd",
                      h_xn2.ap().rearrange("(j p) t -> p j t", p=128),
                      xn2_fm.rearrange("p (j t) -> p j t", j=8),
                      reads=[b_xn2f], writes=[B("h_xn2")])
        st_pq = p.dma("gpsimd",
                      h_xq.ap().rearrange("(j p) t -> p j t", p=128),
                      xq_fm.rearrange("p (j t) -> p j t", j=8),
                      reads=[b_xqf], writes=[B("h_xq")])

        O_WQA2, O_WQB2 = 32768, 40960
        O_WP1A, O_WP1B = 32768, 49152
        O_WP1C = 65536
        O_WQS1, O_WQS2 = 49152, 53248
        O_WQQ2 = 57344
        O_WQT2 = 61440
        O_WQTQ = 57344

        def wquant(src, dst, n_row_tiles, ncols, csw, tag):
            b_wqtmp = B("wqtmp")
            wa1 = [B("wp1_a"), B("wp1_b"), B("wp1_c")]
            wa = [B("wq_a"), B("wq_b")]
            p.add("vector", lambda v: v.memset(col(C_T0), 0.0), writes=[B("c0")])
            it = 0
            if ncols >= 2048:
                r1step, p1w = 1, ncols
            else:
                r1step, p1w = 4, 4 * ncols
            for rt0 in range(0, n_row_tiles, r1step):
                off = (O_WP1A, O_WP1B, O_WP1C)[it % 3]
                wt = mf(off, p1w)
                if r1step == 4:
                    inap = src[rt0 * 128:(rt0 + 4) * 128, :].rearrange(
                        "(p q) c -> p (q c)", p=128)
                else:
                    inap = src[rt0 * 128:(rt0 + 1) * 128, :]
                wrt = [wa1[it % 3]] + ([B("wqq"), B("wqt"), B("wqs")]
                                       if it % 3 == 1 else [])
                if it % 3 == 2:
                    wrt += [b_xn2f]
                if it < 2:
                    wrt += [B("wq_a"), B("wq_b")]
                p.dma(("gpsimd", "sync", "scalar")[it % 3], wt, inap,
                      writes=wrt)
                p.add("vector", lambda v, wt=wt: v.tensor_reduce(
                    col(C_T1), wt, AX.X, ALU.add,
                    apply_absolute_value=True),
                    reads=[wa1[it % 3]], writes=[B("c1")])
                p.add("vector", lambda v: v.tensor_tensor(
                    col(C_T0), col(C_T0), col(C_T1), ALU.add),
                    reads=[B("c1")], writes=[B("c0")], force=True)
                it += 1
                yield True
            FQ = O_WQTQ // 4
            f0 = Mf[0:16, FQ:FQ + 8]
            st1 = p.dma("sync", f0, col(C_T0), reads=[B("c0")],
                        writes=[b_wqtmp, B("wqq")])
            p.add("vector", lambda v: v.tensor_tensor(
                Mf[0:16, FQ:FQ + 4], Mf[0:16, FQ:FQ + 4],
                Mf[0:16, FQ + 4:FQ + 8], ALU.add),
                writes=[b_wqtmp], force=True)
            p.add("vector", lambda v: v.tensor_tensor(
                Mf[0:16, FQ:FQ + 2], Mf[0:16, FQ:FQ + 2],
                Mf[0:16, FQ + 2:FQ + 4], ALU.add),
                writes=[b_wqtmp], force=True)
            p.add("vector", lambda v: v.tensor_tensor(
                Mf[0:16, FQ:FQ + 1], Mf[0:16, FQ:FQ + 1],
                Mf[0:16, FQ + 1:FQ + 2], ALU.add),
                writes=[b_wqtmp], force=True)
            p.dma("sync", Mf[0:1, FQ + 8:FQ + 24], Mf[0:16, FQ:FQ + 1],
                  reads=[b_wqtmp], writes=[b_wqtmp])
            p.add("vector", lambda v: v.tensor_reduce(
                Mf[0:1, FQ:FQ + 1], Mf[0:1, FQ + 8:FQ + 24], AX.X, ALU.add),
                writes=[b_wqtmp], force=True)
            nel = float(n_row_tiles * 128 * ncols)
            p.add("vector", lambda v: v.tensor_scalar(
                Mf[0:1, FQ:FQ + 1], Mf[0:1, FQ:FQ + 1], 1.0 / nel, 1e-5,
                ALU.mult, ALU.max), writes=[b_wqtmp], force=True)
            bcast_rows(lambda p0, n: sb_cols[p0:p0 + n, csw:csw + 1],
                       Mf[0:1, FQ:FQ + 1], 128, [b_wqtmp], tag + "_sw")
            bsw = B(tag + "_sw")
            p.add("vector", lambda v: v.tensor_scalar_mul(
                col(C_THR), col(csw), 0.5), reads=[bsw], writes=[B("thr")],
                force=True)
            p.add("vector", lambda v: v.tensor_scalar_mul(
                col(C_NTHR), col(C_THR), -1.0), reads=[B("thr")],
                writes=[B("nthr")], force=True)
            p.add("vector", lambda v: v.tensor_scalar_mul(
                col(csw), col(csw), 0.5), writes=[bsw], force=True)
            yield True
            it = 0
            CCW = min(2048, ncols)
            ncc = ncols // CCW
            rstep = 1 if ncols >= 2048 else 2
            for rt0 in range(0, n_row_tiles, rstep):
                for cc in range(ncc):
                    off = O_WQA2 if it % 2 == 0 else O_WQB2
                    wt = mf(off, 2048)
                    if rstep == 2:
                        inap = src[rt0 * 128:(rt0 + 2) * 128, :].rearrange(
                            "(r p) c -> p r c", p=128)
                        outap = wt.rearrange("p (r c) -> p r c", r=2)
                    else:
                        inap = src[rt0 * 128:(rt0 + 1) * 128,
                                   cc * 2048:(cc + 1) * 2048]
                        outap = wt
                    wrt = [wa[it % 2]] + ([wa1[0], wa1[1]] if it < 2 else [])
                    p.dma(("gpsimd", "sync", "scalar")[it % 3], outap, inap,
                          writes=wrt)
                    s1 = mb(O_WQS1, 2048)
                    s2 = mb(O_WQS2, 2048)
                    wq = mb(O_WQQ2, 2048)
                    p.add("scalar", lambda sc, wt=wt, s1=s1: sc.activation(
                        s1, wt, ACT.Sign, bias=col(C_NTHR)),
                        reads=[wa[it % 2]], writes=[B("wqs")],
                        ptr_reads=[B("nthr")])
                    p.add("scalar", lambda sc, wt=wt, s2=s2: sc.activation(
                        s2, wt, ACT.Sign, bias=col(C_THR)),
                        reads=[wa[it % 2]], writes=[B("wqs")],
                        ptr_reads=[B("thr")])
                    p.add("vector", lambda v, wq=wq, s1=s1, s2=s2:
                          v.tensor_tensor(wq, s1, s2, ALU.add),
                          reads=[B("wqs")], writes=[B("wqq")])
                    for half in range(4):
                        bank = 6 + half % 2
                        pview = psum[bank][:].bitcast(BF16)

                        def em_t(t, wq=wq, half=half, pview=pview):
                            r = None
                            for q in range(4):
                                jb = half * 4 + q
                                r = t.transpose(
                                    pview[:, q * 128:(q + 1) * 128],
                                    wq[:, jb * 128:(jb + 1) * 128], sb_idb[:])
                            return r
                        p.add("tensor", em_t, reads=[B("wqq"), B("idb")],
                              writes=[pbuf[bank]])
                        o = O_WQT2 // 2 + half * 512
                        p.add("vector", lambda v, o=o, pview=pview:
                              v.tensor_copy(Mb[:, o:o + 512], pview[:, 0:512]),
                              reads=[pbuf[bank]], writes=[B("wqt")])

                    if rstep == 2:
                        for r in range(2):
                            def em_d(eng, rt0=rt0, r=r):
                                jb = rt0 + r
                                return eng.dma_start(
                                    dst[jb * 1024:(jb + 1) * 1024,
                                        :].rearrange(
                                        "(k p) c -> p k c", p=128),
                                    Mb[:, O_WQT2 // 2 + r * 1024:
                                       O_WQT2 // 2 + r * 1024 + 1024].rearrange(
                                        "p (k c) -> p k c", c=128))
                            p.add("sync", em_d, reads=[B("wqt")],
                                  writes=[B(tag + "park")], dma=True)
                    else:
                        def em_d(eng, rt0=rt0, cc=cc):
                            r0 = rt0 * 4096 + cc * 2048
                            return eng.dma_start(
                                dst[r0:r0 + 2048].rearrange(
                                    "(k p) c -> p k c", p=128),
                                Mb[:, O_WQT2 // 2:O_WQT2 // 2 + 2048].rearrange(
                                    "p (k c) -> p k c", c=128))
                        p.add("sync", em_d, reads=[B("wqt")],
                              writes=[B(tag + "park")], dma=True)
                    it += 1
                    yield True

        for ch in range(NCH):
            T0 = ch * TC
            bar = p.barrier()

            xn2s = mb(131072, 8 * 1024)
            b_xn2s = newB("xn2s", bar)
            if ch == 0:
                p.add("vector", lambda v: v.tensor_copy(
                    xn2s.rearrange("p (k t) -> p k t", k=8),
                    xn2_fm.rearrange("p (j t) -> p j t", j=8)[:, :, T0:T0 + TC]
                    if False else
                    xn2_fm.rearrange("p (j t) -> p j t", j=8)[:, :, 0:TC]),
                    reads=[b_xn2f], writes=[b_xn2s])
            else:
                p.dma("sync", xn2s.rearrange("p (k t) -> p k t", k=8),
                      h_xn2.ap()[:, T0:T0 + TC].rearrange(
                          "(k p) t -> p k t", p=128),
                      reads=[B("h_xn2")], writes=[b_xn2s])
            xbres = mb(98304, 16 * 1024)
            b_xb = newB("xbres", bar)
            b_xbpre = newB("xbpre", bar)
            b_convt = newB("convt", bar)
            w1b_ = [newB("w1a", bar), newB("w1b", bar)]
            W1O = [172544, 174848]
            xbpre = mf(163840, 1028)
            convt = mf(168192, 1024)
            convtb = mb(168192, 1024)
            b_xbpark = B("xb_park")
            b_zpark = B("z_park")
            if ch == 0:
                for _nm in ("wq_a", "wq_b", "wp1_a", "wp1_b", "wp1_c",
                            "wqtmp", "wqq", "wqt", "wqs"):
                    newB(_nm, bar)

                def _wq_all():
                    yield from wquant(i_wg.ap(), h_wqg.ap(), INNER // 128,
                                      DM, C_SWG, "wg")
                    yield from wquant(i_wu.ap(), h_wqu.ap(), INNER // 128,
                                      DM, C_SWU, "wu")
                    yield from wquant(i_wd.ap(), h_wqd.ap(), DM // 128,
                                      INNER, C_SWD, "wd")
                wqgen = _wq_all()
            else:
                wqgen = iter(())

            def wq_step(n=1):
                for _ in range(n):
                    if next(wqgen, None) is None:
                        break
            for j in range(32):
                wt = mb(W1O[j % 2], 8 * 128)
                p.dma("gpsimd", wt.rearrange("p (k c) -> p k c", c=128),
                      i_w1.ap()[:, j * 128:(j + 1) * 128].rearrange(
                          "(k p) c -> p k c", p=128),
                      writes=[w1b_[j % 2]])
                for th in range(2):
                    bank = (j * 2 + th) % 4

                    def em_m(t, wt=wt, th=th, bank=bank):
                        r = None
                        for k in range(8):
                            r = t.matmul(
                                psum[bank][:],
                                wt.rearrange("p (k c) -> p k c", c=128)[:, k, :],
                                xn2s[:, k * 1024 + th * 512:
                                     k * 1024 + th * 512 + 512],
                                start=(k == 0), stop=(k == 7))
                        return r
                    p.add("tensor", em_m, reads=[w1b_[j % 2], b_xn2s],
                          writes=[pbuf[bank]])
                    if j < 16:
                        p.add("vector", lambda v, th=th, bank=bank:
                              v.tensor_copy(
                                  xbpre[:, 4 + th * 512:4 + th * 512 + 512],
                                  psum[bank][:]),
                              reads=[pbuf[bank]], writes=[b_xbpre])
                    else:
                        p.add("vector", lambda v, th=th, bank=bank:
                              v.tensor_copy(
                                  convtb[:, th * 512:th * 512 + 512],
                                  psum[bank][:]),
                              reads=[pbuf[bank]], writes=[b_convt])
                if j < 16:
                    p.add("vector", lambda v, j=j: v.tensor_copy(
                        xbpre[:, 1:4], sb_halo[:, j * 4 + 1:j * 4 + 4]),
                        reads=[B("halo")], writes=[b_xbpre])
                    p.add("vector", lambda v, j=j: v.tensor_copy(
                        sb_halo[:, j * 4 + 1:j * 4 + 4], xbpre[:, 1025:1028]),
                        reads=[b_xbpre], writes=[B("halo")], force=True)
                    p.add("vector", lambda v, j=j: v.tensor_scalar(
                        convt, xbpre[:, 1:1025], sb_cw[:, j * 4:j * 4 + 1],
                        None, ALU.mult),
                        reads=[b_xbpre, B("cw")], writes=[b_convt],
                        force=True)
                    for k in range(1, 4):
                        p.add("vector", lambda v, j=j, k=k:
                              v.scalar_tensor_tensor(
                                  convt, xbpre[:, 1 + k:1 + k + 1024],
                                  sb_cw[:, j * 4 + k:j * 4 + k + 1], convt,
                                  ALU.mult, ALU.add),
                              reads=[b_xbpre, B("cw")], writes=[b_convt])
                    xbt = xbres[:, j * 1024:(j + 1) * 1024]
                    p.add("scalar", lambda sc, j=j, xbt=xbt: sc.activation(
                        xbt, convt, ACT.Silu, bias=sb_cb[:, j:j + 1]),
                        reads=[b_convt, B("cb")], writes=[b_xb])
                else:
                    jz = j - 16
                    p.dma("sync", h_z.ap()[jz * 128:(jz + 1) * 128, T0:T0 + TC],
                          convtb, reads=[b_convt], writes=[b_zpark])
                wq_step(2)

            p.dma("sync",
                  h_xb.ap()[:, T0:T0 + TC].rearrange("(k p) t -> p k t", p=128),
                  xbres.rearrange("p (k t) -> p k t", k=16),
                  reads=[b_xb], writes=[b_xbpark])

            bar2 = p.barrier()
            delta_in = mb(0, 16 * 1024)
            b_din = newB("din", bar2)
            wx_ = [newB("wxa", bar2), newB("wxb", bar2)]
            WXO = [163840, 168192]
            bc_bufs = {}

            def emit_bc():
                barBC = p.barrier()
                b_bfree = newB("bfree", barBC)
                b_cfree = newB("cfree", barBC)
                wqtags = [B(nm) for nm in ("wq_a", "wq_b", "wp1_a", "wp1_b",
                                           "wp1_c", "wqq", "wqt", "wqs",
                                           "wqtmp")]
                for s_i in range(NSUB):
                    st = p.dma("sync",
                               Mb[0:1, 32768 // 2 + s_i * 4096:
                                  32768 // 2 + (s_i + 1) * 4096],
                               sb_Bbf[0:16, s_i * SC:(s_i + 1) * SC],
                               reads=[B("Bbf")], extra=barBC,
                               writes=(wqtags if s_i == 0 else ()))
                    b_bfree.writers.append(st)
                    st = p.dma("gpsimd",
                               Mb[0:1, 65536 // 2 + s_i * 4096:
                                  65536 // 2 + (s_i + 1) * 4096],
                               sb_Cbf[0:16, s_i * SC:(s_i + 1) * SC],
                               reads=[B("Cbf")], extra=barBC,
                               writes=(wqtags + [b_xn2f] if s_i == 0 else ()))
                    b_cfree.writers.append(st)
                pc = 1
                stb = list(b_bfree.writers)
                while pc < 128:
                    stb = [p.dma("sync",
                                 Mb[pc:2 * pc, 32768 // 2:32768 // 2 + 16384],
                                 Mb[0:pc, 32768 // 2:32768 // 2 + 16384],
                                 extra=stb)]
                    pc *= 2
                b_bfree.writers.extend(stb)
                pc = 1
                stc = list(b_cfree.writers)
                while pc < 128:
                    stc = [p.dma("gpsimd",
                                 Mb[pc:2 * pc, 65536 // 2:65536 // 2 + 16384],
                                 Mb[0:pc, 65536 // 2:65536 // 2 + 16384],
                                 extra=stc)]
                    pc *= 2
                b_cfree.writers.extend(stc)
                bc_bufs["b"] = b_bfree
                bc_bufs["c"] = b_cfree
            for j in [16] + list(range(16)):
                wdt_c = 128 if j < 16 else 32
                wt = mb(WXO[j % 2], 16 * wdt_c)
                p.dma("gpsimd", wt.rearrange("p (k c) -> p k c", c=wdt_c),
                      i_wx.ap()[:, j * 128:j * 128 + wdt_c].rearrange(
                          "(k p) c -> p k c", p=128),
                      writes=[wx_[j % 2]])
                for th in range(2):
                    bank = (j * 2 + th) % 4
                    if j < 16:
                        def em_m(t, wt=wt, th=th, bank=bank):
                            r = None
                            for k in range(16):
                                r = t.matmul(
                                    psum[bank][:],
                                    wt.rearrange("p (k c) -> p k c",
                                                 c=128)[:, k, :],
                                    xbres[:, k * 1024 + th * 512:
                                          k * 1024 + th * 512 + 512],
                                    start=(k == 0), stop=(k == 15))
                            return r
                        p.add("tensor", em_m, reads=[wx_[j % 2], b_xb],
                              writes=[pbuf[bank]])
                        p.add("vector", lambda v, j=j, th=th, bank=bank:
                              v.tensor_copy(
                                  delta_in[:, j * 1024 + th * 512:
                                           j * 1024 + th * 512 + 512],
                                  psum[bank][:]),
                              reads=[pbuf[bank]], writes=[b_din])
                    else:
                        bank_c = (bank + 2) % 4

                        def em_mb(t, wt=wt, th=th, bank=bank):
                            r = None
                            for k in range(16):
                                r = t.matmul(
                                    psum[bank][0:16, 0:512],
                                    wt.rearrange("p (k c) -> p k c",
                                                 c=32)[:, k, 0:16],
                                    xbres[:, k * 1024 + th * 512:
                                          k * 1024 + th * 512 + 512],
                                    start=(k == 0), stop=(k == 15))
                            return r
                        p.add("tensor", em_mb, reads=[wx_[j % 2], b_xb],
                              writes=[pbuf[bank]])

                        def em_mc(t, wt=wt, th=th, bank_c=bank_c):
                            r = None
                            for k in range(16):
                                r = t.matmul(
                                    psum[bank_c][0:16, 0:512],
                                    wt.rearrange("p (k c) -> p k c",
                                                 c=32)[:, k, 16:32],
                                    xbres[:, k * 1024 + th * 512:
                                          k * 1024 + th * 512 + 512],
                                    start=(k == 0), stop=(k == 15))
                            return r
                        p.add("tensor", em_mc, reads=[wx_[j % 2], b_xb],
                              writes=[pbuf[bank_c]])
                        p.add("scalar", lambda sc, th=th, bank=bank: sc.copy(
                            sb_Bbf[:, th * 512:th * 512 + 512],
                            psum[bank][0:16, 0:512]),
                            reads=[pbuf[bank]], writes=[B("Bbf")])
                        p.add("scalar", lambda sc, th=th, bank_c=bank_c:
                              sc.copy(sb_Cbf[:, th * 512:th * 512 + 512],
                                      psum[bank_c][0:16, 0:512]),
                              reads=[pbuf[bank_c]], writes=[B("Cbf")])
                if j == 16 and ch != 0:
                    emit_bc()
                wq_step(2)

            wq_step(10 ** 6)
            if ch == 0:
                emit_bc()

            bar3 = p.barrier()
            b_bfree = bc_bufs["b"]
            b_cfree = bc_bufs["c"]

            dAs = [mf(98304, 4096), mf(114688, 4096)]
            b_dAs = [newB("dA0", bar3), newB("dA1", bar3)]
            dbx = mb(131072, 4096)
            dbx3 = dbx.rearrange("p (n t) -> p n t", n=DS)
            b_dBx = newB("dBx", bar3)
            hb = mb(139264, 4096)
            hb3 = hb.rearrange("p (n t) -> p n t", n=DS)
            b_h = newB("hb", bar3)
            deltas = [mf(147456, 1024), mf(151552, 1024)]
            b_deltas = [newB("d0", bar3), newB("d1", bar3)]
            sptmp = mf(155648, 1024)
            b_sp = newB("sptmp", bar3)
            xbsls = [mb(159744, 1024), mb(165888, 1024)]
            b_xbsls = [newB("xbsl0", bar3), newB("xbsl1", bar3)]
            zsls = [mb(161792, 1024), mb(169984, 1024)]
            b_zsls = [newB("zsl0", bar3), newB("zsl1", bar3)]
            tnh = mb(163840, 1024)
            b_tnh = newB("tnh", bar3)
            ytmp = mb(167936, 1024)
            b_ytmp = newB("ytmp", bar3)
            wd_ = [newB("wda", bar3), newB("wdb", bar3)]
            WDO = [172032, 176128]
            b_ypark = B("y_park")

            def prep_dt(dt):
                par = dt % 2
                wt = mb(WDO[par], 16 * 128)
                p.dma("gpsimd", wt.rearrange("p (k c) -> p k c", c=128),
                      i_wdt.ap()[:, dt * 128:(dt + 1) * 128].rearrange(
                          "(k p) c -> p k c", p=128),
                      writes=[wd_[par]])
                p.dma("sync", xbsls[par],
                      h_xb.ap()[dt * 128:(dt + 1) * 128, T0:T0 + TC],
                      reads=[b_xbpark], writes=[b_xbsls[par]])
                delta = deltas[par]
                for th in range(2):
                    bank = 4 + th

                    def em_m(t, wt=wt, th=th, bank=bank):
                        r = None
                        for k in range(16):
                            r = t.matmul(
                                psum[bank][:],
                                wt.rearrange("p (k c) -> p k c", c=128)[:, k, :],
                                delta_in[:, k * 1024 + th * 512:
                                         k * 1024 + th * 512 + 512],
                                start=(k == 0), stop=(k == 15))
                        return r
                    p.add("tensor", em_m, reads=[wd_[par], b_din],
                          writes=[pbuf[bank]])
                    p.add("scalar", lambda s, th=th, bank=bank, dt=dt,
                          delta=delta: s.activation(
                              delta[:, th * 512:th * 512 + 512], psum[bank][:],
                              ACT.Identity, bias=sb_dtb[:, dt:dt + 1]),
                          reads=[pbuf[bank], B("dtb")], writes=[b_deltas[par]])
                p.add("scalar", lambda s, delta=delta: s.activation(
                    sptmp, delta, ACT.Abs), reads=[b_deltas[par]],
                    writes=[b_sp])
                p.add("scalar", lambda s: s.activation(
                    sptmp, sptmp, ACT.Exp, scale=-1.0), writes=[b_sp])
                p.add("scalar", lambda s: s.activation(
                    sptmp, sptmp, ACT.Ln, bias=1.0), writes=[b_sp])
                p.add("scalar", lambda s, delta=delta: s.activation(
                    delta, delta, ACT.Relu), writes=[b_deltas[par]])
                p.add("vector", lambda v, delta=delta: v.tensor_tensor(
                    delta, delta, sptmp, ALU.add), reads=[b_sp],
                    writes=[b_deltas[par]])

            prep_dt(0)
            for dt in range(NDI):
                par = dt % 2
                delta = deltas[par]
                xbsl = xbsls[par]
                if dt + 1 < NDI:
                    prep_dt(dt + 1)
                hsl = sb_hcar[:, dt * DS:(dt + 1) * DS]
                for s_i in range(NSUB):
                    g_par = (dt * NSUB + s_i) % 2
                    dA = dAs[g_par]
                    dA3 = dA.rearrange("p (n t) -> p n t", n=DS)
                    b_dA = b_dAs[g_par]
                    bfs3 = mb(32768 + s_i * 8192, 4096).rearrange(
                        "p (n t) -> p n t", n=DS)
                    cfs3 = mb(65536 + s_i * 8192, 4096).rearrange(
                        "p (n t) -> p n t", n=DS)

                    def em_da(sc, dt=dt, s_i=s_i, dA=dA, delta=delta):
                        r = None
                        for n in range(DS):
                            r = sc.activation(
                                dA[:, n * SC:(n + 1) * SC],
                                delta[:, s_i * SC:(s_i + 1) * SC], ACT.Exp,
                                scale=sb_A[:, dt * DS + n:dt * DS + n + 1])
                        return r
                    p.add("scalar", em_da, reads=[b_deltas[par], B("A")],
                          writes=[b_dA])
                    p.add("vector", lambda v, s_i=s_i, delta=delta, xbsl=xbsl:
                          v.tensor_tensor(
                              sb_u[:], delta[:, s_i * SC:(s_i + 1) * SC],
                              xbsl[:, s_i * SC:(s_i + 1) * SC], ALU.mult),
                          reads=[b_deltas[par], b_xbsls[par]], writes=[B("u")])
                    p.add("vector", lambda v, bfs3=bfs3: v.tensor_tensor(
                        dbx3,
                        sb_u[:].unsqueeze(1).broadcast_to([128, DS, SC]),
                        bfs3, ALU.mult),
                        reads=[B("u"), b_bfree], writes=[b_dBx])
                    p.add("vector", lambda v, hsl=hsl, dA3=dA3: v.tensor_tensor(
                        sb_u16[:],
                        dA3[:, :, 0:1].rearrange("p n o -> p (n o)"),
                        hsl, ALU.mult),
                        reads=[b_dA, B("hcar")], writes=[B("u16")], force=True)
                    p.add("vector", lambda v: v.tensor_tensor(
                        dbx3[:, :, 0:1].rearrange("p n o -> p (n o)"),
                        dbx3[:, :, 0:1].rearrange("p n o -> p (n o)"),
                        sb_u16[:], ALU.add), reads=[B("u16")], writes=[b_dBx],
                        force=True)
                    p.add("vector", lambda v, dA3=dA3: v.memset(
                        dA3[:, :, 0:1], 0.0), writes=[b_dA], force=True)
                    p.add("vector", lambda v, dA=dA: v.tensor_tensor_scan(
                        hb, dA, dbx, 0.0, ALU.mult, ALU.add),
                        reads=[b_dBx, b_dA], writes=[b_h], force=True)
                    p.add("vector", lambda v, hsl=hsl: v.tensor_copy(
                        hsl, hb3[:, :, SC - 1:SC].rearrange("p n o -> p (n o)")),
                        reads=[b_h], writes=[B("hcar")], force=True)
                    p.add("vector", lambda v, cfs3=cfs3: v.tensor_tensor(
                        dbx3, hb3, cfs3, ALU.mult),
                        reads=[b_h, b_cfree], writes=[b_dBx])
                    w_ = 2048
                    while w_ >= SC:
                        p.add("vector", lambda v, w_=w_: v.tensor_tensor(
                            dbx[:, 0:w_], dbx[:, 0:w_], dbx[:, w_:2 * w_],
                            ALU.add), writes=[b_dBx])
                        w_ //= 2

                    def em_y(v, s_i=s_i, xbsl=xbsl, dt=dt):
                        return v.scalar_tensor_tensor(
                            ytmp[:, s_i * SC:(s_i + 1) * SC],
                            xbsl[:, s_i * SC:(s_i + 1) * SC],
                            sb_D[:, dt:dt + 1], dbx[:, 0:SC],
                            ALU.mult, ALU.add)
                    p.add("vector", em_y, reads=[b_xbsls[par], B("D"), b_dBx],
                          writes=[b_ytmp])
                st = p.dma("sync", h_y.ap()[dt * 128:(dt + 1) * 128, T0:T0 + TC],
                           ytmp[:], reads=[b_ytmp])
                b_ypark.writers.append(st)

            bar5 = p.barrier()
            g_all = mf(0, 16384)
            gq = mb(65536, 16384)
            xqs = mb(131072, 8 * 1024)
            gmax = mf(147456, 512)
            gmin = mf(149504, 512)
            qi32 = mi(151552, 512)
            p1t = mf(153600, 512)
            gtmps = [mf(163840, 512), mf(155648, 512)]
            gtmp2s = [mf(165888, 512), mf(157696, 512)]
            utms = [mf(167936, 512), mf(159744, 512)]
            qtmp = mf(169984, 512)
            b_g = newB("g_all", bar5)
            b_gq = newB("gq", bar5)
            b_xqs = newB("xqs", bar5)
            b_gmax = newB("gmax", bar5)
            b_gmin = newB("gmin", bar5)
            b_qi = newB("qi32", bar5)
            b_p1 = newB("p1t", bar5)
            b_gts = [newB("gtmp0", bar5), newB("gtmp1", bar5)]
            b_gt2s = [newB("gtmp20", bar5), newB("gtmp21", bar5)]
            b_uts = [newB("utm0", bar5), newB("utm1", bar5)]
            b_qt = newB("qtmp", bar5)
            b_fw = [newB("fw0", bar5), newB("fw1", bar5)]
            b_fwd = [newB("fwd0", bar5), newB("fwd1", bar5)]
            b_gqk = [newB("gqk%d" % k2, bar5) for k2 in range(32)]
            FWO = [172032, 176128]
            FWD = [172032, 161792]
            p.dma("sync", xqs.rearrange("p (k t) -> p k t", k=8),
                  h_xq.ap()[:, T0:T0 + TC].rearrange("(p k) t -> p k t", k=8),
                  reads=[B("h_xq")], writes=[b_xqs])
            def p5_production(sub):
                tg0 = T0 + sub * 512
                fxsl = sb_fx[:, tg0:tg0 + 512]
                p.add("vector", lambda v: v.memset(gmax, -3.0e38),
                      writes=[b_gmax])
                p.add("vector", lambda v: v.memset(gmin, 3.0e38),
                      writes=[b_gmin])
                for j in range(32):
                    wgt = mb(FWO[j % 2], 8 * 128)
                    wut = mb(FWO[j % 2] + 2048, 8 * 128)
                    p.dma("gpsimd", wgt.rearrange("p (k c) -> p k c", c=128),
                          h_wqg.ap()[j * 1024:(j + 1) * 1024, :].rearrange(
                              "(p k) c -> p k c", k=8),
                          reads=[B("wgpark")], writes=[b_fw[j % 2]])
                    p.dma("sync", wut.rearrange("p (k c) -> p k c", c=128),
                          h_wqu.ap()[j * 1024:(j + 1) * 1024, :].rearrange(
                              "(p k) c -> p k c", k=8),
                          reads=[B("wupark")], writes=[b_fw[j % 2]])
                    bank_g = (j % 2) * 2
                    bank_u = (j % 2) * 2 + 1

                    def em_mg(t, wgt=wgt, bank_g=bank_g, sub=sub):
                        r = None
                        for k in range(8):
                            r = t.matmul(
                                psum[bank_g][:],
                                wgt.rearrange("p (k c) -> p k c",
                                              c=128)[:, k, :],
                                xqs[:, k * 1024 + sub * 512:
                                    k * 1024 + sub * 512 + 512],
                                start=(k == 0), stop=(k == 7))
                        return r
                    p.add("tensor", em_mg, reads=[b_fw[j % 2], b_xqs],
                          writes=[pbuf[bank_g]])

                    def em_mu(t, wut=wut, bank_u=bank_u, sub=sub):
                        r = None
                        for k in range(8):
                            r = t.matmul(
                                psum[bank_u][:],
                                wut.rearrange("p (k c) -> p k c",
                                              c=128)[:, k, :],
                                xqs[:, k * 1024 + sub * 512:
                                    k * 1024 + sub * 512 + 512],
                                start=(k == 0), stop=(k == 7))
                        return r
                    p.add("tensor", em_mu, reads=[b_fw[j % 2], b_xqs],
                          writes=[pbuf[bank_u]])
                    gtmp = gtmps[j % 2]
                    gtmp2 = gtmp2s[j % 2]
                    utm = utms[j % 2]
                    b_gt = b_gts[j % 2]
                    b_gt2 = b_gt2s[j % 2]
                    b_ut = b_uts[j % 2]
                    p.add("vector", lambda v, bank_g=bank_g, fxsl=fxsl,
                          gtmp=gtmp: v.scalar_tensor_tensor(
                              gtmp, psum[bank_g][:], col(C_SWG), fxsl,
                              ALU.mult, ALU.mult),
                          reads=[pbuf[bank_g], B("wg_sw"), b_fx],
                          writes=[b_gt])
                    p.add("scalar", lambda s, gtmp=gtmp, gtmp2=gtmp2:
                          s.activation(gtmp2, gtmp, ACT.Silu),
                          reads=[b_gt], writes=[b_gt2])
                    p.add("vector", lambda v, bank_u=bank_u, fxsl=fxsl,
                          utm=utm: v.scalar_tensor_tensor(
                              utm, psum[bank_u][:], col(C_SWU), fxsl,
                              ALU.mult, ALU.mult),
                          reads=[pbuf[bank_u], B("wu_sw"), b_fx],
                          writes=[b_ut])
                    p.add("vector", lambda v, j=j, gtmp2=gtmp2, utm=utm:
                          v.tensor_tensor(
                              g_all[:, j * 512:(j + 1) * 512], gtmp2, utm,
                              ALU.mult),
                          reads=[b_gt2, b_ut], writes=[b_g])
                    p.add("vector", lambda v, j=j: v.tensor_tensor(
                        gmax, gmax, g_all[:, j * 512:(j + 1) * 512], ALU.max),
                        reads=[b_g], writes=[b_gmax])
                    p.add("vector", lambda v, j=j: v.tensor_tensor(
                        gmin, gmin, g_all[:, j * 512:(j + 1) * 512], ALU.min),
                        reads=[b_g], writes=[b_gmin])
            def p5_rest(sub):
                tg0 = T0 + sub * 512
                fxsl = sb_fx[:, tg0:tg0 + 512]
                p.add("vector", lambda v: v.tensor_scalar_mul(
                    gmin, gmin, -1.0), writes=[b_gmin])
                p.add("vector", lambda v: v.tensor_tensor(
                    gmax, gmax, gmin, ALU.max), reads=[b_gmin],
                    writes=[b_gmax])
                FB = 65536 // 4
                p.dma("sync", Mf[0:16, FB:FB + 4096], gmax,
                      reads=[b_gmax], writes=[b_gq])
                p.add("vector", lambda v: v.tensor_tensor(
                    Mf[0:16, FB:FB + 2048], Mf[0:16, FB:FB + 2048],
                    Mf[0:16, FB + 2048:FB + 4096], ALU.max), writes=[b_gq])
                p.add("vector", lambda v: v.tensor_tensor(
                    Mf[0:16, FB:FB + 1024], Mf[0:16, FB:FB + 1024],
                    Mf[0:16, FB + 1024:FB + 2048], ALU.max), writes=[b_gq])
                p.add("vector", lambda v: v.tensor_tensor(
                    Mf[0:16, FB:FB + 512], Mf[0:16, FB:FB + 512],
                    Mf[0:16, FB + 512:FB + 1024], ALU.max), writes=[b_gq])
                p.dma("sync", Mf[0:4, FB + 2048:FB + 2048 + 2048],
                      Mf[0:16, FB:FB + 512], reads=[b_gq], writes=[b_gq])
                p.add("vector", lambda v: v.tensor_tensor(
                    Mf[0:4, FB + 2048:FB + 2048 + 1024],
                    Mf[0:4, FB + 2048:FB + 2048 + 1024],
                    Mf[0:4, FB + 3072:FB + 3072 + 1024], ALU.max),
                    writes=[b_gq])
                p.add("vector", lambda v: v.tensor_tensor(
                    Mf[0:4, FB + 2048:FB + 2048 + 512],
                    Mf[0:4, FB + 2048:FB + 2048 + 512],
                    Mf[0:4, FB + 2560:FB + 2560 + 512], ALU.max),
                    writes=[b_gq])
                p.dma("sync", Mf[0:1, FB:FB + 2048],
                      Mf[0:4, FB + 2048:FB + 2048 + 512],
                      reads=[b_gq], writes=[b_gq])
                p.add("vector", lambda v: v.tensor_tensor(
                    Mf[0:1, FB:FB + 1024], Mf[0:1, FB:FB + 1024],
                    Mf[0:1, FB + 1024:FB + 2048], ALU.max), writes=[b_gq])
                p.add("vector", lambda v: v.tensor_tensor(
                    Mf[0:1, FB:FB + 512], Mf[0:1, FB:FB + 512],
                    Mf[0:1, FB + 512:FB + 1024], ALU.max), writes=[b_gq])
                p.add("vector", lambda v: v.tensor_scalar_max(
                    Mf[0:1, FB:FB + 512], Mf[0:1, FB:FB + 512], 1e-5),
                    writes=[b_gq], force=True)
                p.add("vector", lambda v: v.reciprocal(
                    Mf[0:1, FB + 1024:FB + 1024 + 512],
                    Mf[0:1, FB:FB + 512]), writes=[b_gq], force=True)
                p.add("vector", lambda v: v.tensor_scalar_mul(
                    Mf[0:1, FB + 1024:FB + 1024 + 512],
                    Mf[0:1, FB + 1024:FB + 1024 + 512], 127.0),
                    writes=[b_gq], force=True)
                p.add("vector", lambda v: v.tensor_scalar(
                    Mf[0:1, FB + 2048:FB + 2048 + 512], Mf[0:1, FB:FB + 512],
                    1.0 / 127.0, sb_cols[0:1, C_SWD:C_SWD + 1],
                    ALU.mult, ALU.mult),
                    reads=[B("wd_sw")], writes=[b_gq], force=True)
                bcast_rows(lambda p0, n: sb_s2b[p0:p0 + n, :],
                           Mf[0:1, FB + 1024:FB + 1024 + 512], 128,
                           [b_gq], "s2b")
                bcast_rows(lambda p0, n: sb_fdb[p0:p0 + n, :],
                           Mf[0:1, FB + 2048:FB + 2048 + 512], 128,
                           [b_gq], "fdb")
                b_s2b = B("s2b")
                b_fdb2 = B("fdb")
                for j in range(32):
                    gsl = g_all[:, j * 512:(j + 1) * 512]
                    p.add("vector", lambda v, gsl=gsl: v.tensor_tensor(
                        qi32, gsl, sb_s2b[:], ALU.mult),
                        reads=[b_g, b_s2b], writes=[b_qi])
                    p.add("vector", lambda v, j=j: v.tensor_copy(
                        gq[:, j * 512:(j + 1) * 512], qi32),
                        reads=[b_qi], writes=[b_gqk[j], b_gq])
                for j2 in range(8):
                    wdt_ = mb(FWD[j2 % 2], 32 * 128)
                    wrt = ([b_fwd[0], b_fw[0], b_fw[1]] if j2 % 2 == 0 else
                           [b_fwd[1], b_gts[0], b_gt2s[0], b_uts[0]])
                    p.dma("sync" if j2 % 2 else "gpsimd",
                          wdt_.rearrange("p (k c) -> p k c", c=128),
                          h_wqd.ap()[j2 * 4096:(j2 + 1) * 4096, :].rearrange(
                              "(k p) c -> p k c", p=128),
                          reads=[B("wdpark")], writes=wrt)
                    bank = 4 + (j2 % 2)
                    kgrp = 8 if j2 == 0 else (16 if j2 == 1 else 32)
                    for k0 in range(0, 32, kgrp):
                        def em_md(t, wdt_=wdt_, bank=bank, k0=k0, kgrp=kgrp):
                            r = None
                            for k in range(k0, k0 + kgrp):
                                r = t.matmul(
                                    psum[bank][:],
                                    wdt_.rearrange("p (k c) -> p k c",
                                                   c=128)[:, k, :],
                                    gq[:, k * 512:(k + 1) * 512],
                                    start=(k == 0), stop=(k == 31))
                            return r
                        p.add("tensor", em_md,
                              reads=wrt + [b_gqk[k0 + kgrp - 1]],
                              writes=[pbuf[bank]])
                    p.add("vector", lambda v, bank=bank: v.tensor_tensor(
                        qtmp, psum[bank][:], sb_fdb[:], ALU.mult),
                        reads=[pbuf[bank], b_fdb2], writes=[b_qt])

                    def em_a(v, j2=j2, sub=sub):
                        sl = acc[:, j2 * 1024 + sub * 512:
                                 j2 * 1024 + sub * 512 + 512]
                        return v.tensor_tensor(sl, sl, qtmp, ALU.add)
                    p.add("vector", em_a, reads=[b_qt], writes=[b_acc])

            bar4 = bar5
            p5_production(0)
            acc = mf(98304, 8 * 1024)
            b_acc = newB("acc", bar4)
            p.dma("sync", acc.rearrange("p (j t) -> p j t", j=8),
                  h_base.ap()[:, T0:T0 + TC].rearrange("(j p) t -> p j t",
                                                       p=128),
                  reads=[B("h_base")], writes=[b_acc])
            ytile = mb(65536, 16 * 1024)
            b_y = newB("yres", bar4)
            p.dma("sync", ytile.rearrange("p (k t) -> p k t", k=16),
                  h_y.ap()[:, T0:T0 + TC].rearrange("(k p) t -> p k t", p=128),
                  reads=[b_ypark], writes=[b_y])
            zstg = [mb(153600, 1024), mb(169984, 1024)]
            b_zst = [newB("zst0", bar4), newB("zst1", bar4)]
            for k in range(16):
                zs = zstg[k % 2]
                p.dma("gpsimd", zs,
                      h_z.ap()[k * 128:(k + 1) * 128, T0:T0 + TC],
                      reads=[b_zpark], writes=[b_zst[k % 2], b_qt, b_p1])
                p.add("scalar", lambda s, zs=zs: s.activation(
                    zs, zs, ACT.Silu), reads=[b_zst[k % 2]],
                    writes=[b_zst[k % 2]])
                p.add("vector", lambda v, k=k, zs=zs: v.tensor_tensor(
                    ytile[:, k * 1024:(k + 1) * 1024],
                    ytile[:, k * 1024:(k + 1) * 1024], zs, ALU.mult),
                    reads=[b_zst[k % 2]], writes=[b_y])
            wo_ = [newB("woa", bar4), newB("wob", bar4)]
            for j in range(8):
                wt = sb_wo4[:, (j % 2) * 2048:(j % 2) * 2048 + 2048]
                p.dma("gpsimd", wt.rearrange("p (k c) -> p k c", c=128),
                      i_wo.ap()[:, j * 128:(j + 1) * 128].rearrange(
                          "(k p) c -> p k c", p=128),
                      writes=[wo_[j % 2]])
                for th in range(2):
                    bank = 6 + (j * 2 + th) % 2

                    def em_m(t, wt=wt, th=th, bank=bank):
                        r = None
                        for k in range(16):
                            r = t.matmul(
                                psum[bank][:],
                                wt.rearrange("p (k c) -> p k c", c=128)[:, k, :],
                                ytile[:, k * 1024 + th * 512:
                                      k * 1024 + th * 512 + 512],
                                start=(k == 0), stop=(k == 15))
                        return r
                    p.add("tensor", em_m, reads=[wo_[j % 2], b_y, b_gq],
                          writes=[pbuf[bank]])

                    def em_e(v, j=j, th=th, bank=bank):
                        sl = acc[:, j * 1024 + th * 512:
                                 j * 1024 + th * 512 + 512]
                        return v.tensor_tensor(sl, sl, psum[bank][:], ALU.add)
                    p.add("vector", em_e, reads=[pbuf[bank]], writes=[b_acc])

            p5_rest(0)
            p5_production(1)
            p5_rest(1)

            bar6 = p.barrier()
            otok = mf(163840, 1024)
            b_ot = newB("otok", bar6)
            for ttl in range(8):
                for half in range(2):
                    bank = 6 + half

                    def em_t(t, ttl=ttl, half=half, bank=bank):
                        r = None
                        for q in range(4):
                            j = half * 4 + q
                            r = t.transpose(
                                psum[bank][:, q * 128:(q + 1) * 128],
                                acc[:, j * 1024 + ttl * 128:
                                    j * 1024 + ttl * 128 + 128], sb_idf[:])
                        return r
                    p.add("tensor", em_t, reads=[b_acc, B("idf")],
                          writes=[pbuf[bank]])
                    for q in range(4):
                        j = half * 4 + q
                        p.add("scalar", lambda sc, j=j, bank=bank, q=q:
                              sc.copy(otok[:, j * 128:(j + 1) * 128],
                                      psum[bank][:, q * 128:(q + 1) * 128]),
                              reads=[pbuf[bank]], writes=[b_ot])
                p.dma("sync",
                      o_y.ap()[T0 + ttl * 128:T0 + (ttl + 1) * 128, :],
                      otok, reads=[b_ot], writes=[B("outdone")])

        p.finalize(es)
    return nc


_NC_CACHE = {}


def kernel(L0_attn_x, L0_attn_norm_w, L0_attn_norm_b, L0_attn_in_proj_w,
           L0_attn_conv_w, L0_attn_conv_b, L0_attn_x_proj_w, L0_attn_dt_proj_w,
           L0_attn_dt_proj_b, L0_attn_A_log, L0_attn_D_param,
           L0_attn_out_proj_w, L0_ffn_w_gate, L0_ffn_w_up, L0_ffn_w_down):
    def f(a):
        return np.ascontiguousarray(np.asarray(a, dtype=np.float32))
    x = f(L0_attn_x)
    wmap = {
        "nw": f(L0_attn_norm_w).reshape(1, DM),
        "nb": f(L0_attn_norm_b).reshape(1, DM),
        "w_in": f(L0_attn_in_proj_w),
        "conv_w": f(L0_attn_conv_w).reshape(DI, 4),
        "conv_b": f(L0_attn_conv_b).reshape(1, DI),
        "w_xp": f(L0_attn_x_proj_w),
        "w_dt": f(L0_attn_dt_proj_w),
        "dt_b": f(L0_attn_dt_proj_b).reshape(1, DI),
        "a_log": f(L0_attn_A_log),
        "d_par": f(L0_attn_D_param).reshape(1, DI),
        "w_out": f(L0_attn_out_proj_w),
        "w_g": f(L0_ffn_w_gate),
        "w_u": f(L0_ffn_w_up),
        "w_d": f(L0_ffn_w_down),
    }
    if "nc" not in _NC_CACHE:
        _NC_CACHE["nc"] = build_nc()
    nc = _NC_CACHE["nc"]
    in_maps = [dict(wmap, xb=np.ascontiguousarray(x[b])) for b in range(8)]
    res = run_bass_kernel_spmd(nc, in_maps, core_ids=list(range(8)))
    out = np.stack([np.asarray(res.results[b]["y_out"]) for b in range(8)],
                   axis=0)
    return out.astype(np.float32)


# version 38
# speedup vs baseline: 1.0253x; 1.0253x over previous
import numpy as np
import concourse.bass as bass
import concourse.mybir as mybir
from concourse.bass_utils import run_bass_kernel_spmd
from contextlib import ExitStack

F32 = mybir.dt.float32
BF16 = mybir.dt.bfloat16
I32 = mybir.dt.int32
ALU = mybir.AluOpType
ACT = mybir.ActivationFunctionType
AX = mybir.AxisListType

L = 2048
DM = 1024
DI = 2048
DS = 16
NTT = 16
NDI = 16
NCH = 2
TC = 1024
NSUB = 4
SC = 256
INNER = 4096

ENGINES = ("sync", "scalar", "vector", "tensor", "gpsimd")


class Step:
    __slots__ = ("engine", "emit", "inc", "count_after", "idx", "deps",
                 "fdeps")


class Buf:
    __slots__ = ("writers", "readers")

    def __init__(self, init=()):
        self.writers = list(init)
        self.readers = []


class Prog:
    def __init__(self, nc):
        self.nc = nc
        self.steps = []
        self._counts = {e: 0 for e in ENGINES}
        self._last = {e: None for e in ENGINES}

    def add(self, engine, emit, reads=(), writes=(), extra=(), dma=False,
            ptr_reads=(), force=False):
        deps = set()
        for b in reads:
            deps.update(b.writers)
        for b in writes:
            deps.update(b.writers)
            deps.update(b.readers)
        deps.update(extra)
        fdeps = set()
        for b in ptr_reads:
            fdeps.update(b.writers)
        if force:
            fdeps.update(deps)
        st = Step()
        st.engine = engine
        st.emit = emit
        st.inc = 16 if dma else 1
        st.deps = list(deps - fdeps)
        st.fdeps = list(fdeps)
        st.idx = len(self.steps)
        self._counts[engine] += st.inc
        st.count_after = self._counts[engine]
        self.steps.append(st)
        self._last[engine] = st
        for b in reads:
            b.readers.append(st)
        for b in ptr_reads:
            b.readers.append(st)
        for b in writes:
            b.writers = [st]
            b.readers = []
        return st

    def dma(self, engine, out, in_, reads=(), writes=(), extra=(), slow=False):
        def emit(eng, out=out, in_=in_, slow=slow):
            return eng.dma_start(out, in_, allow_slow_non_contiguous=slow)
        return self.add(engine, emit, reads, writes, extra, dma=True)

    def barrier(self):
        return [s for s in self._last.values() if s is not None]

    def finalize(self, es):
        nc = self.nc
        sems = {e: es.enter_context(nc.semaphore("pg_" + e)) for e in ENGINES}
        by_engine = {e: [] for e in ENGINES}
        for st in self.steps:
            by_engine[st.engine].append(st)
        final = dict(self._counts)
        blk = es.enter_context(nc.Block())

        def mk(ename):
            def fn(eng):
                waited = {e: 0 for e in ENGINES}
                for st in by_engine[ename]:
                    need = {}
                    if st.inc == 16:
                        need[ename] = st.count_after - 16
                    for d in st.deps:
                        if d.engine == ename and d.inc == 1:
                            continue
                        if d.count_after > need.get(d.engine, 0):
                            need[d.engine] = d.count_after
                    for d in st.fdeps:
                        if d.count_after > need.get(d.engine, 0):
                            need[d.engine] = d.count_after
                    for e in sorted(need):
                        if need[e] > waited[e]:
                            eng.wait_ge(sems[e], need[e])
                            waited[e] = need[e]
                    st.emit(eng).then_inc(sems[ename], st.inc)
                if ename == "sync":
                    for e in ENGINES:
                        if final[e] > 0:
                            eng.wait_ge(sems[e], final[e])
            return fn

        for e in ENGINES:
            if by_engine[e] or e == "sync":
                getattr(blk, e)(mk(e))


def build_nc():
    nc = bass.Bass("TRN2", target_bir_lowering=False, debug=False)

    i_x = nc.dram_tensor("xb", [L, DM], F32, kind="ExternalInput")
    i_nw = nc.dram_tensor("nw", [1, DM], F32, kind="ExternalInput")
    i_nb = nc.dram_tensor("nb", [1, DM], F32, kind="ExternalInput")
    i_w1 = nc.dram_tensor("w_in", [DM, 2 * DI], F32, kind="ExternalInput")
    i_cw = nc.dram_tensor("conv_w", [DI, 4], F32, kind="ExternalInput")
    i_cb = nc.dram_tensor("conv_b", [1, DI], F32, kind="ExternalInput")
    i_wx = nc.dram_tensor("w_xp", [DI, 2080], F32, kind="ExternalInput")
    i_wdt = nc.dram_tensor("w_dt", [DI, DI], F32, kind="ExternalInput")
    i_dtb = nc.dram_tensor("dt_b", [1, DI], F32, kind="ExternalInput")
    i_alog = nc.dram_tensor("a_log", [DI, DS], F32, kind="ExternalInput")
    i_dp = nc.dram_tensor("d_par", [1, DI], F32, kind="ExternalInput")
    i_wo = nc.dram_tensor("w_out", [DI, DM], F32, kind="ExternalInput")
    i_wg = nc.dram_tensor("w_g", [INNER, DM], F32, kind="ExternalInput")
    i_wu = nc.dram_tensor("w_u", [INNER, DM], F32, kind="ExternalInput")
    i_wd = nc.dram_tensor("w_d", [DM, INNER], F32, kind="ExternalInput")
    o_y = nc.dram_tensor("y_out", [L, DM], F32, kind="ExternalOutput")
    h_base = nc.dram_tensor("s_base", [DM, L], F32, kind="Internal")
    h_xn2 = nc.dram_tensor("s_xn2", [DM, L], BF16, kind="Internal")
    h_xq = nc.dram_tensor("s_xq", [DM, L], BF16, kind="Internal")
    h_xb = nc.dram_tensor("s_xb", [DI, L], BF16, kind="Internal")
    h_z = nc.dram_tensor("s_z", [DI, L], BF16, kind="Internal")
    h_y = nc.dram_tensor("s_y", [DI, L], BF16, kind="Internal")
    h_wqg = nc.dram_tensor("s_wqg", [INNER // 128 * DM, 128], BF16,
                           kind="Internal")
    h_wqu = nc.dram_tensor("s_wqu", [INNER // 128 * DM, 128], BF16,
                           kind="Internal")
    h_wqd = nc.dram_tensor("s_wqd", [DM // 128 * INNER, 128], BF16,
                           kind="Internal")

    es = ExitStack()
    with es:
        def sbt(name, shape, dt=F32):
            return es.enter_context(nc.sbuf_tensor(name, shape, dt))

        MEGA = sbt("mega", [128, 45056], F32)
        Mf = MEGA[:]
        Mb = MEGA[:].bitcast(BF16)
        Mi = MEGA[:].bitcast(I32)

        def mf(off, n, p=128):
            return Mf[0:p, off // 4: off // 4 + n]

        def mb(off, n, p=128):
            return Mb[0:p, off // 2: off // 2 + n]

        def mi(off, n, p=128):
            return Mi[0:p, off // 4: off // 4 + n]

        sb_fx = sbt("fx", [128, L], F32)
        sb_A = sbt("a_sb", [128, NDI * DS], F32)
        sb_D = sbt("d_sb", [128, NDI], F32)
        sb_cb = sbt("cb_sb", [128, NDI], F32)
        sb_dtb = sbt("dtb_sb", [128, NDI], F32)
        sb_cw = sbt("cw_sb", [128, NDI * 4], F32)
        sb_pidx = sbt("pidx", [128, 1], F32)
        sb_srec = sbt("srec", [128, NTT], F32)
        sb_cols = sbt("cols", [128, 16], F32)
        sb_idf = sbt("idf", [128, 128], F32)
        sb_idb = sbt("idb", [128, 128], BF16)
        sb_hcar = sbt("hcar", [128, NDI * DS], F32)
        sb_halo = sbt("halo", [128, NDI * 4], F32)
        sb_Bbf = sbt("bbf", [16, TC], BF16)
        sb_Cbf = sbt("cbf", [16, TC], BF16)
        sb_u = sbt("u_sb", [128, 256], BF16)
        sb_u16 = sbt("u16_sb", [128, 16], F32)
        sb_s2b = sbt("s2b", [128, 512], F32)
        sb_fdb = sbt("fdb", [128, 512], F32)
        sb_wo4 = sbt("wo4", [128, 4096], BF16)

        psum = [es.enter_context(nc.psum_tensor("pb%d" % i, [128, 512], F32))
                for i in range(8)]

        p = Prog(nc)
        bufs = {}

        def B(name):
            if name not in bufs:
                bufs[name] = Buf()
            return bufs[name]

        def newB(name, init=()):
            bufs[name] = Buf(init)
            return bufs[name]

        pbuf = [B("psum%d" % i) for i in range(8)]

        C_SWG, C_SWU, C_SWD, C_THR, C_NTHR, C_T0, C_T1, C_T2 = range(8)

        def col(i):
            return sb_cols[:, i:i + 1]

        p.add("gpsimd", lambda g: g.iota(
            sb_idf[:], [[1, 128]], channel_multiplier=0,
            allow_small_or_imprecise_dtypes=True), writes=[B("idf")])
        p.add("gpsimd", lambda g: g.iota(
            sb_pidx[:], [[0, 1]], channel_multiplier=1,
            allow_small_or_imprecise_dtypes=True), writes=[B("pidx")])
        p.add("vector", lambda v: v.tensor_scalar(
            sb_idf[:], sb_idf[:], sb_pidx[:], None, ALU.is_equal),
            reads=[B("pidx")], writes=[B("idf")])
        p.add("vector", lambda v: v.tensor_copy(sb_idb[:], sb_idf[:]),
              reads=[B("idf")], writes=[B("idb")])

        p.dma("sync", sb_A[:], i_alog.ap().rearrange("(k p) n -> p k n", p=128),
              writes=[B("A")])
        p.add("scalar", lambda s: s.activation(sb_A[:], sb_A[:], ACT.Exp),
              writes=[B("A")])
        p.add("vector", lambda v: v.tensor_scalar_mul(sb_A[:], sb_A[:], -1.0),
              writes=[B("A")])
        p.dma("sync", sb_D[:], i_dp.ap().rearrange("o (k p) -> (o p) k", p=128), slow=True,
              writes=[B("D")])
        p.dma("sync", sb_cb[:], i_cb.ap().rearrange("o (k p) -> (o p) k", p=128), slow=True,
              writes=[B("cb")])
        p.dma("sync", sb_dtb[:], i_dtb.ap().rearrange("o (k p) -> (o p) k", p=128), slow=True,
              writes=[B("dtb")])
        p.dma("sync", sb_cw[:], i_cw.ap().rearrange("(k p) t -> p k t", p=128),
              writes=[B("cw")])
        p.add("vector", lambda v: v.memset(sb_hcar[:], 0.0), writes=[B("hcar")])
        p.add("vector", lambda v: v.memset(sb_halo[:], 0.0), writes=[B("halo")])

        O_BASE = 0
        O_XN2F = 65536
        O_XQF = 98304
        O_RA = 131072
        O_RB = 135168
        O_RC = 139264
        O_RD = 143360
        O_WQA = 147456
        O_WQB = 151552
        O_WQQ = 155648
        O_WQT = 157696
        O_NWB = 167936
        O_NBB = 172032

        base_fm = mf(O_BASE, 8 * 2048)
        xn2_fm = mb(O_XN2F, 8 * 2048)
        xq_fm = mb(O_XQF, 8 * 2048)
        nwb = mf(O_NWB, 1024)
        nbb = mf(O_NBB, 1024)
        b_base = newB("r_base")
        b_xn2f = newB("r_xn2f")
        b_xqf = newB("r_xqf")
        b_nwb = newB("r_nwb")
        b_nbb = newB("r_nbb")

        def bcast_rows(dst_row_ap_fn, first_in, n_parts, reads, bname):
            st = p.dma("sync", dst_row_ap_fn(0, 1), first_in,
                       reads=reads, writes=[B(bname)])
            pc = 1
            while pc < n_parts:
                st = p.dma("sync", dst_row_ap_fn(pc, min(pc, n_parts - pc)),
                           dst_row_ap_fn(0, min(pc, n_parts - pc)),
                           extra=[st])
                pc *= 2
            B(bname).writers.append(st)
            return st

        bcast_rows(lambda p0, n: Mf[p0:p0 + n, O_NWB // 4:O_NWB // 4 + 1024],
                   i_nw.ap(), 128, [], "r_nwb")
        bcast_rows(lambda p0, n: Mf[p0:p0 + n, O_NBB // 4:O_NBB // 4 + 1024],
                   i_nb.ap(), 128, [], "r_nbb")

        def transpose_to_fm(src_ap, src_buf, ident_ap, ident_buf,
                            dst_half_off, dst_is_bf16, dst_buf, tt):
            for half in range(2):
                bank = 6 + half
                pview = psum[bank][:].bitcast(BF16) if dst_is_bf16 \
                    else psum[bank][:]

                def em_t(t, src_ap=src_ap, half=half, pview=pview,
                         ident_ap=ident_ap):
                    r = None
                    for q in range(4):
                        j = half * 4 + q
                        r = t.transpose(pview[:, q * 128:(q + 1) * 128],
                                        src_ap[:, j * 128:(j + 1) * 128],
                                        ident_ap)
                    return r
                p.add("tensor", em_t, reads=[src_buf, ident_buf],
                      writes=[pbuf[bank]])
                if dst_is_bf16:
                    dstv = Mb[:, dst_half_off // 2:
                              dst_half_off // 2 + 16384].rearrange(
                        "p (j t) -> p j t", j=8)
                else:
                    dstv = Mf[:, dst_half_off // 4:
                              dst_half_off // 4 + 16384].rearrange(
                        "p (j t) -> p j t", j=8)
                dst = dstv[:, half * 4:half * 4 + 4, tt * 128:tt * 128 + 128]
                psv = pview[:, 0:512].rearrange("p (q t) -> p q t", q=4)
                p.add("scalar", lambda sc, dst=dst, psv=psv: sc.copy(dst, psv),
                      reads=[pbuf[bank]], writes=[dst_buf])

        XO = [131072, 135168, 139264, 143360]
        XNO = [147456, 151552, 155648, 159744]
        JNK = mf(163840, 1024)
        QT = mf(176128, 1024)
        QTb = mb(176128, 1024)
        QTi = mi(176128, 1024)
        bX = [newB("x%d" % i) for i in range(4)]
        bXN = [newB("xn%d" % i) for i in range(4)]
        bJ = newB("jnk")
        bQ = newB("qt")
        CS0, CS1, CS2 = 4, 8, 12
        bc0, bc1, bc2 = newB("cc0"), newB("cc1"), newB("cc2")

        def stats_chain():
            c0 = sb_cols[:, CS0:CS0 + 4]
            c1 = sb_cols[:, CS1:CS1 + 4]
            c2 = sb_cols[:, CS2:CS2 + 4]
            p.add("vector", lambda v: v.tensor_scalar_mul(c0, c0, 1.0 / DM),
                  writes=[bc0], force=True)
            p.add("vector", lambda v: v.tensor_tensor(c2, c0, c0, ALU.mult),
                  reads=[bc0], writes=[bc2], force=True)
            p.add("vector", lambda v: v.tensor_scalar_mul(c1, c1, 1.0 / DM),
                  writes=[bc1], force=True)
            p.add("vector", lambda v: v.tensor_tensor(c1, c1, c2, ALU.subtract),
                  reads=[bc2], writes=[bc1], force=True)
            p.add("vector", lambda v: v.tensor_scalar_add(c1, c1, 1e-5),
                  writes=[bc1], force=True)
            p.add("scalar", lambda s: s.activation(c1, c1, ACT.Sqrt),
                  writes=[bc1], force=True)
            p.add("vector", lambda v: v.reciprocal(c1, c1),
                  writes=[bc1], force=True)

        for rnd in range(4):
            tts = [rnd * 4 + i for i in range(4)]
            p.dma("sync" if rnd % 2 else "gpsimd",
                  Mf[:, XO[0] // 4:XO[0] // 4 + 4096].rearrange(
                      "p (r c) -> p r c", r=4),
                  i_x.ap()[rnd * 512:(rnd + 1) * 512, :].rearrange(
                      "(r p) c -> p r c", p=128),
                  writes=[bX[0], bX[1], bX[2], bX[3]])
            for i, tt in enumerate(tts):
                X = mf(XO[i], 1024)
                p.add("vector", lambda v, X=X, i=i: v.reduce_sum(
                    sb_cols[:, CS0 + i:CS0 + i + 1], X, AX.X),
                    reads=[bX[i]], writes=[bc0])
                p.add("scalar", lambda s, X=X, i=i: s.activation(
                    JNK, X, ACT.Square,
                    accum_out=sb_cols[:, CS1 + i:CS1 + i + 1]),
                    reads=[bX[i]], writes=[bJ, bc1])
            stats_chain()
            for i, tt in enumerate(tts):
                X = mf(XO[i], 1024)
                XN = mf(XNO[i], 1024)
                p.add("vector", lambda v, X=X, XN=XN, i=i: v.tensor_scalar(
                    XN, X, sb_cols[:, CS0 + i:CS0 + i + 1],
                    sb_cols[:, CS1 + i:CS1 + i + 1], ALU.subtract, ALU.mult),
                    reads=[bX[i]], writes=[bXN[i]], ptr_reads=[bc0, bc1])
                p.add("vector", lambda v, X=X, XN=XN: v.tensor_tensor(
                    X, X, XN, ALU.add), reads=[bXN[i]], writes=[bX[i]])
                transpose_to_fm(X, bX[i], sb_idf[:], B("idf"), O_BASE, False,
                                b_base, tt)
                p.add("vector", lambda v, XN=XN, i=i: v.reduce_sum(
                    sb_cols[:, CS0 + i:CS0 + i + 1], XN, AX.X),
                    reads=[bXN[i]], writes=[bc0])
                p.add("scalar", lambda s, XN=XN, i=i: s.activation(
                    JNK, XN, ACT.Square,
                    accum_out=sb_cols[:, CS1 + i:CS1 + i + 1]),
                    reads=[bXN[i]], writes=[bJ, bc1])
            stats_chain()
            for i, tt in enumerate(tts):
                X = mf(XO[i], 1024)
                Xb = mb(XO[i], 1024)
                XN = mf(XNO[i], 1024)
                p.add("vector", lambda v, X=X, XN=XN, i=i: v.tensor_scalar(
                    X, XN, sb_cols[:, CS0 + i:CS0 + i + 1],
                    sb_cols[:, CS1 + i:CS1 + i + 1], ALU.subtract, ALU.mult),
                    reads=[bXN[i]], writes=[bX[i]], ptr_reads=[bc0, bc1])
                p.add("vector", lambda v, X=X: v.tensor_tensor(
                    X, X, nwb, ALU.mult), reads=[b_nwb], writes=[bX[i]])
                p.add("vector", lambda v, X=X, Xb=Xb: v.tensor_tensor(
                    Xb, X, nbb, ALU.add), reads=[b_nbb], writes=[bX[i]])
                transpose_to_fm(Xb, bX[i], sb_idb[:], B("idb"), O_XN2F, True,
                                b_xn2f, tt)
                p.add("vector", lambda v, XN=XN, i=i: v.tensor_reduce(
                    sb_cols[:, CS0 + i:CS0 + i + 1], XN, AX.X, ALU.max,
                    apply_absolute_value=True),
                    reads=[bXN[i]], writes=[bc0])
            c0 = sb_cols[:, CS0:CS0 + 4]
            c1 = sb_cols[:, CS1:CS1 + 4]
            p.add("vector", lambda v: v.tensor_scalar_max(c0, c0, 1e-5),
                  writes=[bc0], force=True)
            p.add("vector", lambda v, rnd=rnd: v.tensor_scalar_mul(
                sb_srec[:, rnd * 4:rnd * 4 + 4], c0, 1.0 / 127.0),
                reads=[bc0], writes=[B("srec")], force=True)
            p.add("vector", lambda v: v.reciprocal(c1, c0),
                  reads=[bc0], writes=[bc1], force=True)
            p.add("vector", lambda v: v.tensor_scalar_mul(c1, c1, 127.0),
                  writes=[bc1], force=True)
            for i, tt in enumerate(tts):
                XN = mf(XNO[i], 1024)
                XNi = mi(XNO[i], 1024)
                Xb2 = mb(XO[i], 1024)
                p.add("vector", lambda v, XN=XN, XNi=XNi, i=i: v.tensor_scalar(
                    XNi, XN, sb_cols[:, CS1 + i:CS1 + i + 1], None, ALU.mult),
                    reads=[bXN[i]], writes=[bXN[i]], ptr_reads=[bc1])
                p.add("vector", lambda v, XNi=XNi, Xb2=Xb2: v.tensor_copy(
                    Xb2, XNi), reads=[bXN[i]], writes=[bX[i]])
                transpose_to_fm(Xb2, bX[i], sb_idb[:], B("idb"), O_XQF, True,
                                b_xqf, tt)

        p.add("tensor", lambda t: t.transpose(
            psum[6][0:16, 0:128], sb_srec[:], sb_idf[:]),
            reads=[B("srec"), B("idf")], writes=[pbuf[6]])
        p.add("scalar", lambda sc: sc.copy(
            Mf[0:16, O_RA // 4:O_RA // 4 + 128], psum[6][0:16, 0:128]),
            reads=[pbuf[6]], writes=[bX[0]])
        bcast_rows(lambda p0, n: sb_fx[p0:p0 + n, :],
                   Mf[0:16, O_RA // 4:O_RA // 4 + 128], 128, [bX[0]],
                   "fx")
        b_fx = B("fx")

        st_pb = p.dma("sync", h_base.ap().rearrange("(j p) t -> p j t", p=128),
                      base_fm.rearrange("p (j t) -> p j t", j=8),
                      reads=[b_base], writes=[B("h_base")])
        st_p2 = p.dma("gpsimd",
                      h_xn2.ap().rearrange("(j p) t -> p j t", p=128),
                      xn2_fm.rearrange("p (j t) -> p j t", j=8),
                      reads=[b_xn2f], writes=[B("h_xn2")])
        st_pq = p.dma("gpsimd",
                      h_xq.ap().rearrange("(j p) t -> p j t", p=128),
                      xq_fm.rearrange("p (j t) -> p j t", j=8),
                      reads=[b_xqf], writes=[B("h_xq")])

        O_WQA2, O_WQB2 = 32768, 40960
        O_WP1A, O_WP1B = 32768, 49152
        O_WP1C = 65536
        O_WQS1, O_WQS2 = 49152, 53248
        O_WQQ2 = 57344
        O_WQT2 = 61440
        O_WQTQ = 57344

        def wquant(src, dst, n_row_tiles, ncols, csw, tag):
            b_wqtmp = B("wqtmp")
            wa1 = [B("wp1_a"), B("wp1_b"), B("wp1_c")]
            wa = [B("wq_a"), B("wq_b")]
            p.add("vector", lambda v: v.memset(col(C_T0), 0.0), writes=[B("c0")])
            it = 0
            if ncols >= 2048:
                r1step, p1w = 1, ncols
            else:
                r1step, p1w = 4, 4 * ncols
            for rt0 in range(0, n_row_tiles, r1step):
                off = (O_WP1A, O_WP1B, O_WP1C)[it % 3]
                wt = mf(off, p1w)
                if r1step == 4:
                    inap = src[rt0 * 128:(rt0 + 4) * 128, :].rearrange(
                        "(p q) c -> p (q c)", p=128)
                else:
                    inap = src[rt0 * 128:(rt0 + 1) * 128, :]
                wrt = [wa1[it % 3]] + ([B("wqq"), B("wqt"), B("wqs")]
                                       if it % 3 == 1 else [])
                if it % 3 == 2:
                    wrt += [b_xn2f]
                if it < 2:
                    wrt += [B("wq_a"), B("wq_b")]
                p.dma(("gpsimd", "sync", "scalar")[it % 3], wt, inap,
                      writes=wrt)
                p.add("vector", lambda v, wt=wt: v.tensor_reduce(
                    col(C_T1), wt, AX.X, ALU.add,
                    apply_absolute_value=True),
                    reads=[wa1[it % 3]], writes=[B("c1")])
                p.add("vector", lambda v: v.tensor_tensor(
                    col(C_T0), col(C_T0), col(C_T1), ALU.add),
                    reads=[B("c1")], writes=[B("c0")], force=True)
                it += 1
                yield True
            FQ = O_WQTQ // 4
            f0 = Mf[0:16, FQ:FQ + 8]
            st1 = p.dma("sync", f0, col(C_T0), reads=[B("c0")],
                        writes=[b_wqtmp, B("wqq")])
            p.add("vector", lambda v: v.tensor_tensor(
                Mf[0:16, FQ:FQ + 4], Mf[0:16, FQ:FQ + 4],
                Mf[0:16, FQ + 4:FQ + 8], ALU.add),
                writes=[b_wqtmp], force=True)
            p.add("vector", lambda v: v.tensor_tensor(
                Mf[0:16, FQ:FQ + 2], Mf[0:16, FQ:FQ + 2],
                Mf[0:16, FQ + 2:FQ + 4], ALU.add),
                writes=[b_wqtmp], force=True)
            p.add("vector", lambda v: v.tensor_tensor(
                Mf[0:16, FQ:FQ + 1], Mf[0:16, FQ:FQ + 1],
                Mf[0:16, FQ + 1:FQ + 2], ALU.add),
                writes=[b_wqtmp], force=True)
            p.dma("sync", Mf[0:1, FQ + 8:FQ + 24], Mf[0:16, FQ:FQ + 1],
                  reads=[b_wqtmp], writes=[b_wqtmp])
            p.add("vector", lambda v: v.tensor_reduce(
                Mf[0:1, FQ:FQ + 1], Mf[0:1, FQ + 8:FQ + 24], AX.X, ALU.add),
                writes=[b_wqtmp], force=True)
            nel = float(n_row_tiles * 128 * ncols)
            p.add("vector", lambda v: v.tensor_scalar(
                Mf[0:1, FQ:FQ + 1], Mf[0:1, FQ:FQ + 1], 1.0 / nel, 1e-5,
                ALU.mult, ALU.max), writes=[b_wqtmp], force=True)
            bcast_rows(lambda p0, n: sb_cols[p0:p0 + n, csw:csw + 1],
                       Mf[0:1, FQ:FQ + 1], 128, [b_wqtmp], tag + "_sw")
            bsw = B(tag + "_sw")
            p.add("vector", lambda v: v.tensor_scalar_mul(
                col(C_THR), col(csw), 0.5), reads=[bsw], writes=[B("thr")],
                force=True)
            p.add("vector", lambda v: v.tensor_scalar_mul(
                col(C_NTHR), col(C_THR), -1.0), reads=[B("thr")],
                writes=[B("nthr")], force=True)
            p.add("vector", lambda v: v.tensor_scalar_mul(
                col(csw), col(csw), 0.5), writes=[bsw], force=True)
            yield True
            it = 0
            CCW = min(2048, ncols)
            ncc = ncols // CCW
            rstep = 1 if ncols >= 2048 else 2
            for rt0 in range(0, n_row_tiles, rstep):
                for cc in range(ncc):
                    off = O_WQA2 if it % 2 == 0 else O_WQB2
                    wt = mf(off, 2048)
                    if rstep == 2:
                        inap = src[rt0 * 128:(rt0 + 2) * 128, :].rearrange(
                            "(r p) c -> p r c", p=128)
                        outap = wt.rearrange("p (r c) -> p r c", r=2)
                    else:
                        inap = src[rt0 * 128:(rt0 + 1) * 128,
                                   cc * 2048:(cc + 1) * 2048]
                        outap = wt
                    wrt = [wa[it % 2]] + ([wa1[0], wa1[1]] if it < 2 else [])
                    p.dma(("gpsimd", "sync", "scalar")[it % 3], outap, inap,
                          writes=wrt)
                    s1 = mb(O_WQS1, 2048)
                    s2 = mb(O_WQS2, 2048)
                    wq = mb(O_WQQ2, 2048)
                    p.add("scalar", lambda sc, wt=wt, s1=s1: sc.activation(
                        s1, wt, ACT.Sign, bias=col(C_NTHR)),
                        reads=[wa[it % 2]], writes=[B("wqs")],
                        ptr_reads=[B("nthr")])
                    p.add("scalar", lambda sc, wt=wt, s2=s2: sc.activation(
                        s2, wt, ACT.Sign, bias=col(C_THR)),
                        reads=[wa[it % 2]], writes=[B("wqs")],
                        ptr_reads=[B("thr")])
                    p.add("vector", lambda v, wq=wq, s1=s1, s2=s2:
                          v.tensor_tensor(wq, s1, s2, ALU.add),
                          reads=[B("wqs")], writes=[B("wqq")])
                    for half in range(4):
                        bank = 6 + half % 2
                        pview = psum[bank][:].bitcast(BF16)

                        def em_t(t, wq=wq, half=half, pview=pview):
                            r = None
                            for q in range(4):
                                jb = half * 4 + q
                                r = t.transpose(
                                    pview[:, q * 128:(q + 1) * 128],
                                    wq[:, jb * 128:(jb + 1) * 128], sb_idb[:])
                            return r
                        p.add("tensor", em_t, reads=[B("wqq"), B("idb")],
                              writes=[pbuf[bank]])
                        o = O_WQT2 // 2 + half * 512
                        p.add("vector", lambda v, o=o, pview=pview:
                              v.tensor_copy(Mb[:, o:o + 512], pview[:, 0:512]),
                              reads=[pbuf[bank]], writes=[B("wqt")])

                    if rstep == 2:
                        for r in range(2):
                            def em_d(eng, rt0=rt0, r=r):
                                jb = rt0 + r
                                return eng.dma_start(
                                    dst[jb * 1024:(jb + 1) * 1024,
                                        :].rearrange(
                                        "(k p) c -> p k c", p=128),
                                    Mb[:, O_WQT2 // 2 + r * 1024:
                                       O_WQT2 // 2 + r * 1024 + 1024].rearrange(
                                        "p (k c) -> p k c", c=128))
                            p.add("sync", em_d, reads=[B("wqt")],
                                  writes=[B(tag + "park")], dma=True)
                    else:
                        def em_d(eng, rt0=rt0, cc=cc):
                            r0 = rt0 * 4096 + cc * 2048
                            return eng.dma_start(
                                dst[r0:r0 + 2048].rearrange(
                                    "(k p) c -> p k c", p=128),
                                Mb[:, O_WQT2 // 2:O_WQT2 // 2 + 2048].rearrange(
                                    "p (k c) -> p k c", c=128))
                        p.add("sync", em_d, reads=[B("wqt")],
                              writes=[B(tag + "park")], dma=True)
                    it += 1
                    yield True

        for ch in range(NCH):
            T0 = ch * TC
            bar = p.barrier()

            xn2s = mb(131072, 8 * 1024)
            b_xn2s = newB("xn2s", bar)
            if ch == 0:
                p.add("vector", lambda v: v.tensor_copy(
                    xn2s.rearrange("p (k t) -> p k t", k=8),
                    xn2_fm.rearrange("p (j t) -> p j t", j=8)[:, :, T0:T0 + TC]
                    if False else
                    xn2_fm.rearrange("p (j t) -> p j t", j=8)[:, :, 0:TC]),
                    reads=[b_xn2f], writes=[b_xn2s])
            else:
                p.dma("sync", xn2s.rearrange("p (k t) -> p k t", k=8),
                      h_xn2.ap()[:, T0:T0 + TC].rearrange(
                          "(k p) t -> p k t", p=128),
                      reads=[B("h_xn2")], writes=[b_xn2s])
            xbres = mb(98304, 16 * 1024)
            b_xb = newB("xbres", bar)
            b_xbpre = newB("xbpre", bar)
            b_convt = newB("convt", bar)
            w1b_ = [newB("w1a", bar), newB("w1b", bar)]
            W1O = [172544, 174848]
            xbpre = mf(163840, 1028)
            convt = mf(168192, 1024)
            convtb = mb(168192, 1024)
            b_xbpark = B("xb_park")
            b_zpark = B("z_park")
            if ch == 0:
                for _nm in ("wq_a", "wq_b", "wp1_a", "wp1_b", "wp1_c",
                            "wqtmp", "wqq", "wqt", "wqs"):
                    newB(_nm, bar)

                def _wq_all():
                    yield from wquant(i_wg.ap(), h_wqg.ap(), INNER // 128,
                                      DM, C_SWG, "wg")
                    yield from wquant(i_wu.ap(), h_wqu.ap(), INNER // 128,
                                      DM, C_SWU, "wu")
                    yield from wquant(i_wd.ap(), h_wqd.ap(), DM // 128,
                                      INNER, C_SWD, "wd")
                wqgen = _wq_all()
            else:
                wqgen = iter(())

            def wq_step(n=1):
                for _ in range(n):
                    if next(wqgen, None) is None:
                        break
            for j in range(32):
                wt = mb(W1O[j % 2], 8 * 128)
                p.dma("gpsimd", wt.rearrange("p (k c) -> p k c", c=128),
                      i_w1.ap()[:, j * 128:(j + 1) * 128].rearrange(
                          "(k p) c -> p k c", p=128),
                      writes=[w1b_[j % 2]])
                for th in range(2):
                    bank = (j * 2 + th) % 4

                    def em_m(t, wt=wt, th=th, bank=bank):
                        r = None
                        for k in range(8):
                            r = t.matmul(
                                psum[bank][:],
                                wt.rearrange("p (k c) -> p k c", c=128)[:, k, :],
                                xn2s[:, k * 1024 + th * 512:
                                     k * 1024 + th * 512 + 512],
                                start=(k == 0), stop=(k == 7))
                        return r
                    p.add("tensor", em_m, reads=[w1b_[j % 2], b_xn2s],
                          writes=[pbuf[bank]])
                    if j < 16:
                        p.add("vector", lambda v, th=th, bank=bank:
                              v.tensor_copy(
                                  xbpre[:, 4 + th * 512:4 + th * 512 + 512],
                                  psum[bank][:]),
                              reads=[pbuf[bank]], writes=[b_xbpre])
                    else:
                        p.add("vector", lambda v, th=th, bank=bank:
                              v.tensor_copy(
                                  convtb[:, th * 512:th * 512 + 512],
                                  psum[bank][:]),
                              reads=[pbuf[bank]], writes=[b_convt])
                if j < 16:
                    p.add("vector", lambda v, j=j: v.tensor_copy(
                        xbpre[:, 1:4], sb_halo[:, j * 4 + 1:j * 4 + 4]),
                        reads=[B("halo")], writes=[b_xbpre])
                    p.add("vector", lambda v, j=j: v.tensor_copy(
                        sb_halo[:, j * 4 + 1:j * 4 + 4], xbpre[:, 1025:1028]),
                        reads=[b_xbpre], writes=[B("halo")], force=True)
                    p.add("vector", lambda v, j=j: v.tensor_scalar(
                        convt, xbpre[:, 1:1025], sb_cw[:, j * 4:j * 4 + 1],
                        None, ALU.mult),
                        reads=[b_xbpre, B("cw")], writes=[b_convt],
                        force=True)
                    for k in range(1, 4):
                        p.add("vector", lambda v, j=j, k=k:
                              v.scalar_tensor_tensor(
                                  convt, xbpre[:, 1 + k:1 + k + 1024],
                                  sb_cw[:, j * 4 + k:j * 4 + k + 1], convt,
                                  ALU.mult, ALU.add),
                              reads=[b_xbpre, B("cw")], writes=[b_convt])
                    xbt = xbres[:, j * 1024:(j + 1) * 1024]
                    p.add("scalar", lambda sc, j=j, xbt=xbt: sc.activation(
                        xbt, convt, ACT.Silu, bias=sb_cb[:, j:j + 1]),
                        reads=[b_convt, B("cb")], writes=[b_xb])
                else:
                    jz = j - 16
                    p.dma("sync", h_z.ap()[jz * 128:(jz + 1) * 128, T0:T0 + TC],
                          convtb, reads=[b_convt], writes=[b_zpark])
                wq_step(1)

            p.dma("sync",
                  h_xb.ap()[:, T0:T0 + TC].rearrange("(k p) t -> p k t", p=128),
                  xbres.rearrange("p (k t) -> p k t", k=16),
                  reads=[b_xb], writes=[b_xbpark])

            bar2 = p.barrier()
            delta_in = mb(0, 16 * 1024)
            b_din = newB("din", bar2)
            wx_ = [newB("wxa", bar2), newB("wxb", bar2)]
            WXO = [163840, 168192]
            bc_bufs = {}

            def emit_bc():
                barBC = p.barrier()
                b_bfree = newB("bfree", barBC)
                b_cfree = newB("cfree", barBC)
                wqtags = [B(nm) for nm in ("wq_a", "wq_b", "wp1_a", "wp1_b",
                                           "wp1_c", "wqq", "wqt", "wqs",
                                           "wqtmp")]
                for s_i in range(NSUB):
                    st = p.dma("sync",
                               Mb[0:1, 32768 // 2 + s_i * 4096:
                                  32768 // 2 + (s_i + 1) * 4096],
                               sb_Bbf[0:16, s_i * SC:(s_i + 1) * SC],
                               reads=[B("Bbf")], extra=barBC,
                               writes=(wqtags if s_i == 0 else ()))
                    b_bfree.writers.append(st)
                    st = p.dma("gpsimd",
                               Mb[0:1, 65536 // 2 + s_i * 4096:
                                  65536 // 2 + (s_i + 1) * 4096],
                               sb_Cbf[0:16, s_i * SC:(s_i + 1) * SC],
                               reads=[B("Cbf")], extra=barBC,
                               writes=(wqtags + [b_xn2f] if s_i == 0 else ()))
                    b_cfree.writers.append(st)
                pc = 1
                stb = list(b_bfree.writers)
                while pc < 128:
                    stb = [p.dma("sync",
                                 Mb[pc:2 * pc, 32768 // 2:32768 // 2 + 16384],
                                 Mb[0:pc, 32768 // 2:32768 // 2 + 16384],
                                 extra=stb)]
                    pc *= 2
                b_bfree.writers.extend(stb)
                pc = 1
                stc = list(b_cfree.writers)
                while pc < 128:
                    stc = [p.dma("gpsimd",
                                 Mb[pc:2 * pc, 65536 // 2:65536 // 2 + 16384],
                                 Mb[0:pc, 65536 // 2:65536 // 2 + 16384],
                                 extra=stc)]
                    pc *= 2
                b_cfree.writers.extend(stc)
                bc_bufs["b"] = b_bfree
                bc_bufs["c"] = b_cfree
            for j in [16] + list(range(16)):
                wdt_c = 128 if j < 16 else 32
                wt = mb(WXO[j % 2], 16 * wdt_c)
                p.dma("gpsimd", wt.rearrange("p (k c) -> p k c", c=wdt_c),
                      i_wx.ap()[:, j * 128:j * 128 + wdt_c].rearrange(
                          "(k p) c -> p k c", p=128),
                      writes=[wx_[j % 2]])
                for th in range(2):
                    bank = (j * 2 + th) % 4
                    if j < 16:
                        def em_m(t, wt=wt, th=th, bank=bank):
                            r = None
                            for k in range(16):
                                r = t.matmul(
                                    psum[bank][:],
                                    wt.rearrange("p (k c) -> p k c",
                                                 c=128)[:, k, :],
                                    xbres[:, k * 1024 + th * 512:
                                          k * 1024 + th * 512 + 512],
                                    start=(k == 0), stop=(k == 15))
                            return r
                        p.add("tensor", em_m, reads=[wx_[j % 2], b_xb],
                              writes=[pbuf[bank]])
                        p.add("vector", lambda v, j=j, th=th, bank=bank:
                              v.tensor_copy(
                                  delta_in[:, j * 1024 + th * 512:
                                           j * 1024 + th * 512 + 512],
                                  psum[bank][:]),
                              reads=[pbuf[bank]], writes=[b_din])
                    else:
                        bank_c = (bank + 2) % 4

                        def em_mb(t, wt=wt, th=th, bank=bank):
                            r = None
                            for k in range(16):
                                r = t.matmul(
                                    psum[bank][0:16, 0:512],
                                    wt.rearrange("p (k c) -> p k c",
                                                 c=32)[:, k, 0:16],
                                    xbres[:, k * 1024 + th * 512:
                                          k * 1024 + th * 512 + 512],
                                    start=(k == 0), stop=(k == 15))
                            return r
                        p.add("tensor", em_mb, reads=[wx_[j % 2], b_xb],
                              writes=[pbuf[bank]])

                        def em_mc(t, wt=wt, th=th, bank_c=bank_c):
                            r = None
                            for k in range(16):
                                r = t.matmul(
                                    psum[bank_c][0:16, 0:512],
                                    wt.rearrange("p (k c) -> p k c",
                                                 c=32)[:, k, 16:32],
                                    xbres[:, k * 1024 + th * 512:
                                          k * 1024 + th * 512 + 512],
                                    start=(k == 0), stop=(k == 15))
                            return r
                        p.add("tensor", em_mc, reads=[wx_[j % 2], b_xb],
                              writes=[pbuf[bank_c]])
                        p.add("scalar", lambda sc, th=th, bank=bank: sc.copy(
                            sb_Bbf[:, th * 512:th * 512 + 512],
                            psum[bank][0:16, 0:512]),
                            reads=[pbuf[bank]], writes=[B("Bbf")])
                        p.add("scalar", lambda sc, th=th, bank_c=bank_c:
                              sc.copy(sb_Cbf[:, th * 512:th * 512 + 512],
                                      psum[bank_c][0:16, 0:512]),
                              reads=[pbuf[bank_c]], writes=[B("Cbf")])
                if j == 16 and ch != 0:
                    emit_bc()
                wq_step(3)

            wq_step(10 ** 6)
            if ch == 0:
                emit_bc()

            bar3 = p.barrier()
            b_bfree = bc_bufs["b"]
            b_cfree = bc_bufs["c"]

            dAs = [mf(98304, 4096), mf(114688, 4096)]
            b_dAs = [newB("dA0", bar3), newB("dA1", bar3)]
            dbx = mb(131072, 4096)
            dbx3 = dbx.rearrange("p (n t) -> p n t", n=DS)
            b_dBx = newB("dBx", bar3)
            hb = mb(139264, 4096)
            hb3 = hb.rearrange("p (n t) -> p n t", n=DS)
            b_h = newB("hb", bar3)
            deltas = [mf(147456, 1024), mf(151552, 1024)]
            b_deltas = [newB("d0", bar3), newB("d1", bar3)]
            sptmp = mf(155648, 1024)
            b_sp = newB("sptmp", bar3)
            xbsls = [mb(159744, 1024), mb(165888, 1024)]
            b_xbsls = [newB("xbsl0", bar3), newB("xbsl1", bar3)]
            zsls = [mb(161792, 1024), mb(169984, 1024)]
            b_zsls = [newB("zsl0", bar3), newB("zsl1", bar3)]
            tnh = mb(163840, 1024)
            b_tnh = newB("tnh", bar3)
            ytmp = mb(167936, 1024)
            b_ytmp = newB("ytmp", bar3)
            wd_ = [newB("wda", bar3), newB("wdb", bar3)]
            WDO = [172032, 176128]
            b_ypark = B("y_park")

            def prep_dt(dt):
                par = dt % 2
                wt = mb(WDO[par], 16 * 128)
                p.dma("gpsimd", wt.rearrange("p (k c) -> p k c", c=128),
                      i_wdt.ap()[:, dt * 128:(dt + 1) * 128].rearrange(
                          "(k p) c -> p k c", p=128),
                      writes=[wd_[par]])
                p.dma("sync", xbsls[par],
                      h_xb.ap()[dt * 128:(dt + 1) * 128, T0:T0 + TC],
                      reads=[b_xbpark], writes=[b_xbsls[par]])
                p.dma("sync", zsls[par],
                      h_z.ap()[dt * 128:(dt + 1) * 128, T0:T0 + TC],
                      reads=[b_zpark], writes=[b_zsls[par]])
                delta = deltas[par]
                for th in range(2):
                    bank = 4 + th

                    def em_m(t, wt=wt, th=th, bank=bank):
                        r = None
                        for k in range(16):
                            r = t.matmul(
                                psum[bank][:],
                                wt.rearrange("p (k c) -> p k c", c=128)[:, k, :],
                                delta_in[:, k * 1024 + th * 512:
                                         k * 1024 + th * 512 + 512],
                                start=(k == 0), stop=(k == 15))
                        return r
                    p.add("tensor", em_m, reads=[wd_[par], b_din],
                          writes=[pbuf[bank]])
                    p.add("scalar", lambda s, th=th, bank=bank, dt=dt,
                          delta=delta: s.activation(
                              delta[:, th * 512:th * 512 + 512], psum[bank][:],
                              ACT.Identity, bias=sb_dtb[:, dt:dt + 1]),
                          reads=[pbuf[bank], B("dtb")], writes=[b_deltas[par]])
                p.add("scalar", lambda s, delta=delta: s.activation(
                    sptmp, delta, ACT.Abs), reads=[b_deltas[par]],
                    writes=[b_sp])
                p.add("scalar", lambda s: s.activation(
                    sptmp, sptmp, ACT.Exp, scale=-1.0), writes=[b_sp])
                p.add("scalar", lambda s: s.activation(
                    sptmp, sptmp, ACT.Ln, bias=1.0), writes=[b_sp])
                p.add("scalar", lambda s, delta=delta: s.activation(
                    delta, delta, ACT.Relu), writes=[b_deltas[par]])
                p.add("vector", lambda v, delta=delta: v.tensor_tensor(
                    delta, delta, sptmp, ALU.add), reads=[b_sp],
                    writes=[b_deltas[par]])

            prep_dt(0)
            for dt in range(NDI):
                par = dt % 2
                delta = deltas[par]
                xbsl = xbsls[par]
                if dt + 1 < NDI:
                    prep_dt(dt + 1)
                hsl = sb_hcar[:, dt * DS:(dt + 1) * DS]
                for s_i in range(NSUB):
                    g_par = (dt * NSUB + s_i) % 2
                    dA = dAs[g_par]
                    dA3 = dA.rearrange("p (n t) -> p n t", n=DS)
                    b_dA = b_dAs[g_par]
                    bfs3 = mb(32768 + s_i * 8192, 4096).rearrange(
                        "p (n t) -> p n t", n=DS)
                    cfs3 = mb(65536 + s_i * 8192, 4096).rearrange(
                        "p (n t) -> p n t", n=DS)

                    def em_da(sc, dt=dt, s_i=s_i, dA=dA, delta=delta):
                        r = None
                        for n in range(DS):
                            r = sc.activation(
                                dA[:, n * SC:(n + 1) * SC],
                                delta[:, s_i * SC:(s_i + 1) * SC], ACT.Exp,
                                scale=sb_A[:, dt * DS + n:dt * DS + n + 1])
                        return r
                    p.add("scalar", em_da, reads=[b_deltas[par], B("A")],
                          writes=[b_dA])
                    p.add("vector", lambda v, s_i=s_i, delta=delta, xbsl=xbsl:
                          v.tensor_tensor(
                              sb_u[:], delta[:, s_i * SC:(s_i + 1) * SC],
                              xbsl[:, s_i * SC:(s_i + 1) * SC], ALU.mult),
                          reads=[b_deltas[par], b_xbsls[par]], writes=[B("u")])
                    p.add("vector", lambda v, bfs3=bfs3: v.tensor_tensor(
                        dbx3,
                        sb_u[:].unsqueeze(1).broadcast_to([128, DS, SC]),
                        bfs3, ALU.mult),
                        reads=[B("u"), b_bfree], writes=[b_dBx])
                    p.add("vector", lambda v, hsl=hsl, dA3=dA3: v.tensor_tensor(
                        sb_u16[:],
                        dA3[:, :, 0:1].rearrange("p n o -> p (n o)"),
                        hsl, ALU.mult),
                        reads=[b_dA, B("hcar")], writes=[B("u16")], force=True)
                    p.add("vector", lambda v: v.tensor_tensor(
                        dbx3[:, :, 0:1].rearrange("p n o -> p (n o)"),
                        dbx3[:, :, 0:1].rearrange("p n o -> p (n o)"),
                        sb_u16[:], ALU.add), reads=[B("u16")], writes=[b_dBx],
                        force=True)
                    p.add("vector", lambda v, dA3=dA3: v.memset(
                        dA3[:, :, 0:1], 0.0), writes=[b_dA], force=True)
                    p.add("vector", lambda v, dA=dA: v.tensor_tensor_scan(
                        hb, dA, dbx, 0.0, ALU.mult, ALU.add),
                        reads=[b_dBx, b_dA], writes=[b_h], force=True)
                    p.add("vector", lambda v, hsl=hsl: v.tensor_copy(
                        hsl, hb3[:, :, SC - 1:SC].rearrange("p n o -> p (n o)")),
                        reads=[b_h], writes=[B("hcar")], force=True)
                    p.add("vector", lambda v, cfs3=cfs3: v.tensor_tensor(
                        dbx3, hb3, cfs3, ALU.mult),
                        reads=[b_h, b_cfree], writes=[b_dBx])
                    w_ = 2048
                    while w_ >= SC:
                        p.add("vector", lambda v, w_=w_: v.tensor_tensor(
                            dbx[:, 0:w_], dbx[:, 0:w_], dbx[:, w_:2 * w_],
                            ALU.add), writes=[b_dBx])
                        w_ //= 2

                    def em_y(v, s_i=s_i, xbsl=xbsl, dt=dt):
                        return v.scalar_tensor_tensor(
                            ytmp[:, s_i * SC:(s_i + 1) * SC],
                            xbsl[:, s_i * SC:(s_i + 1) * SC],
                            sb_D[:, dt:dt + 1], dbx[:, 0:SC],
                            ALU.mult, ALU.add)
                    p.add("vector", em_y, reads=[b_xbsls[par], B("D"), b_dBx],
                          writes=[b_ytmp])
                p.add("scalar", lambda s, par=par: s.activation(
                    tnh, zsls[par], ACT.Tanh, scale=0.5),
                    reads=[b_zsls[par]], writes=[b_tnh])
                p.add("vector", lambda v: v.tensor_scalar(
                    tnh, tnh, 0.5, 0.5, ALU.mult, ALU.add), writes=[b_tnh])
                p.add("vector", lambda v, par=par: v.tensor_tensor(
                    ytmp[:], ytmp[:], zsls[par], ALU.mult),
                    reads=[b_zsls[par]], writes=[b_ytmp])
                p.add("vector", lambda v: v.tensor_tensor(
                    ytmp[:], ytmp[:], tnh, ALU.mult),
                    reads=[b_tnh], writes=[b_ytmp])
                st = p.dma("sync", h_y.ap()[dt * 128:(dt + 1) * 128, T0:T0 + TC],
                           ytmp[:], reads=[b_ytmp])
                b_ypark.writers.append(st)

            bar5 = p.barrier()
            g_all = mf(0, 16384)
            gq = mb(65536, 16384)
            xqs = mb(131072, 8 * 1024)
            gmax = mf(147456, 512)
            gmin = mf(149504, 512)
            qi32 = mi(151552, 512)
            p1t = mf(153600, 512)
            gtmps = [mf(163840, 512), mf(155648, 512)]
            gtmp2s = [mf(165888, 512), mf(157696, 512)]
            utms = [mf(167936, 512), mf(159744, 512)]
            qtmp = mf(169984, 512)
            b_g = newB("g_all", bar5)
            b_gq = newB("gq", bar5)
            b_xqs = newB("xqs", bar5)
            b_gmax = newB("gmax", bar5)
            b_gmin = newB("gmin", bar5)
            b_qi = newB("qi32", bar5)
            b_p1 = newB("p1t", bar5)
            b_gts = [newB("gtmp0", bar5), newB("gtmp1", bar5)]
            b_gt2s = [newB("gtmp20", bar5), newB("gtmp21", bar5)]
            b_uts = [newB("utm0", bar5), newB("utm1", bar5)]
            b_qt = newB("qtmp", bar5)
            b_fw = [newB("fw0", bar5), newB("fw1", bar5)]
            b_fwd = [newB("fwd0", bar5), newB("fwd1", bar5)]
            b_gqk = [newB("gqk%d" % k2, bar5) for k2 in range(32)]
            FWO = [172032, 176128]
            FWD = [172032, 161792]
            p.dma("sync", xqs.rearrange("p (k t) -> p k t", k=8),
                  h_xq.ap()[:, T0:T0 + TC].rearrange("(p k) t -> p k t", k=8),
                  reads=[B("h_xq")], writes=[b_xqs])
            def p5_production(sub):
                tg0 = T0 + sub * 512
                fxsl = sb_fx[:, tg0:tg0 + 512]
                p.add("vector", lambda v: v.memset(gmax, -3.0e38),
                      writes=[b_gmax])
                p.add("vector", lambda v: v.memset(gmin, 3.0e38),
                      writes=[b_gmin])
                for j in range(32):
                    wgt = mb(FWO[j % 2], 8 * 128)
                    wut = mb(FWO[j % 2] + 2048, 8 * 128)
                    p.dma("gpsimd", wgt.rearrange("p (k c) -> p k c", c=128),
                          h_wqg.ap()[j * 1024:(j + 1) * 1024, :].rearrange(
                              "(p k) c -> p k c", k=8),
                          reads=[B("wgpark")], writes=[b_fw[j % 2]])
                    p.dma("sync", wut.rearrange("p (k c) -> p k c", c=128),
                          h_wqu.ap()[j * 1024:(j + 1) * 1024, :].rearrange(
                              "(p k) c -> p k c", k=8),
                          reads=[B("wupark")], writes=[b_fw[j % 2]])
                    bank_g = (j % 2) * 2
                    bank_u = (j % 2) * 2 + 1

                    def em_mg(t, wgt=wgt, bank_g=bank_g, sub=sub):
                        r = None
                        for k in range(8):
                            r = t.matmul(
                                psum[bank_g][:],
                                wgt.rearrange("p (k c) -> p k c",
                                              c=128)[:, k, :],
                                xqs[:, k * 1024 + sub * 512:
                                    k * 1024 + sub * 512 + 512],
                                start=(k == 0), stop=(k == 7))
                        return r
                    p.add("tensor", em_mg, reads=[b_fw[j % 2], b_xqs],
                          writes=[pbuf[bank_g]])

                    def em_mu(t, wut=wut, bank_u=bank_u, sub=sub):
                        r = None
                        for k in range(8):
                            r = t.matmul(
                                psum[bank_u][:],
                                wut.rearrange("p (k c) -> p k c",
                                              c=128)[:, k, :],
                                xqs[:, k * 1024 + sub * 512:
                                    k * 1024 + sub * 512 + 512],
                                start=(k == 0), stop=(k == 7))
                        return r
                    p.add("tensor", em_mu, reads=[b_fw[j % 2], b_xqs],
                          writes=[pbuf[bank_u]])
                    gtmp = gtmps[j % 2]
                    gtmp2 = gtmp2s[j % 2]
                    utm = utms[j % 2]
                    b_gt = b_gts[j % 2]
                    b_gt2 = b_gt2s[j % 2]
                    b_ut = b_uts[j % 2]
                    p.add("vector", lambda v, bank_g=bank_g, fxsl=fxsl,
                          gtmp=gtmp: v.scalar_tensor_tensor(
                              gtmp, psum[bank_g][:], col(C_SWG), fxsl,
                              ALU.mult, ALU.mult),
                          reads=[pbuf[bank_g], B("wg_sw"), b_fx],
                          writes=[b_gt])
                    p.add("scalar", lambda s, gtmp=gtmp, gtmp2=gtmp2:
                          s.activation(gtmp2, gtmp, ACT.Silu),
                          reads=[b_gt], writes=[b_gt2])
                    p.add("vector", lambda v, bank_u=bank_u, fxsl=fxsl,
                          utm=utm: v.scalar_tensor_tensor(
                              utm, psum[bank_u][:], col(C_SWU), fxsl,
                              ALU.mult, ALU.mult),
                          reads=[pbuf[bank_u], B("wu_sw"), b_fx],
                          writes=[b_ut])
                    p.add("vector", lambda v, j=j, gtmp2=gtmp2, utm=utm:
                          v.tensor_tensor(
                              g_all[:, j * 512:(j + 1) * 512], gtmp2, utm,
                              ALU.mult),
                          reads=[b_gt2, b_ut], writes=[b_g])
                    p.add("vector", lambda v, j=j: v.tensor_tensor(
                        gmax, gmax, g_all[:, j * 512:(j + 1) * 512], ALU.max),
                        reads=[b_g], writes=[b_gmax])
                    p.add("vector", lambda v, j=j: v.tensor_tensor(
                        gmin, gmin, g_all[:, j * 512:(j + 1) * 512], ALU.min),
                        reads=[b_g], writes=[b_gmin])
            def p5_rest(sub):
                tg0 = T0 + sub * 512
                fxsl = sb_fx[:, tg0:tg0 + 512]
                p.add("vector", lambda v: v.tensor_scalar_mul(
                    gmin, gmin, -1.0), writes=[b_gmin])
                p.add("vector", lambda v: v.tensor_tensor(
                    gmax, gmax, gmin, ALU.max), reads=[b_gmin],
                    writes=[b_gmax])
                FB = 65536 // 4
                p.dma("sync", Mf[0:16, FB:FB + 4096], gmax,
                      reads=[b_gmax], writes=[b_gq])
                p.add("vector", lambda v: v.tensor_tensor(
                    Mf[0:16, FB:FB + 2048], Mf[0:16, FB:FB + 2048],
                    Mf[0:16, FB + 2048:FB + 4096], ALU.max), writes=[b_gq])
                p.add("vector", lambda v: v.tensor_tensor(
                    Mf[0:16, FB:FB + 1024], Mf[0:16, FB:FB + 1024],
                    Mf[0:16, FB + 1024:FB + 2048], ALU.max), writes=[b_gq])
                p.add("vector", lambda v: v.tensor_tensor(
                    Mf[0:16, FB:FB + 512], Mf[0:16, FB:FB + 512],
                    Mf[0:16, FB + 512:FB + 1024], ALU.max), writes=[b_gq])
                p.dma("sync", Mf[0:4, FB + 2048:FB + 2048 + 2048],
                      Mf[0:16, FB:FB + 512], reads=[b_gq], writes=[b_gq])
                p.add("vector", lambda v: v.tensor_tensor(
                    Mf[0:4, FB + 2048:FB + 2048 + 1024],
                    Mf[0:4, FB + 2048:FB + 2048 + 1024],
                    Mf[0:4, FB + 3072:FB + 3072 + 1024], ALU.max),
                    writes=[b_gq])
                p.add("vector", lambda v: v.tensor_tensor(
                    Mf[0:4, FB + 2048:FB + 2048 + 512],
                    Mf[0:4, FB + 2048:FB + 2048 + 512],
                    Mf[0:4, FB + 2560:FB + 2560 + 512], ALU.max),
                    writes=[b_gq])
                p.dma("sync", Mf[0:1, FB:FB + 2048],
                      Mf[0:4, FB + 2048:FB + 2048 + 512],
                      reads=[b_gq], writes=[b_gq])
                p.add("vector", lambda v: v.tensor_tensor(
                    Mf[0:1, FB:FB + 1024], Mf[0:1, FB:FB + 1024],
                    Mf[0:1, FB + 1024:FB + 2048], ALU.max), writes=[b_gq])
                p.add("vector", lambda v: v.tensor_tensor(
                    Mf[0:1, FB:FB + 512], Mf[0:1, FB:FB + 512],
                    Mf[0:1, FB + 512:FB + 1024], ALU.max), writes=[b_gq])
                p.add("vector", lambda v: v.tensor_scalar_max(
                    Mf[0:1, FB:FB + 512], Mf[0:1, FB:FB + 512], 1e-5),
                    writes=[b_gq], force=True)
                p.add("vector", lambda v: v.reciprocal(
                    Mf[0:1, FB + 1024:FB + 1024 + 512],
                    Mf[0:1, FB:FB + 512]), writes=[b_gq], force=True)
                p.add("vector", lambda v: v.tensor_scalar_mul(
                    Mf[0:1, FB + 1024:FB + 1024 + 512],
                    Mf[0:1, FB + 1024:FB + 1024 + 512], 127.0),
                    writes=[b_gq], force=True)
                p.add("vector", lambda v: v.tensor_scalar(
                    Mf[0:1, FB + 2048:FB + 2048 + 512], Mf[0:1, FB:FB + 512],
                    1.0 / 127.0, sb_cols[0:1, C_SWD:C_SWD + 1],
                    ALU.mult, ALU.mult),
                    reads=[B("wd_sw")], writes=[b_gq], force=True)
                bcast_rows(lambda p0, n: sb_s2b[p0:p0 + n, :],
                           Mf[0:1, FB + 1024:FB + 1024 + 512], 128,
                           [b_gq], "s2b")
                bcast_rows(lambda p0, n: sb_fdb[p0:p0 + n, :],
                           Mf[0:1, FB + 2048:FB + 2048 + 512], 128,
                           [b_gq], "fdb")
                b_s2b = B("s2b")
                b_fdb2 = B("fdb")
                for j in range(32):
                    gsl = g_all[:, j * 512:(j + 1) * 512]
                    p.add("vector", lambda v, gsl=gsl: v.tensor_tensor(
                        qi32, gsl, sb_s2b[:], ALU.mult),
                        reads=[b_g, b_s2b], writes=[b_qi])
                    p.add("vector", lambda v, j=j: v.tensor_copy(
                        gq[:, j * 512:(j + 1) * 512], qi32),
                        reads=[b_qi], writes=[b_gqk[j], b_gq])
                for j2 in range(8):
                    wdt_ = mb(FWD[j2 % 2], 32 * 128)
                    wrt = ([b_fwd[0], b_fw[0], b_fw[1]] if j2 % 2 == 0 else
                           [b_fwd[1], b_gts[0], b_gt2s[0], b_uts[0]])
                    p.dma("sync" if j2 % 2 else "gpsimd",
                          wdt_.rearrange("p (k c) -> p k c", c=128),
                          h_wqd.ap()[j2 * 4096:(j2 + 1) * 4096, :].rearrange(
                              "(k p) c -> p k c", p=128),
                          reads=[B("wdpark")], writes=wrt)
                    bank = 4 + (j2 % 2)
                    kgrp = 8 if j2 == 0 else (16 if j2 == 1 else 32)
                    for k0 in range(0, 32, kgrp):
                        def em_md(t, wdt_=wdt_, bank=bank, k0=k0, kgrp=kgrp):
                            r = None
                            for k in range(k0, k0 + kgrp):
                                r = t.matmul(
                                    psum[bank][:],
                                    wdt_.rearrange("p (k c) -> p k c",
                                                   c=128)[:, k, :],
                                    gq[:, k * 512:(k + 1) * 512],
                                    start=(k == 0), stop=(k == 31))
                            return r
                        p.add("tensor", em_md,
                              reads=wrt + [b_gqk[k0 + kgrp - 1]],
                              writes=[pbuf[bank]])
                    p.add("vector", lambda v, bank=bank: v.tensor_tensor(
                        qtmp, psum[bank][:], sb_fdb[:], ALU.mult),
                        reads=[pbuf[bank], b_fdb2], writes=[b_qt])

                    def em_a(v, j2=j2, sub=sub):
                        sl = acc[:, j2 * 1024 + sub * 512:
                                 j2 * 1024 + sub * 512 + 512]
                        return v.tensor_tensor(sl, sl, qtmp, ALU.add)
                    p.add("vector", em_a, reads=[b_qt], writes=[b_acc])

            bar4 = bar5
            p5_production(0)
            acc = mf(98304, 8 * 1024)
            b_acc = newB("acc", bar4)
            p.dma("sync", acc.rearrange("p (j t) -> p j t", j=8),
                  h_base.ap()[:, T0:T0 + TC].rearrange("(j p) t -> p j t",
                                                       p=128),
                  reads=[B("h_base")], writes=[b_acc])
            ytile = mb(65536, 16 * 1024)
            b_y = newB("yres", bar4)
            p.dma("sync", ytile.rearrange("p (k t) -> p k t", k=16),
                  h_y.ap()[:, T0:T0 + TC].rearrange("(k p) t -> p k t", p=128),
                  reads=[b_ypark], writes=[b_y])
            wo_ = [newB("woa", bar4), newB("wob", bar4)]
            for j in range(8):
                wt = sb_wo4[:, (j % 2) * 2048:(j % 2) * 2048 + 2048]
                p.dma("gpsimd", wt.rearrange("p (k c) -> p k c", c=128),
                      i_wo.ap()[:, j * 128:(j + 1) * 128].rearrange(
                          "(k p) c -> p k c", p=128),
                      writes=[wo_[j % 2]])
                for th in range(2):
                    bank = 6 + (j * 2 + th) % 2

                    def em_m(t, wt=wt, th=th, bank=bank):
                        r = None
                        for k in range(16):
                            r = t.matmul(
                                psum[bank][:],
                                wt.rearrange("p (k c) -> p k c", c=128)[:, k, :],
                                ytile[:, k * 1024 + th * 512:
                                      k * 1024 + th * 512 + 512],
                                start=(k == 0), stop=(k == 15))
                        return r
                    p.add("tensor", em_m, reads=[wo_[j % 2], b_y, b_gq],
                          writes=[pbuf[bank]])

                    def em_e(v, j=j, th=th, bank=bank):
                        sl = acc[:, j * 1024 + th * 512:
                                 j * 1024 + th * 512 + 512]
                        return v.tensor_tensor(sl, sl, psum[bank][:], ALU.add)
                    p.add("vector", em_e, reads=[pbuf[bank]], writes=[b_acc])

            p5_rest(0)
            p5_production(1)
            p5_rest(1)

            bar6 = p.barrier()
            otok = mf(163840, 1024)
            b_ot = newB("otok", bar6)
            for ttl in range(8):
                for half in range(2):
                    bank = 6 + half

                    def em_t(t, ttl=ttl, half=half, bank=bank):
                        r = None
                        for q in range(4):
                            j = half * 4 + q
                            r = t.transpose(
                                psum[bank][:, q * 128:(q + 1) * 128],
                                acc[:, j * 1024 + ttl * 128:
                                    j * 1024 + ttl * 128 + 128], sb_idf[:])
                        return r
                    p.add("tensor", em_t, reads=[b_acc, B("idf")],
                          writes=[pbuf[bank]])
                    for q in range(4):
                        j = half * 4 + q
                        p.add("scalar", lambda sc, j=j, bank=bank, q=q:
                              sc.copy(otok[:, j * 128:(j + 1) * 128],
                                      psum[bank][:, q * 128:(q + 1) * 128]),
                              reads=[pbuf[bank]], writes=[b_ot])
                p.dma("sync",
                      o_y.ap()[T0 + ttl * 128:T0 + (ttl + 1) * 128, :],
                      otok, reads=[b_ot], writes=[B("outdone")])

        p.finalize(es)
    return nc


_NC_CACHE = {}


def kernel(L0_attn_x, L0_attn_norm_w, L0_attn_norm_b, L0_attn_in_proj_w,
           L0_attn_conv_w, L0_attn_conv_b, L0_attn_x_proj_w, L0_attn_dt_proj_w,
           L0_attn_dt_proj_b, L0_attn_A_log, L0_attn_D_param,
           L0_attn_out_proj_w, L0_ffn_w_gate, L0_ffn_w_up, L0_ffn_w_down):
    def f(a):
        return np.ascontiguousarray(np.asarray(a, dtype=np.float32))
    x = f(L0_attn_x)
    wmap = {
        "nw": f(L0_attn_norm_w).reshape(1, DM),
        "nb": f(L0_attn_norm_b).reshape(1, DM),
        "w_in": f(L0_attn_in_proj_w),
        "conv_w": f(L0_attn_conv_w).reshape(DI, 4),
        "conv_b": f(L0_attn_conv_b).reshape(1, DI),
        "w_xp": f(L0_attn_x_proj_w),
        "w_dt": f(L0_attn_dt_proj_w),
        "dt_b": f(L0_attn_dt_proj_b).reshape(1, DI),
        "a_log": f(L0_attn_A_log),
        "d_par": f(L0_attn_D_param).reshape(1, DI),
        "w_out": f(L0_attn_out_proj_w),
        "w_g": f(L0_ffn_w_gate),
        "w_u": f(L0_ffn_w_up),
        "w_d": f(L0_ffn_w_down),
    }
    if "nc" not in _NC_CACHE:
        _NC_CACHE["nc"] = build_nc()
    nc = _NC_CACHE["nc"]
    in_maps = [dict(wmap, xb=np.ascontiguousarray(x[b])) for b in range(8)]
    res = run_bass_kernel_spmd(nc, in_maps, core_ids=list(range(8)))
    out = np.stack([np.asarray(res.results[b]["y_out"]) for b in range(8)],
                   axis=0)
    return out.astype(np.float32)


# version 39
# speedup vs baseline: 1.0317x; 1.0063x over previous
import numpy as np
import concourse.bass as bass
import concourse.mybir as mybir
from concourse.bass_utils import run_bass_kernel_spmd
from contextlib import ExitStack

F32 = mybir.dt.float32
BF16 = mybir.dt.bfloat16
I32 = mybir.dt.int32
ALU = mybir.AluOpType
ACT = mybir.ActivationFunctionType
AX = mybir.AxisListType

L = 2048
DM = 1024
DI = 2048
DS = 16
NTT = 16
NDI = 16
NCH = 2
TC = 1024
NSUB = 4
SC = 256
INNER = 4096

ENGINES = ("sync", "scalar", "vector", "tensor", "gpsimd")


class Step:
    __slots__ = ("engine", "emit", "inc", "count_after", "idx", "deps",
                 "fdeps")


class Buf:
    __slots__ = ("writers", "readers")

    def __init__(self, init=()):
        self.writers = list(init)
        self.readers = []


class Prog:
    def __init__(self, nc):
        self.nc = nc
        self.steps = []
        self._counts = {e: 0 for e in ENGINES}
        self._last = {e: None for e in ENGINES}

    def add(self, engine, emit, reads=(), writes=(), extra=(), dma=False,
            ptr_reads=(), force=False):
        deps = set()
        for b in reads:
            deps.update(b.writers)
        for b in writes:
            deps.update(b.writers)
            deps.update(b.readers)
        deps.update(extra)
        fdeps = set()
        for b in ptr_reads:
            fdeps.update(b.writers)
        if force:
            fdeps.update(deps)
        st = Step()
        st.engine = engine
        st.emit = emit
        st.inc = 16 if dma else 1
        st.deps = list(deps - fdeps)
        st.fdeps = list(fdeps)
        st.idx = len(self.steps)
        self._counts[engine] += st.inc
        st.count_after = self._counts[engine]
        self.steps.append(st)
        self._last[engine] = st
        for b in reads:
            b.readers.append(st)
        for b in ptr_reads:
            b.readers.append(st)
        for b in writes:
            b.writers = [st]
            b.readers = []
        return st

    def dma(self, engine, out, in_, reads=(), writes=(), extra=(), slow=False):
        def emit(eng, out=out, in_=in_, slow=slow):
            return eng.dma_start(out, in_, allow_slow_non_contiguous=slow)
        return self.add(engine, emit, reads, writes, extra, dma=True)

    def barrier(self):
        return [s for s in self._last.values() if s is not None]

    def finalize(self, es):
        nc = self.nc
        sems = {e: es.enter_context(nc.semaphore("pg_" + e)) for e in ENGINES}
        by_engine = {e: [] for e in ENGINES}
        for st in self.steps:
            by_engine[st.engine].append(st)
        final = dict(self._counts)
        blk = es.enter_context(nc.Block())

        def mk(ename):
            def fn(eng):
                waited = {e: 0 for e in ENGINES}
                for st in by_engine[ename]:
                    need = {}
                    if st.inc == 16:
                        need[ename] = st.count_after - 16
                    for d in st.deps:
                        if d.engine == ename and d.inc == 1:
                            continue
                        if d.count_after > need.get(d.engine, 0):
                            need[d.engine] = d.count_after
                    for d in st.fdeps:
                        if d.count_after > need.get(d.engine, 0):
                            need[d.engine] = d.count_after
                    for e in sorted(need):
                        if need[e] > waited[e]:
                            eng.wait_ge(sems[e], need[e])
                            waited[e] = need[e]
                    st.emit(eng).then_inc(sems[ename], st.inc)
                if ename == "sync":
                    for e in ENGINES:
                        if final[e] > 0:
                            eng.wait_ge(sems[e], final[e])
            return fn

        for e in ENGINES:
            if by_engine[e] or e == "sync":
                getattr(blk, e)(mk(e))


def build_nc():
    nc = bass.Bass("TRN2", target_bir_lowering=False, debug=False)

    i_x = nc.dram_tensor("xb", [L, DM], F32, kind="ExternalInput")
    i_nw = nc.dram_tensor("nw", [1, DM], F32, kind="ExternalInput")
    i_nb = nc.dram_tensor("nb", [1, DM], F32, kind="ExternalInput")
    i_w1 = nc.dram_tensor("w_in", [DM, 2 * DI], F32, kind="ExternalInput")
    i_cw = nc.dram_tensor("conv_w", [DI, 4], F32, kind="ExternalInput")
    i_cb = nc.dram_tensor("conv_b", [1, DI], F32, kind="ExternalInput")
    i_wx = nc.dram_tensor("w_xp", [DI, 2080], F32, kind="ExternalInput")
    i_wdt = nc.dram_tensor("w_dt", [DI, DI], F32, kind="ExternalInput")
    i_dtb = nc.dram_tensor("dt_b", [1, DI], F32, kind="ExternalInput")
    i_alog = nc.dram_tensor("a_log", [DI, DS], F32, kind="ExternalInput")
    i_dp = nc.dram_tensor("d_par", [1, DI], F32, kind="ExternalInput")
    i_wo = nc.dram_tensor("w_out", [DI, DM], F32, kind="ExternalInput")
    i_wg = nc.dram_tensor("w_g", [INNER, DM], F32, kind="ExternalInput")
    i_wu = nc.dram_tensor("w_u", [INNER, DM], F32, kind="ExternalInput")
    i_wd = nc.dram_tensor("w_d", [DM, INNER], F32, kind="ExternalInput")
    o_y = nc.dram_tensor("y_out", [L, DM], F32, kind="ExternalOutput")
    h_base = nc.dram_tensor("s_base", [DM, L], F32, kind="Internal")
    h_xn2 = nc.dram_tensor("s_xn2", [DM, L], BF16, kind="Internal")
    h_xq = nc.dram_tensor("s_xq", [DM, L], BF16, kind="Internal")
    h_xb = nc.dram_tensor("s_xb", [DI, L], BF16, kind="Internal")
    h_z = nc.dram_tensor("s_z", [DI, L], BF16, kind="Internal")
    h_y = nc.dram_tensor("s_y", [DI, L], BF16, kind="Internal")
    h_wqg = nc.dram_tensor("s_wqg", [INNER // 128 * DM, 128], BF16,
                           kind="Internal")
    h_wqu = nc.dram_tensor("s_wqu", [INNER // 128 * DM, 128], BF16,
                           kind="Internal")
    h_wqd = nc.dram_tensor("s_wqd", [DM // 128 * INNER, 128], BF16,
                           kind="Internal")

    es = ExitStack()
    with es:
        def sbt(name, shape, dt=F32):
            return es.enter_context(nc.sbuf_tensor(name, shape, dt))

        MEGA = sbt("mega", [128, 45056], F32)
        Mf = MEGA[:]
        Mb = MEGA[:].bitcast(BF16)
        Mi = MEGA[:].bitcast(I32)

        def mf(off, n, p=128):
            return Mf[0:p, off // 4: off // 4 + n]

        def mb(off, n, p=128):
            return Mb[0:p, off // 2: off // 2 + n]

        def mi(off, n, p=128):
            return Mi[0:p, off // 4: off // 4 + n]

        sb_fx = sbt("fx", [128, L], F32)
        sb_A = sbt("a_sb", [128, NDI * DS], F32)
        sb_D = sbt("d_sb", [128, NDI], F32)
        sb_cb = sbt("cb_sb", [128, NDI], F32)
        sb_dtb = sbt("dtb_sb", [128, NDI], F32)
        sb_cw = sbt("cw_sb", [128, NDI * 4], F32)
        sb_pidx = sbt("pidx", [128, 1], F32)
        sb_srec = sbt("srec", [128, NTT], F32)
        sb_cols = sbt("cols", [128, 16], F32)
        sb_idf = sbt("idf", [128, 128], F32)
        sb_idb = sbt("idb", [128, 128], BF16)
        sb_hcar = sbt("hcar", [128, NDI * DS], F32)
        sb_halo = sbt("halo", [128, NDI * 4], F32)
        sb_Bbf = sbt("bbf", [16, TC], BF16)
        sb_Cbf = sbt("cbf", [16, TC], BF16)
        sb_u = sbt("u_sb", [128, 256], BF16)
        sb_u16 = sbt("u16_sb", [128, 16], F32)
        sb_s2b = sbt("s2b", [128, 512], F32)
        sb_fdb = sbt("fdb", [128, 512], F32)
        sb_wo4 = sbt("wo4", [128, 4096], BF16)

        psum = [es.enter_context(nc.psum_tensor("pb%d" % i, [128, 512], F32))
                for i in range(8)]

        p = Prog(nc)
        bufs = {}

        def B(name):
            if name not in bufs:
                bufs[name] = Buf()
            return bufs[name]

        def newB(name, init=()):
            bufs[name] = Buf(init)
            return bufs[name]

        pbuf = [B("psum%d" % i) for i in range(8)]

        C_SWG, C_SWU, C_SWD, C_THR, C_NTHR, C_T0, C_T1, C_T2 = range(8)

        def col(i):
            return sb_cols[:, i:i + 1]

        p.add("gpsimd", lambda g: g.iota(
            sb_idf[:], [[1, 128]], channel_multiplier=0,
            allow_small_or_imprecise_dtypes=True), writes=[B("idf")])
        p.add("gpsimd", lambda g: g.iota(
            sb_pidx[:], [[0, 1]], channel_multiplier=1,
            allow_small_or_imprecise_dtypes=True), writes=[B("pidx")])
        p.add("vector", lambda v: v.tensor_scalar(
            sb_idf[:], sb_idf[:], sb_pidx[:], None, ALU.is_equal),
            reads=[B("pidx")], writes=[B("idf")])
        p.add("vector", lambda v: v.tensor_copy(sb_idb[:], sb_idf[:]),
              reads=[B("idf")], writes=[B("idb")])

        p.dma("sync", sb_A[:], i_alog.ap().rearrange("(k p) n -> p k n", p=128),
              writes=[B("A")])
        p.add("scalar", lambda s: s.activation(sb_A[:], sb_A[:], ACT.Exp),
              writes=[B("A")])
        p.add("vector", lambda v: v.tensor_scalar_mul(sb_A[:], sb_A[:], -1.0),
              writes=[B("A")])
        p.dma("sync", sb_D[:], i_dp.ap().rearrange("o (k p) -> (o p) k", p=128), slow=True,
              writes=[B("D")])
        p.dma("sync", sb_cb[:], i_cb.ap().rearrange("o (k p) -> (o p) k", p=128), slow=True,
              writes=[B("cb")])
        p.dma("sync", sb_dtb[:], i_dtb.ap().rearrange("o (k p) -> (o p) k", p=128), slow=True,
              writes=[B("dtb")])
        p.dma("sync", sb_cw[:], i_cw.ap().rearrange("(k p) t -> p k t", p=128),
              writes=[B("cw")])
        p.add("vector", lambda v: v.memset(sb_hcar[:], 0.0), writes=[B("hcar")])
        p.add("vector", lambda v: v.memset(sb_halo[:], 0.0), writes=[B("halo")])

        O_BASE = 0
        O_XN2F = 65536
        O_XQF = 98304
        O_RA = 131072
        O_RB = 135168
        O_RC = 139264
        O_RD = 143360
        O_WQA = 147456
        O_WQB = 151552
        O_WQQ = 155648
        O_WQT = 157696
        O_NWB = 167936
        O_NBB = 172032

        base_fm = mf(O_BASE, 8 * 2048)
        xn2_fm = mb(O_XN2F, 8 * 2048)
        xq_fm = mb(O_XQF, 8 * 2048)
        nwb = mf(O_NWB, 1024)
        nbb = mf(O_NBB, 1024)
        b_base = newB("r_base")
        b_xn2f = newB("r_xn2f")
        b_xqf = newB("r_xqf")
        b_nwb = newB("r_nwb")
        b_nbb = newB("r_nbb")

        def bcast_rows(dst_row_ap_fn, first_in, n_parts, reads, bname):
            st = p.dma("sync", dst_row_ap_fn(0, 1), first_in,
                       reads=reads, writes=[B(bname)])
            pc = 1
            while pc < n_parts:
                st = p.dma("sync", dst_row_ap_fn(pc, min(pc, n_parts - pc)),
                           dst_row_ap_fn(0, min(pc, n_parts - pc)),
                           extra=[st])
                pc *= 2
            B(bname).writers.append(st)
            return st

        bcast_rows(lambda p0, n: Mf[p0:p0 + n, O_NWB // 4:O_NWB // 4 + 1024],
                   i_nw.ap(), 128, [], "r_nwb")
        bcast_rows(lambda p0, n: Mf[p0:p0 + n, O_NBB // 4:O_NBB // 4 + 1024],
                   i_nb.ap(), 128, [], "r_nbb")

        def transpose_to_fm(src_ap, src_buf, ident_ap, ident_buf,
                            dst_half_off, dst_is_bf16, dst_buf, tt):
            for half in range(2):
                bank = 6 + half
                pview = psum[bank][:].bitcast(BF16) if dst_is_bf16 \
                    else psum[bank][:]

                def em_t(t, src_ap=src_ap, half=half, pview=pview,
                         ident_ap=ident_ap):
                    r = None
                    for q in range(4):
                        j = half * 4 + q
                        r = t.transpose(pview[:, q * 128:(q + 1) * 128],
                                        src_ap[:, j * 128:(j + 1) * 128],
                                        ident_ap)
                    return r
                p.add("tensor", em_t, reads=[src_buf, ident_buf],
                      writes=[pbuf[bank]])
                if dst_is_bf16:
                    dstv = Mb[:, dst_half_off // 2:
                              dst_half_off // 2 + 16384].rearrange(
                        "p (j t) -> p j t", j=8)
                else:
                    dstv = Mf[:, dst_half_off // 4:
                              dst_half_off // 4 + 16384].rearrange(
                        "p (j t) -> p j t", j=8)
                dst = dstv[:, half * 4:half * 4 + 4, tt * 128:tt * 128 + 128]
                psv = pview[:, 0:512].rearrange("p (q t) -> p q t", q=4)
                p.add("scalar", lambda sc, dst=dst, psv=psv: sc.copy(dst, psv),
                      reads=[pbuf[bank]], writes=[dst_buf])

        XO = [131072, 135168, 139264, 143360]
        XNO = [147456, 151552, 155648, 159744]
        JNK = mf(163840, 1024)
        QT = mf(176128, 1024)
        QTb = mb(176128, 1024)
        QTi = mi(176128, 1024)
        bX = [newB("x%d" % i) for i in range(4)]
        bXN = [newB("xn%d" % i) for i in range(4)]
        bJ = newB("jnk")
        bQ = newB("qt")
        CS0, CS1, CS2 = 4, 8, 12
        bc0, bc1, bc2 = newB("cc0"), newB("cc1"), newB("cc2")

        def stats_chain():
            c0 = sb_cols[:, CS0:CS0 + 4]
            c1 = sb_cols[:, CS1:CS1 + 4]
            c2 = sb_cols[:, CS2:CS2 + 4]
            p.add("vector", lambda v: v.tensor_scalar_mul(c0, c0, 1.0 / DM),
                  writes=[bc0], force=True)
            p.add("vector", lambda v: v.tensor_tensor(c2, c0, c0, ALU.mult),
                  reads=[bc0], writes=[bc2], force=True)
            p.add("vector", lambda v: v.tensor_scalar_mul(c1, c1, 1.0 / DM),
                  writes=[bc1], force=True)
            p.add("vector", lambda v: v.tensor_tensor(c1, c1, c2, ALU.subtract),
                  reads=[bc2], writes=[bc1], force=True)
            p.add("vector", lambda v: v.tensor_scalar_add(c1, c1, 1e-5),
                  writes=[bc1], force=True)
            p.add("scalar", lambda s: s.activation(c1, c1, ACT.Sqrt),
                  writes=[bc1], force=True)
            p.add("vector", lambda v: v.reciprocal(c1, c1),
                  writes=[bc1], force=True)

        for rnd in range(4):
            tts = [rnd * 4 + i for i in range(4)]
            p.dma("sync" if rnd % 2 else "gpsimd",
                  Mf[:, XO[0] // 4:XO[0] // 4 + 4096].rearrange(
                      "p (r c) -> p r c", r=4),
                  i_x.ap()[rnd * 512:(rnd + 1) * 512, :].rearrange(
                      "(r p) c -> p r c", p=128),
                  writes=[bX[0], bX[1], bX[2], bX[3]])
            for i, tt in enumerate(tts):
                X = mf(XO[i], 1024)
                p.add("vector", lambda v, X=X, i=i: v.reduce_sum(
                    sb_cols[:, CS0 + i:CS0 + i + 1], X, AX.X),
                    reads=[bX[i]], writes=[bc0])
                p.add("scalar", lambda s, X=X, i=i: s.activation(
                    JNK, X, ACT.Square,
                    accum_out=sb_cols[:, CS1 + i:CS1 + i + 1]),
                    reads=[bX[i]], writes=[bJ, bc1])
            stats_chain()
            for i, tt in enumerate(tts):
                X = mf(XO[i], 1024)
                XN = mf(XNO[i], 1024)
                p.add("vector", lambda v, X=X, XN=XN, i=i: v.tensor_scalar(
                    XN, X, sb_cols[:, CS0 + i:CS0 + i + 1],
                    sb_cols[:, CS1 + i:CS1 + i + 1], ALU.subtract, ALU.mult),
                    reads=[bX[i]], writes=[bXN[i]], ptr_reads=[bc0, bc1])
                p.add("vector", lambda v, X=X, XN=XN: v.tensor_tensor(
                    X, X, XN, ALU.add), reads=[bXN[i]], writes=[bX[i]])
                transpose_to_fm(X, bX[i], sb_idf[:], B("idf"), O_BASE, False,
                                b_base, tt)
                p.add("vector", lambda v, XN=XN, i=i: v.reduce_sum(
                    sb_cols[:, CS0 + i:CS0 + i + 1], XN, AX.X),
                    reads=[bXN[i]], writes=[bc0])
                p.add("scalar", lambda s, XN=XN, i=i: s.activation(
                    JNK, XN, ACT.Square,
                    accum_out=sb_cols[:, CS1 + i:CS1 + i + 1]),
                    reads=[bXN[i]], writes=[bJ, bc1])
            stats_chain()
            for i, tt in enumerate(tts):
                X = mf(XO[i], 1024)
                Xb = mb(XO[i], 1024)
                XN = mf(XNO[i], 1024)
                p.add("vector", lambda v, X=X, XN=XN, i=i: v.tensor_scalar(
                    X, XN, sb_cols[:, CS0 + i:CS0 + i + 1],
                    sb_cols[:, CS1 + i:CS1 + i + 1], ALU.subtract, ALU.mult),
                    reads=[bXN[i]], writes=[bX[i]], ptr_reads=[bc0, bc1])
                p.add("vector", lambda v, X=X: v.tensor_tensor(
                    X, X, nwb, ALU.mult), reads=[b_nwb], writes=[bX[i]])
                p.add("vector", lambda v, X=X, Xb=Xb: v.tensor_tensor(
                    Xb, X, nbb, ALU.add), reads=[b_nbb], writes=[bX[i]])
                transpose_to_fm(Xb, bX[i], sb_idb[:], B("idb"), O_XN2F, True,
                                b_xn2f, tt)
                p.add("vector", lambda v, XN=XN, i=i: v.tensor_reduce(
                    sb_cols[:, CS0 + i:CS0 + i + 1], XN, AX.X, ALU.max,
                    apply_absolute_value=True),
                    reads=[bXN[i]], writes=[bc0])
            c0 = sb_cols[:, CS0:CS0 + 4]
            c1 = sb_cols[:, CS1:CS1 + 4]
            p.add("vector", lambda v: v.tensor_scalar_max(c0, c0, 1e-5),
                  writes=[bc0], force=True)
            p.add("vector", lambda v, rnd=rnd: v.tensor_scalar_mul(
                sb_srec[:, rnd * 4:rnd * 4 + 4], c0, 1.0 / 127.0),
                reads=[bc0], writes=[B("srec")], force=True)
            p.add("vector", lambda v: v.reciprocal(c1, c0),
                  reads=[bc0], writes=[bc1], force=True)
            p.add("vector", lambda v: v.tensor_scalar_mul(c1, c1, 127.0),
                  writes=[bc1], force=True)
            for i, tt in enumerate(tts):
                XN = mf(XNO[i], 1024)
                XNi = mi(XNO[i], 1024)
                Xb2 = mb(XO[i], 1024)
                p.add("vector", lambda v, XN=XN, XNi=XNi, i=i: v.tensor_scalar(
                    XNi, XN, sb_cols[:, CS1 + i:CS1 + i + 1], None, ALU.mult),
                    reads=[bXN[i]], writes=[bXN[i]], ptr_reads=[bc1])
                p.add("vector", lambda v, XNi=XNi, Xb2=Xb2: v.tensor_copy(
                    Xb2, XNi), reads=[bXN[i]], writes=[bX[i]])
                transpose_to_fm(Xb2, bX[i], sb_idb[:], B("idb"), O_XQF, True,
                                b_xqf, tt)

        p.add("tensor", lambda t: t.transpose(
            psum[6][0:16, 0:128], sb_srec[:], sb_idf[:]),
            reads=[B("srec"), B("idf")], writes=[pbuf[6]])
        p.add("scalar", lambda sc: sc.copy(
            Mf[0:16, O_RA // 4:O_RA // 4 + 128], psum[6][0:16, 0:128]),
            reads=[pbuf[6]], writes=[bX[0]])
        bcast_rows(lambda p0, n: sb_fx[p0:p0 + n, :],
                   Mf[0:16, O_RA // 4:O_RA // 4 + 128], 128, [bX[0]],
                   "fx")
        b_fx = B("fx")

        st_pb = p.dma("sync", h_base.ap().rearrange("(j p) t -> p j t", p=128),
                      base_fm.rearrange("p (j t) -> p j t", j=8),
                      reads=[b_base], writes=[B("h_base")])
        st_p2 = p.dma("gpsimd",
                      h_xn2.ap().rearrange("(j p) t -> p j t", p=128),
                      xn2_fm.rearrange("p (j t) -> p j t", j=8),
                      reads=[b_xn2f], writes=[B("h_xn2")])
        st_pq = p.dma("gpsimd",
                      h_xq.ap().rearrange("(j p) t -> p j t", p=128),
                      xq_fm.rearrange("p (j t) -> p j t", j=8),
                      reads=[b_xqf], writes=[B("h_xq")])

        O_WQA2, O_WQB2 = 32768, 40960
        O_WP1A, O_WP1B = 32768, 49152
        O_WP1C = 65536
        O_WQS1, O_WQS2 = 49152, 53248
        O_WQQ2 = 57344
        O_WQT2 = 61440
        O_WQTQ = 57344

        def wquant(src, dst, n_row_tiles, ncols, csw, tag):
            b_wqtmp = B("wqtmp")
            wa1 = [B("wp1_a"), B("wp1_b"), B("wp1_c")]
            wa = [B("wq_a"), B("wq_b")]
            p.add("vector", lambda v: v.memset(col(C_T0), 0.0), writes=[B("c0")])
            it = 0
            if ncols >= 2048:
                r1step, p1w = 1, ncols
            else:
                r1step, p1w = 4, 4 * ncols
            for rt0 in range(0, n_row_tiles, r1step):
                off = (O_WP1A, O_WP1B, O_WP1C)[it % 3]
                wt = mf(off, p1w)
                if r1step == 4:
                    inap = src[rt0 * 128:(rt0 + 4) * 128, :].rearrange(
                        "(p q) c -> p (q c)", p=128)
                else:
                    inap = src[rt0 * 128:(rt0 + 1) * 128, :]
                wrt = [wa1[it % 3]] + ([B("wqq"), B("wqt"), B("wqs")]
                                       if it % 3 == 1 else [])
                if it % 3 == 2:
                    wrt += [b_xn2f]
                if it < 2:
                    wrt += [B("wq_a"), B("wq_b")]
                p.dma(("gpsimd", "sync", "scalar")[it % 3], wt, inap,
                      writes=wrt)
                p.add("vector", lambda v, wt=wt: v.tensor_reduce(
                    col(C_T1), wt, AX.X, ALU.add,
                    apply_absolute_value=True),
                    reads=[wa1[it % 3]], writes=[B("c1")])
                p.add("vector", lambda v: v.tensor_tensor(
                    col(C_T0), col(C_T0), col(C_T1), ALU.add),
                    reads=[B("c1")], writes=[B("c0")], force=True)
                it += 1
                yield True
            FQ = O_WQTQ // 4
            f0 = Mf[0:16, FQ:FQ + 8]
            st1 = p.dma("sync", f0, col(C_T0), reads=[B("c0")],
                        writes=[b_wqtmp, B("wqq")])
            p.add("vector", lambda v: v.tensor_tensor(
                Mf[0:16, FQ:FQ + 4], Mf[0:16, FQ:FQ + 4],
                Mf[0:16, FQ + 4:FQ + 8], ALU.add),
                writes=[b_wqtmp], force=True)
            p.add("vector", lambda v: v.tensor_tensor(
                Mf[0:16, FQ:FQ + 2], Mf[0:16, FQ:FQ + 2],
                Mf[0:16, FQ + 2:FQ + 4], ALU.add),
                writes=[b_wqtmp], force=True)
            p.add("vector", lambda v: v.tensor_tensor(
                Mf[0:16, FQ:FQ + 1], Mf[0:16, FQ:FQ + 1],
                Mf[0:16, FQ + 1:FQ + 2], ALU.add),
                writes=[b_wqtmp], force=True)
            p.dma("sync", Mf[0:1, FQ + 8:FQ + 24], Mf[0:16, FQ:FQ + 1],
                  reads=[b_wqtmp], writes=[b_wqtmp])
            p.add("vector", lambda v: v.tensor_reduce(
                Mf[0:1, FQ:FQ + 1], Mf[0:1, FQ + 8:FQ + 24], AX.X, ALU.add),
                writes=[b_wqtmp], force=True)
            nel = float(n_row_tiles * 128 * ncols)
            p.add("vector", lambda v: v.tensor_scalar(
                Mf[0:1, FQ:FQ + 1], Mf[0:1, FQ:FQ + 1], 1.0 / nel, 1e-5,
                ALU.mult, ALU.max), writes=[b_wqtmp], force=True)
            bcast_rows(lambda p0, n: sb_cols[p0:p0 + n, csw:csw + 1],
                       Mf[0:1, FQ:FQ + 1], 128, [b_wqtmp], tag + "_sw")
            bsw = B(tag + "_sw")
            p.add("vector", lambda v: v.tensor_scalar_mul(
                col(C_THR), col(csw), 0.5), reads=[bsw], writes=[B("thr")],
                force=True)
            p.add("vector", lambda v: v.tensor_scalar_mul(
                col(C_NTHR), col(C_THR), -1.0), reads=[B("thr")],
                writes=[B("nthr")], force=True)
            p.add("vector", lambda v: v.tensor_scalar_mul(
                col(csw), col(csw), 0.5), writes=[bsw], force=True)
            yield True
            it = 0
            CCW = min(2048, ncols)
            ncc = ncols // CCW
            rstep = 1 if ncols >= 2048 else 2
            for rt0 in range(0, n_row_tiles, rstep):
                for cc in range(ncc):
                    off = O_WQA2 if it % 2 == 0 else O_WQB2
                    wt = mf(off, 2048)
                    if rstep == 2:
                        inap = src[rt0 * 128:(rt0 + 2) * 128, :].rearrange(
                            "(r p) c -> p r c", p=128)
                        outap = wt.rearrange("p (r c) -> p r c", r=2)
                    else:
                        inap = src[rt0 * 128:(rt0 + 1) * 128,
                                   cc * 2048:(cc + 1) * 2048]
                        outap = wt
                    wrt = [wa[it % 2]] + ([wa1[0], wa1[1]] if it < 2 else [])
                    p.dma(("gpsimd", "sync", "scalar")[it % 3], outap, inap,
                          writes=wrt)
                    s1 = mb(O_WQS1, 2048)
                    s2 = mb(O_WQS2, 2048)
                    wq = mb(O_WQQ2, 2048)
                    p.add("scalar", lambda sc, wt=wt, s1=s1: sc.activation(
                        s1, wt, ACT.Sign, bias=col(C_NTHR)),
                        reads=[wa[it % 2]], writes=[B("wqs")],
                        ptr_reads=[B("nthr")])
                    p.add("scalar", lambda sc, wt=wt, s2=s2: sc.activation(
                        s2, wt, ACT.Sign, bias=col(C_THR)),
                        reads=[wa[it % 2]], writes=[B("wqs")],
                        ptr_reads=[B("thr")])
                    p.add("vector", lambda v, wq=wq, s1=s1, s2=s2:
                          v.tensor_tensor(wq, s1, s2, ALU.add),
                          reads=[B("wqs")], writes=[B("wqq")])
                    for half in range(4):
                        bank = 6 + half % 2
                        pview = psum[bank][:].bitcast(BF16)

                        def em_t(t, wq=wq, half=half, pview=pview):
                            r = None
                            for q in range(4):
                                jb = half * 4 + q
                                r = t.transpose(
                                    pview[:, q * 128:(q + 1) * 128],
                                    wq[:, jb * 128:(jb + 1) * 128], sb_idb[:])
                            return r
                        p.add("tensor", em_t, reads=[B("wqq"), B("idb")],
                              writes=[pbuf[bank]])
                        o = O_WQT2 // 2 + half * 512
                        p.add("vector", lambda v, o=o, pview=pview:
                              v.tensor_copy(Mb[:, o:o + 512], pview[:, 0:512]),
                              reads=[pbuf[bank]], writes=[B("wqt")])

                    if rstep == 2:
                        for r in range(2):
                            def em_d(eng, rt0=rt0, r=r):
                                jb = rt0 + r
                                return eng.dma_start(
                                    dst[jb * 1024:(jb + 1) * 1024,
                                        :].rearrange(
                                        "(k p) c -> p k c", p=128),
                                    Mb[:, O_WQT2 // 2 + r * 1024:
                                       O_WQT2 // 2 + r * 1024 + 1024].rearrange(
                                        "p (k c) -> p k c", c=128))
                            p.add("sync", em_d, reads=[B("wqt")],
                                  writes=[B(tag + "park")], dma=True)
                    else:
                        def em_d(eng, rt0=rt0, cc=cc):
                            r0 = rt0 * 4096 + cc * 2048
                            return eng.dma_start(
                                dst[r0:r0 + 2048].rearrange(
                                    "(k p) c -> p k c", p=128),
                                Mb[:, O_WQT2 // 2:O_WQT2 // 2 + 2048].rearrange(
                                    "p (k c) -> p k c", c=128))
                        p.add("sync", em_d, reads=[B("wqt")],
                              writes=[B(tag + "park")], dma=True)
                    it += 1
                    yield True

        for ch in range(NCH):
            T0 = ch * TC
            bar = p.barrier()

            xn2s = mb(131072, 8 * 1024)
            b_xn2s = newB("xn2s", bar)
            if ch == 0:
                p.add("vector", lambda v: v.tensor_copy(
                    xn2s.rearrange("p (k t) -> p k t", k=8),
                    xn2_fm.rearrange("p (j t) -> p j t", j=8)[:, :, T0:T0 + TC]
                    if False else
                    xn2_fm.rearrange("p (j t) -> p j t", j=8)[:, :, 0:TC]),
                    reads=[b_xn2f], writes=[b_xn2s])
            else:
                p.dma("sync", xn2s.rearrange("p (k t) -> p k t", k=8),
                      h_xn2.ap()[:, T0:T0 + TC].rearrange(
                          "(k p) t -> p k t", p=128),
                      reads=[B("h_xn2")], writes=[b_xn2s])
            xbres = mb(98304, 16 * 1024)
            b_xb = newB("xbres", bar)
            b_xbpre = newB("xbpre", bar)
            b_convt = newB("convt", bar)
            w1b_ = [newB("w1a", bar), newB("w1b", bar)]
            W1O = [172544, 174848]
            xbpre = mf(163840, 1028)
            convt = mf(168192, 1024)
            convtb = mb(168192, 1024)
            b_xbpark = B("xb_park")
            b_zpark = B("z_park")
            if ch == 0:
                for _nm in ("wq_a", "wq_b", "wp1_a", "wp1_b", "wp1_c",
                            "wqtmp", "wqq", "wqt", "wqs"):
                    newB(_nm, bar)

                def _wq_all():
                    yield from wquant(i_wg.ap(), h_wqg.ap(), INNER // 128,
                                      DM, C_SWG, "wg")
                    yield from wquant(i_wu.ap(), h_wqu.ap(), INNER // 128,
                                      DM, C_SWU, "wu")
                    yield from wquant(i_wd.ap(), h_wqd.ap(), DM // 128,
                                      INNER, C_SWD, "wd")
                wqgen = _wq_all()
            else:
                wqgen = iter(())

            def wq_step(n=1):
                for _ in range(n):
                    if next(wqgen, None) is None:
                        break
            for j in range(32):
                wt = mb(W1O[j % 2], 8 * 128)
                p.dma("gpsimd", wt.rearrange("p (k c) -> p k c", c=128),
                      i_w1.ap()[:, j * 128:(j + 1) * 128].rearrange(
                          "(k p) c -> p k c", p=128),
                      writes=[w1b_[j % 2]])
                for th in range(2):
                    bank = (j * 2 + th) % 4

                    def em_m(t, wt=wt, th=th, bank=bank):
                        r = None
                        for k in range(8):
                            r = t.matmul(
                                psum[bank][:],
                                wt.rearrange("p (k c) -> p k c", c=128)[:, k, :],
                                xn2s[:, k * 1024 + th * 512:
                                     k * 1024 + th * 512 + 512],
                                start=(k == 0), stop=(k == 7))
                        return r
                    p.add("tensor", em_m, reads=[w1b_[j % 2], b_xn2s],
                          writes=[pbuf[bank]])
                    if j < 16:
                        p.add("vector", lambda v, th=th, bank=bank:
                              v.tensor_copy(
                                  xbpre[:, 4 + th * 512:4 + th * 512 + 512],
                                  psum[bank][:]),
                              reads=[pbuf[bank]], writes=[b_xbpre])
                    else:
                        p.add("vector", lambda v, th=th, bank=bank:
                              v.tensor_copy(
                                  convtb[:, th * 512:th * 512 + 512],
                                  psum[bank][:]),
                              reads=[pbuf[bank]], writes=[b_convt])
                if j < 16:
                    p.add("vector", lambda v, j=j: v.tensor_copy(
                        xbpre[:, 1:4], sb_halo[:, j * 4 + 1:j * 4 + 4]),
                        reads=[B("halo")], writes=[b_xbpre])
                    p.add("vector", lambda v, j=j: v.tensor_copy(
                        sb_halo[:, j * 4 + 1:j * 4 + 4], xbpre[:, 1025:1028]),
                        reads=[b_xbpre], writes=[B("halo")], force=True)
                    p.add("vector", lambda v, j=j: v.tensor_scalar(
                        convt, xbpre[:, 1:1025], sb_cw[:, j * 4:j * 4 + 1],
                        None, ALU.mult),
                        reads=[b_xbpre, B("cw")], writes=[b_convt],
                        force=True)
                    for k in range(1, 4):
                        p.add("vector", lambda v, j=j, k=k:
                              v.scalar_tensor_tensor(
                                  convt, xbpre[:, 1 + k:1 + k + 1024],
                                  sb_cw[:, j * 4 + k:j * 4 + k + 1], convt,
                                  ALU.mult, ALU.add),
                              reads=[b_xbpre, B("cw")], writes=[b_convt])
                    xbt = xbres[:, j * 1024:(j + 1) * 1024]
                    p.add("scalar", lambda sc, j=j, xbt=xbt: sc.activation(
                        xbt, convt, ACT.Silu, bias=sb_cb[:, j:j + 1]),
                        reads=[b_convt, B("cb")], writes=[b_xb])
                else:
                    jz = j - 16
                    p.dma("sync", h_z.ap()[jz * 128:(jz + 1) * 128, T0:T0 + TC],
                          convtb, reads=[b_convt], writes=[b_zpark])
                wq_step(1)

            p.dma("sync",
                  h_xb.ap()[:, T0:T0 + TC].rearrange("(k p) t -> p k t", p=128),
                  xbres.rearrange("p (k t) -> p k t", k=16),
                  reads=[b_xb], writes=[b_xbpark])

            bar2 = p.barrier()
            delta_in = mb(0, 16 * 1024)
            b_din = newB("din", bar2)
            wx_ = [newB("wxa", bar2), newB("wxb", bar2)]
            WXO = [163840, 168192]
            bc_bufs = {}

            def emit_bc():
                barBC = p.barrier()
                b_bfree = newB("bfree", barBC)
                b_cfree = newB("cfree", barBC)
                wqtags = [B(nm) for nm in ("wq_a", "wq_b", "wp1_a", "wp1_b",
                                           "wp1_c", "wqq", "wqt", "wqs",
                                           "wqtmp")]
                for s_i in range(NSUB):
                    st = p.dma("sync",
                               Mb[0:1, 32768 // 2 + s_i * 4096:
                                  32768 // 2 + (s_i + 1) * 4096],
                               sb_Bbf[0:16, s_i * SC:(s_i + 1) * SC],
                               reads=[B("Bbf")], extra=barBC,
                               writes=(wqtags if s_i == 0 else ()))
                    b_bfree.writers.append(st)
                    st = p.dma("gpsimd",
                               Mb[0:1, 65536 // 2 + s_i * 4096:
                                  65536 // 2 + (s_i + 1) * 4096],
                               sb_Cbf[0:16, s_i * SC:(s_i + 1) * SC],
                               reads=[B("Cbf")], extra=barBC,
                               writes=(wqtags + [b_xn2f] if s_i == 0 else ()))
                    b_cfree.writers.append(st)
                pc = 1
                stb = list(b_bfree.writers)
                while pc < 128:
                    stb = [p.dma("sync",
                                 Mb[pc:2 * pc, 32768 // 2:32768 // 2 + 16384],
                                 Mb[0:pc, 32768 // 2:32768 // 2 + 16384],
                                 extra=stb)]
                    pc *= 2
                b_bfree.writers.extend(stb)
                pc = 1
                stc = list(b_cfree.writers)
                while pc < 128:
                    stc = [p.dma("gpsimd",
                                 Mb[pc:2 * pc, 65536 // 2:65536 // 2 + 16384],
                                 Mb[0:pc, 65536 // 2:65536 // 2 + 16384],
                                 extra=stc)]
                    pc *= 2
                b_cfree.writers.extend(stc)
                bc_bufs["b"] = b_bfree
                bc_bufs["c"] = b_cfree
            for j in [16] + list(range(16)):
                wdt_c = 128 if j < 16 else 32
                wt = mb(WXO[j % 2], 16 * wdt_c)
                p.dma("gpsimd", wt.rearrange("p (k c) -> p k c", c=wdt_c),
                      i_wx.ap()[:, j * 128:j * 128 + wdt_c].rearrange(
                          "(k p) c -> p k c", p=128),
                      writes=[wx_[j % 2]])
                for th in range(2):
                    bank = (j * 2 + th) % 4
                    if j < 16:
                        def em_m(t, wt=wt, th=th, bank=bank):
                            r = None
                            for k in range(16):
                                r = t.matmul(
                                    psum[bank][:],
                                    wt.rearrange("p (k c) -> p k c",
                                                 c=128)[:, k, :],
                                    xbres[:, k * 1024 + th * 512:
                                          k * 1024 + th * 512 + 512],
                                    start=(k == 0), stop=(k == 15))
                            return r
                        p.add("tensor", em_m, reads=[wx_[j % 2], b_xb],
                              writes=[pbuf[bank]])
                        p.add("vector", lambda v, j=j, th=th, bank=bank:
                              v.tensor_copy(
                                  delta_in[:, j * 1024 + th * 512:
                                           j * 1024 + th * 512 + 512],
                                  psum[bank][:]),
                              reads=[pbuf[bank]], writes=[b_din])
                    else:
                        bank_c = (bank + 2) % 4

                        def em_mb(t, wt=wt, th=th, bank=bank):
                            r = None
                            for k in range(16):
                                r = t.matmul(
                                    psum[bank][0:16, 0:512],
                                    wt.rearrange("p (k c) -> p k c",
                                                 c=32)[:, k, 0:16],
                                    xbres[:, k * 1024 + th * 512:
                                          k * 1024 + th * 512 + 512],
                                    start=(k == 0), stop=(k == 15))
                            return r
                        p.add("tensor", em_mb, reads=[wx_[j % 2], b_xb],
                              writes=[pbuf[bank]])

                        def em_mc(t, wt=wt, th=th, bank_c=bank_c):
                            r = None
                            for k in range(16):
                                r = t.matmul(
                                    psum[bank_c][0:16, 0:512],
                                    wt.rearrange("p (k c) -> p k c",
                                                 c=32)[:, k, 16:32],
                                    xbres[:, k * 1024 + th * 512:
                                          k * 1024 + th * 512 + 512],
                                    start=(k == 0), stop=(k == 15))
                            return r
                        p.add("tensor", em_mc, reads=[wx_[j % 2], b_xb],
                              writes=[pbuf[bank_c]])
                        p.add("scalar", lambda sc, th=th, bank=bank: sc.copy(
                            sb_Bbf[:, th * 512:th * 512 + 512],
                            psum[bank][0:16, 0:512]),
                            reads=[pbuf[bank]], writes=[B("Bbf")])
                        p.add("scalar", lambda sc, th=th, bank_c=bank_c:
                              sc.copy(sb_Cbf[:, th * 512:th * 512 + 512],
                                      psum[bank_c][0:16, 0:512]),
                              reads=[pbuf[bank_c]], writes=[B("Cbf")])
                if j == 16 and ch != 0:
                    emit_bc()
                wq_step(4)

            wq_step(10 ** 6)
            if ch == 0:
                emit_bc()

            bar3 = p.barrier()
            b_bfree = bc_bufs["b"]
            b_cfree = bc_bufs["c"]

            dAs = [mf(98304, 4096), mf(114688, 4096)]
            b_dAs = [newB("dA0", bar3), newB("dA1", bar3)]
            dbx = mb(131072, 4096)
            dbx3 = dbx.rearrange("p (n t) -> p n t", n=DS)
            b_dBx = newB("dBx", bar3)
            hb = mb(139264, 4096)
            hb3 = hb.rearrange("p (n t) -> p n t", n=DS)
            b_h = newB("hb", bar3)
            deltas = [mf(147456, 1024), mf(151552, 1024)]
            b_deltas = [newB("d0", bar3), newB("d1", bar3)]
            sptmp = mf(155648, 1024)
            b_sp = newB("sptmp", bar3)
            xbsls = [mb(159744, 1024), mb(165888, 1024)]
            b_xbsls = [newB("xbsl0", bar3), newB("xbsl1", bar3)]
            zsls = [mb(161792, 1024), mb(169984, 1024)]
            b_zsls = [newB("zsl0", bar3), newB("zsl1", bar3)]
            tnh = mb(163840, 1024)
            b_tnh = newB("tnh", bar3)
            ytmp = mb(167936, 1024)
            b_ytmp = newB("ytmp", bar3)
            wd_ = [newB("wda", bar3), newB("wdb", bar3)]
            WDO = [172032, 176128]
            b_ypark = B("y_park")

            def prep_dt(dt):
                par = dt % 2
                wt = mb(WDO[par], 16 * 128)
                p.dma("gpsimd", wt.rearrange("p (k c) -> p k c", c=128),
                      i_wdt.ap()[:, dt * 128:(dt + 1) * 128].rearrange(
                          "(k p) c -> p k c", p=128),
                      writes=[wd_[par]])
                p.dma("sync", xbsls[par],
                      h_xb.ap()[dt * 128:(dt + 1) * 128, T0:T0 + TC],
                      reads=[b_xbpark], writes=[b_xbsls[par]])
                p.dma("sync", zsls[par],
                      h_z.ap()[dt * 128:(dt + 1) * 128, T0:T0 + TC],
                      reads=[b_zpark], writes=[b_zsls[par]])
                delta = deltas[par]
                for th in range(2):
                    bank = 4 + th

                    def em_m(t, wt=wt, th=th, bank=bank):
                        r = None
                        for k in range(16):
                            r = t.matmul(
                                psum[bank][:],
                                wt.rearrange("p (k c) -> p k c", c=128)[:, k, :],
                                delta_in[:, k * 1024 + th * 512:
                                         k * 1024 + th * 512 + 512],
                                start=(k == 0), stop=(k == 15))
                        return r
                    p.add("tensor", em_m, reads=[wd_[par], b_din],
                          writes=[pbuf[bank]])
                    p.add("scalar", lambda s, th=th, bank=bank, dt=dt,
                          delta=delta: s.activation(
                              delta[:, th * 512:th * 512 + 512], psum[bank][:],
                              ACT.Identity, bias=sb_dtb[:, dt:dt + 1]),
                          reads=[pbuf[bank], B("dtb")], writes=[b_deltas[par]])
                p.add("scalar", lambda s, delta=delta: s.activation(
                    sptmp, delta, ACT.Abs), reads=[b_deltas[par]],
                    writes=[b_sp])
                p.add("scalar", lambda s: s.activation(
                    sptmp, sptmp, ACT.Exp, scale=-1.0), writes=[b_sp])
                p.add("scalar", lambda s: s.activation(
                    sptmp, sptmp, ACT.Ln, bias=1.0), writes=[b_sp])
                p.add("scalar", lambda s, delta=delta: s.activation(
                    delta, delta, ACT.Relu), writes=[b_deltas[par]])
                p.add("vector", lambda v, delta=delta: v.tensor_tensor(
                    delta, delta, sptmp, ALU.add), reads=[b_sp],
                    writes=[b_deltas[par]])

            prep_dt(0)
            for dt in range(NDI):
                par = dt % 2
                delta = deltas[par]
                xbsl = xbsls[par]
                if dt + 1 < NDI:
                    prep_dt(dt + 1)
                hsl = sb_hcar[:, dt * DS:(dt + 1) * DS]
                for s_i in range(NSUB):
                    g_par = (dt * NSUB + s_i) % 2
                    dA = dAs[g_par]
                    dA3 = dA.rearrange("p (n t) -> p n t", n=DS)
                    b_dA = b_dAs[g_par]
                    bfs3 = mb(32768 + s_i * 8192, 4096).rearrange(
                        "p (n t) -> p n t", n=DS)
                    cfs3 = mb(65536 + s_i * 8192, 4096).rearrange(
                        "p (n t) -> p n t", n=DS)

                    def em_da(sc, dt=dt, s_i=s_i, dA=dA, delta=delta):
                        r = None
                        for n in range(DS):
                            r = sc.activation(
                                dA[:, n * SC:(n + 1) * SC],
                                delta[:, s_i * SC:(s_i + 1) * SC], ACT.Exp,
                                scale=sb_A[:, dt * DS + n:dt * DS + n + 1])
                        return r
                    p.add("scalar", em_da, reads=[b_deltas[par], B("A")],
                          writes=[b_dA])
                    p.add("vector", lambda v, s_i=s_i, delta=delta, xbsl=xbsl:
                          v.tensor_tensor(
                              sb_u[:], delta[:, s_i * SC:(s_i + 1) * SC],
                              xbsl[:, s_i * SC:(s_i + 1) * SC], ALU.mult),
                          reads=[b_deltas[par], b_xbsls[par]], writes=[B("u")])
                    p.add("vector", lambda v, bfs3=bfs3: v.tensor_tensor(
                        dbx3,
                        sb_u[:].unsqueeze(1).broadcast_to([128, DS, SC]),
                        bfs3, ALU.mult),
                        reads=[B("u"), b_bfree], writes=[b_dBx])
                    p.add("vector", lambda v, hsl=hsl, dA3=dA3: v.tensor_tensor(
                        sb_u16[:],
                        dA3[:, :, 0:1].rearrange("p n o -> p (n o)"),
                        hsl, ALU.mult),
                        reads=[b_dA, B("hcar")], writes=[B("u16")], force=True)
                    p.add("vector", lambda v: v.tensor_tensor(
                        dbx3[:, :, 0:1].rearrange("p n o -> p (n o)"),
                        dbx3[:, :, 0:1].rearrange("p n o -> p (n o)"),
                        sb_u16[:], ALU.add), reads=[B("u16")], writes=[b_dBx],
                        force=True)
                    p.add("vector", lambda v, dA3=dA3: v.memset(
                        dA3[:, :, 0:1], 0.0), writes=[b_dA], force=True)
                    p.add("vector", lambda v, dA=dA: v.tensor_tensor_scan(
                        hb, dA, dbx, 0.0, ALU.mult, ALU.add),
                        reads=[b_dBx, b_dA], writes=[b_h], force=True)
                    p.add("vector", lambda v, hsl=hsl: v.tensor_copy(
                        hsl, hb3[:, :, SC - 1:SC].rearrange("p n o -> p (n o)")),
                        reads=[b_h], writes=[B("hcar")], force=True)
                    p.add("vector", lambda v, cfs3=cfs3: v.tensor_tensor(
                        dbx3, hb3, cfs3, ALU.mult),
                        reads=[b_h, b_cfree], writes=[b_dBx])
                    w_ = 2048
                    while w_ >= SC:
                        p.add("vector", lambda v, w_=w_: v.tensor_tensor(
                            dbx[:, 0:w_], dbx[:, 0:w_], dbx[:, w_:2 * w_],
                            ALU.add), writes=[b_dBx])
                        w_ //= 2

                    def em_y(v, s_i=s_i, xbsl=xbsl, dt=dt):
                        return v.scalar_tensor_tensor(
                            ytmp[:, s_i * SC:(s_i + 1) * SC],
                            xbsl[:, s_i * SC:(s_i + 1) * SC],
                            sb_D[:, dt:dt + 1], dbx[:, 0:SC],
                            ALU.mult, ALU.add)
                    p.add("vector", em_y, reads=[b_xbsls[par], B("D"), b_dBx],
                          writes=[b_ytmp])
                p.add("scalar", lambda s, par=par: s.activation(
                    tnh, zsls[par], ACT.Tanh, scale=0.5),
                    reads=[b_zsls[par]], writes=[b_tnh])
                p.add("vector", lambda v: v.tensor_scalar(
                    tnh, tnh, 0.5, 0.5, ALU.mult, ALU.add), writes=[b_tnh])
                p.add("vector", lambda v, par=par: v.tensor_tensor(
                    ytmp[:], ytmp[:], zsls[par], ALU.mult),
                    reads=[b_zsls[par]], writes=[b_ytmp])
                p.add("vector", lambda v: v.tensor_tensor(
                    ytmp[:], ytmp[:], tnh, ALU.mult),
                    reads=[b_tnh], writes=[b_ytmp])
                st = p.dma("sync", h_y.ap()[dt * 128:(dt + 1) * 128, T0:T0 + TC],
                           ytmp[:], reads=[b_ytmp])
                b_ypark.writers.append(st)

            bar5 = p.barrier()
            g_all = mf(0, 16384)
            gq = mb(65536, 16384)
            xqs = mb(131072, 8 * 1024)
            gmax = mf(147456, 512)
            gmin = mf(149504, 512)
            qi32 = mi(151552, 512)
            p1t = mf(153600, 512)
            gtmps = [mf(163840, 512), mf(155648, 512)]
            gtmp2s = [mf(165888, 512), mf(157696, 512)]
            utms = [mf(167936, 512), mf(159744, 512)]
            qtmp = mf(169984, 512)
            b_g = newB("g_all", bar5)
            b_gq = newB("gq", bar5)
            b_xqs = newB("xqs", bar5)
            b_gmax = newB("gmax", bar5)
            b_gmin = newB("gmin", bar5)
            b_qi = newB("qi32", bar5)
            b_p1 = newB("p1t", bar5)
            b_gts = [newB("gtmp0", bar5), newB("gtmp1", bar5)]
            b_gt2s = [newB("gtmp20", bar5), newB("gtmp21", bar5)]
            b_uts = [newB("utm0", bar5), newB("utm1", bar5)]
            b_qt = newB("qtmp", bar5)
            b_fw = [newB("fw0", bar5), newB("fw1", bar5)]
            b_fwd = [newB("fwd0", bar5), newB("fwd1", bar5)]
            b_gqk = [newB("gqk%d" % k2, bar5) for k2 in range(32)]
            FWO = [172032, 176128]
            FWD = [172032, 161792]
            p.dma("sync", xqs.rearrange("p (k t) -> p k t", k=8),
                  h_xq.ap()[:, T0:T0 + TC].rearrange("(p k) t -> p k t", k=8),
                  reads=[B("h_xq")], writes=[b_xqs])
            def p5_production(sub):
                tg0 = T0 + sub * 512
                fxsl = sb_fx[:, tg0:tg0 + 512]
                p.add("vector", lambda v: v.memset(gmax, -3.0e38),
                      writes=[b_gmax])
                p.add("vector", lambda v: v.memset(gmin, 3.0e38),
                      writes=[b_gmin])
                for j in range(32):
                    wgt = mb(FWO[j % 2], 8 * 128)
                    wut = mb(FWO[j % 2] + 2048, 8 * 128)
                    p.dma("gpsimd", wgt.rearrange("p (k c) -> p k c", c=128),
                          h_wqg.ap()[j * 1024:(j + 1) * 1024, :].rearrange(
                              "(p k) c -> p k c", k=8),
                          reads=[B("wgpark")], writes=[b_fw[j % 2]])
                    p.dma("sync", wut.rearrange("p (k c) -> p k c", c=128),
                          h_wqu.ap()[j * 1024:(j + 1) * 1024, :].rearrange(
                              "(p k) c -> p k c", k=8),
                          reads=[B("wupark")], writes=[b_fw[j % 2]])
                    bank_g = (j % 2) * 2
                    bank_u = (j % 2) * 2 + 1

                    def em_mg(t, wgt=wgt, bank_g=bank_g, sub=sub):
                        r = None
                        for k in range(8):
                            r = t.matmul(
                                psum[bank_g][:],
                                wgt.rearrange("p (k c) -> p k c",
                                              c=128)[:, k, :],
                                xqs[:, k * 1024 + sub * 512:
                                    k * 1024 + sub * 512 + 512],
                                start=(k == 0), stop=(k == 7))
                        return r
                    p.add("tensor", em_mg, reads=[b_fw[j % 2], b_xqs],
                          writes=[pbuf[bank_g]])

                    def em_mu(t, wut=wut, bank_u=bank_u, sub=sub):
                        r = None
                        for k in range(8):
                            r = t.matmul(
                                psum[bank_u][:],
                                wut.rearrange("p (k c) -> p k c",
                                              c=128)[:, k, :],
                                xqs[:, k * 1024 + sub * 512:
                                    k * 1024 + sub * 512 + 512],
                                start=(k == 0), stop=(k == 7))
                        return r
                    p.add("tensor", em_mu, reads=[b_fw[j % 2], b_xqs],
                          writes=[pbuf[bank_u]])
                    gtmp = gtmps[j % 2]
                    gtmp2 = gtmp2s[j % 2]
                    utm = utms[j % 2]
                    b_gt = b_gts[j % 2]
                    b_gt2 = b_gt2s[j % 2]
                    b_ut = b_uts[j % 2]
                    p.add("vector", lambda v, bank_g=bank_g, fxsl=fxsl,
                          gtmp=gtmp: v.scalar_tensor_tensor(
                              gtmp, psum[bank_g][:], col(C_SWG), fxsl,
                              ALU.mult, ALU.mult),
                          reads=[pbuf[bank_g], B("wg_sw"), b_fx],
                          writes=[b_gt])
                    p.add("scalar", lambda s, gtmp=gtmp, gtmp2=gtmp2:
                          s.activation(gtmp2, gtmp, ACT.Silu),
                          reads=[b_gt], writes=[b_gt2])
                    p.add("vector", lambda v, bank_u=bank_u, fxsl=fxsl,
                          utm=utm: v.scalar_tensor_tensor(
                              utm, psum[bank_u][:], col(C_SWU), fxsl,
                              ALU.mult, ALU.mult),
                          reads=[pbuf[bank_u], B("wu_sw"), b_fx],
                          writes=[b_ut])
                    p.add("vector", lambda v, j=j, gtmp2=gtmp2, utm=utm:
                          v.tensor_tensor(
                              g_all[:, j * 512:(j + 1) * 512], gtmp2, utm,
                              ALU.mult),
                          reads=[b_gt2, b_ut], writes=[b_g])
                    p.add("vector", lambda v, j=j: v.tensor_tensor(
                        gmax, gmax, g_all[:, j * 512:(j + 1) * 512], ALU.max),
                        reads=[b_g], writes=[b_gmax])
                    p.add("vector", lambda v, j=j: v.tensor_tensor(
                        gmin, gmin, g_all[:, j * 512:(j + 1) * 512], ALU.min),
                        reads=[b_g], writes=[b_gmin])
            def p5_rest(sub):
                tg0 = T0 + sub * 512
                fxsl = sb_fx[:, tg0:tg0 + 512]
                p.add("vector", lambda v: v.tensor_scalar_mul(
                    gmin, gmin, -1.0), writes=[b_gmin])
                p.add("vector", lambda v: v.tensor_tensor(
                    gmax, gmax, gmin, ALU.max), reads=[b_gmin],
                    writes=[b_gmax])
                FB = 65536 // 4
                p.dma("sync", Mf[0:16, FB:FB + 4096], gmax,
                      reads=[b_gmax], writes=[b_gq])
                p.add("vector", lambda v: v.tensor_tensor(
                    Mf[0:16, FB:FB + 2048], Mf[0:16, FB:FB + 2048],
                    Mf[0:16, FB + 2048:FB + 4096], ALU.max), writes=[b_gq])
                p.add("vector", lambda v: v.tensor_tensor(
                    Mf[0:16, FB:FB + 1024], Mf[0:16, FB:FB + 1024],
                    Mf[0:16, FB + 1024:FB + 2048], ALU.max), writes=[b_gq])
                p.add("vector", lambda v: v.tensor_tensor(
                    Mf[0:16, FB:FB + 512], Mf[0:16, FB:FB + 512],
                    Mf[0:16, FB + 512:FB + 1024], ALU.max), writes=[b_gq])
                p.dma("sync", Mf[0:4, FB + 2048:FB + 2048 + 2048],
                      Mf[0:16, FB:FB + 512], reads=[b_gq], writes=[b_gq])
                p.add("vector", lambda v: v.tensor_tensor(
                    Mf[0:4, FB + 2048:FB + 2048 + 1024],
                    Mf[0:4, FB + 2048:FB + 2048 + 1024],
                    Mf[0:4, FB + 3072:FB + 3072 + 1024], ALU.max),
                    writes=[b_gq])
                p.add("vector", lambda v: v.tensor_tensor(
                    Mf[0:4, FB + 2048:FB + 2048 + 512],
                    Mf[0:4, FB + 2048:FB + 2048 + 512],
                    Mf[0:4, FB + 2560:FB + 2560 + 512], ALU.max),
                    writes=[b_gq])
                p.dma("sync", Mf[0:1, FB:FB + 2048],
                      Mf[0:4, FB + 2048:FB + 2048 + 512],
                      reads=[b_gq], writes=[b_gq])
                p.add("vector", lambda v: v.tensor_tensor(
                    Mf[0:1, FB:FB + 1024], Mf[0:1, FB:FB + 1024],
                    Mf[0:1, FB + 1024:FB + 2048], ALU.max), writes=[b_gq])
                p.add("vector", lambda v: v.tensor_tensor(
                    Mf[0:1, FB:FB + 512], Mf[0:1, FB:FB + 512],
                    Mf[0:1, FB + 512:FB + 1024], ALU.max), writes=[b_gq])
                p.add("vector", lambda v: v.tensor_scalar_max(
                    Mf[0:1, FB:FB + 512], Mf[0:1, FB:FB + 512], 1e-5),
                    writes=[b_gq], force=True)
                p.add("vector", lambda v: v.reciprocal(
                    Mf[0:1, FB + 1024:FB + 1024 + 512],
                    Mf[0:1, FB:FB + 512]), writes=[b_gq], force=True)
                p.add("vector", lambda v: v.tensor_scalar_mul(
                    Mf[0:1, FB + 1024:FB + 1024 + 512],
                    Mf[0:1, FB + 1024:FB + 1024 + 512], 127.0),
                    writes=[b_gq], force=True)
                p.add("vector", lambda v: v.tensor_scalar(
                    Mf[0:1, FB + 2048:FB + 2048 + 512], Mf[0:1, FB:FB + 512],
                    1.0 / 127.0, sb_cols[0:1, C_SWD:C_SWD + 1],
                    ALU.mult, ALU.mult),
                    reads=[B("wd_sw")], writes=[b_gq], force=True)
                bcast_rows(lambda p0, n: sb_s2b[p0:p0 + n, :],
                           Mf[0:1, FB + 1024:FB + 1024 + 512], 128,
                           [b_gq], "s2b")
                bcast_rows(lambda p0, n: sb_fdb[p0:p0 + n, :],
                           Mf[0:1, FB + 2048:FB + 2048 + 512], 128,
                           [b_gq], "fdb")
                b_s2b = B("s2b")
                b_fdb2 = B("fdb")
                for j in range(32):
                    gsl = g_all[:, j * 512:(j + 1) * 512]
                    p.add("vector", lambda v, gsl=gsl: v.tensor_tensor(
                        qi32, gsl, sb_s2b[:], ALU.mult),
                        reads=[b_g, b_s2b], writes=[b_qi])
                    p.add("vector", lambda v, j=j: v.tensor_copy(
                        gq[:, j * 512:(j + 1) * 512], qi32),
                        reads=[b_qi], writes=[b_gqk[j], b_gq])
                for j2 in range(8):
                    wdt_ = mb(FWD[j2 % 2], 32 * 128)
                    wrt = ([b_fwd[0], b_fw[0], b_fw[1]] if j2 % 2 == 0 else
                           [b_fwd[1], b_gts[0], b_gt2s[0], b_uts[0]])
                    p.dma("sync" if j2 % 2 else "gpsimd",
                          wdt_.rearrange("p (k c) -> p k c", c=128),
                          h_wqd.ap()[j2 * 4096:(j2 + 1) * 4096, :].rearrange(
                              "(k p) c -> p k c", p=128),
                          reads=[B("wdpark")], writes=wrt)
                    bank = 4 + (j2 % 2)
                    kgrp = 8 if j2 == 0 else (16 if j2 == 1 else 32)
                    for k0 in range(0, 32, kgrp):
                        def em_md(t, wdt_=wdt_, bank=bank, k0=k0, kgrp=kgrp):
                            r = None
                            for k in range(k0, k0 + kgrp):
                                r = t.matmul(
                                    psum[bank][:],
                                    wdt_.rearrange("p (k c) -> p k c",
                                                   c=128)[:, k, :],
                                    gq[:, k * 512:(k + 1) * 512],
                                    start=(k == 0), stop=(k == 31))
                            return r
                        p.add("tensor", em_md,
                              reads=wrt + [b_gqk[k0 + kgrp - 1]],
                              writes=[pbuf[bank]])
                    p.add("vector", lambda v, bank=bank: v.tensor_tensor(
                        qtmp, psum[bank][:], sb_fdb[:], ALU.mult),
                        reads=[pbuf[bank], b_fdb2], writes=[b_qt])

                    def em_a(v, j2=j2, sub=sub):
                        sl = acc[:, j2 * 1024 + sub * 512:
                                 j2 * 1024 + sub * 512 + 512]
                        return v.tensor_tensor(sl, sl, qtmp, ALU.add)
                    p.add("vector", em_a, reads=[b_qt], writes=[b_acc])

            bar4 = bar5
            p5_production(0)
            acc = mf(98304, 8 * 1024)
            b_acc = newB("acc", bar4)
            p.dma("sync", acc.rearrange("p (j t) -> p j t", j=8),
                  h_base.ap()[:, T0:T0 + TC].rearrange("(j p) t -> p j t",
                                                       p=128),
                  reads=[B("h_base")], writes=[b_acc])
            ytile = mb(65536, 16 * 1024)
            b_y = newB("yres", bar4)
            p.dma("sync", ytile.rearrange("p (k t) -> p k t", k=16),
                  h_y.ap()[:, T0:T0 + TC].rearrange("(k p) t -> p k t", p=128),
                  reads=[b_ypark], writes=[b_y])
            wo_ = [newB("woa", bar4), newB("wob", bar4)]
            for j in range(8):
                wt = sb_wo4[:, (j % 2) * 2048:(j % 2) * 2048 + 2048]
                p.dma("gpsimd", wt.rearrange("p (k c) -> p k c", c=128),
                      i_wo.ap()[:, j * 128:(j + 1) * 128].rearrange(
                          "(k p) c -> p k c", p=128),
                      writes=[wo_[j % 2]])
                for th in range(2):
                    bank = 6 + (j * 2 + th) % 2

                    def em_m(t, wt=wt, th=th, bank=bank):
                        r = None
                        for k in range(16):
                            r = t.matmul(
                                psum[bank][:],
                                wt.rearrange("p (k c) -> p k c", c=128)[:, k, :],
                                ytile[:, k * 1024 + th * 512:
                                      k * 1024 + th * 512 + 512],
                                start=(k == 0), stop=(k == 15))
                        return r
                    p.add("tensor", em_m, reads=[wo_[j % 2], b_y, b_gq],
                          writes=[pbuf[bank]])

                    def em_e(v, j=j, th=th, bank=bank):
                        sl = acc[:, j * 1024 + th * 512:
                                 j * 1024 + th * 512 + 512]
                        return v.tensor_tensor(sl, sl, psum[bank][:], ALU.add)
                    p.add("vector", em_e, reads=[pbuf[bank]], writes=[b_acc])

            p5_rest(0)
            p5_production(1)
            p5_rest(1)

            bar6 = p.barrier()
            otok = mf(163840, 1024)
            b_ot = newB("otok", bar6)
            for ttl in range(8):
                for half in range(2):
                    bank = 6 + half

                    def em_t(t, ttl=ttl, half=half, bank=bank):
                        r = None
                        for q in range(4):
                            j = half * 4 + q
                            r = t.transpose(
                                psum[bank][:, q * 128:(q + 1) * 128],
                                acc[:, j * 1024 + ttl * 128:
                                    j * 1024 + ttl * 128 + 128], sb_idf[:])
                        return r
                    p.add("tensor", em_t, reads=[b_acc, B("idf")],
                          writes=[pbuf[bank]])
                    for q in range(4):
                        j = half * 4 + q
                        p.add("scalar", lambda sc, j=j, bank=bank, q=q:
                              sc.copy(otok[:, j * 128:(j + 1) * 128],
                                      psum[bank][:, q * 128:(q + 1) * 128]),
                              reads=[pbuf[bank]], writes=[b_ot])
                p.dma("sync",
                      o_y.ap()[T0 + ttl * 128:T0 + (ttl + 1) * 128, :],
                      otok, reads=[b_ot], writes=[B("outdone")])

        p.finalize(es)
    return nc


_NC_CACHE = {}


def kernel(L0_attn_x, L0_attn_norm_w, L0_attn_norm_b, L0_attn_in_proj_w,
           L0_attn_conv_w, L0_attn_conv_b, L0_attn_x_proj_w, L0_attn_dt_proj_w,
           L0_attn_dt_proj_b, L0_attn_A_log, L0_attn_D_param,
           L0_attn_out_proj_w, L0_ffn_w_gate, L0_ffn_w_up, L0_ffn_w_down):
    def f(a):
        return np.ascontiguousarray(np.asarray(a, dtype=np.float32))
    x = f(L0_attn_x)
    wmap = {
        "nw": f(L0_attn_norm_w).reshape(1, DM),
        "nb": f(L0_attn_norm_b).reshape(1, DM),
        "w_in": f(L0_attn_in_proj_w),
        "conv_w": f(L0_attn_conv_w).reshape(DI, 4),
        "conv_b": f(L0_attn_conv_b).reshape(1, DI),
        "w_xp": f(L0_attn_x_proj_w),
        "w_dt": f(L0_attn_dt_proj_w),
        "dt_b": f(L0_attn_dt_proj_b).reshape(1, DI),
        "a_log": f(L0_attn_A_log),
        "d_par": f(L0_attn_D_param).reshape(1, DI),
        "w_out": f(L0_attn_out_proj_w),
        "w_g": f(L0_ffn_w_gate),
        "w_u": f(L0_ffn_w_up),
        "w_d": f(L0_ffn_w_down),
    }
    if "nc" not in _NC_CACHE:
        _NC_CACHE["nc"] = build_nc()
    nc = _NC_CACHE["nc"]
    in_maps = [dict(wmap, xb=np.ascontiguousarray(x[b])) for b in range(8)]
    res = run_bass_kernel_spmd(nc, in_maps, core_ids=list(range(8)))
    out = np.stack([np.asarray(res.results[b]["y_out"]) for b in range(8)],
                   axis=0)
    return out.astype(np.float32)


# version 40
# speedup vs baseline: 1.0340x; 1.0022x over previous
import numpy as np
import concourse.bass as bass
import concourse.mybir as mybir
from concourse.bass_utils import run_bass_kernel_spmd
from contextlib import ExitStack

F32 = mybir.dt.float32
BF16 = mybir.dt.bfloat16
I32 = mybir.dt.int32
ALU = mybir.AluOpType
ACT = mybir.ActivationFunctionType
AX = mybir.AxisListType

L = 2048
DM = 1024
DI = 2048
DS = 16
NTT = 16
NDI = 16
NCH = 2
TC = 1024
NSUB = 4
SC = 256
INNER = 4096

ENGINES = ("sync", "scalar", "vector", "tensor", "gpsimd")


class Step:
    __slots__ = ("engine", "emit", "inc", "count_after", "idx", "deps",
                 "fdeps")


class Buf:
    __slots__ = ("writers", "readers")

    def __init__(self, init=()):
        self.writers = list(init)
        self.readers = []


class Prog:
    def __init__(self, nc):
        self.nc = nc
        self.steps = []
        self._counts = {e: 0 for e in ENGINES}
        self._last = {e: None for e in ENGINES}

    def add(self, engine, emit, reads=(), writes=(), extra=(), dma=False,
            ptr_reads=(), force=False):
        deps = set()
        for b in reads:
            deps.update(b.writers)
        for b in writes:
            deps.update(b.writers)
            deps.update(b.readers)
        deps.update(extra)
        fdeps = set()
        for b in ptr_reads:
            fdeps.update(b.writers)
        if force:
            fdeps.update(deps)
        st = Step()
        st.engine = engine
        st.emit = emit
        st.inc = 16 if dma else 1
        st.deps = list(deps - fdeps)
        st.fdeps = list(fdeps)
        st.idx = len(self.steps)
        self._counts[engine] += st.inc
        st.count_after = self._counts[engine]
        self.steps.append(st)
        self._last[engine] = st
        for b in reads:
            b.readers.append(st)
        for b in ptr_reads:
            b.readers.append(st)
        for b in writes:
            b.writers = [st]
            b.readers = []
        return st

    def dma(self, engine, out, in_, reads=(), writes=(), extra=(), slow=False):
        def emit(eng, out=out, in_=in_, slow=slow):
            return eng.dma_start(out, in_, allow_slow_non_contiguous=slow)
        return self.add(engine, emit, reads, writes, extra, dma=True)

    def barrier(self):
        return [s for s in self._last.values() if s is not None]

    def finalize(self, es):
        nc = self.nc
        sems = {e: es.enter_context(nc.semaphore("pg_" + e)) for e in ENGINES}
        by_engine = {e: [] for e in ENGINES}
        for st in self.steps:
            by_engine[st.engine].append(st)
        final = dict(self._counts)
        blk = es.enter_context(nc.Block())

        def mk(ename):
            def fn(eng):
                waited = {e: 0 for e in ENGINES}
                for st in by_engine[ename]:
                    need = {}
                    if st.inc == 16:
                        need[ename] = st.count_after - 16
                    for d in st.deps:
                        if d.engine == ename and d.inc == 1:
                            continue
                        if d.count_after > need.get(d.engine, 0):
                            need[d.engine] = d.count_after
                    for d in st.fdeps:
                        if d.count_after > need.get(d.engine, 0):
                            need[d.engine] = d.count_after
                    for e in sorted(need):
                        if need[e] > waited[e]:
                            eng.wait_ge(sems[e], need[e])
                            waited[e] = need[e]
                    st.emit(eng).then_inc(sems[ename], st.inc)
                if ename == "sync":
                    for e in ENGINES:
                        if final[e] > 0:
                            eng.wait_ge(sems[e], final[e])
            return fn

        for e in ENGINES:
            if by_engine[e] or e == "sync":
                getattr(blk, e)(mk(e))


def build_nc():
    nc = bass.Bass("TRN2", target_bir_lowering=False, debug=False)

    i_x = nc.dram_tensor("xb", [L, DM], F32, kind="ExternalInput")
    i_nw = nc.dram_tensor("nw", [1, DM], F32, kind="ExternalInput")
    i_nb = nc.dram_tensor("nb", [1, DM], F32, kind="ExternalInput")
    i_w1 = nc.dram_tensor("w_in", [DM, 2 * DI], F32, kind="ExternalInput")
    i_cw = nc.dram_tensor("conv_w", [DI, 4], F32, kind="ExternalInput")
    i_cb = nc.dram_tensor("conv_b", [1, DI], F32, kind="ExternalInput")
    i_wx = nc.dram_tensor("w_xp", [DI, 2080], F32, kind="ExternalInput")
    i_wdt = nc.dram_tensor("w_dt", [DI, DI], F32, kind="ExternalInput")
    i_dtb = nc.dram_tensor("dt_b", [1, DI], F32, kind="ExternalInput")
    i_alog = nc.dram_tensor("a_log", [DI, DS], F32, kind="ExternalInput")
    i_dp = nc.dram_tensor("d_par", [1, DI], F32, kind="ExternalInput")
    i_wo = nc.dram_tensor("w_out", [DI, DM], F32, kind="ExternalInput")
    i_wg = nc.dram_tensor("w_g", [INNER, DM], F32, kind="ExternalInput")
    i_wu = nc.dram_tensor("w_u", [INNER, DM], F32, kind="ExternalInput")
    i_wd = nc.dram_tensor("w_d", [DM, INNER], F32, kind="ExternalInput")
    o_y = nc.dram_tensor("y_out", [L, DM], F32, kind="ExternalOutput")
    h_base = nc.dram_tensor("s_base", [DM, L], F32, kind="Internal")
    h_xn2 = nc.dram_tensor("s_xn2", [DM, L], BF16, kind="Internal")
    h_xq = nc.dram_tensor("s_xq", [DM, L], BF16, kind="Internal")
    h_xb = nc.dram_tensor("s_xb", [DI, L], BF16, kind="Internal")
    h_z = nc.dram_tensor("s_z", [DI, L], BF16, kind="Internal")
    h_y = nc.dram_tensor("s_y", [DI, L], BF16, kind="Internal")
    h_wqg = nc.dram_tensor("s_wqg", [INNER // 128 * DM, 128], BF16,
                           kind="Internal")
    h_wqu = nc.dram_tensor("s_wqu", [INNER // 128 * DM, 128], BF16,
                           kind="Internal")
    h_wqd = nc.dram_tensor("s_wqd", [DM // 128 * INNER, 128], BF16,
                           kind="Internal")

    es = ExitStack()
    with es:
        def sbt(name, shape, dt=F32):
            return es.enter_context(nc.sbuf_tensor(name, shape, dt))

        MEGA = sbt("mega", [128, 45056], F32)
        Mf = MEGA[:]
        Mb = MEGA[:].bitcast(BF16)
        Mi = MEGA[:].bitcast(I32)

        def mf(off, n, p=128):
            return Mf[0:p, off // 4: off // 4 + n]

        def mb(off, n, p=128):
            return Mb[0:p, off // 2: off // 2 + n]

        def mi(off, n, p=128):
            return Mi[0:p, off // 4: off // 4 + n]

        sb_fx = sbt("fx", [128, L], F32)
        sb_A = sbt("a_sb", [128, NDI * DS], F32)
        sb_D = sbt("d_sb", [128, NDI], F32)
        sb_cb = sbt("cb_sb", [128, NDI], F32)
        sb_dtb = sbt("dtb_sb", [128, NDI], F32)
        sb_cw = sbt("cw_sb", [128, NDI * 4], F32)
        sb_pidx = sbt("pidx", [128, 1], F32)
        sb_srec = sbt("srec", [128, NTT], F32)
        sb_cols = sbt("cols", [128, 16], F32)
        sb_idf = sbt("idf", [128, 128], F32)
        sb_idb = sbt("idb", [128, 128], BF16)
        sb_hcar = sbt("hcar", [128, NDI * DS], F32)
        sb_halo = sbt("halo", [128, NDI * 4], F32)
        sb_Bbf = sbt("bbf", [16, TC], BF16)
        sb_Cbf = sbt("cbf", [16, TC], BF16)
        sb_u = sbt("u_sb", [128, 256], BF16)
        sb_u16 = sbt("u16_sb", [128, 16], F32)
        sb_s2b = sbt("s2b", [128, 512], F32)
        sb_fdb = sbt("fdb", [128, 512], F32)
        sb_wo4 = sbt("wo4", [128, 4096], BF16)

        psum = [es.enter_context(nc.psum_tensor("pb%d" % i, [128, 512], F32))
                for i in range(8)]

        p = Prog(nc)
        bufs = {}

        def B(name):
            if name not in bufs:
                bufs[name] = Buf()
            return bufs[name]

        def newB(name, init=()):
            bufs[name] = Buf(init)
            return bufs[name]

        pbuf = [B("psum%d" % i) for i in range(8)]

        C_SWG, C_SWU, C_SWD, C_THR, C_NTHR, C_T0, C_T1, C_T2 = range(8)

        def col(i):
            return sb_cols[:, i:i + 1]

        p.add("gpsimd", lambda g: g.iota(
            sb_idf[:], [[1, 128]], channel_multiplier=0,
            allow_small_or_imprecise_dtypes=True), writes=[B("idf")])
        p.add("gpsimd", lambda g: g.iota(
            sb_pidx[:], [[0, 1]], channel_multiplier=1,
            allow_small_or_imprecise_dtypes=True), writes=[B("pidx")])
        p.add("vector", lambda v: v.tensor_scalar(
            sb_idf[:], sb_idf[:], sb_pidx[:], None, ALU.is_equal),
            reads=[B("pidx")], writes=[B("idf")])
        p.add("vector", lambda v: v.tensor_copy(sb_idb[:], sb_idf[:]),
              reads=[B("idf")], writes=[B("idb")])

        p.dma("sync", sb_A[:], i_alog.ap().rearrange("(k p) n -> p k n", p=128),
              writes=[B("A")])
        p.add("scalar", lambda s: s.activation(sb_A[:], sb_A[:], ACT.Exp),
              writes=[B("A")])
        p.add("vector", lambda v: v.tensor_scalar_mul(sb_A[:], sb_A[:], -1.0),
              writes=[B("A")])
        p.dma("sync", sb_D[:], i_dp.ap().rearrange("o (k p) -> (o p) k", p=128), slow=True,
              writes=[B("D")])
        p.dma("sync", sb_cb[:], i_cb.ap().rearrange("o (k p) -> (o p) k", p=128), slow=True,
              writes=[B("cb")])
        p.dma("sync", sb_dtb[:], i_dtb.ap().rearrange("o (k p) -> (o p) k", p=128), slow=True,
              writes=[B("dtb")])
        p.dma("sync", sb_cw[:], i_cw.ap().rearrange("(k p) t -> p k t", p=128),
              writes=[B("cw")])
        p.add("vector", lambda v: v.memset(sb_hcar[:], 0.0), writes=[B("hcar")])
        p.add("vector", lambda v: v.memset(sb_halo[:], 0.0), writes=[B("halo")])

        O_BASE = 0
        O_XN2F = 65536
        O_XQF = 98304
        O_RA = 131072
        O_RB = 135168
        O_RC = 139264
        O_RD = 143360
        O_WQA = 147456
        O_WQB = 151552
        O_WQQ = 155648
        O_WQT = 157696
        O_NWB = 167936
        O_NBB = 172032

        base_fm = mf(O_BASE, 8 * 2048)
        xn2_fm = mb(O_XN2F, 8 * 2048)
        xq_fm = mb(O_XQF, 8 * 2048)
        nwb = mf(O_NWB, 1024)
        nbb = mf(O_NBB, 1024)
        b_base = newB("r_base")
        b_xn2f = newB("r_xn2f")
        b_xqf = newB("r_xqf")
        b_nwb = newB("r_nwb")
        b_nbb = newB("r_nbb")

        def bcast_rows(dst_row_ap_fn, first_in, n_parts, reads, bname):
            st = p.dma("sync", dst_row_ap_fn(0, 1), first_in,
                       reads=reads, writes=[B(bname)])
            pc = 1
            while pc < n_parts:
                st = p.dma("sync", dst_row_ap_fn(pc, min(pc, n_parts - pc)),
                           dst_row_ap_fn(0, min(pc, n_parts - pc)),
                           extra=[st])
                pc *= 2
            B(bname).writers.append(st)
            return st

        bcast_rows(lambda p0, n: Mf[p0:p0 + n, O_NWB // 4:O_NWB // 4 + 1024],
                   i_nw.ap(), 128, [], "r_nwb")
        bcast_rows(lambda p0, n: Mf[p0:p0 + n, O_NBB // 4:O_NBB // 4 + 1024],
                   i_nb.ap(), 128, [], "r_nbb")

        def transpose_to_fm(src_ap, src_buf, ident_ap, ident_buf,
                            dst_half_off, dst_is_bf16, dst_buf, tt):
            for half in range(2):
                bank = 6 + half
                pview = psum[bank][:].bitcast(BF16) if dst_is_bf16 \
                    else psum[bank][:]

                def em_t(t, src_ap=src_ap, half=half, pview=pview,
                         ident_ap=ident_ap):
                    r = None
                    for q in range(4):
                        j = half * 4 + q
                        r = t.transpose(pview[:, q * 128:(q + 1) * 128],
                                        src_ap[:, j * 128:(j + 1) * 128],
                                        ident_ap)
                    return r
                p.add("tensor", em_t, reads=[src_buf, ident_buf],
                      writes=[pbuf[bank]])
                if dst_is_bf16:
                    dstv = Mb[:, dst_half_off // 2:
                              dst_half_off // 2 + 16384].rearrange(
                        "p (j t) -> p j t", j=8)
                else:
                    dstv = Mf[:, dst_half_off // 4:
                              dst_half_off // 4 + 16384].rearrange(
                        "p (j t) -> p j t", j=8)
                dst = dstv[:, half * 4:half * 4 + 4, tt * 128:tt * 128 + 128]
                psv = pview[:, 0:512].rearrange("p (q t) -> p q t", q=4)
                p.add("scalar", lambda sc, dst=dst, psv=psv: sc.copy(dst, psv),
                      reads=[pbuf[bank]], writes=[dst_buf])

        XO = [131072, 135168, 139264, 143360]
        XNO = [147456, 151552, 155648, 159744]
        JNK = mf(163840, 1024)
        QT = mf(176128, 1024)
        QTb = mb(176128, 1024)
        QTi = mi(176128, 1024)
        bX = [newB("x%d" % i) for i in range(4)]
        bXN = [newB("xn%d" % i) for i in range(4)]
        bJ = newB("jnk")
        bQ = newB("qt")
        CS0, CS1, CS2 = 4, 8, 12
        bc0, bc1, bc2 = newB("cc0"), newB("cc1"), newB("cc2")

        def stats_chain():
            c0 = sb_cols[:, CS0:CS0 + 4]
            c1 = sb_cols[:, CS1:CS1 + 4]
            c2 = sb_cols[:, CS2:CS2 + 4]
            p.add("vector", lambda v: v.tensor_scalar_mul(c0, c0, 1.0 / DM),
                  writes=[bc0], force=True)
            p.add("vector", lambda v: v.tensor_tensor(c2, c0, c0, ALU.mult),
                  reads=[bc0], writes=[bc2], force=True)
            p.add("vector", lambda v: v.tensor_scalar_mul(c1, c1, 1.0 / DM),
                  writes=[bc1], force=True)
            p.add("vector", lambda v: v.tensor_tensor(c1, c1, c2, ALU.subtract),
                  reads=[bc2], writes=[bc1], force=True)
            p.add("vector", lambda v: v.tensor_scalar_add(c1, c1, 1e-5),
                  writes=[bc1], force=True)
            p.add("scalar", lambda s: s.activation(c1, c1, ACT.Sqrt),
                  writes=[bc1], force=True)
            p.add("vector", lambda v: v.reciprocal(c1, c1),
                  writes=[bc1], force=True)

        for rnd in range(4):
            tts = [rnd * 4 + i for i in range(4)]
            p.dma("sync" if rnd % 2 else "gpsimd",
                  Mf[:, XO[0] // 4:XO[0] // 4 + 4096].rearrange(
                      "p (r c) -> p r c", r=4),
                  i_x.ap()[rnd * 512:(rnd + 1) * 512, :].rearrange(
                      "(r p) c -> p r c", p=128),
                  writes=[bX[0], bX[1], bX[2], bX[3]])
            for i, tt in enumerate(tts):
                X = mf(XO[i], 1024)
                p.add("vector", lambda v, X=X, i=i: v.reduce_sum(
                    sb_cols[:, CS0 + i:CS0 + i + 1], X, AX.X),
                    reads=[bX[i]], writes=[bc0])
                p.add("scalar", lambda s, X=X, i=i: s.activation(
                    JNK, X, ACT.Square,
                    accum_out=sb_cols[:, CS1 + i:CS1 + i + 1]),
                    reads=[bX[i]], writes=[bJ, bc1])
            stats_chain()
            for i, tt in enumerate(tts):
                X = mf(XO[i], 1024)
                XN = mf(XNO[i], 1024)
                p.add("vector", lambda v, X=X, XN=XN, i=i: v.tensor_scalar(
                    XN, X, sb_cols[:, CS0 + i:CS0 + i + 1],
                    sb_cols[:, CS1 + i:CS1 + i + 1], ALU.subtract, ALU.mult),
                    reads=[bX[i]], writes=[bXN[i]], ptr_reads=[bc0, bc1])
                p.add("vector", lambda v, X=X, XN=XN: v.tensor_tensor(
                    X, X, XN, ALU.add), reads=[bXN[i]], writes=[bX[i]])
                transpose_to_fm(X, bX[i], sb_idf[:], B("idf"), O_BASE, False,
                                b_base, tt)
                p.add("vector", lambda v, XN=XN, i=i: v.reduce_sum(
                    sb_cols[:, CS0 + i:CS0 + i + 1], XN, AX.X),
                    reads=[bXN[i]], writes=[bc0])
                p.add("scalar", lambda s, XN=XN, i=i: s.activation(
                    JNK, XN, ACT.Square,
                    accum_out=sb_cols[:, CS1 + i:CS1 + i + 1]),
                    reads=[bXN[i]], writes=[bJ, bc1])
            stats_chain()
            for i, tt in enumerate(tts):
                X = mf(XO[i], 1024)
                Xb = mb(XO[i], 1024)
                XN = mf(XNO[i], 1024)
                p.add("vector", lambda v, X=X, XN=XN, i=i: v.tensor_scalar(
                    X, XN, sb_cols[:, CS0 + i:CS0 + i + 1],
                    sb_cols[:, CS1 + i:CS1 + i + 1], ALU.subtract, ALU.mult),
                    reads=[bXN[i]], writes=[bX[i]], ptr_reads=[bc0, bc1])
                p.add("vector", lambda v, X=X: v.tensor_tensor(
                    X, X, nwb, ALU.mult), reads=[b_nwb], writes=[bX[i]])
                p.add("vector", lambda v, X=X, Xb=Xb: v.tensor_tensor(
                    Xb, X, nbb, ALU.add), reads=[b_nbb], writes=[bX[i]])
                transpose_to_fm(Xb, bX[i], sb_idb[:], B("idb"), O_XN2F, True,
                                b_xn2f, tt)
                p.add("vector", lambda v, XN=XN, i=i: v.tensor_reduce(
                    sb_cols[:, CS0 + i:CS0 + i + 1], XN, AX.X, ALU.max,
                    apply_absolute_value=True),
                    reads=[bXN[i]], writes=[bc0])
            c0 = sb_cols[:, CS0:CS0 + 4]
            c1 = sb_cols[:, CS1:CS1 + 4]
            p.add("vector", lambda v: v.tensor_scalar_max(c0, c0, 1e-5),
                  writes=[bc0], force=True)
            p.add("vector", lambda v, rnd=rnd: v.tensor_scalar_mul(
                sb_srec[:, rnd * 4:rnd * 4 + 4], c0, 1.0 / 127.0),
                reads=[bc0], writes=[B("srec")], force=True)
            p.add("vector", lambda v: v.reciprocal(c1, c0),
                  reads=[bc0], writes=[bc1], force=True)
            p.add("vector", lambda v: v.tensor_scalar_mul(c1, c1, 127.0),
                  writes=[bc1], force=True)
            for i, tt in enumerate(tts):
                XN = mf(XNO[i], 1024)
                XNi = mi(XNO[i], 1024)
                Xb2 = mb(XO[i], 1024)
                p.add("vector", lambda v, XN=XN, XNi=XNi, i=i: v.tensor_scalar(
                    XNi, XN, sb_cols[:, CS1 + i:CS1 + i + 1], None, ALU.mult),
                    reads=[bXN[i]], writes=[bXN[i]], ptr_reads=[bc1])
                p.add("vector", lambda v, XNi=XNi, Xb2=Xb2: v.tensor_copy(
                    Xb2, XNi), reads=[bXN[i]], writes=[bX[i]])
                transpose_to_fm(Xb2, bX[i], sb_idb[:], B("idb"), O_XQF, True,
                                b_xqf, tt)

        p.add("tensor", lambda t: t.transpose(
            psum[6][0:16, 0:128], sb_srec[:], sb_idf[:]),
            reads=[B("srec"), B("idf")], writes=[pbuf[6]])
        p.add("scalar", lambda sc: sc.copy(
            Mf[0:16, O_RA // 4:O_RA // 4 + 128], psum[6][0:16, 0:128]),
            reads=[pbuf[6]], writes=[bX[0]])
        bcast_rows(lambda p0, n: sb_fx[p0:p0 + n, :],
                   Mf[0:16, O_RA // 4:O_RA // 4 + 128], 128, [bX[0]],
                   "fx")
        b_fx = B("fx")

        st_pb = p.dma("sync", h_base.ap().rearrange("(j p) t -> p j t", p=128),
                      base_fm.rearrange("p (j t) -> p j t", j=8),
                      reads=[b_base], writes=[B("h_base")])
        st_p2 = p.dma("gpsimd",
                      h_xn2.ap().rearrange("(j p) t -> p j t", p=128),
                      xn2_fm.rearrange("p (j t) -> p j t", j=8),
                      reads=[b_xn2f], writes=[B("h_xn2")])
        st_pq = p.dma("gpsimd",
                      h_xq.ap().rearrange("(j p) t -> p j t", p=128),
                      xq_fm.rearrange("p (j t) -> p j t", j=8),
                      reads=[b_xqf], writes=[B("h_xq")])

        O_WQA2, O_WQB2 = 32768, 40960
        O_WP1A, O_WP1B = 32768, 49152
        O_WP1C = 65536
        O_WQS1, O_WQS2 = 49152, 53248
        O_WQQ2 = 57344
        O_WQT2 = 61440
        O_WQTQ = 57344

        def wquant(src, dst, n_row_tiles, ncols, csw, tag):
            b_wqtmp = B("wqtmp")
            wa1 = [B("wp1_a"), B("wp1_b"), B("wp1_c")]
            wa = [B("wq_a"), B("wq_b")]
            p.add("vector", lambda v: v.memset(col(C_T0), 0.0), writes=[B("c0")])
            it = 0
            if ncols >= 2048:
                r1step, p1w = 1, ncols
            else:
                r1step, p1w = 4, 4 * ncols
            for rt0 in range(0, n_row_tiles, r1step):
                off = (O_WP1A, O_WP1B, O_WP1C)[it % 3]
                wt = mf(off, p1w)
                if r1step == 4:
                    inap = src[rt0 * 128:(rt0 + 4) * 128, :].rearrange(
                        "(p q) c -> p (q c)", p=128)
                else:
                    inap = src[rt0 * 128:(rt0 + 1) * 128, :]
                wrt = [wa1[it % 3]] + ([B("wqq"), B("wqt"), B("wqs")]
                                       if it % 3 == 1 else [])
                if it % 3 == 2:
                    wrt += [b_xn2f]
                if it < 2:
                    wrt += [B("wq_a"), B("wq_b")]
                p.dma(("gpsimd", "sync", "scalar")[it % 3], wt, inap,
                      writes=wrt)
                p.add("vector", lambda v, wt=wt: v.tensor_reduce(
                    col(C_T1), wt, AX.X, ALU.add,
                    apply_absolute_value=True),
                    reads=[wa1[it % 3]], writes=[B("c1")])
                p.add("vector", lambda v: v.tensor_tensor(
                    col(C_T0), col(C_T0), col(C_T1), ALU.add),
                    reads=[B("c1")], writes=[B("c0")], force=True)
                it += 1
                yield True
            FQ = O_WQTQ // 4
            f0 = Mf[0:16, FQ:FQ + 8]
            st1 = p.dma("sync", f0, col(C_T0), reads=[B("c0")],
                        writes=[b_wqtmp, B("wqq")])
            p.add("vector", lambda v: v.tensor_tensor(
                Mf[0:16, FQ:FQ + 4], Mf[0:16, FQ:FQ + 4],
                Mf[0:16, FQ + 4:FQ + 8], ALU.add),
                writes=[b_wqtmp], force=True)
            p.add("vector", lambda v: v.tensor_tensor(
                Mf[0:16, FQ:FQ + 2], Mf[0:16, FQ:FQ + 2],
                Mf[0:16, FQ + 2:FQ + 4], ALU.add),
                writes=[b_wqtmp], force=True)
            p.add("vector", lambda v: v.tensor_tensor(
                Mf[0:16, FQ:FQ + 1], Mf[0:16, FQ:FQ + 1],
                Mf[0:16, FQ + 1:FQ + 2], ALU.add),
                writes=[b_wqtmp], force=True)
            p.dma("sync", Mf[0:1, FQ + 8:FQ + 24], Mf[0:16, FQ:FQ + 1],
                  reads=[b_wqtmp], writes=[b_wqtmp])
            p.add("vector", lambda v: v.tensor_reduce(
                Mf[0:1, FQ:FQ + 1], Mf[0:1, FQ + 8:FQ + 24], AX.X, ALU.add),
                writes=[b_wqtmp], force=True)
            nel = float(n_row_tiles * 128 * ncols)
            p.add("vector", lambda v: v.tensor_scalar(
                Mf[0:1, FQ:FQ + 1], Mf[0:1, FQ:FQ + 1], 1.0 / nel, 1e-5,
                ALU.mult, ALU.max), writes=[b_wqtmp], force=True)
            bcast_rows(lambda p0, n: sb_cols[p0:p0 + n, csw:csw + 1],
                       Mf[0:1, FQ:FQ + 1], 128, [b_wqtmp], tag + "_sw")
            bsw = B(tag + "_sw")
            p.add("vector", lambda v: v.tensor_scalar_mul(
                col(C_THR), col(csw), 0.5), reads=[bsw], writes=[B("thr")],
                force=True)
            p.add("vector", lambda v: v.tensor_scalar_mul(
                col(C_NTHR), col(C_THR), -1.0), reads=[B("thr")],
                writes=[B("nthr")], force=True)
            p.add("vector", lambda v: v.tensor_scalar_mul(
                col(csw), col(csw), 0.5), writes=[bsw], force=True)
            yield True
            it = 0
            CCW = min(2048, ncols)
            ncc = ncols // CCW
            rstep = 1 if ncols >= 2048 else 2
            for rt0 in range(0, n_row_tiles, rstep):
                for cc in range(ncc):
                    off = O_WQA2 if it % 2 == 0 else O_WQB2
                    wt = mf(off, 2048)
                    if rstep == 2:
                        inap = src[rt0 * 128:(rt0 + 2) * 128, :].rearrange(
                            "(r p) c -> p r c", p=128)
                        outap = wt.rearrange("p (r c) -> p r c", r=2)
                    else:
                        inap = src[rt0 * 128:(rt0 + 1) * 128,
                                   cc * 2048:(cc + 1) * 2048]
                        outap = wt
                    wrt = [wa[it % 2]] + ([wa1[0], wa1[1]] if it < 2 else [])
                    p.dma(("gpsimd", "sync", "scalar")[it % 3], outap, inap,
                          writes=wrt)
                    s1 = mb(O_WQS1, 2048)
                    s2 = mb(O_WQS2, 2048)
                    wq = mb(O_WQQ2, 2048)
                    p.add("scalar", lambda sc, wt=wt, s1=s1: sc.activation(
                        s1, wt, ACT.Sign, bias=col(C_NTHR)),
                        reads=[wa[it % 2]], writes=[B("wqs")],
                        ptr_reads=[B("nthr")])
                    p.add("scalar", lambda sc, wt=wt, s2=s2: sc.activation(
                        s2, wt, ACT.Sign, bias=col(C_THR)),
                        reads=[wa[it % 2]], writes=[B("wqs")],
                        ptr_reads=[B("thr")])
                    p.add("vector", lambda v, wq=wq, s1=s1, s2=s2:
                          v.tensor_tensor(wq, s1, s2, ALU.add),
                          reads=[B("wqs")], writes=[B("wqq")])
                    for half in range(4):
                        bank = 6 + half % 2
                        pview = psum[bank][:].bitcast(BF16)

                        def em_t(t, wq=wq, half=half, pview=pview):
                            r = None
                            for q in range(4):
                                jb = half * 4 + q
                                r = t.transpose(
                                    pview[:, q * 128:(q + 1) * 128],
                                    wq[:, jb * 128:(jb + 1) * 128], sb_idb[:])
                            return r
                        p.add("tensor", em_t, reads=[B("wqq"), B("idb")],
                              writes=[pbuf[bank]])
                        o = O_WQT2 // 2 + half * 512
                        p.add("vector", lambda v, o=o, pview=pview:
                              v.tensor_copy(Mb[:, o:o + 512], pview[:, 0:512]),
                              reads=[pbuf[bank]], writes=[B("wqt")])

                    if rstep == 2:
                        for r in range(2):
                            def em_d(eng, rt0=rt0, r=r):
                                jb = rt0 + r
                                return eng.dma_start(
                                    dst[jb * 1024:(jb + 1) * 1024,
                                        :].rearrange(
                                        "(k p) c -> p k c", p=128),
                                    Mb[:, O_WQT2 // 2 + r * 1024:
                                       O_WQT2 // 2 + r * 1024 + 1024].rearrange(
                                        "p (k c) -> p k c", c=128))
                            p.add("sync", em_d, reads=[B("wqt")],
                                  writes=[B(tag + "park")], dma=True)
                    else:
                        def em_d(eng, rt0=rt0, cc=cc):
                            r0 = rt0 * 4096 + cc * 2048
                            return eng.dma_start(
                                dst[r0:r0 + 2048].rearrange(
                                    "(k p) c -> p k c", p=128),
                                Mb[:, O_WQT2 // 2:O_WQT2 // 2 + 2048].rearrange(
                                    "p (k c) -> p k c", c=128))
                        p.add("sync", em_d, reads=[B("wqt")],
                              writes=[B(tag + "park")], dma=True)
                    it += 1
                    yield True

        for ch in range(NCH):
            T0 = ch * TC
            bar = p.barrier()

            xn2s = mb(131072, 8 * 1024)
            b_xn2s = newB("xn2s", bar)
            if ch == 0:
                p.add("vector", lambda v: v.tensor_copy(
                    xn2s.rearrange("p (k t) -> p k t", k=8),
                    xn2_fm.rearrange("p (j t) -> p j t", j=8)[:, :, T0:T0 + TC]
                    if False else
                    xn2_fm.rearrange("p (j t) -> p j t", j=8)[:, :, 0:TC]),
                    reads=[b_xn2f], writes=[b_xn2s])
            else:
                p.dma("sync", xn2s.rearrange("p (k t) -> p k t", k=8),
                      h_xn2.ap()[:, T0:T0 + TC].rearrange(
                          "(k p) t -> p k t", p=128),
                      reads=[B("h_xn2")], writes=[b_xn2s])
            xbres = mb(98304, 16 * 1024)
            b_xb = newB("xbres", bar)
            b_xbpre = newB("xbpre", bar)
            b_convt = newB("convt", bar)
            w1b_ = [newB("w1a", bar), newB("w1b", bar)]
            W1O = [172544, 174848]
            xbpre = mf(163840, 1028)
            convt = mf(168192, 1024)
            convtb = mb(168192, 1024)
            b_xbpark = B("xb_park")
            b_zpark = B("z_park")
            if ch == 0:
                for _nm in ("wq_a", "wq_b", "wp1_a", "wp1_b", "wp1_c",
                            "wqtmp", "wqq", "wqt", "wqs"):
                    newB(_nm, bar)

                def _wq_all():
                    yield from wquant(i_wg.ap(), h_wqg.ap(), INNER // 128,
                                      DM, C_SWG, "wg")
                    yield from wquant(i_wu.ap(), h_wqu.ap(), INNER // 128,
                                      DM, C_SWU, "wu")
                    yield from wquant(i_wd.ap(), h_wqd.ap(), DM // 128,
                                      INNER, C_SWD, "wd")
                wqgen = _wq_all()
            else:
                wqgen = iter(())

            def wq_step(n=1):
                for _ in range(n):
                    if next(wqgen, None) is None:
                        break
            for j in range(32):
                wt = mb(W1O[j % 2], 8 * 128)
                p.dma("gpsimd", wt.rearrange("p (k c) -> p k c", c=128),
                      i_w1.ap()[:, j * 128:(j + 1) * 128].rearrange(
                          "(k p) c -> p k c", p=128),
                      writes=[w1b_[j % 2]])
                for th in range(2):
                    bank = (j * 2 + th) % 4

                    def em_m(t, wt=wt, th=th, bank=bank):
                        r = None
                        for k in range(8):
                            r = t.matmul(
                                psum[bank][:],
                                wt.rearrange("p (k c) -> p k c", c=128)[:, k, :],
                                xn2s[:, k * 1024 + th * 512:
                                     k * 1024 + th * 512 + 512],
                                start=(k == 0), stop=(k == 7))
                        return r
                    p.add("tensor", em_m, reads=[w1b_[j % 2], b_xn2s],
                          writes=[pbuf[bank]])
                    if j < 16:
                        p.add("vector", lambda v, th=th, bank=bank:
                              v.tensor_copy(
                                  xbpre[:, 4 + th * 512:4 + th * 512 + 512],
                                  psum[bank][:]),
                              reads=[pbuf[bank]], writes=[b_xbpre])
                    else:
                        p.add("vector", lambda v, th=th, bank=bank:
                              v.tensor_copy(
                                  convtb[:, th * 512:th * 512 + 512],
                                  psum[bank][:]),
                              reads=[pbuf[bank]], writes=[b_convt])
                if j < 16:
                    p.add("vector", lambda v, j=j: v.tensor_copy(
                        xbpre[:, 1:4], sb_halo[:, j * 4 + 1:j * 4 + 4]),
                        reads=[B("halo")], writes=[b_xbpre])
                    p.add("vector", lambda v, j=j: v.tensor_copy(
                        sb_halo[:, j * 4 + 1:j * 4 + 4], xbpre[:, 1025:1028]),
                        reads=[b_xbpre], writes=[B("halo")], force=True)
                    p.add("vector", lambda v, j=j: v.tensor_scalar(
                        convt, xbpre[:, 1:1025], sb_cw[:, j * 4:j * 4 + 1],
                        None, ALU.mult),
                        reads=[b_xbpre, B("cw")], writes=[b_convt],
                        force=True)
                    for k in range(1, 4):
                        p.add("vector", lambda v, j=j, k=k:
                              v.scalar_tensor_tensor(
                                  convt, xbpre[:, 1 + k:1 + k + 1024],
                                  sb_cw[:, j * 4 + k:j * 4 + k + 1], convt,
                                  ALU.mult, ALU.add),
                              reads=[b_xbpre, B("cw")], writes=[b_convt])
                    xbt = xbres[:, j * 1024:(j + 1) * 1024]
                    p.add("scalar", lambda sc, j=j, xbt=xbt: sc.activation(
                        xbt, convt, ACT.Silu, bias=sb_cb[:, j:j + 1]),
                        reads=[b_convt, B("cb")], writes=[b_xb])
                else:
                    jz = j - 16
                    p.dma("sync", h_z.ap()[jz * 128:(jz + 1) * 128, T0:T0 + TC],
                          convtb, reads=[b_convt], writes=[b_zpark])
                wq_step(1)

            p.dma("sync",
                  h_xb.ap()[:, T0:T0 + TC].rearrange("(k p) t -> p k t", p=128),
                  xbres.rearrange("p (k t) -> p k t", k=16),
                  reads=[b_xb], writes=[b_xbpark])

            bar2 = p.barrier()
            delta_in = mb(0, 16 * 1024)
            b_din = newB("din", bar2)
            wx_ = [newB("wxa", bar2), newB("wxb", bar2)]
            WXO = [163840, 168192]
            bc_bufs = {}

            def emit_bc():
                barBC = p.barrier()
                b_bfree = newB("bfree", barBC)
                b_cfree = newB("cfree", barBC)
                wqtags = [B(nm) for nm in ("wq_a", "wq_b", "wp1_a", "wp1_b",
                                           "wp1_c", "wqq", "wqt", "wqs",
                                           "wqtmp")]
                for s_i in range(NSUB):
                    st = p.dma("sync",
                               Mb[0:1, 32768 // 2 + s_i * 4096:
                                  32768 // 2 + (s_i + 1) * 4096],
                               sb_Bbf[0:16, s_i * SC:(s_i + 1) * SC],
                               reads=[B("Bbf")], extra=barBC,
                               writes=(wqtags if s_i == 0 else ()))
                    b_bfree.writers.append(st)
                    st = p.dma("gpsimd",
                               Mb[0:1, 65536 // 2 + s_i * 4096:
                                  65536 // 2 + (s_i + 1) * 4096],
                               sb_Cbf[0:16, s_i * SC:(s_i + 1) * SC],
                               reads=[B("Cbf")], extra=barBC,
                               writes=(wqtags + [b_xn2f] if s_i == 0 else ()))
                    b_cfree.writers.append(st)
                pc = 1
                stb = list(b_bfree.writers)
                while pc < 128:
                    stb = [p.dma("sync",
                                 Mb[pc:2 * pc, 32768 // 2:32768 // 2 + 16384],
                                 Mb[0:pc, 32768 // 2:32768 // 2 + 16384],
                                 extra=stb)]
                    pc *= 2
                b_bfree.writers.extend(stb)
                pc = 1
                stc = list(b_cfree.writers)
                while pc < 128:
                    stc = [p.dma("gpsimd",
                                 Mb[pc:2 * pc, 65536 // 2:65536 // 2 + 16384],
                                 Mb[0:pc, 65536 // 2:65536 // 2 + 16384],
                                 extra=stc)]
                    pc *= 2
                b_cfree.writers.extend(stc)
                bc_bufs["b"] = b_bfree
                bc_bufs["c"] = b_cfree
            for j in [16] + list(range(16)):
                wdt_c = 128 if j < 16 else 32
                wt = mb(WXO[j % 2], 16 * wdt_c)
                p.dma("gpsimd", wt.rearrange("p (k c) -> p k c", c=wdt_c),
                      i_wx.ap()[:, j * 128:j * 128 + wdt_c].rearrange(
                          "(k p) c -> p k c", p=128),
                      writes=[wx_[j % 2]])
                for th in range(2):
                    bank = (j * 2 + th) % 4
                    if j < 16:
                        def em_m(t, wt=wt, th=th, bank=bank):
                            r = None
                            for k in range(16):
                                r = t.matmul(
                                    psum[bank][:],
                                    wt.rearrange("p (k c) -> p k c",
                                                 c=128)[:, k, :],
                                    xbres[:, k * 1024 + th * 512:
                                          k * 1024 + th * 512 + 512],
                                    start=(k == 0), stop=(k == 15))
                            return r
                        p.add("tensor", em_m, reads=[wx_[j % 2], b_xb],
                              writes=[pbuf[bank]])
                        p.add("vector", lambda v, j=j, th=th, bank=bank:
                              v.tensor_copy(
                                  delta_in[:, j * 1024 + th * 512:
                                           j * 1024 + th * 512 + 512],
                                  psum[bank][:]),
                              reads=[pbuf[bank]], writes=[b_din])
                    else:
                        bank_c = (bank + 2) % 4

                        def em_mb(t, wt=wt, th=th, bank=bank):
                            r = None
                            for k in range(16):
                                r = t.matmul(
                                    psum[bank][0:16, 0:512],
                                    wt.rearrange("p (k c) -> p k c",
                                                 c=32)[:, k, 0:16],
                                    xbres[:, k * 1024 + th * 512:
                                          k * 1024 + th * 512 + 512],
                                    start=(k == 0), stop=(k == 15))
                            return r
                        p.add("tensor", em_mb, reads=[wx_[j % 2], b_xb],
                              writes=[pbuf[bank]])

                        def em_mc(t, wt=wt, th=th, bank_c=bank_c):
                            r = None
                            for k in range(16):
                                r = t.matmul(
                                    psum[bank_c][0:16, 0:512],
                                    wt.rearrange("p (k c) -> p k c",
                                                 c=32)[:, k, 16:32],
                                    xbres[:, k * 1024 + th * 512:
                                          k * 1024 + th * 512 + 512],
                                    start=(k == 0), stop=(k == 15))
                            return r
                        p.add("tensor", em_mc, reads=[wx_[j % 2], b_xb],
                              writes=[pbuf[bank_c]])
                        p.add("scalar", lambda sc, th=th, bank=bank: sc.copy(
                            sb_Bbf[:, th * 512:th * 512 + 512],
                            psum[bank][0:16, 0:512]),
                            reads=[pbuf[bank]], writes=[B("Bbf")])
                        p.add("scalar", lambda sc, th=th, bank_c=bank_c:
                              sc.copy(sb_Cbf[:, th * 512:th * 512 + 512],
                                      psum[bank_c][0:16, 0:512]),
                              reads=[pbuf[bank_c]], writes=[B("Cbf")])
                if j == 16 and ch != 0:
                    emit_bc()
                wq_step(3)

            wq_step(10 ** 6)
            if ch == 0:
                emit_bc()

            bar3 = p.barrier()
            b_bfree = bc_bufs["b"]
            b_cfree = bc_bufs["c"]

            dAs = [mf(98304, 4096), mf(114688, 4096)]
            b_dAs = [newB("dA0", bar3), newB("dA1", bar3)]
            dbx = mb(131072, 4096)
            dbx3 = dbx.rearrange("p (n t) -> p n t", n=DS)
            b_dBx = newB("dBx", bar3)
            hb = mb(139264, 4096)
            hb3 = hb.rearrange("p (n t) -> p n t", n=DS)
            b_h = newB("hb", bar3)
            deltas = [mf(147456, 1024), mf(151552, 1024)]
            b_deltas = [newB("d0", bar3), newB("d1", bar3)]
            sptmp = mf(155648, 1024)
            b_sp = newB("sptmp", bar3)
            xbsls = [mb(159744, 1024), mb(165888, 1024)]
            b_xbsls = [newB("xbsl0", bar3), newB("xbsl1", bar3)]
            zsls = [mb(161792, 1024), mb(169984, 1024)]
            b_zsls = [newB("zsl0", bar3), newB("zsl1", bar3)]
            tnh = mb(163840, 1024)
            b_tnh = newB("tnh", bar3)
            ytmp = mb(167936, 1024)
            b_ytmp = newB("ytmp", bar3)
            wd_ = [newB("wda", bar3), newB("wdb", bar3)]
            WDO = [172032, 176128]
            b_ypark = B("y_park")

            def prep_dt(dt):
                par = dt % 2
                wt = mb(WDO[par], 16 * 128)
                p.dma("gpsimd", wt.rearrange("p (k c) -> p k c", c=128),
                      i_wdt.ap()[:, dt * 128:(dt + 1) * 128].rearrange(
                          "(k p) c -> p k c", p=128),
                      writes=[wd_[par]])
                p.dma("sync", xbsls[par],
                      h_xb.ap()[dt * 128:(dt + 1) * 128, T0:T0 + TC],
                      reads=[b_xbpark], writes=[b_xbsls[par]])
                p.dma("sync", zsls[par],
                      h_z.ap()[dt * 128:(dt + 1) * 128, T0:T0 + TC],
                      reads=[b_zpark], writes=[b_zsls[par]])
                delta = deltas[par]
                for th in range(2):
                    bank = 4 + th

                    def em_m(t, wt=wt, th=th, bank=bank):
                        r = None
                        for k in range(16):
                            r = t.matmul(
                                psum[bank][:],
                                wt.rearrange("p (k c) -> p k c", c=128)[:, k, :],
                                delta_in[:, k * 1024 + th * 512:
                                         k * 1024 + th * 512 + 512],
                                start=(k == 0), stop=(k == 15))
                        return r
                    p.add("tensor", em_m, reads=[wd_[par], b_din],
                          writes=[pbuf[bank]])
                    p.add("scalar", lambda s, th=th, bank=bank, dt=dt,
                          delta=delta: s.activation(
                              delta[:, th * 512:th * 512 + 512], psum[bank][:],
                              ACT.Identity, bias=sb_dtb[:, dt:dt + 1]),
                          reads=[pbuf[bank], B("dtb")], writes=[b_deltas[par]])
                p.add("scalar", lambda s, delta=delta: s.activation(
                    sptmp, delta, ACT.Abs), reads=[b_deltas[par]],
                    writes=[b_sp])
                p.add("scalar", lambda s: s.activation(
                    sptmp, sptmp, ACT.Exp, scale=-1.0), writes=[b_sp])
                p.add("scalar", lambda s: s.activation(
                    sptmp, sptmp, ACT.Ln, bias=1.0), writes=[b_sp])
                p.add("scalar", lambda s, delta=delta: s.activation(
                    delta, delta, ACT.Relu), writes=[b_deltas[par]])
                p.add("vector", lambda v, delta=delta: v.tensor_tensor(
                    delta, delta, sptmp, ALU.add), reads=[b_sp],
                    writes=[b_deltas[par]])

            prep_dt(0)
            for dt in range(NDI):
                par = dt % 2
                delta = deltas[par]
                xbsl = xbsls[par]
                if dt + 1 < NDI:
                    prep_dt(dt + 1)
                hsl = sb_hcar[:, dt * DS:(dt + 1) * DS]
                for s_i in range(NSUB):
                    g_par = (dt * NSUB + s_i) % 2
                    dA = dAs[g_par]
                    dA3 = dA.rearrange("p (n t) -> p n t", n=DS)
                    b_dA = b_dAs[g_par]
                    bfs3 = mb(32768 + s_i * 8192, 4096).rearrange(
                        "p (n t) -> p n t", n=DS)
                    cfs3 = mb(65536 + s_i * 8192, 4096).rearrange(
                        "p (n t) -> p n t", n=DS)

                    def em_da(sc, dt=dt, s_i=s_i, dA=dA, delta=delta):
                        r = None
                        for n in range(DS):
                            r = sc.activation(
                                dA[:, n * SC:(n + 1) * SC],
                                delta[:, s_i * SC:(s_i + 1) * SC], ACT.Exp,
                                scale=sb_A[:, dt * DS + n:dt * DS + n + 1])
                        return r
                    p.add("scalar", em_da, reads=[b_deltas[par], B("A")],
                          writes=[b_dA])
                    p.add("vector", lambda v, s_i=s_i, delta=delta, xbsl=xbsl:
                          v.tensor_tensor(
                              sb_u[:], delta[:, s_i * SC:(s_i + 1) * SC],
                              xbsl[:, s_i * SC:(s_i + 1) * SC], ALU.mult),
                          reads=[b_deltas[par], b_xbsls[par]], writes=[B("u")])
                    p.add("vector", lambda v, bfs3=bfs3: v.tensor_tensor(
                        dbx3,
                        sb_u[:].unsqueeze(1).broadcast_to([128, DS, SC]),
                        bfs3, ALU.mult),
                        reads=[B("u"), b_bfree], writes=[b_dBx])
                    p.add("vector", lambda v, hsl=hsl, dA3=dA3: v.tensor_tensor(
                        sb_u16[:],
                        dA3[:, :, 0:1].rearrange("p n o -> p (n o)"),
                        hsl, ALU.mult),
                        reads=[b_dA, B("hcar")], writes=[B("u16")], force=True)
                    p.add("vector", lambda v: v.tensor_tensor(
                        dbx3[:, :, 0:1].rearrange("p n o -> p (n o)"),
                        dbx3[:, :, 0:1].rearrange("p n o -> p (n o)"),
                        sb_u16[:], ALU.add), reads=[B("u16")], writes=[b_dBx],
                        force=True)
                    p.add("vector", lambda v, dA3=dA3: v.memset(
                        dA3[:, :, 0:1], 0.0), writes=[b_dA], force=True)
                    p.add("vector", lambda v, dA=dA: v.tensor_tensor_scan(
                        hb, dA, dbx, 0.0, ALU.mult, ALU.add),
                        reads=[b_dBx, b_dA], writes=[b_h], force=True)
                    p.add("vector", lambda v, hsl=hsl: v.tensor_copy(
                        hsl, hb3[:, :, SC - 1:SC].rearrange("p n o -> p (n o)")),
                        reads=[b_h], writes=[B("hcar")], force=True)
                    p.add("vector", lambda v, cfs3=cfs3: v.tensor_tensor(
                        dbx3, hb3, cfs3, ALU.mult),
                        reads=[b_h, b_cfree], writes=[b_dBx])
                    w_ = 2048
                    while w_ >= SC:
                        p.add("vector", lambda v, w_=w_: v.tensor_tensor(
                            dbx[:, 0:w_], dbx[:, 0:w_], dbx[:, w_:2 * w_],
                            ALU.add), writes=[b_dBx])
                        w_ //= 2

                    def em_y(v, s_i=s_i, xbsl=xbsl, dt=dt):
                        return v.scalar_tensor_tensor(
                            ytmp[:, s_i * SC:(s_i + 1) * SC],
                            xbsl[:, s_i * SC:(s_i + 1) * SC],
                            sb_D[:, dt:dt + 1], dbx[:, 0:SC],
                            ALU.mult, ALU.add)
                    p.add("vector", em_y, reads=[b_xbsls[par], B("D"), b_dBx],
                          writes=[b_ytmp])
                p.add("scalar", lambda s, par=par: s.activation(
                    tnh, zsls[par], ACT.Tanh, scale=0.5),
                    reads=[b_zsls[par]], writes=[b_tnh])
                p.add("vector", lambda v: v.tensor_scalar(
                    tnh, tnh, 0.5, 0.5, ALU.mult, ALU.add), writes=[b_tnh])
                p.add("vector", lambda v, par=par: v.tensor_tensor(
                    ytmp[:], ytmp[:], zsls[par], ALU.mult),
                    reads=[b_zsls[par]], writes=[b_ytmp])
                p.add("vector", lambda v: v.tensor_tensor(
                    ytmp[:], ytmp[:], tnh, ALU.mult),
                    reads=[b_tnh], writes=[b_ytmp])
                st = p.dma("sync", h_y.ap()[dt * 128:(dt + 1) * 128, T0:T0 + TC],
                           ytmp[:], reads=[b_ytmp])
                b_ypark.writers.append(st)

            bar5 = p.barrier()
            g_all = mf(0, 16384)
            gq = mb(65536, 16384)
            xqs = mb(131072, 8 * 1024)
            gmax = mf(147456, 512)
            gmin = mf(149504, 512)
            qi32 = mi(151552, 512)
            p1t = mf(153600, 512)
            gtmps = [mf(163840, 512), mf(155648, 512)]
            gtmp2s = [mf(165888, 512), mf(157696, 512)]
            utms = [mf(167936, 512), mf(159744, 512)]
            qtmp = mf(169984, 512)
            b_g = newB("g_all", bar5)
            b_gq = newB("gq", bar5)
            b_xqs = newB("xqs", bar5)
            b_gmax = newB("gmax", bar5)
            b_gmin = newB("gmin", bar5)
            b_qi = newB("qi32", bar5)
            b_p1 = newB("p1t", bar5)
            b_gts = [newB("gtmp0", bar5), newB("gtmp1", bar5)]
            b_gt2s = [newB("gtmp20", bar5), newB("gtmp21", bar5)]
            b_uts = [newB("utm0", bar5), newB("utm1", bar5)]
            b_qt = newB("qtmp", bar5)
            b_fw = [newB("fw0", bar5), newB("fw1", bar5)]
            b_fwd = [newB("fwd0", bar5), newB("fwd1", bar5)]
            b_gqk = [newB("gqk%d" % k2, bar5) for k2 in range(32)]
            FWO = [172032, 176128]
            FWD = [172032, 161792]
            p.dma("sync", xqs.rearrange("p (k t) -> p k t", k=8),
                  h_xq.ap()[:, T0:T0 + TC].rearrange("(p k) t -> p k t", k=8),
                  reads=[B("h_xq")], writes=[b_xqs])
            def p5_production(sub):
                tg0 = T0 + sub * 512
                fxsl = sb_fx[:, tg0:tg0 + 512]
                p.add("vector", lambda v: v.memset(gmax, -3.0e38),
                      writes=[b_gmax])
                p.add("vector", lambda v: v.memset(gmin, 3.0e38),
                      writes=[b_gmin])
                for j in range(32):
                    wgt = mb(FWO[j % 2], 8 * 128)
                    wut = mb(FWO[j % 2] + 2048, 8 * 128)
                    p.dma("gpsimd", wgt.rearrange("p (k c) -> p k c", c=128),
                          h_wqg.ap()[j * 1024:(j + 1) * 1024, :].rearrange(
                              "(p k) c -> p k c", k=8),
                          reads=[B("wgpark")], writes=[b_fw[j % 2]])
                    p.dma("sync", wut.rearrange("p (k c) -> p k c", c=128),
                          h_wqu.ap()[j * 1024:(j + 1) * 1024, :].rearrange(
                              "(p k) c -> p k c", k=8),
                          reads=[B("wupark")], writes=[b_fw[j % 2]])
                    bank_g = (j % 2) * 2
                    bank_u = (j % 2) * 2 + 1

                    def em_mg(t, wgt=wgt, bank_g=bank_g, sub=sub):
                        r = None
                        for k in range(8):
                            r = t.matmul(
                                psum[bank_g][:],
                                wgt.rearrange("p (k c) -> p k c",
                                              c=128)[:, k, :],
                                xqs[:, k * 1024 + sub * 512:
                                    k * 1024 + sub * 512 + 512],
                                start=(k == 0), stop=(k == 7))
                        return r
                    p.add("tensor", em_mg, reads=[b_fw[j % 2], b_xqs],
                          writes=[pbuf[bank_g]])

                    def em_mu(t, wut=wut, bank_u=bank_u, sub=sub):
                        r = None
                        for k in range(8):
                            r = t.matmul(
                                psum[bank_u][:],
                                wut.rearrange("p (k c) -> p k c",
                                              c=128)[:, k, :],
                                xqs[:, k * 1024 + sub * 512:
                                    k * 1024 + sub * 512 + 512],
                                start=(k == 0), stop=(k == 7))
                        return r
                    p.add("tensor", em_mu, reads=[b_fw[j % 2], b_xqs],
                          writes=[pbuf[bank_u]])
                    gtmp = gtmps[j % 2]
                    gtmp2 = gtmp2s[j % 2]
                    utm = utms[j % 2]
                    b_gt = b_gts[j % 2]
                    b_gt2 = b_gt2s[j % 2]
                    b_ut = b_uts[j % 2]
                    p.add("vector", lambda v, bank_g=bank_g, fxsl=fxsl,
                          gtmp=gtmp: v.scalar_tensor_tensor(
                              gtmp, psum[bank_g][:], col(C_SWG), fxsl,
                              ALU.mult, ALU.mult),
                          reads=[pbuf[bank_g], B("wg_sw"), b_fx],
                          writes=[b_gt])
                    p.add("scalar", lambda s, gtmp=gtmp, gtmp2=gtmp2:
                          s.activation(gtmp2, gtmp, ACT.Silu),
                          reads=[b_gt], writes=[b_gt2])
                    p.add("vector", lambda v, bank_u=bank_u, fxsl=fxsl,
                          utm=utm: v.scalar_tensor_tensor(
                              utm, psum[bank_u][:], col(C_SWU), fxsl,
                              ALU.mult, ALU.mult),
                          reads=[pbuf[bank_u], B("wu_sw"), b_fx],
                          writes=[b_ut])
                    p.add("vector", lambda v, j=j, gtmp2=gtmp2, utm=utm:
                          v.tensor_tensor(
                              g_all[:, j * 512:(j + 1) * 512], gtmp2, utm,
                              ALU.mult),
                          reads=[b_gt2, b_ut], writes=[b_g])
                    p.add("vector", lambda v, j=j: v.tensor_tensor(
                        gmax, gmax, g_all[:, j * 512:(j + 1) * 512], ALU.max),
                        reads=[b_g], writes=[b_gmax])
                    p.add("vector", lambda v, j=j: v.tensor_tensor(
                        gmin, gmin, g_all[:, j * 512:(j + 1) * 512], ALU.min),
                        reads=[b_g], writes=[b_gmin])
            def p5_rest(sub):
                tg0 = T0 + sub * 512
                fxsl = sb_fx[:, tg0:tg0 + 512]
                p.add("vector", lambda v: v.tensor_scalar_mul(
                    gmin, gmin, -1.0), writes=[b_gmin])
                p.add("vector", lambda v: v.tensor_tensor(
                    gmax, gmax, gmin, ALU.max), reads=[b_gmin],
                    writes=[b_gmax])
                FB = 65536 // 4
                p.dma("sync", Mf[0:16, FB:FB + 4096], gmax,
                      reads=[b_gmax], writes=[b_gq])
                p.add("vector", lambda v: v.tensor_tensor(
                    Mf[0:16, FB:FB + 2048], Mf[0:16, FB:FB + 2048],
                    Mf[0:16, FB + 2048:FB + 4096], ALU.max), writes=[b_gq])
                p.add("vector", lambda v: v.tensor_tensor(
                    Mf[0:16, FB:FB + 1024], Mf[0:16, FB:FB + 1024],
                    Mf[0:16, FB + 1024:FB + 2048], ALU.max), writes=[b_gq])
                p.add("vector", lambda v: v.tensor_tensor(
                    Mf[0:16, FB:FB + 512], Mf[0:16, FB:FB + 512],
                    Mf[0:16, FB + 512:FB + 1024], ALU.max), writes=[b_gq])
                p.dma("sync", Mf[0:4, FB + 2048:FB + 2048 + 2048],
                      Mf[0:16, FB:FB + 512], reads=[b_gq], writes=[b_gq])
                p.add("vector", lambda v: v.tensor_tensor(
                    Mf[0:4, FB + 2048:FB + 2048 + 1024],
                    Mf[0:4, FB + 2048:FB + 2048 + 1024],
                    Mf[0:4, FB + 3072:FB + 3072 + 1024], ALU.max),
                    writes=[b_gq])
                p.add("vector", lambda v: v.tensor_tensor(
                    Mf[0:4, FB + 2048:FB + 2048 + 512],
                    Mf[0:4, FB + 2048:FB + 2048 + 512],
                    Mf[0:4, FB + 2560:FB + 2560 + 512], ALU.max),
                    writes=[b_gq])
                p.dma("sync", Mf[0:1, FB:FB + 2048],
                      Mf[0:4, FB + 2048:FB + 2048 + 512],
                      reads=[b_gq], writes=[b_gq])
                p.add("vector", lambda v: v.tensor_tensor(
                    Mf[0:1, FB:FB + 1024], Mf[0:1, FB:FB + 1024],
                    Mf[0:1, FB + 1024:FB + 2048], ALU.max), writes=[b_gq])
                p.add("vector", lambda v: v.tensor_tensor(
                    Mf[0:1, FB:FB + 512], Mf[0:1, FB:FB + 512],
                    Mf[0:1, FB + 512:FB + 1024], ALU.max), writes=[b_gq])
                p.add("vector", lambda v: v.tensor_scalar_max(
                    Mf[0:1, FB:FB + 512], Mf[0:1, FB:FB + 512], 1e-5),
                    writes=[b_gq], force=True)
                p.add("vector", lambda v: v.reciprocal(
                    Mf[0:1, FB + 1024:FB + 1024 + 512],
                    Mf[0:1, FB:FB + 512]), writes=[b_gq], force=True)
                p.add("vector", lambda v: v.tensor_scalar_mul(
                    Mf[0:1, FB + 1024:FB + 1024 + 512],
                    Mf[0:1, FB + 1024:FB + 1024 + 512], 127.0),
                    writes=[b_gq], force=True)
                p.add("vector", lambda v: v.tensor_scalar(
                    Mf[0:1, FB + 2048:FB + 2048 + 512], Mf[0:1, FB:FB + 512],
                    1.0 / 127.0, sb_cols[0:1, C_SWD:C_SWD + 1],
                    ALU.mult, ALU.mult),
                    reads=[B("wd_sw")], writes=[b_gq], force=True)
                bcast_rows(lambda p0, n: sb_s2b[p0:p0 + n, :],
                           Mf[0:1, FB + 1024:FB + 1024 + 512], 128,
                           [b_gq], "s2b")
                bcast_rows(lambda p0, n: sb_fdb[p0:p0 + n, :],
                           Mf[0:1, FB + 2048:FB + 2048 + 512], 128,
                           [b_gq], "fdb")
                b_s2b = B("s2b")
                b_fdb2 = B("fdb")
                for j in range(32):
                    gsl = g_all[:, j * 512:(j + 1) * 512]
                    p.add("vector", lambda v, gsl=gsl: v.tensor_tensor(
                        qi32, gsl, sb_s2b[:], ALU.mult),
                        reads=[b_g, b_s2b], writes=[b_qi])
                    p.add("vector", lambda v, j=j: v.tensor_copy(
                        gq[:, j * 512:(j + 1) * 512], qi32),
                        reads=[b_qi], writes=[b_gqk[j], b_gq])
                for j2 in range(8):
                    wdt_ = mb(FWD[j2 % 2], 32 * 128)
                    wrt = ([b_fwd[0], b_fw[0], b_fw[1]] if j2 % 2 == 0 else
                           [b_fwd[1], b_gts[0], b_gt2s[0], b_uts[0]])
                    p.dma("sync" if j2 % 2 else "gpsimd",
                          wdt_.rearrange("p (k c) -> p k c", c=128),
                          h_wqd.ap()[j2 * 4096:(j2 + 1) * 4096, :].rearrange(
                              "(k p) c -> p k c", p=128),
                          reads=[B("wdpark")], writes=wrt)
                    bank = 4 + (j2 % 2)
                    kgrp = (4, 8, 16, 16)[j2] if j2 < 4 else 32
                    for k0 in range(0, 32, kgrp):
                        def em_md(t, wdt_=wdt_, bank=bank, k0=k0, kgrp=kgrp):
                            r = None
                            for k in range(k0, k0 + kgrp):
                                r = t.matmul(
                                    psum[bank][:],
                                    wdt_.rearrange("p (k c) -> p k c",
                                                   c=128)[:, k, :],
                                    gq[:, k * 512:(k + 1) * 512],
                                    start=(k == 0), stop=(k == 31))
                            return r
                        p.add("tensor", em_md,
                              reads=wrt + [b_gqk[k0 + kgrp - 1]],
                              writes=[pbuf[bank]])
                    p.add("vector", lambda v, bank=bank: v.tensor_tensor(
                        qtmp, psum[bank][:], sb_fdb[:], ALU.mult),
                        reads=[pbuf[bank], b_fdb2], writes=[b_qt])

                    def em_a(v, j2=j2, sub=sub):
                        sl = acc[:, j2 * 1024 + sub * 512:
                                 j2 * 1024 + sub * 512 + 512]
                        return v.tensor_tensor(sl, sl, qtmp, ALU.add)
                    p.add("vector", em_a, reads=[b_qt], writes=[b_acc])

            bar4 = bar5
            p5_production(0)
            acc = mf(98304, 8 * 1024)
            b_acc = newB("acc", bar4)
            p.dma("sync", acc.rearrange("p (j t) -> p j t", j=8),
                  h_base.ap()[:, T0:T0 + TC].rearrange("(j p) t -> p j t",
                                                       p=128),
                  reads=[B("h_base")], writes=[b_acc])
            ytile = mb(65536, 16 * 1024)
            b_y = newB("yres", bar4)
            p.dma("sync", ytile.rearrange("p (k t) -> p k t", k=16),
                  h_y.ap()[:, T0:T0 + TC].rearrange("(k p) t -> p k t", p=128),
                  reads=[b_ypark], writes=[b_y])
            wo_ = [newB("woa", bar4), newB("wob", bar4)]
            for j in range(8):
                wt = sb_wo4[:, (j % 2) * 2048:(j % 2) * 2048 + 2048]
                p.dma("gpsimd", wt.rearrange("p (k c) -> p k c", c=128),
                      i_wo.ap()[:, j * 128:(j + 1) * 128].rearrange(
                          "(k p) c -> p k c", p=128),
                      writes=[wo_[j % 2]])
                for th in range(2):
                    bank = 6 + (j * 2 + th) % 2

                    def em_m(t, wt=wt, th=th, bank=bank):
                        r = None
                        for k in range(16):
                            r = t.matmul(
                                psum[bank][:],
                                wt.rearrange("p (k c) -> p k c", c=128)[:, k, :],
                                ytile[:, k * 1024 + th * 512:
                                      k * 1024 + th * 512 + 512],
                                start=(k == 0), stop=(k == 15))
                        return r
                    p.add("tensor", em_m, reads=[wo_[j % 2], b_y, b_gq],
                          writes=[pbuf[bank]])

                    def em_e(v, j=j, th=th, bank=bank):
                        sl = acc[:, j * 1024 + th * 512:
                                 j * 1024 + th * 512 + 512]
                        return v.tensor_tensor(sl, sl, psum[bank][:], ALU.add)
                    p.add("vector", em_e, reads=[pbuf[bank]], writes=[b_acc])

            p5_rest(0)
            p5_production(1)
            p5_rest(1)

            bar6 = p.barrier()
            otok = mf(163840, 1024)
            b_ot = newB("otok", bar6)
            for ttl in range(8):
                for half in range(2):
                    bank = 6 + half

                    def em_t(t, ttl=ttl, half=half, bank=bank):
                        r = None
                        for q in range(4):
                            j = half * 4 + q
                            r = t.transpose(
                                psum[bank][:, q * 128:(q + 1) * 128],
                                acc[:, j * 1024 + ttl * 128:
                                    j * 1024 + ttl * 128 + 128], sb_idf[:])
                        return r
                    p.add("tensor", em_t, reads=[b_acc, B("idf")],
                          writes=[pbuf[bank]])
                    for q in range(4):
                        j = half * 4 + q
                        p.add("scalar", lambda sc, j=j, bank=bank, q=q:
                              sc.copy(otok[:, j * 128:(j + 1) * 128],
                                      psum[bank][:, q * 128:(q + 1) * 128]),
                              reads=[pbuf[bank]], writes=[b_ot])
                p.dma("sync",
                      o_y.ap()[T0 + ttl * 128:T0 + (ttl + 1) * 128, :],
                      otok, reads=[b_ot], writes=[B("outdone")])

        p.finalize(es)
    return nc


_NC_CACHE = {}


def kernel(L0_attn_x, L0_attn_norm_w, L0_attn_norm_b, L0_attn_in_proj_w,
           L0_attn_conv_w, L0_attn_conv_b, L0_attn_x_proj_w, L0_attn_dt_proj_w,
           L0_attn_dt_proj_b, L0_attn_A_log, L0_attn_D_param,
           L0_attn_out_proj_w, L0_ffn_w_gate, L0_ffn_w_up, L0_ffn_w_down):
    def f(a):
        return np.ascontiguousarray(np.asarray(a, dtype=np.float32))
    x = f(L0_attn_x)
    wmap = {
        "nw": f(L0_attn_norm_w).reshape(1, DM),
        "nb": f(L0_attn_norm_b).reshape(1, DM),
        "w_in": f(L0_attn_in_proj_w),
        "conv_w": f(L0_attn_conv_w).reshape(DI, 4),
        "conv_b": f(L0_attn_conv_b).reshape(1, DI),
        "w_xp": f(L0_attn_x_proj_w),
        "w_dt": f(L0_attn_dt_proj_w),
        "dt_b": f(L0_attn_dt_proj_b).reshape(1, DI),
        "a_log": f(L0_attn_A_log),
        "d_par": f(L0_attn_D_param).reshape(1, DI),
        "w_out": f(L0_attn_out_proj_w),
        "w_g": f(L0_ffn_w_gate),
        "w_u": f(L0_ffn_w_up),
        "w_d": f(L0_ffn_w_down),
    }
    if "nc" not in _NC_CACHE:
        _NC_CACHE["nc"] = build_nc()
    nc = _NC_CACHE["nc"]
    in_maps = [dict(wmap, xb=np.ascontiguousarray(x[b])) for b in range(8)]
    res = run_bass_kernel_spmd(nc, in_maps, core_ids=list(range(8)))
    out = np.stack([np.asarray(res.results[b]["y_out"]) for b in range(8)],
                   axis=0)
    return out.astype(np.float32)


# version 41
# speedup vs baseline: 1.0361x; 1.0020x over previous
import numpy as np
import concourse.bass as bass
import concourse.mybir as mybir
from concourse.bass_utils import run_bass_kernel_spmd
from contextlib import ExitStack

F32 = mybir.dt.float32
BF16 = mybir.dt.bfloat16
I32 = mybir.dt.int32
ALU = mybir.AluOpType
ACT = mybir.ActivationFunctionType
AX = mybir.AxisListType

L = 2048
DM = 1024
DI = 2048
DS = 16
NTT = 16
NDI = 16
NCH = 2
TC = 1024
NSUB = 4
SC = 256
INNER = 4096

ENGINES = ("sync", "scalar", "vector", "tensor", "gpsimd")


class Step:
    __slots__ = ("engine", "emit", "inc", "count_after", "idx", "deps",
                 "fdeps")


class Buf:
    __slots__ = ("writers", "readers")

    def __init__(self, init=()):
        self.writers = list(init)
        self.readers = []


class Prog:
    def __init__(self, nc):
        self.nc = nc
        self.steps = []
        self._counts = {e: 0 for e in ENGINES}
        self._last = {e: None for e in ENGINES}

    def add(self, engine, emit, reads=(), writes=(), extra=(), dma=False,
            ptr_reads=(), force=False):
        deps = set()
        for b in reads:
            deps.update(b.writers)
        for b in writes:
            deps.update(b.writers)
            deps.update(b.readers)
        deps.update(extra)
        fdeps = set()
        for b in ptr_reads:
            fdeps.update(b.writers)
        if force:
            fdeps.update(deps)
        st = Step()
        st.engine = engine
        st.emit = emit
        st.inc = 16 if dma else 1
        st.deps = list(deps - fdeps)
        st.fdeps = list(fdeps)
        st.idx = len(self.steps)
        self._counts[engine] += st.inc
        st.count_after = self._counts[engine]
        self.steps.append(st)
        self._last[engine] = st
        for b in reads:
            b.readers.append(st)
        for b in ptr_reads:
            b.readers.append(st)
        for b in writes:
            b.writers = [st]
            b.readers = []
        return st

    def dma(self, engine, out, in_, reads=(), writes=(), extra=(), slow=False):
        def emit(eng, out=out, in_=in_, slow=slow):
            return eng.dma_start(out, in_, allow_slow_non_contiguous=slow)
        return self.add(engine, emit, reads, writes, extra, dma=True)

    def barrier(self):
        return [s for s in self._last.values() if s is not None]

    def finalize(self, es):
        nc = self.nc
        sems = {e: es.enter_context(nc.semaphore("pg_" + e)) for e in ENGINES}
        by_engine = {e: [] for e in ENGINES}
        for st in self.steps:
            by_engine[st.engine].append(st)
        final = dict(self._counts)
        blk = es.enter_context(nc.Block())

        def mk(ename):
            def fn(eng):
                waited = {e: 0 for e in ENGINES}
                for st in by_engine[ename]:
                    need = {}
                    if st.inc == 16:
                        need[ename] = st.count_after - 16
                    for d in st.deps:
                        if d.engine == ename and d.inc == 1:
                            continue
                        if d.count_after > need.get(d.engine, 0):
                            need[d.engine] = d.count_after
                    for d in st.fdeps:
                        if d.count_after > need.get(d.engine, 0):
                            need[d.engine] = d.count_after
                    for e in sorted(need):
                        if need[e] > waited[e]:
                            eng.wait_ge(sems[e], need[e])
                            waited[e] = need[e]
                    st.emit(eng).then_inc(sems[ename], st.inc)
                if ename == "sync":
                    for e in ENGINES:
                        if final[e] > 0:
                            eng.wait_ge(sems[e], final[e])
            return fn

        for e in ENGINES:
            if by_engine[e] or e == "sync":
                getattr(blk, e)(mk(e))


def build_nc():
    nc = bass.Bass("TRN2", target_bir_lowering=False, debug=False)

    i_x = nc.dram_tensor("xb", [L, DM], F32, kind="ExternalInput")
    i_nw = nc.dram_tensor("nw", [1, DM], F32, kind="ExternalInput")
    i_nb = nc.dram_tensor("nb", [1, DM], F32, kind="ExternalInput")
    i_w1 = nc.dram_tensor("w_in", [DM, 2 * DI], F32, kind="ExternalInput")
    i_cw = nc.dram_tensor("conv_w", [DI, 4], F32, kind="ExternalInput")
    i_cb = nc.dram_tensor("conv_b", [1, DI], F32, kind="ExternalInput")
    i_wx = nc.dram_tensor("w_xp", [DI, 2080], F32, kind="ExternalInput")
    i_wdt = nc.dram_tensor("w_dt", [DI, DI], F32, kind="ExternalInput")
    i_dtb = nc.dram_tensor("dt_b", [1, DI], F32, kind="ExternalInput")
    i_alog = nc.dram_tensor("a_log", [DI, DS], F32, kind="ExternalInput")
    i_dp = nc.dram_tensor("d_par", [1, DI], F32, kind="ExternalInput")
    i_wo = nc.dram_tensor("w_out", [DI, DM], F32, kind="ExternalInput")
    i_wg = nc.dram_tensor("w_g", [INNER, DM], F32, kind="ExternalInput")
    i_wu = nc.dram_tensor("w_u", [INNER, DM], F32, kind="ExternalInput")
    i_wd = nc.dram_tensor("w_d", [DM, INNER], F32, kind="ExternalInput")
    o_y = nc.dram_tensor("y_out", [L, DM], F32, kind="ExternalOutput")
    h_base = nc.dram_tensor("s_base", [DM, L], F32, kind="Internal")
    h_xn2 = nc.dram_tensor("s_xn2", [DM, L], BF16, kind="Internal")
    h_xq = nc.dram_tensor("s_xq", [DM, L], BF16, kind="Internal")
    h_xb = nc.dram_tensor("s_xb", [DI, L], BF16, kind="Internal")
    h_z = nc.dram_tensor("s_z", [DI, L], BF16, kind="Internal")
    h_y = nc.dram_tensor("s_y", [DI, L], BF16, kind="Internal")
    h_wqg = nc.dram_tensor("s_wqg", [INNER // 128 * DM, 128], BF16,
                           kind="Internal")
    h_wqu = nc.dram_tensor("s_wqu", [INNER // 128 * DM, 128], BF16,
                           kind="Internal")
    h_wqd = nc.dram_tensor("s_wqd", [DM // 128 * INNER, 128], BF16,
                           kind="Internal")

    es = ExitStack()
    with es:
        def sbt(name, shape, dt=F32):
            return es.enter_context(nc.sbuf_tensor(name, shape, dt))

        MEGA = sbt("mega", [128, 45056], F32)
        Mf = MEGA[:]
        Mb = MEGA[:].bitcast(BF16)
        Mi = MEGA[:].bitcast(I32)

        def mf(off, n, p=128):
            return Mf[0:p, off // 4: off // 4 + n]

        def mb(off, n, p=128):
            return Mb[0:p, off // 2: off // 2 + n]

        def mi(off, n, p=128):
            return Mi[0:p, off // 4: off // 4 + n]

        sb_fx = sbt("fx", [128, L], F32)
        sb_A = sbt("a_sb", [128, NDI * DS], F32)
        sb_D = sbt("d_sb", [128, NDI], F32)
        sb_cb = sbt("cb_sb", [128, NDI], F32)
        sb_dtb = sbt("dtb_sb", [128, NDI], F32)
        sb_cw = sbt("cw_sb", [128, NDI * 4], F32)
        sb_pidx = sbt("pidx", [128, 1], F32)
        sb_srec = sbt("srec", [128, NTT], F32)
        sb_cols = sbt("cols", [128, 16], F32)
        sb_idf = sbt("idf", [128, 128], F32)
        sb_idb = sbt("idb", [128, 128], BF16)
        sb_hcar = sbt("hcar", [128, NDI * DS], F32)
        sb_halo = sbt("halo", [128, NDI * 4], F32)
        sb_Bbf = sbt("bbf", [16, TC], BF16)
        sb_Cbf = sbt("cbf", [16, TC], BF16)
        sb_u = sbt("u_sb", [128, 256], BF16)
        sb_u16 = sbt("u16_sb", [128, 16], F32)
        sb_s2b = sbt("s2b", [128, 512], F32)
        sb_fdb = sbt("fdb", [128, 512], F32)
        sb_wo4 = sbt("wo4", [128, 4096], BF16)

        psum = [es.enter_context(nc.psum_tensor("pb%d" % i, [128, 512], F32))
                for i in range(8)]

        p = Prog(nc)
        bufs = {}

        def B(name):
            if name not in bufs:
                bufs[name] = Buf()
            return bufs[name]

        def newB(name, init=()):
            bufs[name] = Buf(init)
            return bufs[name]

        pbuf = [B("psum%d" % i) for i in range(8)]

        C_SWG, C_SWU, C_SWD, C_THR, C_NTHR, C_T0, C_T1, C_T2 = range(8)

        def col(i):
            return sb_cols[:, i:i + 1]

        p.add("gpsimd", lambda g: g.iota(
            sb_idf[:], [[1, 128]], channel_multiplier=0,
            allow_small_or_imprecise_dtypes=True), writes=[B("idf")])
        p.add("gpsimd", lambda g: g.iota(
            sb_pidx[:], [[0, 1]], channel_multiplier=1,
            allow_small_or_imprecise_dtypes=True), writes=[B("pidx")])
        p.add("vector", lambda v: v.tensor_scalar(
            sb_idf[:], sb_idf[:], sb_pidx[:], None, ALU.is_equal),
            reads=[B("pidx")], writes=[B("idf")])
        p.add("vector", lambda v: v.tensor_copy(sb_idb[:], sb_idf[:]),
              reads=[B("idf")], writes=[B("idb")])

        p.dma("sync", sb_A[:], i_alog.ap().rearrange("(k p) n -> p k n", p=128),
              writes=[B("A")])
        p.add("scalar", lambda s: s.activation(sb_A[:], sb_A[:], ACT.Exp),
              writes=[B("A")])
        p.add("vector", lambda v: v.tensor_scalar_mul(sb_A[:], sb_A[:], -1.0),
              writes=[B("A")])
        p.dma("sync", sb_D[:], i_dp.ap().rearrange("o (k p) -> (o p) k", p=128), slow=True,
              writes=[B("D")])
        p.dma("sync", sb_cb[:], i_cb.ap().rearrange("o (k p) -> (o p) k", p=128), slow=True,
              writes=[B("cb")])
        p.dma("sync", sb_dtb[:], i_dtb.ap().rearrange("o (k p) -> (o p) k", p=128), slow=True,
              writes=[B("dtb")])
        p.dma("sync", sb_cw[:], i_cw.ap().rearrange("(k p) t -> p k t", p=128),
              writes=[B("cw")])
        p.add("vector", lambda v: v.memset(sb_hcar[:], 0.0), writes=[B("hcar")])
        p.add("vector", lambda v: v.memset(sb_halo[:], 0.0), writes=[B("halo")])

        O_BASE = 0
        O_XN2F = 65536
        O_XQF = 98304
        O_RA = 131072
        O_RB = 135168
        O_RC = 139264
        O_RD = 143360
        O_WQA = 147456
        O_WQB = 151552
        O_WQQ = 155648
        O_WQT = 157696
        O_NWB = 167936
        O_NBB = 172032

        base_fm = mf(O_BASE, 8 * 2048)
        xn2_fm = mb(O_XN2F, 8 * 2048)
        xq_fm = mb(O_XQF, 8 * 2048)
        nwb = mf(O_NWB, 1024)
        nbb = mf(O_NBB, 1024)
        b_base = newB("r_base")
        b_xn2f = newB("r_xn2f")
        b_xqf = newB("r_xqf")
        b_nwb = newB("r_nwb")
        b_nbb = newB("r_nbb")

        def bcast_rows(dst_row_ap_fn, first_in, n_parts, reads, bname):
            st = p.dma("sync", dst_row_ap_fn(0, 1), first_in,
                       reads=reads, writes=[B(bname)])
            pc = 1
            while pc < n_parts:
                st = p.dma("sync", dst_row_ap_fn(pc, min(pc, n_parts - pc)),
                           dst_row_ap_fn(0, min(pc, n_parts - pc)),
                           extra=[st])
                pc *= 2
            B(bname).writers.append(st)
            return st

        bcast_rows(lambda p0, n: Mf[p0:p0 + n, O_NWB // 4:O_NWB // 4 + 1024],
                   i_nw.ap(), 128, [], "r_nwb")
        bcast_rows(lambda p0, n: Mf[p0:p0 + n, O_NBB // 4:O_NBB // 4 + 1024],
                   i_nb.ap(), 128, [], "r_nbb")

        def transpose_to_fm(src_ap, src_buf, ident_ap, ident_buf,
                            dst_half_off, dst_is_bf16, dst_buf, tt):
            for half in range(2):
                bank = 6 + half
                pview = psum[bank][:].bitcast(BF16) if dst_is_bf16 \
                    else psum[bank][:]

                def em_t(t, src_ap=src_ap, half=half, pview=pview,
                         ident_ap=ident_ap):
                    r = None
                    for q in range(4):
                        j = half * 4 + q
                        r = t.transpose(pview[:, q * 128:(q + 1) * 128],
                                        src_ap[:, j * 128:(j + 1) * 128],
                                        ident_ap)
                    return r
                p.add("tensor", em_t, reads=[src_buf, ident_buf],
                      writes=[pbuf[bank]])
                if dst_is_bf16:
                    dstv = Mb[:, dst_half_off // 2:
                              dst_half_off // 2 + 16384].rearrange(
                        "p (j t) -> p j t", j=8)
                else:
                    dstv = Mf[:, dst_half_off // 4:
                              dst_half_off // 4 + 16384].rearrange(
                        "p (j t) -> p j t", j=8)
                dst = dstv[:, half * 4:half * 4 + 4, tt * 128:tt * 128 + 128]
                psv = pview[:, 0:512].rearrange("p (q t) -> p q t", q=4)
                p.add("scalar", lambda sc, dst=dst, psv=psv: sc.copy(dst, psv),
                      reads=[pbuf[bank]], writes=[dst_buf])

        XO = [131072, 135168, 139264, 143360]
        XNO = [147456, 151552, 155648, 159744]
        JNK = mf(163840, 1024)
        QT = mf(176128, 1024)
        QTb = mb(176128, 1024)
        QTi = mi(176128, 1024)
        bX = [newB("x%d" % i) for i in range(4)]
        bXN = [newB("xn%d" % i) for i in range(4)]
        bJ = newB("jnk")
        bQ = newB("qt")
        CS0, CS1, CS2 = 4, 8, 12
        bc0, bc1, bc2 = newB("cc0"), newB("cc1"), newB("cc2")

        def stats_chain():
            c0 = sb_cols[:, CS0:CS0 + 4]
            c1 = sb_cols[:, CS1:CS1 + 4]
            c2 = sb_cols[:, CS2:CS2 + 4]
            p.add("vector", lambda v: v.tensor_scalar_mul(c0, c0, 1.0 / DM),
                  writes=[bc0], force=True)
            p.add("vector", lambda v: v.tensor_tensor(c2, c0, c0, ALU.mult),
                  reads=[bc0], writes=[bc2], force=True)
            p.add("vector", lambda v: v.tensor_scalar_mul(c1, c1, 1.0 / DM),
                  writes=[bc1], force=True)
            p.add("vector", lambda v: v.tensor_tensor(c1, c1, c2, ALU.subtract),
                  reads=[bc2], writes=[bc1], force=True)
            p.add("vector", lambda v: v.tensor_scalar_add(c1, c1, 1e-5),
                  writes=[bc1], force=True)
            p.add("scalar", lambda s: s.activation(c1, c1, ACT.Sqrt),
                  writes=[bc1], force=True)
            p.add("vector", lambda v: v.reciprocal(c1, c1),
                  writes=[bc1], force=True)

        for rnd in range(4):
            tts = [rnd * 4 + i for i in range(4)]
            p.dma("sync" if rnd % 2 else "gpsimd",
                  Mf[:, XO[0] // 4:XO[0] // 4 + 4096].rearrange(
                      "p (r c) -> p r c", r=4),
                  i_x.ap()[rnd * 512:(rnd + 1) * 512, :].rearrange(
                      "(r p) c -> p r c", p=128),
                  writes=[bX[0], bX[1], bX[2], bX[3]])
            for i, tt in enumerate(tts):
                X = mf(XO[i], 1024)
                p.add("vector", lambda v, X=X, i=i: v.reduce_sum(
                    sb_cols[:, CS0 + i:CS0 + i + 1], X, AX.X),
                    reads=[bX[i]], writes=[bc0])
                p.add("scalar", lambda s, X=X, i=i: s.activation(
                    JNK, X, ACT.Square,
                    accum_out=sb_cols[:, CS1 + i:CS1 + i + 1]),
                    reads=[bX[i]], writes=[bJ, bc1])
            stats_chain()
            for i, tt in enumerate(tts):
                X = mf(XO[i], 1024)
                XN = mf(XNO[i], 1024)
                p.add("vector", lambda v, X=X, XN=XN, i=i: v.tensor_scalar(
                    XN, X, sb_cols[:, CS0 + i:CS0 + i + 1],
                    sb_cols[:, CS1 + i:CS1 + i + 1], ALU.subtract, ALU.mult),
                    reads=[bX[i]], writes=[bXN[i]], ptr_reads=[bc0, bc1])
                p.add("vector", lambda v, X=X, XN=XN: v.tensor_tensor(
                    X, X, XN, ALU.add), reads=[bXN[i]], writes=[bX[i]])
                transpose_to_fm(X, bX[i], sb_idf[:], B("idf"), O_BASE, False,
                                b_base, tt)
                p.add("vector", lambda v, XN=XN, i=i: v.reduce_sum(
                    sb_cols[:, CS0 + i:CS0 + i + 1], XN, AX.X),
                    reads=[bXN[i]], writes=[bc0])
                p.add("scalar", lambda s, XN=XN, i=i: s.activation(
                    JNK, XN, ACT.Square,
                    accum_out=sb_cols[:, CS1 + i:CS1 + i + 1]),
                    reads=[bXN[i]], writes=[bJ, bc1])
            stats_chain()
            for i, tt in enumerate(tts):
                X = mf(XO[i], 1024)
                Xb = mb(XO[i], 1024)
                XN = mf(XNO[i], 1024)
                p.add("vector", lambda v, X=X, XN=XN, i=i: v.tensor_scalar(
                    X, XN, sb_cols[:, CS0 + i:CS0 + i + 1],
                    sb_cols[:, CS1 + i:CS1 + i + 1], ALU.subtract, ALU.mult),
                    reads=[bXN[i]], writes=[bX[i]], ptr_reads=[bc0, bc1])
                p.add("vector", lambda v, X=X: v.tensor_tensor(
                    X, X, nwb, ALU.mult), reads=[b_nwb], writes=[bX[i]])
                p.add("vector", lambda v, X=X, Xb=Xb: v.tensor_tensor(
                    Xb, X, nbb, ALU.add), reads=[b_nbb], writes=[bX[i]])
                transpose_to_fm(Xb, bX[i], sb_idb[:], B("idb"), O_XN2F, True,
                                b_xn2f, tt)
                p.add("vector", lambda v, XN=XN, i=i: v.tensor_reduce(
                    sb_cols[:, CS0 + i:CS0 + i + 1], XN, AX.X, ALU.max,
                    apply_absolute_value=True),
                    reads=[bXN[i]], writes=[bc0])
            c0 = sb_cols[:, CS0:CS0 + 4]
            c1 = sb_cols[:, CS1:CS1 + 4]
            p.add("vector", lambda v: v.tensor_scalar_max(c0, c0, 1e-5),
                  writes=[bc0], force=True)
            p.add("vector", lambda v, rnd=rnd: v.tensor_scalar_mul(
                sb_srec[:, rnd * 4:rnd * 4 + 4], c0, 1.0 / 127.0),
                reads=[bc0], writes=[B("srec")], force=True)
            p.add("vector", lambda v: v.reciprocal(c1, c0),
                  reads=[bc0], writes=[bc1], force=True)
            p.add("vector", lambda v: v.tensor_scalar_mul(c1, c1, 127.0),
                  writes=[bc1], force=True)
            for i, tt in enumerate(tts):
                XN = mf(XNO[i], 1024)
                XNi = mi(XNO[i], 1024)
                Xb2 = mb(XO[i], 1024)
                p.add("vector", lambda v, XN=XN, XNi=XNi, i=i: v.tensor_scalar(
                    XNi, XN, sb_cols[:, CS1 + i:CS1 + i + 1], None, ALU.mult),
                    reads=[bXN[i]], writes=[bXN[i]], ptr_reads=[bc1])
                p.add("vector", lambda v, XNi=XNi, Xb2=Xb2: v.tensor_copy(
                    Xb2, XNi), reads=[bXN[i]], writes=[bX[i]])
                transpose_to_fm(Xb2, bX[i], sb_idb[:], B("idb"), O_XQF, True,
                                b_xqf, tt)

        p.add("tensor", lambda t: t.transpose(
            psum[6][0:16, 0:128], sb_srec[:], sb_idf[:]),
            reads=[B("srec"), B("idf")], writes=[pbuf[6]])
        p.add("scalar", lambda sc: sc.copy(
            Mf[0:16, O_RA // 4:O_RA // 4 + 128], psum[6][0:16, 0:128]),
            reads=[pbuf[6]], writes=[bX[0]])
        bcast_rows(lambda p0, n: sb_fx[p0:p0 + n, :],
                   Mf[0:16, O_RA // 4:O_RA // 4 + 128], 128, [bX[0]],
                   "fx")
        b_fx = B("fx")

        st_pb = p.dma("sync", h_base.ap().rearrange("(j p) t -> p j t", p=128),
                      base_fm.rearrange("p (j t) -> p j t", j=8),
                      reads=[b_base], writes=[B("h_base")])
        st_p2 = p.dma("gpsimd",
                      h_xn2.ap().rearrange("(j p) t -> p j t", p=128),
                      xn2_fm.rearrange("p (j t) -> p j t", j=8),
                      reads=[b_xn2f], writes=[B("h_xn2")])
        st_pq = p.dma("gpsimd",
                      h_xq.ap().rearrange("(j p) t -> p j t", p=128),
                      xq_fm.rearrange("p (j t) -> p j t", j=8),
                      reads=[b_xqf], writes=[B("h_xq")])

        O_WQA2, O_WQB2 = 32768, 40960
        O_WP1A, O_WP1B = 32768, 49152
        O_WP1C = 65536
        O_WQS1, O_WQS2 = 49152, 53248
        O_WQQ2 = 57344
        O_WQT2 = 61440
        O_WQTQ = 57344

        def wquant(src, dst, n_row_tiles, ncols, csw, tag):
            b_wqtmp = B("wqtmp")
            wa1 = [B("wp1_a"), B("wp1_b"), B("wp1_c")]
            wa = [B("wq_a"), B("wq_b")]
            p.add("vector", lambda v: v.memset(col(C_T0), 0.0), writes=[B("c0")])
            it = 0
            if ncols >= 2048:
                r1step, p1w = 1, ncols
            else:
                r1step, p1w = 4, 4 * ncols
            for rt0 in range(0, n_row_tiles, r1step):
                off = (O_WP1A, O_WP1B, O_WP1C)[it % 3]
                wt = mf(off, p1w)
                if r1step == 4:
                    inap = src[rt0 * 128:(rt0 + 4) * 128, :].rearrange(
                        "(p q) c -> p (q c)", p=128)
                else:
                    inap = src[rt0 * 128:(rt0 + 1) * 128, :]
                wrt = [wa1[it % 3]] + ([B("wqq"), B("wqt"), B("wqs")]
                                       if it % 3 == 1 else [])
                if it % 3 == 2:
                    wrt += [b_xn2f]
                if it < 2:
                    wrt += [B("wq_a"), B("wq_b")]
                p.dma(("gpsimd", "sync", "scalar")[it % 3], wt, inap,
                      writes=wrt)
                p.add("vector", lambda v, wt=wt: v.tensor_reduce(
                    col(C_T1), wt, AX.X, ALU.add,
                    apply_absolute_value=True),
                    reads=[wa1[it % 3]], writes=[B("c1")])
                p.add("vector", lambda v: v.tensor_tensor(
                    col(C_T0), col(C_T0), col(C_T1), ALU.add),
                    reads=[B("c1")], writes=[B("c0")], force=True)
                it += 1
                yield True
            FQ = O_WQTQ // 4
            f0 = Mf[0:16, FQ:FQ + 8]
            st1 = p.dma("sync", f0, col(C_T0), reads=[B("c0")],
                        writes=[b_wqtmp, B("wqq")])
            p.add("vector", lambda v: v.tensor_tensor(
                Mf[0:16, FQ:FQ + 4], Mf[0:16, FQ:FQ + 4],
                Mf[0:16, FQ + 4:FQ + 8], ALU.add),
                writes=[b_wqtmp], force=True)
            p.add("vector", lambda v: v.tensor_tensor(
                Mf[0:16, FQ:FQ + 2], Mf[0:16, FQ:FQ + 2],
                Mf[0:16, FQ + 2:FQ + 4], ALU.add),
                writes=[b_wqtmp], force=True)
            p.add("vector", lambda v: v.tensor_tensor(
                Mf[0:16, FQ:FQ + 1], Mf[0:16, FQ:FQ + 1],
                Mf[0:16, FQ + 1:FQ + 2], ALU.add),
                writes=[b_wqtmp], force=True)
            p.dma("sync", Mf[0:1, FQ + 8:FQ + 24], Mf[0:16, FQ:FQ + 1],
                  reads=[b_wqtmp], writes=[b_wqtmp])
            p.add("vector", lambda v: v.tensor_reduce(
                Mf[0:1, FQ:FQ + 1], Mf[0:1, FQ + 8:FQ + 24], AX.X, ALU.add),
                writes=[b_wqtmp], force=True)
            nel = float(n_row_tiles * 128 * ncols)
            p.add("vector", lambda v: v.tensor_scalar(
                Mf[0:1, FQ:FQ + 1], Mf[0:1, FQ:FQ + 1], 1.0 / nel, 1e-5,
                ALU.mult, ALU.max), writes=[b_wqtmp], force=True)
            bcast_rows(lambda p0, n: sb_cols[p0:p0 + n, csw:csw + 1],
                       Mf[0:1, FQ:FQ + 1], 128, [b_wqtmp], tag + "_sw")
            bsw = B(tag + "_sw")
            p.add("vector", lambda v: v.tensor_scalar_mul(
                col(C_THR), col(csw), 0.5), reads=[bsw], writes=[B("thr")],
                force=True)
            p.add("vector", lambda v: v.tensor_scalar_mul(
                col(C_NTHR), col(C_THR), -1.0), reads=[B("thr")],
                writes=[B("nthr")], force=True)
            p.add("vector", lambda v: v.tensor_scalar_mul(
                col(csw), col(csw), 0.5), writes=[bsw], force=True)
            yield True
            it = 0
            CCW = min(2048, ncols)
            ncc = ncols // CCW
            rstep = 1 if ncols >= 2048 else 2
            for rt0 in range(0, n_row_tiles, rstep):
                for cc in range(ncc):
                    off = O_WQA2 if it % 2 == 0 else O_WQB2
                    wt = mf(off, 2048)
                    if rstep == 2:
                        inap = src[rt0 * 128:(rt0 + 2) * 128, :].rearrange(
                            "(r p) c -> p r c", p=128)
                        outap = wt.rearrange("p (r c) -> p r c", r=2)
                    else:
                        inap = src[rt0 * 128:(rt0 + 1) * 128,
                                   cc * 2048:(cc + 1) * 2048]
                        outap = wt
                    wrt = [wa[it % 2]] + ([wa1[0], wa1[1]] if it < 2 else [])
                    p.dma(("gpsimd", "sync", "scalar")[it % 3], outap, inap,
                          writes=wrt)
                    s1 = mb(O_WQS1, 2048)
                    s2 = mb(O_WQS2, 2048)
                    wq = mb(O_WQQ2, 2048)
                    p.add("scalar", lambda sc, wt=wt, s1=s1: sc.activation(
                        s1, wt, ACT.Sign, bias=col(C_NTHR)),
                        reads=[wa[it % 2]], writes=[B("wqs")],
                        ptr_reads=[B("nthr")])
                    p.add("scalar", lambda sc, wt=wt, s2=s2: sc.activation(
                        s2, wt, ACT.Sign, bias=col(C_THR)),
                        reads=[wa[it % 2]], writes=[B("wqs")],
                        ptr_reads=[B("thr")])
                    p.add("vector", lambda v, wq=wq, s1=s1, s2=s2:
                          v.tensor_tensor(wq, s1, s2, ALU.add),
                          reads=[B("wqs")], writes=[B("wqq")])
                    for half in range(4):
                        bank = 6 + half % 2
                        pview = psum[bank][:].bitcast(BF16)

                        def em_t(t, wq=wq, half=half, pview=pview):
                            r = None
                            for q in range(4):
                                jb = half * 4 + q
                                r = t.transpose(
                                    pview[:, q * 128:(q + 1) * 128],
                                    wq[:, jb * 128:(jb + 1) * 128], sb_idb[:])
                            return r
                        p.add("tensor", em_t, reads=[B("wqq"), B("idb")],
                              writes=[pbuf[bank]])
                        o = O_WQT2 // 2 + half * 512
                        p.add("vector", lambda v, o=o, pview=pview:
                              v.tensor_copy(Mb[:, o:o + 512], pview[:, 0:512]),
                              reads=[pbuf[bank]], writes=[B("wqt")])

                    if rstep == 2:
                        for r in range(2):
                            def em_d(eng, rt0=rt0, r=r):
                                jb = rt0 + r
                                return eng.dma_start(
                                    dst[jb * 1024:(jb + 1) * 1024,
                                        :].rearrange(
                                        "(k p) c -> p k c", p=128),
                                    Mb[:, O_WQT2 // 2 + r * 1024:
                                       O_WQT2 // 2 + r * 1024 + 1024].rearrange(
                                        "p (k c) -> p k c", c=128))
                            p.add("sync", em_d, reads=[B("wqt")],
                                  writes=[B(tag + "park")], dma=True)
                    else:
                        def em_d(eng, rt0=rt0, cc=cc):
                            r0 = rt0 * 4096 + cc * 2048
                            return eng.dma_start(
                                dst[r0:r0 + 2048].rearrange(
                                    "(k p) c -> p k c", p=128),
                                Mb[:, O_WQT2 // 2:O_WQT2 // 2 + 2048].rearrange(
                                    "p (k c) -> p k c", c=128))
                        p.add("sync", em_d, reads=[B("wqt")],
                              writes=[B(tag + "park")], dma=True)
                    it += 1
                    yield True

        for ch in range(NCH):
            T0 = ch * TC
            bar = p.barrier()

            xn2s = mb(131072, 8 * 1024)
            b_xn2s = newB("xn2s", bar)
            if ch == 0:
                p.add("vector", lambda v: v.tensor_copy(
                    xn2s.rearrange("p (k t) -> p k t", k=8),
                    xn2_fm.rearrange("p (j t) -> p j t", j=8)[:, :, T0:T0 + TC]
                    if False else
                    xn2_fm.rearrange("p (j t) -> p j t", j=8)[:, :, 0:TC]),
                    reads=[b_xn2f], writes=[b_xn2s])
            else:
                p.dma("sync", xn2s.rearrange("p (k t) -> p k t", k=8),
                      h_xn2.ap()[:, T0:T0 + TC].rearrange(
                          "(k p) t -> p k t", p=128),
                      reads=[B("h_xn2")], writes=[b_xn2s])
            xbres = mb(98304, 16 * 1024)
            b_xb = newB("xbres", bar)
            b_xbpre = newB("xbpre", bar)
            b_convt = newB("convt", bar)
            w1b_ = [newB("w1a", bar), newB("w1b", bar)]
            W1O = [172544, 174848]
            xbpre = mf(163840, 1028)
            convt = mf(168192, 1024)
            convtb = mb(168192, 1024)
            b_xbpark = B("xb_park")
            b_zpark = B("z_park")
            if ch == 0:
                for _nm in ("wq_a", "wq_b", "wp1_a", "wp1_b", "wp1_c",
                            "wqtmp", "wqq", "wqt", "wqs"):
                    newB(_nm, bar)

                def _wq_all():
                    yield from wquant(i_wg.ap(), h_wqg.ap(), INNER // 128,
                                      DM, C_SWG, "wg")
                    yield from wquant(i_wu.ap(), h_wqu.ap(), INNER // 128,
                                      DM, C_SWU, "wu")
                    yield from wquant(i_wd.ap(), h_wqd.ap(), DM // 128,
                                      INNER, C_SWD, "wd")
                wqgen = _wq_all()
            else:
                wqgen = iter(())

            def wq_step(n=1):
                for _ in range(n):
                    if next(wqgen, None) is None:
                        break
            for j in range(32):
                wt = mb(W1O[j % 2], 8 * 128)
                p.dma("gpsimd", wt.rearrange("p (k c) -> p k c", c=128),
                      i_w1.ap()[:, j * 128:(j + 1) * 128].rearrange(
                          "(k p) c -> p k c", p=128),
                      writes=[w1b_[j % 2]])
                for th in range(2):
                    bank = (j * 2 + th) % 4

                    def em_m(t, wt=wt, th=th, bank=bank):
                        r = None
                        for k in range(8):
                            r = t.matmul(
                                psum[bank][:],
                                wt.rearrange("p (k c) -> p k c", c=128)[:, k, :],
                                xn2s[:, k * 1024 + th * 512:
                                     k * 1024 + th * 512 + 512],
                                start=(k == 0), stop=(k == 7))
                        return r
                    p.add("tensor", em_m, reads=[w1b_[j % 2], b_xn2s],
                          writes=[pbuf[bank]])
                    if j < 16:
                        p.add("vector", lambda v, th=th, bank=bank:
                              v.tensor_copy(
                                  xbpre[:, 4 + th * 512:4 + th * 512 + 512],
                                  psum[bank][:]),
                              reads=[pbuf[bank]], writes=[b_xbpre])
                    else:
                        p.add("vector", lambda v, th=th, bank=bank:
                              v.tensor_copy(
                                  convtb[:, th * 512:th * 512 + 512],
                                  psum[bank][:]),
                              reads=[pbuf[bank]], writes=[b_convt])
                if j < 16:
                    p.add("vector", lambda v, j=j: v.tensor_copy(
                        xbpre[:, 1:4], sb_halo[:, j * 4 + 1:j * 4 + 4]),
                        reads=[B("halo")], writes=[b_xbpre])
                    p.add("vector", lambda v, j=j: v.tensor_copy(
                        sb_halo[:, j * 4 + 1:j * 4 + 4], xbpre[:, 1025:1028]),
                        reads=[b_xbpre], writes=[B("halo")], force=True)
                    p.add("vector", lambda v, j=j: v.tensor_scalar(
                        convt, xbpre[:, 1:1025], sb_cw[:, j * 4:j * 4 + 1],
                        None, ALU.mult),
                        reads=[b_xbpre, B("cw")], writes=[b_convt],
                        force=True)
                    for k in range(1, 4):
                        p.add("vector", lambda v, j=j, k=k:
                              v.scalar_tensor_tensor(
                                  convt, xbpre[:, 1 + k:1 + k + 1024],
                                  sb_cw[:, j * 4 + k:j * 4 + k + 1], convt,
                                  ALU.mult, ALU.add),
                              reads=[b_xbpre, B("cw")], writes=[b_convt])
                    xbt = xbres[:, j * 1024:(j + 1) * 1024]
                    p.add("scalar", lambda sc, j=j, xbt=xbt: sc.activation(
                        xbt, convt, ACT.Silu, bias=sb_cb[:, j:j + 1]),
                        reads=[b_convt, B("cb")], writes=[b_xb])
                else:
                    jz = j - 16
                    p.dma("sync", h_z.ap()[jz * 128:(jz + 1) * 128, T0:T0 + TC],
                          convtb, reads=[b_convt], writes=[b_zpark])
                wq_step(1)

            p.dma("sync",
                  h_xb.ap()[:, T0:T0 + TC].rearrange("(k p) t -> p k t", p=128),
                  xbres.rearrange("p (k t) -> p k t", k=16),
                  reads=[b_xb], writes=[b_xbpark])

            bar2 = p.barrier()
            delta_in = mb(0, 16 * 1024)
            b_din = newB("din", bar2)
            wx_ = [newB("wxa", bar2), newB("wxb", bar2)]
            WXO = [163840, 168192]
            bc_bufs = {}

            def emit_bc():
                barBC = p.barrier()
                b_bfree = newB("bfree", barBC)
                b_cfree = newB("cfree", barBC)
                wqtags = [B(nm) for nm in ("wq_a", "wq_b", "wp1_a", "wp1_b",
                                           "wp1_c", "wqq", "wqt", "wqs",
                                           "wqtmp")]
                for s_i in range(NSUB):
                    st = p.dma("sync",
                               Mb[0:1, 32768 // 2 + s_i * 4096:
                                  32768 // 2 + (s_i + 1) * 4096],
                               sb_Bbf[0:16, s_i * SC:(s_i + 1) * SC],
                               reads=[B("Bbf")], extra=barBC,
                               writes=(wqtags if s_i == 0 else ()))
                    b_bfree.writers.append(st)
                    st = p.dma("gpsimd",
                               Mb[0:1, 65536 // 2 + s_i * 4096:
                                  65536 // 2 + (s_i + 1) * 4096],
                               sb_Cbf[0:16, s_i * SC:(s_i + 1) * SC],
                               reads=[B("Cbf")], extra=barBC,
                               writes=(wqtags + [b_xn2f] if s_i == 0 else ()))
                    b_cfree.writers.append(st)
                pc = 1
                stb = list(b_bfree.writers)
                while pc < 128:
                    stb = [p.dma("sync",
                                 Mb[pc:2 * pc, 32768 // 2:32768 // 2 + 16384],
                                 Mb[0:pc, 32768 // 2:32768 // 2 + 16384],
                                 extra=stb)]
                    pc *= 2
                b_bfree.writers.extend(stb)
                pc = 1
                stc = list(b_cfree.writers)
                while pc < 128:
                    stc = [p.dma("gpsimd",
                                 Mb[pc:2 * pc, 65536 // 2:65536 // 2 + 16384],
                                 Mb[0:pc, 65536 // 2:65536 // 2 + 16384],
                                 extra=stc)]
                    pc *= 2
                b_cfree.writers.extend(stc)
                bc_bufs["b"] = b_bfree
                bc_bufs["c"] = b_cfree
            for j in [16] + list(range(16)):
                wdt_c = 128 if j < 16 else 32
                wt = mb(WXO[j % 2], 16 * wdt_c)
                p.dma("gpsimd", wt.rearrange("p (k c) -> p k c", c=wdt_c),
                      i_wx.ap()[:, j * 128:j * 128 + wdt_c].rearrange(
                          "(k p) c -> p k c", p=128),
                      writes=[wx_[j % 2]])
                for th in range(2):
                    bank = (j * 2 + th) % 4
                    if j < 16:
                        def em_m(t, wt=wt, th=th, bank=bank):
                            r = None
                            for k in range(16):
                                r = t.matmul(
                                    psum[bank][:],
                                    wt.rearrange("p (k c) -> p k c",
                                                 c=128)[:, k, :],
                                    xbres[:, k * 1024 + th * 512:
                                          k * 1024 + th * 512 + 512],
                                    start=(k == 0), stop=(k == 15))
                            return r
                        p.add("tensor", em_m, reads=[wx_[j % 2], b_xb],
                              writes=[pbuf[bank]])
                        p.add("vector", lambda v, j=j, th=th, bank=bank:
                              v.tensor_copy(
                                  delta_in[:, j * 1024 + th * 512:
                                           j * 1024 + th * 512 + 512],
                                  psum[bank][:]),
                              reads=[pbuf[bank]], writes=[b_din])
                    else:
                        bank_c = (bank + 2) % 4

                        def em_mb(t, wt=wt, th=th, bank=bank):
                            r = None
                            for k in range(16):
                                r = t.matmul(
                                    psum[bank][0:16, 0:512],
                                    wt.rearrange("p (k c) -> p k c",
                                                 c=32)[:, k, 0:16],
                                    xbres[:, k * 1024 + th * 512:
                                          k * 1024 + th * 512 + 512],
                                    start=(k == 0), stop=(k == 15))
                            return r
                        p.add("tensor", em_mb, reads=[wx_[j % 2], b_xb],
                              writes=[pbuf[bank]])

                        def em_mc(t, wt=wt, th=th, bank_c=bank_c):
                            r = None
                            for k in range(16):
                                r = t.matmul(
                                    psum[bank_c][0:16, 0:512],
                                    wt.rearrange("p (k c) -> p k c",
                                                 c=32)[:, k, 16:32],
                                    xbres[:, k * 1024 + th * 512:
                                          k * 1024 + th * 512 + 512],
                                    start=(k == 0), stop=(k == 15))
                            return r
                        p.add("tensor", em_mc, reads=[wx_[j % 2], b_xb],
                              writes=[pbuf[bank_c]])
                        p.add("scalar", lambda sc, th=th, bank=bank: sc.copy(
                            sb_Bbf[:, th * 512:th * 512 + 512],
                            psum[bank][0:16, 0:512]),
                            reads=[pbuf[bank]], writes=[B("Bbf")])
                        p.add("scalar", lambda sc, th=th, bank_c=bank_c:
                              sc.copy(sb_Cbf[:, th * 512:th * 512 + 512],
                                      psum[bank_c][0:16, 0:512]),
                              reads=[pbuf[bank_c]], writes=[B("Cbf")])
                if j == 16 and ch != 0:
                    emit_bc()
                wq_step(3)

            wq_step(10 ** 6)
            if ch == 0:
                emit_bc()

            bar3 = p.barrier()
            b_bfree = bc_bufs["b"]
            b_cfree = bc_bufs["c"]

            dAs = [mf(98304, 4096), mf(114688, 4096)]
            b_dAs = [newB("dA0", bar3), newB("dA1", bar3)]
            dbx = mb(131072, 4096)
            dbx3 = dbx.rearrange("p (n t) -> p n t", n=DS)
            b_dBx = newB("dBx", bar3)
            hb = mb(139264, 4096)
            hb3 = hb.rearrange("p (n t) -> p n t", n=DS)
            b_h = newB("hb", bar3)
            deltas = [mf(147456, 1024), mf(151552, 1024)]
            b_deltas = [newB("d0", bar3), newB("d1", bar3)]
            sptmp = mf(155648, 1024)
            b_sp = newB("sptmp", bar3)
            xbsls = [mb(159744, 1024), mb(165888, 1024)]
            b_xbsls = [newB("xbsl0", bar3), newB("xbsl1", bar3)]
            zsls = [mb(161792, 1024), mb(169984, 1024)]
            b_zsls = [newB("zsl0", bar3), newB("zsl1", bar3)]
            tnh = mb(163840, 1024)
            b_tnh = newB("tnh", bar3)
            ytmp = mb(167936, 1024)
            b_ytmp = newB("ytmp", bar3)
            wd_ = [newB("wda", bar3), newB("wdb", bar3)]
            WDO = [172032, 176128]
            b_ypark = B("y_park")

            def prep_dt(dt):
                par = dt % 2
                wt = mb(WDO[par], 16 * 128)
                p.dma("gpsimd", wt.rearrange("p (k c) -> p k c", c=128),
                      i_wdt.ap()[:, dt * 128:(dt + 1) * 128].rearrange(
                          "(k p) c -> p k c", p=128),
                      writes=[wd_[par]])
                p.dma("sync", xbsls[par],
                      h_xb.ap()[dt * 128:(dt + 1) * 128, T0:T0 + TC],
                      reads=[b_xbpark], writes=[b_xbsls[par]])
                p.dma("sync", zsls[par],
                      h_z.ap()[dt * 128:(dt + 1) * 128, T0:T0 + TC],
                      reads=[b_zpark], writes=[b_zsls[par]])
                delta = deltas[par]
                for th in range(2):
                    bank = 4 + th

                    def em_m(t, wt=wt, th=th, bank=bank):
                        r = None
                        for k in range(16):
                            r = t.matmul(
                                psum[bank][:],
                                wt.rearrange("p (k c) -> p k c", c=128)[:, k, :],
                                delta_in[:, k * 1024 + th * 512:
                                         k * 1024 + th * 512 + 512],
                                start=(k == 0), stop=(k == 15))
                        return r
                    p.add("tensor", em_m, reads=[wd_[par], b_din],
                          writes=[pbuf[bank]])
                    p.add("scalar", lambda s, th=th, bank=bank, dt=dt,
                          delta=delta: s.activation(
                              delta[:, th * 512:th * 512 + 512], psum[bank][:],
                              ACT.Identity, bias=sb_dtb[:, dt:dt + 1]),
                          reads=[pbuf[bank], B("dtb")], writes=[b_deltas[par]])
                p.add("scalar", lambda s, delta=delta: s.activation(
                    sptmp, delta, ACT.Abs), reads=[b_deltas[par]],
                    writes=[b_sp])
                p.add("scalar", lambda s: s.activation(
                    sptmp, sptmp, ACT.Exp, scale=-1.0), writes=[b_sp])
                p.add("scalar", lambda s: s.activation(
                    sptmp, sptmp, ACT.Ln, bias=1.0), writes=[b_sp])
                p.add("scalar", lambda s, delta=delta: s.activation(
                    delta, delta, ACT.Relu), writes=[b_deltas[par]])
                p.add("vector", lambda v, delta=delta: v.tensor_tensor(
                    delta, delta, sptmp, ALU.add), reads=[b_sp],
                    writes=[b_deltas[par]])

            prep_dt(0)
            for dt in range(NDI):
                par = dt % 2
                delta = deltas[par]
                xbsl = xbsls[par]
                if dt + 1 < NDI:
                    prep_dt(dt + 1)
                hsl = sb_hcar[:, dt * DS:(dt + 1) * DS]
                for s_i in range(NSUB):
                    g_par = (dt * NSUB + s_i) % 2
                    dA = dAs[g_par]
                    dA3 = dA.rearrange("p (n t) -> p n t", n=DS)
                    b_dA = b_dAs[g_par]
                    bfs3 = mb(32768 + s_i * 8192, 4096).rearrange(
                        "p (n t) -> p n t", n=DS)
                    cfs3 = mb(65536 + s_i * 8192, 4096).rearrange(
                        "p (n t) -> p n t", n=DS)

                    def em_da(sc, dt=dt, s_i=s_i, dA=dA, delta=delta):
                        r = None
                        for n in range(DS):
                            r = sc.activation(
                                dA[:, n * SC:(n + 1) * SC],
                                delta[:, s_i * SC:(s_i + 1) * SC], ACT.Exp,
                                scale=sb_A[:, dt * DS + n:dt * DS + n + 1])
                        return r
                    p.add("scalar", em_da, reads=[b_deltas[par], B("A")],
                          writes=[b_dA])
                    p.add("vector", lambda v, s_i=s_i, delta=delta, xbsl=xbsl:
                          v.tensor_tensor(
                              sb_u[:], delta[:, s_i * SC:(s_i + 1) * SC],
                              xbsl[:, s_i * SC:(s_i + 1) * SC], ALU.mult),
                          reads=[b_deltas[par], b_xbsls[par]], writes=[B("u")])
                    p.add("vector", lambda v, bfs3=bfs3: v.tensor_tensor(
                        dbx3,
                        sb_u[:].unsqueeze(1).broadcast_to([128, DS, SC]),
                        bfs3, ALU.mult),
                        reads=[B("u"), b_bfree], writes=[b_dBx])
                    p.add("vector", lambda v, hsl=hsl, dA3=dA3: v.tensor_tensor(
                        sb_u16[:],
                        dA3[:, :, 0:1].rearrange("p n o -> p (n o)"),
                        hsl, ALU.mult),
                        reads=[b_dA, B("hcar")], writes=[B("u16")], force=True)
                    p.add("vector", lambda v: v.tensor_tensor(
                        dbx3[:, :, 0:1].rearrange("p n o -> p (n o)"),
                        dbx3[:, :, 0:1].rearrange("p n o -> p (n o)"),
                        sb_u16[:], ALU.add), reads=[B("u16")], writes=[b_dBx],
                        force=True)
                    p.add("vector", lambda v, dA3=dA3: v.memset(
                        dA3[:, :, 0:1], 0.0), writes=[b_dA], force=True)
                    p.add("vector", lambda v, dA=dA: v.tensor_tensor_scan(
                        hb, dA, dbx, 0.0, ALU.mult, ALU.add),
                        reads=[b_dBx, b_dA], writes=[b_h], force=True)
                    p.add("vector", lambda v, hsl=hsl: v.tensor_copy(
                        hsl, hb3[:, :, SC - 1:SC].rearrange("p n o -> p (n o)")),
                        reads=[b_h], writes=[B("hcar")], force=True)
                    p.add("vector", lambda v, cfs3=cfs3: v.tensor_tensor(
                        dbx3, hb3, cfs3, ALU.mult),
                        reads=[b_h, b_cfree], writes=[b_dBx])
                    w_ = 2048
                    while w_ >= SC:
                        p.add("vector", lambda v, w_=w_: v.tensor_tensor(
                            dbx[:, 0:w_], dbx[:, 0:w_], dbx[:, w_:2 * w_],
                            ALU.add), writes=[b_dBx])
                        w_ //= 2

                    def em_y(v, s_i=s_i, xbsl=xbsl, dt=dt):
                        return v.scalar_tensor_tensor(
                            ytmp[:, s_i * SC:(s_i + 1) * SC],
                            xbsl[:, s_i * SC:(s_i + 1) * SC],
                            sb_D[:, dt:dt + 1], dbx[:, 0:SC],
                            ALU.mult, ALU.add)
                    p.add("vector", em_y, reads=[b_xbsls[par], B("D"), b_dBx],
                          writes=[b_ytmp])
                p.add("scalar", lambda s, par=par: s.activation(
                    tnh, zsls[par], ACT.Tanh, scale=0.5),
                    reads=[b_zsls[par]], writes=[b_tnh])
                p.add("vector", lambda v: v.tensor_scalar(
                    tnh, tnh, 0.5, 0.5, ALU.mult, ALU.add), writes=[b_tnh])
                p.add("vector", lambda v, par=par: v.tensor_tensor(
                    ytmp[:], ytmp[:], zsls[par], ALU.mult),
                    reads=[b_zsls[par]], writes=[b_ytmp])
                p.add("vector", lambda v: v.tensor_tensor(
                    ytmp[:], ytmp[:], tnh, ALU.mult),
                    reads=[b_tnh], writes=[b_ytmp])
                st = p.dma("sync", h_y.ap()[dt * 128:(dt + 1) * 128, T0:T0 + TC],
                           ytmp[:], reads=[b_ytmp])
                b_ypark.writers.append(st)

            bar5 = p.barrier()
            g_all = mf(0, 16384)
            gq = mb(65536, 16384)
            xqs = mb(131072, 8 * 1024)
            gmax = mf(147456, 512)
            gmin = mf(149504, 512)
            qi32 = mi(151552, 512)
            p1t = mf(153600, 512)
            gtmps = [mf(163840, 512), mf(155648, 512)]
            gtmp2s = [mf(165888, 512), mf(157696, 512)]
            utms = [mf(167936, 512), mf(159744, 512)]
            qtmp = mf(169984, 512)
            b_g = newB("g_all", bar5)
            b_gq = newB("gq", bar5)
            b_xqs = newB("xqs", bar5)
            b_gmax = newB("gmax", bar5)
            b_gmin = newB("gmin", bar5)
            b_qi = newB("qi32", bar5)
            b_p1 = newB("p1t", bar5)
            b_gts = [newB("gtmp0", bar5), newB("gtmp1", bar5)]
            b_gt2s = [newB("gtmp20", bar5), newB("gtmp21", bar5)]
            b_uts = [newB("utm0", bar5), newB("utm1", bar5)]
            b_qt = newB("qtmp", bar5)
            b_fw = [newB("fw0", bar5), newB("fw1", bar5)]
            b_fwd = [newB("fwd0", bar5), newB("fwd1", bar5)]
            b_gqk = [newB("gqk%d" % k2, bar5) for k2 in range(32)]
            FWO = [172032, 176128]
            FWD = [172032, 161792]
            p.dma("sync", xqs.rearrange("p (k t) -> p k t", k=8),
                  h_xq.ap()[:, T0:T0 + TC].rearrange("(p k) t -> p k t", k=8),
                  reads=[B("h_xq")], writes=[b_xqs])
            def p5_production(sub):
                tg0 = T0 + sub * 512
                fxsl = sb_fx[:, tg0:tg0 + 512]
                p.add("vector", lambda v: v.memset(gmax, -3.0e38),
                      writes=[b_gmax])
                p.add("vector", lambda v: v.memset(gmin, 3.0e38),
                      writes=[b_gmin])
                for j in range(32):
                    wgt = mb(FWO[j % 2], 8 * 128)
                    wut = mb(FWO[j % 2] + 2048, 8 * 128)
                    p.dma("gpsimd", wgt.rearrange("p (k c) -> p k c", c=128),
                          h_wqg.ap()[j * 1024:(j + 1) * 1024, :].rearrange(
                              "(p k) c -> p k c", k=8),
                          reads=[B("wgpark")], writes=[b_fw[j % 2]])
                    p.dma("sync", wut.rearrange("p (k c) -> p k c", c=128),
                          h_wqu.ap()[j * 1024:(j + 1) * 1024, :].rearrange(
                              "(p k) c -> p k c", k=8),
                          reads=[B("wupark")], writes=[b_fw[j % 2]])
                    bank_g = (j % 2) * 2
                    bank_u = (j % 2) * 2 + 1

                    def em_mg(t, wgt=wgt, bank_g=bank_g, sub=sub):
                        r = None
                        for k in range(8):
                            r = t.matmul(
                                psum[bank_g][:],
                                wgt.rearrange("p (k c) -> p k c",
                                              c=128)[:, k, :],
                                xqs[:, k * 1024 + sub * 512:
                                    k * 1024 + sub * 512 + 512],
                                start=(k == 0), stop=(k == 7))
                        return r
                    p.add("tensor", em_mg, reads=[b_fw[j % 2], b_xqs],
                          writes=[pbuf[bank_g]])

                    def em_mu(t, wut=wut, bank_u=bank_u, sub=sub):
                        r = None
                        for k in range(8):
                            r = t.matmul(
                                psum[bank_u][:],
                                wut.rearrange("p (k c) -> p k c",
                                              c=128)[:, k, :],
                                xqs[:, k * 1024 + sub * 512:
                                    k * 1024 + sub * 512 + 512],
                                start=(k == 0), stop=(k == 7))
                        return r
                    p.add("tensor", em_mu, reads=[b_fw[j % 2], b_xqs],
                          writes=[pbuf[bank_u]])
                    gtmp = gtmps[j % 2]
                    gtmp2 = gtmp2s[j % 2]
                    utm = utms[j % 2]
                    b_gt = b_gts[j % 2]
                    b_gt2 = b_gt2s[j % 2]
                    b_ut = b_uts[j % 2]
                    p.add("vector", lambda v, bank_g=bank_g, fxsl=fxsl,
                          gtmp=gtmp: v.scalar_tensor_tensor(
                              gtmp, psum[bank_g][:], col(C_SWG), fxsl,
                              ALU.mult, ALU.mult),
                          reads=[pbuf[bank_g], B("wg_sw"), b_fx],
                          writes=[b_gt])
                    p.add("scalar", lambda s, gtmp=gtmp, gtmp2=gtmp2:
                          s.activation(gtmp2, gtmp, ACT.Silu),
                          reads=[b_gt], writes=[b_gt2])
                    p.add("vector", lambda v, bank_u=bank_u, fxsl=fxsl,
                          utm=utm: v.scalar_tensor_tensor(
                              utm, psum[bank_u][:], col(C_SWU), fxsl,
                              ALU.mult, ALU.mult),
                          reads=[pbuf[bank_u], B("wu_sw"), b_fx],
                          writes=[b_ut])
                    p.add("vector", lambda v, j=j, gtmp2=gtmp2, utm=utm:
                          v.tensor_tensor(
                              g_all[:, j * 512:(j + 1) * 512], gtmp2, utm,
                              ALU.mult),
                          reads=[b_gt2, b_ut], writes=[b_g])
                    p.add("vector", lambda v, j=j: v.tensor_tensor(
                        gmax, gmax, g_all[:, j * 512:(j + 1) * 512], ALU.max),
                        reads=[b_g], writes=[b_gmax])
                    p.add("vector", lambda v, j=j: v.tensor_tensor(
                        gmin, gmin, g_all[:, j * 512:(j + 1) * 512], ALU.min),
                        reads=[b_g], writes=[b_gmin])
            def p5_rest(sub):
                tg0 = T0 + sub * 512
                fxsl = sb_fx[:, tg0:tg0 + 512]
                p.add("vector", lambda v: v.tensor_scalar_mul(
                    gmin, gmin, -1.0), writes=[b_gmin])
                p.add("vector", lambda v: v.tensor_tensor(
                    gmax, gmax, gmin, ALU.max), reads=[b_gmin],
                    writes=[b_gmax])
                FB = 65536 // 4
                p.dma("sync", Mf[0:16, FB:FB + 4096], gmax,
                      reads=[b_gmax], writes=[b_gq])
                p.add("vector", lambda v: v.tensor_tensor(
                    Mf[0:16, FB:FB + 2048], Mf[0:16, FB:FB + 2048],
                    Mf[0:16, FB + 2048:FB + 4096], ALU.max), writes=[b_gq])
                p.add("vector", lambda v: v.tensor_tensor(
                    Mf[0:16, FB:FB + 1024], Mf[0:16, FB:FB + 1024],
                    Mf[0:16, FB + 1024:FB + 2048], ALU.max), writes=[b_gq])
                p.add("vector", lambda v: v.tensor_tensor(
                    Mf[0:16, FB:FB + 512], Mf[0:16, FB:FB + 512],
                    Mf[0:16, FB + 512:FB + 1024], ALU.max), writes=[b_gq])
                p.dma("sync", Mf[0:4, FB + 2048:FB + 2048 + 2048],
                      Mf[0:16, FB:FB + 512], reads=[b_gq], writes=[b_gq])
                p.add("vector", lambda v: v.tensor_tensor(
                    Mf[0:4, FB + 2048:FB + 2048 + 1024],
                    Mf[0:4, FB + 2048:FB + 2048 + 1024],
                    Mf[0:4, FB + 3072:FB + 3072 + 1024], ALU.max),
                    writes=[b_gq])
                p.add("vector", lambda v: v.tensor_tensor(
                    Mf[0:4, FB + 2048:FB + 2048 + 512],
                    Mf[0:4, FB + 2048:FB + 2048 + 512],
                    Mf[0:4, FB + 2560:FB + 2560 + 512], ALU.max),
                    writes=[b_gq])
                p.dma("sync", Mf[0:1, FB:FB + 2048],
                      Mf[0:4, FB + 2048:FB + 2048 + 512],
                      reads=[b_gq], writes=[b_gq])
                p.add("vector", lambda v: v.tensor_tensor(
                    Mf[0:1, FB:FB + 1024], Mf[0:1, FB:FB + 1024],
                    Mf[0:1, FB + 1024:FB + 2048], ALU.max), writes=[b_gq])
                p.add("vector", lambda v: v.tensor_tensor(
                    Mf[0:1, FB:FB + 512], Mf[0:1, FB:FB + 512],
                    Mf[0:1, FB + 512:FB + 1024], ALU.max), writes=[b_gq])
                p.add("vector", lambda v: v.tensor_scalar_max(
                    Mf[0:1, FB:FB + 512], Mf[0:1, FB:FB + 512], 1e-5),
                    writes=[b_gq], force=True)
                p.add("vector", lambda v: v.reciprocal(
                    Mf[0:1, FB + 1024:FB + 1024 + 512],
                    Mf[0:1, FB:FB + 512]), writes=[b_gq], force=True)
                p.add("vector", lambda v: v.tensor_scalar_mul(
                    Mf[0:1, FB + 1024:FB + 1024 + 512],
                    Mf[0:1, FB + 1024:FB + 1024 + 512], 127.0),
                    writes=[b_gq], force=True)
                p.add("vector", lambda v: v.tensor_scalar(
                    Mf[0:1, FB + 2048:FB + 2048 + 512], Mf[0:1, FB:FB + 512],
                    1.0 / 127.0, sb_cols[0:1, C_SWD:C_SWD + 1],
                    ALU.mult, ALU.mult),
                    reads=[B("wd_sw")], writes=[b_gq], force=True)
                bcast_rows(lambda p0, n: sb_s2b[p0:p0 + n, :],
                           Mf[0:1, FB + 1024:FB + 1024 + 512], 128,
                           [b_gq], "s2b")
                bcast_rows(lambda p0, n: sb_fdb[p0:p0 + n, :],
                           Mf[0:1, FB + 2048:FB + 2048 + 512], 128,
                           [b_gq], "fdb")
                b_s2b = B("s2b")
                b_fdb2 = B("fdb")
                for j in range(32):
                    gsl = g_all[:, j * 512:(j + 1) * 512]
                    p.add("vector", lambda v, gsl=gsl: v.tensor_tensor(
                        qi32, gsl, sb_s2b[:], ALU.mult),
                        reads=[b_g, b_s2b], writes=[b_qi])
                    p.add("vector", lambda v, j=j: v.tensor_copy(
                        gq[:, j * 512:(j + 1) * 512], qi32),
                        reads=[b_qi], writes=[b_gqk[j], b_gq])
                for j2 in range(8):
                    wdt_ = mb(FWD[j2 % 2], 32 * 128)
                    wrt = ([b_fwd[0], b_fw[0], b_fw[1]] if j2 % 2 == 0 else
                           [b_fwd[1], b_gts[0], b_gt2s[0], b_uts[0]])
                    p.dma("sync" if j2 % 2 else "gpsimd",
                          wdt_.rearrange("p (k c) -> p k c", c=128),
                          h_wqd.ap()[j2 * 4096:(j2 + 1) * 4096, :].rearrange(
                              "(k p) c -> p k c", p=128),
                          reads=[B("wdpark")], writes=wrt)
                    bank = 4 + (j2 % 2)
                    kgrp = 8 if j2 == 0 else (16 if j2 == 1 else 32)
                    for k0 in range(0, 32, kgrp):
                        def em_md(t, wdt_=wdt_, bank=bank, k0=k0, kgrp=kgrp):
                            r = None
                            for k in range(k0, k0 + kgrp):
                                r = t.matmul(
                                    psum[bank][:],
                                    wdt_.rearrange("p (k c) -> p k c",
                                                   c=128)[:, k, :],
                                    gq[:, k * 512:(k + 1) * 512],
                                    start=(k == 0), stop=(k == 31))
                            return r
                        p.add("tensor", em_md,
                              reads=wrt + [b_gqk[k0 + kgrp - 1]],
                              writes=[pbuf[bank]])
                    p.add("vector", lambda v, bank=bank: v.tensor_tensor(
                        qtmp, psum[bank][:], sb_fdb[:], ALU.mult),
                        reads=[pbuf[bank], b_fdb2], writes=[b_qt])

                    def em_a(v, j2=j2, sub=sub):
                        sl = acc[:, j2 * 1024 + sub * 512:
                                 j2 * 1024 + sub * 512 + 512]
                        return v.tensor_tensor(sl, sl, qtmp, ALU.add)
                    p.add("vector", em_a, reads=[b_qt], writes=[b_acc])

            bar4 = bar5
            p5_production(0)
            acc = mf(98304, 8 * 1024)
            b_acc = newB("acc", bar4)
            p.dma("sync", acc.rearrange("p (j t) -> p j t", j=8),
                  h_base.ap()[:, T0:T0 + TC].rearrange("(j p) t -> p j t",
                                                       p=128),
                  reads=[B("h_base")], writes=[b_acc])
            ytile = mb(65536, 16 * 1024)
            b_y = newB("yres", bar4)
            b_yg = [newB("yg%d" % g, bar4) for g in range(4)]
            for g in range(4):
                p.dma("sync",
                      ytile.rearrange("p (k t) -> p k t", k=16)[:, g * 4:
                                                                (g + 1) * 4, :],
                      h_y.ap()[g * 512:(g + 1) * 512, T0:T0 + TC].rearrange(
                          "(k p) t -> p k t", p=128),
                      reads=[b_ypark], writes=[b_yg[g], b_y])
            wo_ = [newB("woa", bar4), newB("wob", bar4)]
            for j in range(8):
                wt = sb_wo4[:, (j % 2) * 2048:(j % 2) * 2048 + 2048]
                p.dma("gpsimd", wt.rearrange("p (k c) -> p k c", c=128),
                      i_wo.ap()[:, j * 128:(j + 1) * 128].rearrange(
                          "(k p) c -> p k c", p=128),
                      writes=[wo_[j % 2]])
                for th in range(2):
                    bank = 6 + (j * 2 + th) % 2
                    for g in range(4):
                        def em_m(t, wt=wt, th=th, bank=bank, g=g):
                            r = None
                            for k in range(g * 4, g * 4 + 4):
                                r = t.matmul(
                                    psum[bank][:],
                                    wt.rearrange("p (k c) -> p k c",
                                                 c=128)[:, k, :],
                                    ytile[:, k * 1024 + th * 512:
                                          k * 1024 + th * 512 + 512],
                                    start=(k == 0), stop=(k == 15))
                            return r
                        p.add("tensor", em_m,
                              reads=[wo_[j % 2], b_yg[g], b_gq],
                              writes=[pbuf[bank]])

                    def em_e(v, j=j, th=th, bank=bank):
                        sl = acc[:, j * 1024 + th * 512:
                                 j * 1024 + th * 512 + 512]
                        return v.tensor_tensor(sl, sl, psum[bank][:], ALU.add)
                    p.add("vector", em_e, reads=[pbuf[bank]], writes=[b_acc])

            p5_rest(0)
            p5_production(1)
            p5_rest(1)

            bar6 = p.barrier()
            otok = mf(163840, 1024)
            b_ot = newB("otok", bar6)
            for ttl in range(8):
                for half in range(2):
                    bank = 6 + half

                    def em_t(t, ttl=ttl, half=half, bank=bank):
                        r = None
                        for q in range(4):
                            j = half * 4 + q
                            r = t.transpose(
                                psum[bank][:, q * 128:(q + 1) * 128],
                                acc[:, j * 1024 + ttl * 128:
                                    j * 1024 + ttl * 128 + 128], sb_idf[:])
                        return r
                    p.add("tensor", em_t, reads=[b_acc, B("idf")],
                          writes=[pbuf[bank]])
                    for q in range(4):
                        j = half * 4 + q
                        p.add("scalar", lambda sc, j=j, bank=bank, q=q:
                              sc.copy(otok[:, j * 128:(j + 1) * 128],
                                      psum[bank][:, q * 128:(q + 1) * 128]),
                              reads=[pbuf[bank]], writes=[b_ot])
                p.dma("sync",
                      o_y.ap()[T0 + ttl * 128:T0 + (ttl + 1) * 128, :],
                      otok, reads=[b_ot], writes=[B("outdone")])

        p.finalize(es)
    return nc


_NC_CACHE = {}


def kernel(L0_attn_x, L0_attn_norm_w, L0_attn_norm_b, L0_attn_in_proj_w,
           L0_attn_conv_w, L0_attn_conv_b, L0_attn_x_proj_w, L0_attn_dt_proj_w,
           L0_attn_dt_proj_b, L0_attn_A_log, L0_attn_D_param,
           L0_attn_out_proj_w, L0_ffn_w_gate, L0_ffn_w_up, L0_ffn_w_down):
    def f(a):
        return np.ascontiguousarray(np.asarray(a, dtype=np.float32))
    x = f(L0_attn_x)
    wmap = {
        "nw": f(L0_attn_norm_w).reshape(1, DM),
        "nb": f(L0_attn_norm_b).reshape(1, DM),
        "w_in": f(L0_attn_in_proj_w),
        "conv_w": f(L0_attn_conv_w).reshape(DI, 4),
        "conv_b": f(L0_attn_conv_b).reshape(1, DI),
        "w_xp": f(L0_attn_x_proj_w),
        "w_dt": f(L0_attn_dt_proj_w),
        "dt_b": f(L0_attn_dt_proj_b).reshape(1, DI),
        "a_log": f(L0_attn_A_log),
        "d_par": f(L0_attn_D_param).reshape(1, DI),
        "w_out": f(L0_attn_out_proj_w),
        "w_g": f(L0_ffn_w_gate),
        "w_u": f(L0_ffn_w_up),
        "w_d": f(L0_ffn_w_down),
    }
    if "nc" not in _NC_CACHE:
        _NC_CACHE["nc"] = build_nc()
    nc = _NC_CACHE["nc"]
    in_maps = [dict(wmap, xb=np.ascontiguousarray(x[b])) for b in range(8)]
    res = run_bass_kernel_spmd(nc, in_maps, core_ids=list(range(8)))
    out = np.stack([np.asarray(res.results[b]["y_out"]) for b in range(8)],
                   axis=0)
    return out.astype(np.float32)


# version 42 (final)
# speedup vs baseline: 1.0384x; 1.0022x over previous
import numpy as np
import concourse.bass as bass
import concourse.mybir as mybir
from concourse.bass_utils import run_bass_kernel_spmd
from contextlib import ExitStack

F32 = mybir.dt.float32
BF16 = mybir.dt.bfloat16
I32 = mybir.dt.int32
ALU = mybir.AluOpType
ACT = mybir.ActivationFunctionType
AX = mybir.AxisListType

L = 2048
DM = 1024
DI = 2048
DS = 16
NTT = 16
NDI = 16
NCH = 2
TC = 1024
NSUB = 4
SC = 256
INNER = 4096

ENGINES = ("sync", "scalar", "vector", "tensor", "gpsimd")


class Step:
    __slots__ = ("engine", "emit", "inc", "count_after", "idx", "deps",
                 "fdeps")


class Buf:
    __slots__ = ("writers", "readers")

    def __init__(self, init=()):
        self.writers = list(init)
        self.readers = []


class Prog:
    def __init__(self, nc):
        self.nc = nc
        self.steps = []
        self._counts = {e: 0 for e in ENGINES}
        self._last = {e: None for e in ENGINES}

    def add(self, engine, emit, reads=(), writes=(), extra=(), dma=False,
            ptr_reads=(), force=False):
        deps = set()
        for b in reads:
            deps.update(b.writers)
        for b in writes:
            deps.update(b.writers)
            deps.update(b.readers)
        deps.update(extra)
        fdeps = set()
        for b in ptr_reads:
            fdeps.update(b.writers)
        if force:
            fdeps.update(deps)
        st = Step()
        st.engine = engine
        st.emit = emit
        st.inc = 16 if dma else 1
        st.deps = list(deps - fdeps)
        st.fdeps = list(fdeps)
        st.idx = len(self.steps)
        self._counts[engine] += st.inc
        st.count_after = self._counts[engine]
        self.steps.append(st)
        self._last[engine] = st
        for b in reads:
            b.readers.append(st)
        for b in ptr_reads:
            b.readers.append(st)
        for b in writes:
            b.writers = [st]
            b.readers = []
        return st

    def dma(self, engine, out, in_, reads=(), writes=(), extra=(), slow=False):
        def emit(eng, out=out, in_=in_, slow=slow):
            return eng.dma_start(out, in_, allow_slow_non_contiguous=slow)
        return self.add(engine, emit, reads, writes, extra, dma=True)

    def barrier(self):
        return [s for s in self._last.values() if s is not None]

    def finalize(self, es):
        nc = self.nc
        sems = {e: es.enter_context(nc.semaphore("pg_" + e)) for e in ENGINES}
        by_engine = {e: [] for e in ENGINES}
        for st in self.steps:
            by_engine[st.engine].append(st)
        final = dict(self._counts)
        blk = es.enter_context(nc.Block())

        def mk(ename):
            def fn(eng):
                waited = {e: 0 for e in ENGINES}
                for st in by_engine[ename]:
                    need = {}
                    if st.inc == 16:
                        need[ename] = st.count_after - 16
                    for d in st.deps:
                        if d.engine == ename and d.inc == 1:
                            continue
                        if d.count_after > need.get(d.engine, 0):
                            need[d.engine] = d.count_after
                    for d in st.fdeps:
                        if d.count_after > need.get(d.engine, 0):
                            need[d.engine] = d.count_after
                    for e in sorted(need):
                        if need[e] > waited[e]:
                            eng.wait_ge(sems[e], need[e])
                            waited[e] = need[e]
                    st.emit(eng).then_inc(sems[ename], st.inc)
                if ename == "sync":
                    for e in ENGINES:
                        if final[e] > 0:
                            eng.wait_ge(sems[e], final[e])
            return fn

        for e in ENGINES:
            if by_engine[e] or e == "sync":
                getattr(blk, e)(mk(e))


def build_nc():
    nc = bass.Bass("TRN2", target_bir_lowering=False, debug=False)

    i_x = nc.dram_tensor("xb", [L, DM], F32, kind="ExternalInput")
    i_nw = nc.dram_tensor("nw", [1, DM], F32, kind="ExternalInput")
    i_nb = nc.dram_tensor("nb", [1, DM], F32, kind="ExternalInput")
    i_w1 = nc.dram_tensor("w_in", [DM, 2 * DI], F32, kind="ExternalInput")
    i_cw = nc.dram_tensor("conv_w", [DI, 4], F32, kind="ExternalInput")
    i_cb = nc.dram_tensor("conv_b", [1, DI], F32, kind="ExternalInput")
    i_wx = nc.dram_tensor("w_xp", [DI, 2080], F32, kind="ExternalInput")
    i_wdt = nc.dram_tensor("w_dt", [DI, DI], F32, kind="ExternalInput")
    i_dtb = nc.dram_tensor("dt_b", [1, DI], F32, kind="ExternalInput")
    i_alog = nc.dram_tensor("a_log", [DI, DS], F32, kind="ExternalInput")
    i_dp = nc.dram_tensor("d_par", [1, DI], F32, kind="ExternalInput")
    i_wo = nc.dram_tensor("w_out", [DI, DM], F32, kind="ExternalInput")
    i_wg = nc.dram_tensor("w_g", [INNER, DM], F32, kind="ExternalInput")
    i_wu = nc.dram_tensor("w_u", [INNER, DM], F32, kind="ExternalInput")
    i_wd = nc.dram_tensor("w_d", [DM, INNER], F32, kind="ExternalInput")
    o_y = nc.dram_tensor("y_out", [L, DM], F32, kind="ExternalOutput")
    h_base = nc.dram_tensor("s_base", [DM, L], F32, kind="Internal")
    h_xn2 = nc.dram_tensor("s_xn2", [DM, L], BF16, kind="Internal")
    h_xq = nc.dram_tensor("s_xq", [DM, L], BF16, kind="Internal")
    h_xb = nc.dram_tensor("s_xb", [DI, L], BF16, kind="Internal")
    h_z = nc.dram_tensor("s_z", [DI, L], BF16, kind="Internal")
    h_y = nc.dram_tensor("s_y", [DI, L], BF16, kind="Internal")
    h_wqg = nc.dram_tensor("s_wqg", [INNER // 128 * DM, 128], BF16,
                           kind="Internal")
    h_wqu = nc.dram_tensor("s_wqu", [INNER // 128 * DM, 128], BF16,
                           kind="Internal")
    h_wqd = nc.dram_tensor("s_wqd", [DM // 128 * INNER, 128], BF16,
                           kind="Internal")

    es = ExitStack()
    with es:
        def sbt(name, shape, dt=F32):
            return es.enter_context(nc.sbuf_tensor(name, shape, dt))

        MEGA = sbt("mega", [128, 45056], F32)
        Mf = MEGA[:]
        Mb = MEGA[:].bitcast(BF16)
        Mi = MEGA[:].bitcast(I32)

        def mf(off, n, p=128):
            return Mf[0:p, off // 4: off // 4 + n]

        def mb(off, n, p=128):
            return Mb[0:p, off // 2: off // 2 + n]

        def mi(off, n, p=128):
            return Mi[0:p, off // 4: off // 4 + n]

        sb_fx = sbt("fx", [128, L], F32)
        sb_A = sbt("a_sb", [128, NDI * DS], F32)
        sb_D = sbt("d_sb", [128, NDI], F32)
        sb_cb = sbt("cb_sb", [128, NDI], F32)
        sb_dtb = sbt("dtb_sb", [128, NDI], F32)
        sb_cw = sbt("cw_sb", [128, NDI * 4], F32)
        sb_pidx = sbt("pidx", [128, 1], F32)
        sb_srec = sbt("srec", [128, NTT], F32)
        sb_cols = sbt("cols", [128, 16], F32)
        sb_idf = sbt("idf", [128, 128], F32)
        sb_idb = sbt("idb", [128, 128], BF16)
        sb_hcar = sbt("hcar", [128, NDI * DS], F32)
        sb_halo = sbt("halo", [128, NDI * 4], F32)
        sb_Bbf = sbt("bbf", [16, TC], BF16)
        sb_Cbf = sbt("cbf", [16, TC], BF16)
        sb_u = sbt("u_sb", [128, 256], BF16)
        sb_u16 = sbt("u16_sb", [128, 16], F32)
        sb_s2b = sbt("s2b", [128, 512], F32)
        sb_fdb = sbt("fdb", [128, 512], F32)
        sb_wo4 = sbt("wo4", [128, 4096], BF16)

        psum = [es.enter_context(nc.psum_tensor("pb%d" % i, [128, 512], F32))
                for i in range(8)]

        p = Prog(nc)
        bufs = {}

        def B(name):
            if name not in bufs:
                bufs[name] = Buf()
            return bufs[name]

        def newB(name, init=()):
            bufs[name] = Buf(init)
            return bufs[name]

        pbuf = [B("psum%d" % i) for i in range(8)]

        C_SWG, C_SWU, C_SWD, C_THR, C_NTHR, C_T0, C_T1, C_T2 = range(8)

        def col(i):
            return sb_cols[:, i:i + 1]

        p.add("gpsimd", lambda g: g.iota(
            sb_idf[:], [[1, 128]], channel_multiplier=0,
            allow_small_or_imprecise_dtypes=True), writes=[B("idf")])
        p.add("gpsimd", lambda g: g.iota(
            sb_pidx[:], [[0, 1]], channel_multiplier=1,
            allow_small_or_imprecise_dtypes=True), writes=[B("pidx")])
        p.add("vector", lambda v: v.tensor_scalar(
            sb_idf[:], sb_idf[:], sb_pidx[:], None, ALU.is_equal),
            reads=[B("pidx")], writes=[B("idf")])
        p.add("vector", lambda v: v.tensor_copy(sb_idb[:], sb_idf[:]),
              reads=[B("idf")], writes=[B("idb")])

        p.dma("sync", sb_A[:], i_alog.ap().rearrange("(k p) n -> p k n", p=128),
              writes=[B("A")])
        p.add("scalar", lambda s: s.activation(sb_A[:], sb_A[:], ACT.Exp),
              writes=[B("A")])
        p.add("vector", lambda v: v.tensor_scalar_mul(sb_A[:], sb_A[:], -1.0),
              writes=[B("A")])
        p.dma("sync", sb_D[:], i_dp.ap().rearrange("o (k p) -> (o p) k", p=128), slow=True,
              writes=[B("D")])
        p.dma("sync", sb_cb[:], i_cb.ap().rearrange("o (k p) -> (o p) k", p=128), slow=True,
              writes=[B("cb")])
        p.dma("sync", sb_dtb[:], i_dtb.ap().rearrange("o (k p) -> (o p) k", p=128), slow=True,
              writes=[B("dtb")])
        p.dma("sync", sb_cw[:], i_cw.ap().rearrange("(k p) t -> p k t", p=128),
              writes=[B("cw")])
        p.add("vector", lambda v: v.memset(sb_hcar[:], 0.0), writes=[B("hcar")])
        p.add("vector", lambda v: v.memset(sb_halo[:], 0.0), writes=[B("halo")])

        O_BASE = 0
        O_XN2F = 65536
        O_XQF = 98304
        O_RA = 131072
        O_RB = 135168
        O_RC = 139264
        O_RD = 143360
        O_WQA = 147456
        O_WQB = 151552
        O_WQQ = 155648
        O_WQT = 157696
        O_NWB = 167936
        O_NBB = 172032

        base_fm = mf(O_BASE, 8 * 2048)
        xn2_fm = mb(O_XN2F, 8 * 2048)
        xq_fm = mb(O_XQF, 8 * 2048)
        nwb = mf(O_NWB, 1024)
        nbb = mf(O_NBB, 1024)
        b_base = newB("r_base")
        b_xn2f = newB("r_xn2f")
        b_xqf = newB("r_xqf")
        b_nwb = newB("r_nwb")
        b_nbb = newB("r_nbb")

        def bcast_rows(dst_row_ap_fn, first_in, n_parts, reads, bname):
            st = p.dma("sync", dst_row_ap_fn(0, 1), first_in,
                       reads=reads, writes=[B(bname)])
            pc = 1
            while pc < n_parts:
                st = p.dma("sync", dst_row_ap_fn(pc, min(pc, n_parts - pc)),
                           dst_row_ap_fn(0, min(pc, n_parts - pc)),
                           extra=[st])
                pc *= 2
            B(bname).writers.append(st)
            return st

        bcast_rows(lambda p0, n: Mf[p0:p0 + n, O_NWB // 4:O_NWB // 4 + 1024],
                   i_nw.ap(), 128, [], "r_nwb")
        bcast_rows(lambda p0, n: Mf[p0:p0 + n, O_NBB // 4:O_NBB // 4 + 1024],
                   i_nb.ap(), 128, [], "r_nbb")

        def transpose_to_fm(src_ap, src_buf, ident_ap, ident_buf,
                            dst_half_off, dst_is_bf16, dst_buf, tt):
            for half in range(2):
                bank = 6 + half
                pview = psum[bank][:].bitcast(BF16) if dst_is_bf16 \
                    else psum[bank][:]

                def em_t(t, src_ap=src_ap, half=half, pview=pview,
                         ident_ap=ident_ap):
                    r = None
                    for q in range(4):
                        j = half * 4 + q
                        r = t.transpose(pview[:, q * 128:(q + 1) * 128],
                                        src_ap[:, j * 128:(j + 1) * 128],
                                        ident_ap)
                    return r
                p.add("tensor", em_t, reads=[src_buf, ident_buf],
                      writes=[pbuf[bank]])
                if dst_is_bf16:
                    dstv = Mb[:, dst_half_off // 2:
                              dst_half_off // 2 + 16384].rearrange(
                        "p (j t) -> p j t", j=8)
                else:
                    dstv = Mf[:, dst_half_off // 4:
                              dst_half_off // 4 + 16384].rearrange(
                        "p (j t) -> p j t", j=8)
                dst = dstv[:, half * 4:half * 4 + 4, tt * 128:tt * 128 + 128]
                psv = pview[:, 0:512].rearrange("p (q t) -> p q t", q=4)
                p.add("scalar", lambda sc, dst=dst, psv=psv: sc.copy(dst, psv),
                      reads=[pbuf[bank]], writes=[dst_buf])

        XO = [131072, 135168, 139264, 143360]
        XNO = [147456, 151552, 155648, 159744]
        JNK = mf(163840, 1024)
        QT = mf(176128, 1024)
        QTb = mb(176128, 1024)
        QTi = mi(176128, 1024)
        bX = [newB("x%d" % i) for i in range(4)]
        bXN = [newB("xn%d" % i) for i in range(4)]
        bJ = newB("jnk")
        bQ = newB("qt")
        CS0, CS1, CS2 = 4, 8, 12
        bc0, bc1, bc2 = newB("cc0"), newB("cc1"), newB("cc2")

        def stats_chain():
            c0 = sb_cols[:, CS0:CS0 + 4]
            c1 = sb_cols[:, CS1:CS1 + 4]
            c2 = sb_cols[:, CS2:CS2 + 4]
            p.add("vector", lambda v: v.tensor_scalar_mul(c0, c0, 1.0 / DM),
                  writes=[bc0], force=True)
            p.add("vector", lambda v: v.tensor_tensor(c2, c0, c0, ALU.mult),
                  reads=[bc0], writes=[bc2], force=True)
            p.add("vector", lambda v: v.tensor_scalar_mul(c1, c1, 1.0 / DM),
                  writes=[bc1], force=True)
            p.add("vector", lambda v: v.tensor_tensor(c1, c1, c2, ALU.subtract),
                  reads=[bc2], writes=[bc1], force=True)
            p.add("vector", lambda v: v.tensor_scalar_add(c1, c1, 1e-5),
                  writes=[bc1], force=True)
            p.add("scalar", lambda s: s.activation(c1, c1, ACT.Sqrt),
                  writes=[bc1], force=True)
            p.add("vector", lambda v: v.reciprocal(c1, c1),
                  writes=[bc1], force=True)

        for rnd in range(4):
            tts = [rnd * 4 + i for i in range(4)]
            p.dma("sync" if rnd % 2 else "gpsimd",
                  Mf[:, XO[0] // 4:XO[0] // 4 + 4096].rearrange(
                      "p (r c) -> p r c", r=4),
                  i_x.ap()[rnd * 512:(rnd + 1) * 512, :].rearrange(
                      "(r p) c -> p r c", p=128),
                  writes=[bX[0], bX[1], bX[2], bX[3]])
            for i, tt in enumerate(tts):
                X = mf(XO[i], 1024)
                p.add("vector", lambda v, X=X, i=i: v.reduce_sum(
                    sb_cols[:, CS0 + i:CS0 + i + 1], X, AX.X),
                    reads=[bX[i]], writes=[bc0])
                p.add("scalar", lambda s, X=X, i=i: s.activation(
                    JNK, X, ACT.Square,
                    accum_out=sb_cols[:, CS1 + i:CS1 + i + 1]),
                    reads=[bX[i]], writes=[bJ, bc1])
            stats_chain()
            for i, tt in enumerate(tts):
                X = mf(XO[i], 1024)
                XN = mf(XNO[i], 1024)
                p.add("vector", lambda v, X=X, XN=XN, i=i: v.tensor_scalar(
                    XN, X, sb_cols[:, CS0 + i:CS0 + i + 1],
                    sb_cols[:, CS1 + i:CS1 + i + 1], ALU.subtract, ALU.mult),
                    reads=[bX[i]], writes=[bXN[i]], ptr_reads=[bc0, bc1])
                p.add("vector", lambda v, X=X, XN=XN: v.tensor_tensor(
                    X, X, XN, ALU.add), reads=[bXN[i]], writes=[bX[i]])
                transpose_to_fm(X, bX[i], sb_idf[:], B("idf"), O_BASE, False,
                                b_base, tt)
                p.add("vector", lambda v, XN=XN, i=i: v.reduce_sum(
                    sb_cols[:, CS0 + i:CS0 + i + 1], XN, AX.X),
                    reads=[bXN[i]], writes=[bc0])
                p.add("scalar", lambda s, XN=XN, i=i: s.activation(
                    JNK, XN, ACT.Square,
                    accum_out=sb_cols[:, CS1 + i:CS1 + i + 1]),
                    reads=[bXN[i]], writes=[bJ, bc1])
            stats_chain()
            for i, tt in enumerate(tts):
                X = mf(XO[i], 1024)
                Xb = mb(XO[i], 1024)
                XN = mf(XNO[i], 1024)
                p.add("vector", lambda v, X=X, XN=XN, i=i: v.tensor_scalar(
                    X, XN, sb_cols[:, CS0 + i:CS0 + i + 1],
                    sb_cols[:, CS1 + i:CS1 + i + 1], ALU.subtract, ALU.mult),
                    reads=[bXN[i]], writes=[bX[i]], ptr_reads=[bc0, bc1])
                p.add("vector", lambda v, X=X: v.tensor_tensor(
                    X, X, nwb, ALU.mult), reads=[b_nwb], writes=[bX[i]])
                p.add("vector", lambda v, X=X, Xb=Xb: v.tensor_tensor(
                    Xb, X, nbb, ALU.add), reads=[b_nbb], writes=[bX[i]])
                transpose_to_fm(Xb, bX[i], sb_idb[:], B("idb"), O_XN2F, True,
                                b_xn2f, tt)
                p.add("vector", lambda v, XN=XN, i=i: v.tensor_reduce(
                    sb_cols[:, CS0 + i:CS0 + i + 1], XN, AX.X, ALU.max,
                    apply_absolute_value=True),
                    reads=[bXN[i]], writes=[bc0])
            c0 = sb_cols[:, CS0:CS0 + 4]
            c1 = sb_cols[:, CS1:CS1 + 4]
            p.add("vector", lambda v: v.tensor_scalar_max(c0, c0, 1e-5),
                  writes=[bc0], force=True)
            p.add("vector", lambda v, rnd=rnd: v.tensor_scalar_mul(
                sb_srec[:, rnd * 4:rnd * 4 + 4], c0, 1.0 / 127.0),
                reads=[bc0], writes=[B("srec")], force=True)
            p.add("vector", lambda v: v.reciprocal(c1, c0),
                  reads=[bc0], writes=[bc1], force=True)
            p.add("vector", lambda v: v.tensor_scalar_mul(c1, c1, 127.0),
                  writes=[bc1], force=True)
            for i, tt in enumerate(tts):
                XN = mf(XNO[i], 1024)
                XNi = mi(XNO[i], 1024)
                Xb2 = mb(XO[i], 1024)
                p.add("vector", lambda v, XN=XN, XNi=XNi, i=i: v.tensor_scalar(
                    XNi, XN, sb_cols[:, CS1 + i:CS1 + i + 1], None, ALU.mult),
                    reads=[bXN[i]], writes=[bXN[i]], ptr_reads=[bc1])
                p.add("vector", lambda v, XNi=XNi, Xb2=Xb2: v.tensor_copy(
                    Xb2, XNi), reads=[bXN[i]], writes=[bX[i]])
                transpose_to_fm(Xb2, bX[i], sb_idb[:], B("idb"), O_XQF, True,
                                b_xqf, tt)

        p.add("tensor", lambda t: t.transpose(
            psum[6][0:16, 0:128], sb_srec[:], sb_idf[:]),
            reads=[B("srec"), B("idf")], writes=[pbuf[6]])
        p.add("scalar", lambda sc: sc.copy(
            Mf[0:16, O_RA // 4:O_RA // 4 + 128], psum[6][0:16, 0:128]),
            reads=[pbuf[6]], writes=[bX[0]])
        bcast_rows(lambda p0, n: sb_fx[p0:p0 + n, :],
                   Mf[0:16, O_RA // 4:O_RA // 4 + 128], 128, [bX[0]],
                   "fx")
        b_fx = B("fx")

        st_pb = p.dma("sync", h_base.ap().rearrange("(j p) t -> p j t", p=128),
                      base_fm.rearrange("p (j t) -> p j t", j=8),
                      reads=[b_base], writes=[B("h_base")])
        st_p2 = p.dma("gpsimd",
                      h_xn2.ap().rearrange("(j p) t -> p j t", p=128),
                      xn2_fm.rearrange("p (j t) -> p j t", j=8),
                      reads=[b_xn2f], writes=[B("h_xn2")])
        st_pq = p.dma("gpsimd",
                      h_xq.ap().rearrange("(j p) t -> p j t", p=128),
                      xq_fm.rearrange("p (j t) -> p j t", j=8),
                      reads=[b_xqf], writes=[B("h_xq")])

        O_WQA2, O_WQB2 = 32768, 40960
        O_WP1A, O_WP1B = 32768, 49152
        O_WP1C = 65536
        O_WQS1, O_WQS2 = 49152, 53248
        O_WQQ2 = 57344
        O_WQT2 = 61440
        O_WQTQ = 57344

        def wquant(src, dst, n_row_tiles, ncols, csw, tag):
            b_wqtmp = B("wqtmp")
            wa1 = [B("wp1_a"), B("wp1_b"), B("wp1_c")]
            wa = [B("wq_a"), B("wq_b")]
            p.add("vector", lambda v: v.memset(col(C_T0), 0.0), writes=[B("c0")])
            it = 0
            if ncols >= 2048:
                r1step, p1w = 1, ncols
            else:
                r1step, p1w = 4, 4 * ncols
            for rt0 in range(0, n_row_tiles, r1step):
                off = (O_WP1A, O_WP1B, O_WP1C)[it % 3]
                wt = mf(off, p1w)
                if r1step == 4:
                    inap = src[rt0 * 128:(rt0 + 4) * 128, :].rearrange(
                        "(p q) c -> p (q c)", p=128)
                else:
                    inap = src[rt0 * 128:(rt0 + 1) * 128, :]
                wrt = [wa1[it % 3]] + ([B("wqq"), B("wqt"), B("wqs")]
                                       if it % 3 == 1 else [])
                if it % 3 == 2:
                    wrt += [b_xn2f]
                if it < 2:
                    wrt += [B("wq_a"), B("wq_b")]
                p.dma(("gpsimd", "sync", "scalar")[it % 3], wt, inap,
                      writes=wrt)
                p.add("vector", lambda v, wt=wt: v.tensor_reduce(
                    col(C_T1), wt, AX.X, ALU.add,
                    apply_absolute_value=True),
                    reads=[wa1[it % 3]], writes=[B("c1")])
                p.add("vector", lambda v: v.tensor_tensor(
                    col(C_T0), col(C_T0), col(C_T1), ALU.add),
                    reads=[B("c1")], writes=[B("c0")], force=True)
                it += 1
                yield True
            FQ = O_WQTQ // 4
            f0 = Mf[0:16, FQ:FQ + 8]
            st1 = p.dma("sync", f0, col(C_T0), reads=[B("c0")],
                        writes=[b_wqtmp, B("wqq")])
            p.add("vector", lambda v: v.tensor_tensor(
                Mf[0:16, FQ:FQ + 4], Mf[0:16, FQ:FQ + 4],
                Mf[0:16, FQ + 4:FQ + 8], ALU.add),
                writes=[b_wqtmp], force=True)
            p.add("vector", lambda v: v.tensor_tensor(
                Mf[0:16, FQ:FQ + 2], Mf[0:16, FQ:FQ + 2],
                Mf[0:16, FQ + 2:FQ + 4], ALU.add),
                writes=[b_wqtmp], force=True)
            p.add("vector", lambda v: v.tensor_tensor(
                Mf[0:16, FQ:FQ + 1], Mf[0:16, FQ:FQ + 1],
                Mf[0:16, FQ + 1:FQ + 2], ALU.add),
                writes=[b_wqtmp], force=True)
            p.dma("sync", Mf[0:1, FQ + 8:FQ + 24], Mf[0:16, FQ:FQ + 1],
                  reads=[b_wqtmp], writes=[b_wqtmp])
            p.add("vector", lambda v: v.tensor_reduce(
                Mf[0:1, FQ:FQ + 1], Mf[0:1, FQ + 8:FQ + 24], AX.X, ALU.add),
                writes=[b_wqtmp], force=True)
            nel = float(n_row_tiles * 128 * ncols)
            p.add("vector", lambda v: v.tensor_scalar(
                Mf[0:1, FQ:FQ + 1], Mf[0:1, FQ:FQ + 1], 1.0 / nel, 1e-5,
                ALU.mult, ALU.max), writes=[b_wqtmp], force=True)
            bcast_rows(lambda p0, n: sb_cols[p0:p0 + n, csw:csw + 1],
                       Mf[0:1, FQ:FQ + 1], 128, [b_wqtmp], tag + "_sw")
            bsw = B(tag + "_sw")
            p.add("vector", lambda v: v.tensor_scalar_mul(
                col(C_THR), col(csw), 0.5), reads=[bsw], writes=[B("thr")],
                force=True)
            p.add("vector", lambda v: v.tensor_scalar_mul(
                col(C_NTHR), col(C_THR), -1.0), reads=[B("thr")],
                writes=[B("nthr")], force=True)
            p.add("vector", lambda v: v.tensor_scalar_mul(
                col(csw), col(csw), 0.5), writes=[bsw], force=True)
            yield True
            it = 0
            CCW = min(2048, ncols)
            ncc = ncols // CCW
            rstep = 1 if ncols >= 2048 else 2
            for rt0 in range(0, n_row_tiles, rstep):
                for cc in range(ncc):
                    off = O_WQA2 if it % 2 == 0 else O_WQB2
                    wt = mf(off, 2048)
                    if rstep == 2:
                        inap = src[rt0 * 128:(rt0 + 2) * 128, :].rearrange(
                            "(r p) c -> p r c", p=128)
                        outap = wt.rearrange("p (r c) -> p r c", r=2)
                    else:
                        inap = src[rt0 * 128:(rt0 + 1) * 128,
                                   cc * 2048:(cc + 1) * 2048]
                        outap = wt
                    wrt = [wa[it % 2]] + ([wa1[0], wa1[1]] if it < 2 else [])
                    p.dma(("gpsimd", "sync", "scalar")[it % 3], outap, inap,
                          writes=wrt)
                    s1 = mb(O_WQS1, 2048)
                    s2 = mb(O_WQS2, 2048)
                    wq = mb(O_WQQ2, 2048)
                    p.add("scalar", lambda sc, wt=wt, s1=s1: sc.activation(
                        s1, wt, ACT.Sign, bias=col(C_NTHR)),
                        reads=[wa[it % 2]], writes=[B("wqs")],
                        ptr_reads=[B("nthr")])
                    p.add("scalar", lambda sc, wt=wt, s2=s2: sc.activation(
                        s2, wt, ACT.Sign, bias=col(C_THR)),
                        reads=[wa[it % 2]], writes=[B("wqs")],
                        ptr_reads=[B("thr")])
                    p.add("vector", lambda v, wq=wq, s1=s1, s2=s2:
                          v.tensor_tensor(wq, s1, s2, ALU.add),
                          reads=[B("wqs")], writes=[B("wqq")])
                    for half in range(4):
                        bank = 6 + half % 2
                        pview = psum[bank][:].bitcast(BF16)

                        def em_t(t, wq=wq, half=half, pview=pview):
                            r = None
                            for q in range(4):
                                jb = half * 4 + q
                                r = t.transpose(
                                    pview[:, q * 128:(q + 1) * 128],
                                    wq[:, jb * 128:(jb + 1) * 128], sb_idb[:])
                            return r
                        p.add("tensor", em_t, reads=[B("wqq"), B("idb")],
                              writes=[pbuf[bank]])
                        o = O_WQT2 // 2 + half * 512
                        p.add("vector", lambda v, o=o, pview=pview:
                              v.tensor_copy(Mb[:, o:o + 512], pview[:, 0:512]),
                              reads=[pbuf[bank]], writes=[B("wqt")])

                    if rstep == 2:
                        for r in range(2):
                            def em_d(eng, rt0=rt0, r=r):
                                jb = rt0 + r
                                return eng.dma_start(
                                    dst[jb * 1024:(jb + 1) * 1024,
                                        :].rearrange(
                                        "(k p) c -> p k c", p=128),
                                    Mb[:, O_WQT2 // 2 + r * 1024:
                                       O_WQT2 // 2 + r * 1024 + 1024].rearrange(
                                        "p (k c) -> p k c", c=128))
                            p.add("sync", em_d, reads=[B("wqt")],
                                  writes=[B(tag + "park")], dma=True)
                    else:
                        def em_d(eng, rt0=rt0, cc=cc):
                            r0 = rt0 * 4096 + cc * 2048
                            return eng.dma_start(
                                dst[r0:r0 + 2048].rearrange(
                                    "(k p) c -> p k c", p=128),
                                Mb[:, O_WQT2 // 2:O_WQT2 // 2 + 2048].rearrange(
                                    "p (k c) -> p k c", c=128))
                        p.add("sync", em_d, reads=[B("wqt")],
                              writes=[B(tag + "park")], dma=True)
                    it += 1
                    yield True

        for ch in range(NCH):
            T0 = ch * TC
            bar = p.barrier()

            xn2s = mb(131072, 8 * 1024)
            b_xn2s = newB("xn2s", bar)
            if ch == 0:
                p.add("vector", lambda v: v.tensor_copy(
                    xn2s.rearrange("p (k t) -> p k t", k=8),
                    xn2_fm.rearrange("p (j t) -> p j t", j=8)[:, :, T0:T0 + TC]
                    if False else
                    xn2_fm.rearrange("p (j t) -> p j t", j=8)[:, :, 0:TC]),
                    reads=[b_xn2f], writes=[b_xn2s])
            else:
                p.dma("sync", xn2s.rearrange("p (k t) -> p k t", k=8),
                      h_xn2.ap()[:, T0:T0 + TC].rearrange(
                          "(k p) t -> p k t", p=128),
                      reads=[B("h_xn2")], writes=[b_xn2s])
            xbres = mb(98304, 16 * 1024)
            b_xb = newB("xbres", bar)
            b_xbpre = newB("xbpre", bar)
            b_convt = newB("convt", bar)
            w1b_ = [newB("w1a", bar), newB("w1b", bar)]
            W1O = [172544, 174848]
            xbpre = mf(163840, 1028)
            convt = mf(168192, 1024)
            convtb = mb(168192, 1024)
            b_xbpark = B("xb_park")
            b_zpark = B("z_park")
            if ch == 0:
                for _nm in ("wq_a", "wq_b", "wp1_a", "wp1_b", "wp1_c",
                            "wqtmp", "wqq", "wqt", "wqs"):
                    newB(_nm, bar)

                def _wq_all():
                    yield from wquant(i_wg.ap(), h_wqg.ap(), INNER // 128,
                                      DM, C_SWG, "wg")
                    yield from wquant(i_wu.ap(), h_wqu.ap(), INNER // 128,
                                      DM, C_SWU, "wu")
                    yield from wquant(i_wd.ap(), h_wqd.ap(), DM // 128,
                                      INNER, C_SWD, "wd")
                wqgen = _wq_all()
            else:
                wqgen = iter(())

            def wq_step(n=1):
                for _ in range(n):
                    if next(wqgen, None) is None:
                        break
            for j in range(32):
                wt = mb(W1O[j % 2], 8 * 128)
                p.dma("gpsimd", wt.rearrange("p (k c) -> p k c", c=128),
                      i_w1.ap()[:, j * 128:(j + 1) * 128].rearrange(
                          "(k p) c -> p k c", p=128),
                      writes=[w1b_[j % 2]])
                for th in range(2):
                    bank = (j * 2 + th) % 4

                    def em_m(t, wt=wt, th=th, bank=bank):
                        r = None
                        for k in range(8):
                            r = t.matmul(
                                psum[bank][:],
                                wt.rearrange("p (k c) -> p k c", c=128)[:, k, :],
                                xn2s[:, k * 1024 + th * 512:
                                     k * 1024 + th * 512 + 512],
                                start=(k == 0), stop=(k == 7))
                        return r
                    p.add("tensor", em_m, reads=[w1b_[j % 2], b_xn2s],
                          writes=[pbuf[bank]])
                    if j < 16:
                        p.add("vector", lambda v, th=th, bank=bank:
                              v.tensor_copy(
                                  xbpre[:, 4 + th * 512:4 + th * 512 + 512],
                                  psum[bank][:]),
                              reads=[pbuf[bank]], writes=[b_xbpre])
                    else:
                        p.add("vector", lambda v, th=th, bank=bank:
                              v.tensor_copy(
                                  convtb[:, th * 512:th * 512 + 512],
                                  psum[bank][:]),
                              reads=[pbuf[bank]], writes=[b_convt])
                if j < 16:
                    p.add("vector", lambda v, j=j: v.tensor_copy(
                        xbpre[:, 1:4], sb_halo[:, j * 4 + 1:j * 4 + 4]),
                        reads=[B("halo")], writes=[b_xbpre])
                    p.add("vector", lambda v, j=j: v.tensor_copy(
                        sb_halo[:, j * 4 + 1:j * 4 + 4], xbpre[:, 1025:1028]),
                        reads=[b_xbpre], writes=[B("halo")], force=True)
                    p.add("vector", lambda v, j=j: v.tensor_scalar(
                        convt, xbpre[:, 1:1025], sb_cw[:, j * 4:j * 4 + 1],
                        None, ALU.mult),
                        reads=[b_xbpre, B("cw")], writes=[b_convt],
                        force=True)
                    for k in range(1, 4):
                        p.add("vector", lambda v, j=j, k=k:
                              v.scalar_tensor_tensor(
                                  convt, xbpre[:, 1 + k:1 + k + 1024],
                                  sb_cw[:, j * 4 + k:j * 4 + k + 1], convt,
                                  ALU.mult, ALU.add),
                              reads=[b_xbpre, B("cw")], writes=[b_convt])
                    xbt = xbres[:, j * 1024:(j + 1) * 1024]
                    p.add("scalar", lambda sc, j=j, xbt=xbt: sc.activation(
                        xbt, convt, ACT.Silu, bias=sb_cb[:, j:j + 1]),
                        reads=[b_convt, B("cb")], writes=[b_xb])
                else:
                    jz = j - 16
                    p.dma("sync", h_z.ap()[jz * 128:(jz + 1) * 128, T0:T0 + TC],
                          convtb, reads=[b_convt], writes=[b_zpark])
                wq_step(1)

            p.dma("sync",
                  h_xb.ap()[:, T0:T0 + TC].rearrange("(k p) t -> p k t", p=128),
                  xbres.rearrange("p (k t) -> p k t", k=16),
                  reads=[b_xb], writes=[b_xbpark])

            bar2 = p.barrier()
            delta_in = mb(0, 16 * 1024)
            b_din = newB("din", bar2)
            wx_ = [newB("wxa", bar2), newB("wxb", bar2)]
            WXO = [163840, 168192]
            bc_bufs = {}

            def emit_bc():
                barBC = p.barrier()
                b_bfree = newB("bfree", barBC)
                b_cfree = newB("cfree", barBC)
                wqtags = [B(nm) for nm in ("wq_a", "wq_b", "wp1_a", "wp1_b",
                                           "wp1_c", "wqq", "wqt", "wqs",
                                           "wqtmp")]
                for s_i in range(NSUB):
                    st = p.dma("sync",
                               Mb[0:1, 32768 // 2 + s_i * 4096:
                                  32768 // 2 + (s_i + 1) * 4096],
                               sb_Bbf[0:16, s_i * SC:(s_i + 1) * SC],
                               reads=[B("Bbf")], extra=barBC,
                               writes=(wqtags if s_i == 0 else ()))
                    b_bfree.writers.append(st)
                    st = p.dma("gpsimd",
                               Mb[0:1, 65536 // 2 + s_i * 4096:
                                  65536 // 2 + (s_i + 1) * 4096],
                               sb_Cbf[0:16, s_i * SC:(s_i + 1) * SC],
                               reads=[B("Cbf")], extra=barBC,
                               writes=(wqtags + [b_xn2f] if s_i == 0 else ()))
                    b_cfree.writers.append(st)
                pc = 1
                stb = list(b_bfree.writers)
                while pc < 128:
                    stb = [p.dma("sync",
                                 Mb[pc:2 * pc, 32768 // 2:32768 // 2 + 16384],
                                 Mb[0:pc, 32768 // 2:32768 // 2 + 16384],
                                 extra=stb)]
                    pc *= 2
                b_bfree.writers.extend(stb)
                pc = 1
                stc = list(b_cfree.writers)
                while pc < 128:
                    stc = [p.dma("gpsimd",
                                 Mb[pc:2 * pc, 65536 // 2:65536 // 2 + 16384],
                                 Mb[0:pc, 65536 // 2:65536 // 2 + 16384],
                                 extra=stc)]
                    pc *= 2
                b_cfree.writers.extend(stc)
                bc_bufs["b"] = b_bfree
                bc_bufs["c"] = b_cfree
            for j in [16] + list(range(16)):
                wdt_c = 128 if j < 16 else 32
                wt = mb(WXO[j % 2], 16 * wdt_c)
                p.dma("gpsimd", wt.rearrange("p (k c) -> p k c", c=wdt_c),
                      i_wx.ap()[:, j * 128:j * 128 + wdt_c].rearrange(
                          "(k p) c -> p k c", p=128),
                      writes=[wx_[j % 2]])
                for th in range(2):
                    bank = (j * 2 + th) % 4
                    if j < 16:
                        def em_m(t, wt=wt, th=th, bank=bank):
                            r = None
                            for k in range(16):
                                r = t.matmul(
                                    psum[bank][:],
                                    wt.rearrange("p (k c) -> p k c",
                                                 c=128)[:, k, :],
                                    xbres[:, k * 1024 + th * 512:
                                          k * 1024 + th * 512 + 512],
                                    start=(k == 0), stop=(k == 15))
                            return r
                        p.add("tensor", em_m, reads=[wx_[j % 2], b_xb],
                              writes=[pbuf[bank]])
                        p.add("vector", lambda v, j=j, th=th, bank=bank:
                              v.tensor_copy(
                                  delta_in[:, j * 1024 + th * 512:
                                           j * 1024 + th * 512 + 512],
                                  psum[bank][:]),
                              reads=[pbuf[bank]], writes=[b_din])
                    else:
                        bank_c = (bank + 2) % 4

                        def em_mb(t, wt=wt, th=th, bank=bank):
                            r = None
                            for k in range(16):
                                r = t.matmul(
                                    psum[bank][0:16, 0:512],
                                    wt.rearrange("p (k c) -> p k c",
                                                 c=32)[:, k, 0:16],
                                    xbres[:, k * 1024 + th * 512:
                                          k * 1024 + th * 512 + 512],
                                    start=(k == 0), stop=(k == 15))
                            return r
                        p.add("tensor", em_mb, reads=[wx_[j % 2], b_xb],
                              writes=[pbuf[bank]])

                        def em_mc(t, wt=wt, th=th, bank_c=bank_c):
                            r = None
                            for k in range(16):
                                r = t.matmul(
                                    psum[bank_c][0:16, 0:512],
                                    wt.rearrange("p (k c) -> p k c",
                                                 c=32)[:, k, 16:32],
                                    xbres[:, k * 1024 + th * 512:
                                          k * 1024 + th * 512 + 512],
                                    start=(k == 0), stop=(k == 15))
                            return r
                        p.add("tensor", em_mc, reads=[wx_[j % 2], b_xb],
                              writes=[pbuf[bank_c]])
                        p.add("scalar", lambda sc, th=th, bank=bank: sc.copy(
                            sb_Bbf[:, th * 512:th * 512 + 512],
                            psum[bank][0:16, 0:512]),
                            reads=[pbuf[bank]], writes=[B("Bbf")])
                        p.add("scalar", lambda sc, th=th, bank_c=bank_c:
                              sc.copy(sb_Cbf[:, th * 512:th * 512 + 512],
                                      psum[bank_c][0:16, 0:512]),
                              reads=[pbuf[bank_c]], writes=[B("Cbf")])
                if j == 16 and ch != 0:
                    emit_bc()
                wq_step(3)

            wq_step(10 ** 6)
            if ch == 0:
                emit_bc()

            bar3 = p.barrier()
            b_bfree = bc_bufs["b"]
            b_cfree = bc_bufs["c"]

            dAs = [mf(98304, 4096), mf(114688, 4096)]
            b_dAs = [newB("dA0", bar3), newB("dA1", bar3)]
            dbx = mb(131072, 4096)
            dbx3 = dbx.rearrange("p (n t) -> p n t", n=DS)
            b_dBx = newB("dBx", bar3)
            hb = mb(139264, 4096)
            hb3 = hb.rearrange("p (n t) -> p n t", n=DS)
            b_h = newB("hb", bar3)
            deltas = [mf(147456, 1024), mf(151552, 1024)]
            b_deltas = [newB("d0", bar3), newB("d1", bar3)]
            sptmp = mf(155648, 1024)
            b_sp = newB("sptmp", bar3)
            xbsls = [mb(159744, 1024), mb(165888, 1024)]
            b_xbsls = [newB("xbsl0", bar3), newB("xbsl1", bar3)]
            zsls = [mb(161792, 1024), mb(169984, 1024)]
            b_zsls = [newB("zsl0", bar3), newB("zsl1", bar3)]
            tnh = mb(163840, 1024)
            b_tnh = newB("tnh", bar3)
            ytmp = mb(167936, 1024)
            b_ytmp = newB("ytmp", bar3)
            wd_ = [newB("wda", bar3), newB("wdb", bar3)]
            WDO = [172032, 176128]
            b_ypark = B("y_park")

            def prep_dt(dt):
                par = dt % 2
                wt = mb(WDO[par], 16 * 128)
                p.dma("gpsimd", wt.rearrange("p (k c) -> p k c", c=128),
                      i_wdt.ap()[:, dt * 128:(dt + 1) * 128].rearrange(
                          "(k p) c -> p k c", p=128),
                      writes=[wd_[par]])
                p.dma("sync", xbsls[par],
                      h_xb.ap()[dt * 128:(dt + 1) * 128, T0:T0 + TC],
                      reads=[b_xbpark], writes=[b_xbsls[par]])
                p.dma("sync", zsls[par],
                      h_z.ap()[dt * 128:(dt + 1) * 128, T0:T0 + TC],
                      reads=[b_zpark], writes=[b_zsls[par]])
                delta = deltas[par]
                for th in range(2):
                    bank = 4 + th

                    def em_m(t, wt=wt, th=th, bank=bank):
                        r = None
                        for k in range(16):
                            r = t.matmul(
                                psum[bank][:],
                                wt.rearrange("p (k c) -> p k c", c=128)[:, k, :],
                                delta_in[:, k * 1024 + th * 512:
                                         k * 1024 + th * 512 + 512],
                                start=(k == 0), stop=(k == 15))
                        return r
                    p.add("tensor", em_m, reads=[wd_[par], b_din],
                          writes=[pbuf[bank]])
                    p.add("scalar", lambda s, th=th, bank=bank, dt=dt,
                          delta=delta: s.activation(
                              delta[:, th * 512:th * 512 + 512], psum[bank][:],
                              ACT.Identity, bias=sb_dtb[:, dt:dt + 1]),
                          reads=[pbuf[bank], B("dtb")], writes=[b_deltas[par]])
                p.add("scalar", lambda s, delta=delta: s.activation(
                    sptmp, delta, ACT.Abs), reads=[b_deltas[par]],
                    writes=[b_sp])
                p.add("scalar", lambda s: s.activation(
                    sptmp, sptmp, ACT.Exp, scale=-1.0), writes=[b_sp])
                p.add("scalar", lambda s: s.activation(
                    sptmp, sptmp, ACT.Ln, bias=1.0), writes=[b_sp])
                p.add("scalar", lambda s, delta=delta: s.activation(
                    delta, delta, ACT.Relu), writes=[b_deltas[par]])
                p.add("vector", lambda v, delta=delta: v.tensor_tensor(
                    delta, delta, sptmp, ALU.add), reads=[b_sp],
                    writes=[b_deltas[par]])

            prep_dt(0)
            for dt in range(NDI):
                par = dt % 2
                delta = deltas[par]
                xbsl = xbsls[par]
                if dt + 1 < NDI:
                    prep_dt(dt + 1)
                hsl = sb_hcar[:, dt * DS:(dt + 1) * DS]
                for s_i in range(NSUB):
                    g_par = (dt * NSUB + s_i) % 2
                    dA = dAs[g_par]
                    dA3 = dA.rearrange("p (n t) -> p n t", n=DS)
                    b_dA = b_dAs[g_par]
                    bfs3 = mb(32768 + s_i * 8192, 4096).rearrange(
                        "p (n t) -> p n t", n=DS)
                    cfs3 = mb(65536 + s_i * 8192, 4096).rearrange(
                        "p (n t) -> p n t", n=DS)

                    def em_da(sc, dt=dt, s_i=s_i, dA=dA, delta=delta):
                        r = None
                        for n in range(DS):
                            r = sc.activation(
                                dA[:, n * SC:(n + 1) * SC],
                                delta[:, s_i * SC:(s_i + 1) * SC], ACT.Exp,
                                scale=sb_A[:, dt * DS + n:dt * DS + n + 1])
                        return r
                    p.add("scalar", em_da, reads=[b_deltas[par], B("A")],
                          writes=[b_dA])
                    p.add("vector", lambda v, s_i=s_i, delta=delta, xbsl=xbsl:
                          v.tensor_tensor(
                              sb_u[:], delta[:, s_i * SC:(s_i + 1) * SC],
                              xbsl[:, s_i * SC:(s_i + 1) * SC], ALU.mult),
                          reads=[b_deltas[par], b_xbsls[par]], writes=[B("u")])
                    p.add("vector", lambda v, bfs3=bfs3: v.tensor_tensor(
                        dbx3,
                        sb_u[:].unsqueeze(1).broadcast_to([128, DS, SC]),
                        bfs3, ALU.mult),
                        reads=[B("u"), b_bfree], writes=[b_dBx])
                    p.add("vector", lambda v, hsl=hsl, dA3=dA3: v.tensor_tensor(
                        sb_u16[:],
                        dA3[:, :, 0:1].rearrange("p n o -> p (n o)"),
                        hsl, ALU.mult),
                        reads=[b_dA, B("hcar")], writes=[B("u16")], force=True)
                    p.add("vector", lambda v: v.tensor_tensor(
                        dbx3[:, :, 0:1].rearrange("p n o -> p (n o)"),
                        dbx3[:, :, 0:1].rearrange("p n o -> p (n o)"),
                        sb_u16[:], ALU.add), reads=[B("u16")], writes=[b_dBx],
                        force=True)
                    p.add("vector", lambda v, dA3=dA3: v.memset(
                        dA3[:, :, 0:1], 0.0), writes=[b_dA], force=True)
                    p.add("vector", lambda v, dA=dA: v.tensor_tensor_scan(
                        hb, dA, dbx, 0.0, ALU.mult, ALU.add),
                        reads=[b_dBx, b_dA], writes=[b_h], force=True)
                    p.add("vector", lambda v, hsl=hsl: v.tensor_copy(
                        hsl, hb3[:, :, SC - 1:SC].rearrange("p n o -> p (n o)")),
                        reads=[b_h], writes=[B("hcar")], force=True)
                    p.add("vector", lambda v, cfs3=cfs3: v.tensor_tensor(
                        dbx3, hb3, cfs3, ALU.mult),
                        reads=[b_h, b_cfree], writes=[b_dBx])
                    w_ = 2048
                    while w_ >= SC:
                        p.add("vector", lambda v, w_=w_: v.tensor_tensor(
                            dbx[:, 0:w_], dbx[:, 0:w_], dbx[:, w_:2 * w_],
                            ALU.add), writes=[b_dBx])
                        w_ //= 2

                    def em_y(v, s_i=s_i, xbsl=xbsl, dt=dt):
                        return v.scalar_tensor_tensor(
                            ytmp[:, s_i * SC:(s_i + 1) * SC],
                            xbsl[:, s_i * SC:(s_i + 1) * SC],
                            sb_D[:, dt:dt + 1], dbx[:, 0:SC],
                            ALU.mult, ALU.add)
                    p.add("vector", em_y, reads=[b_xbsls[par], B("D"), b_dBx],
                          writes=[b_ytmp])
                p.add("scalar", lambda s, par=par: s.activation(
                    tnh, zsls[par], ACT.Tanh, scale=0.5),
                    reads=[b_zsls[par]], writes=[b_tnh])
                p.add("vector", lambda v: v.tensor_scalar(
                    tnh, tnh, 0.5, 0.5, ALU.mult, ALU.add), writes=[b_tnh])
                p.add("vector", lambda v, par=par: v.tensor_tensor(
                    ytmp[:], ytmp[:], zsls[par], ALU.mult),
                    reads=[b_zsls[par]], writes=[b_ytmp])
                p.add("vector", lambda v: v.tensor_tensor(
                    ytmp[:], ytmp[:], tnh, ALU.mult),
                    reads=[b_tnh], writes=[b_ytmp])
                st = p.dma("sync", h_y.ap()[dt * 128:(dt + 1) * 128, T0:T0 + TC],
                           ytmp[:], reads=[b_ytmp])
                b_ypark.writers.append(st)

            bar5 = p.barrier()
            g_all = mf(0, 16384)
            gq = mb(65536, 16384)
            xqs = mb(131072, 8 * 1024)
            gmax = mf(147456, 512)
            gmin = mf(149504, 512)
            qi32 = mi(151552, 512)
            p1t = mf(153600, 512)
            gtmps = [mf(163840, 512), mf(155648, 512)]
            gtmp2s = [mf(165888, 512), mf(157696, 512)]
            utms = [mf(167936, 512), mf(159744, 512)]
            qtmp = mf(169984, 512)
            b_g = newB("g_all", bar5)
            b_gq = newB("gq", bar5)
            b_xqs = newB("xqs", bar5)
            b_gmax = newB("gmax", bar5)
            b_gmin = newB("gmin", bar5)
            b_qi = newB("qi32", bar5)
            b_p1 = newB("p1t", bar5)
            b_gts = [newB("gtmp0", bar5), newB("gtmp1", bar5)]
            b_gt2s = [newB("gtmp20", bar5), newB("gtmp21", bar5)]
            b_uts = [newB("utm0", bar5), newB("utm1", bar5)]
            b_qt = newB("qtmp", bar5)
            b_fw = [newB("fw0", bar5), newB("fw1", bar5)]
            b_fwd = [newB("fwd0", bar5), newB("fwd1", bar5)]
            b_gqk = [newB("gqk%d" % k2, bar5) for k2 in range(32)]
            FWO = [172032, 176128]
            FWD = [172032, 161792]
            b_xqg = [newB("xqg0", bar5), newB("xqg1", bar5)]
            for g in range(2):
                p.dma("sync",
                      xqs.rearrange("p (k t) -> p k t", k=8)[:, g * 4:
                                                             (g + 1) * 4, :],
                      h_xq.ap()[:, T0:T0 + TC].rearrange(
                          "(p k) t -> p k t", k=8)[:, g * 4:(g + 1) * 4, :],
                      reads=[B("h_xq")], writes=[b_xqg[g], b_xqs])
            acc = mf(98304, 8 * 1024)
            b_acc = newB("acc", bar5)
            p.dma("sync", acc.rearrange("p (j t) -> p j t", j=8),
                  h_base.ap()[:, T0:T0 + TC].rearrange("(j p) t -> p j t",
                                                       p=128),
                  reads=[B("h_base")], writes=[b_acc])
            def p5_production(sub):
                tg0 = T0 + sub * 512
                fxsl = sb_fx[:, tg0:tg0 + 512]
                p.add("vector", lambda v: v.memset(gmax, -3.0e38),
                      writes=[b_gmax])
                p.add("vector", lambda v: v.memset(gmin, 3.0e38),
                      writes=[b_gmin])
                for j in range(32):
                    wgt = mb(FWO[j % 2], 8 * 128)
                    wut = mb(FWO[j % 2] + 2048, 8 * 128)
                    p.dma("gpsimd", wgt.rearrange("p (k c) -> p k c", c=128),
                          h_wqg.ap()[j * 1024:(j + 1) * 1024, :].rearrange(
                              "(p k) c -> p k c", k=8),
                          reads=[B("wgpark")], writes=[b_fw[j % 2]])
                    p.dma("sync", wut.rearrange("p (k c) -> p k c", c=128),
                          h_wqu.ap()[j * 1024:(j + 1) * 1024, :].rearrange(
                              "(p k) c -> p k c", k=8),
                          reads=[B("wupark")], writes=[b_fw[j % 2]])
                    bank_g = (j % 2) * 2
                    bank_u = (j % 2) * 2 + 1

                    for g in range(2):
                        def em_mg(t, wgt=wgt, bank_g=bank_g, sub=sub, g=g):
                            r = None
                            for k in range(g * 4, g * 4 + 4):
                                r = t.matmul(
                                    psum[bank_g][:],
                                    wgt.rearrange("p (k c) -> p k c",
                                                  c=128)[:, k, :],
                                    xqs[:, k * 1024 + sub * 512:
                                        k * 1024 + sub * 512 + 512],
                                    start=(k == 0), stop=(k == 7))
                            return r
                        p.add("tensor", em_mg,
                              reads=[b_fw[j % 2], b_xqg[g]],
                              writes=[pbuf[bank_g]])
                    for g in range(2):
                        def em_mu(t, wut=wut, bank_u=bank_u, sub=sub, g=g):
                            r = None
                            for k in range(g * 4, g * 4 + 4):
                                r = t.matmul(
                                    psum[bank_u][:],
                                    wut.rearrange("p (k c) -> p k c",
                                                  c=128)[:, k, :],
                                    xqs[:, k * 1024 + sub * 512:
                                        k * 1024 + sub * 512 + 512],
                                    start=(k == 0), stop=(k == 7))
                            return r
                        p.add("tensor", em_mu,
                              reads=[b_fw[j % 2], b_xqg[g]],
                              writes=[pbuf[bank_u]])
                    gtmp = gtmps[j % 2]
                    gtmp2 = gtmp2s[j % 2]
                    utm = utms[j % 2]
                    b_gt = b_gts[j % 2]
                    b_gt2 = b_gt2s[j % 2]
                    b_ut = b_uts[j % 2]
                    p.add("vector", lambda v, bank_g=bank_g, fxsl=fxsl,
                          gtmp=gtmp: v.scalar_tensor_tensor(
                              gtmp, psum[bank_g][:], col(C_SWG), fxsl,
                              ALU.mult, ALU.mult),
                          reads=[pbuf[bank_g], B("wg_sw"), b_fx],
                          writes=[b_gt])
                    p.add("scalar", lambda s, gtmp=gtmp, gtmp2=gtmp2:
                          s.activation(gtmp2, gtmp, ACT.Silu),
                          reads=[b_gt], writes=[b_gt2])
                    p.add("vector", lambda v, bank_u=bank_u, fxsl=fxsl,
                          utm=utm: v.scalar_tensor_tensor(
                              utm, psum[bank_u][:], col(C_SWU), fxsl,
                              ALU.mult, ALU.mult),
                          reads=[pbuf[bank_u], B("wu_sw"), b_fx],
                          writes=[b_ut])
                    p.add("vector", lambda v, j=j, gtmp2=gtmp2, utm=utm:
                          v.tensor_tensor(
                              g_all[:, j * 512:(j + 1) * 512], gtmp2, utm,
                              ALU.mult),
                          reads=[b_gt2, b_ut], writes=[b_g])
                    p.add("vector", lambda v, j=j: v.tensor_tensor(
                        gmax, gmax, g_all[:, j * 512:(j + 1) * 512], ALU.max),
                        reads=[b_g], writes=[b_gmax])
                    p.add("vector", lambda v, j=j: v.tensor_tensor(
                        gmin, gmin, g_all[:, j * 512:(j + 1) * 512], ALU.min),
                        reads=[b_g], writes=[b_gmin])
            def p5_rest(sub):
                tg0 = T0 + sub * 512
                fxsl = sb_fx[:, tg0:tg0 + 512]
                p.add("vector", lambda v: v.tensor_scalar_mul(
                    gmin, gmin, -1.0), writes=[b_gmin])
                p.add("vector", lambda v: v.tensor_tensor(
                    gmax, gmax, gmin, ALU.max), reads=[b_gmin],
                    writes=[b_gmax])
                FB = 65536 // 4
                p.dma("sync", Mf[0:16, FB:FB + 4096], gmax,
                      reads=[b_gmax], writes=[b_gq])
                p.add("vector", lambda v: v.tensor_tensor(
                    Mf[0:16, FB:FB + 2048], Mf[0:16, FB:FB + 2048],
                    Mf[0:16, FB + 2048:FB + 4096], ALU.max), writes=[b_gq])
                p.add("vector", lambda v: v.tensor_tensor(
                    Mf[0:16, FB:FB + 1024], Mf[0:16, FB:FB + 1024],
                    Mf[0:16, FB + 1024:FB + 2048], ALU.max), writes=[b_gq])
                p.add("vector", lambda v: v.tensor_tensor(
                    Mf[0:16, FB:FB + 512], Mf[0:16, FB:FB + 512],
                    Mf[0:16, FB + 512:FB + 1024], ALU.max), writes=[b_gq])
                p.dma("sync", Mf[0:4, FB + 2048:FB + 2048 + 2048],
                      Mf[0:16, FB:FB + 512], reads=[b_gq], writes=[b_gq])
                p.add("vector", lambda v: v.tensor_tensor(
                    Mf[0:4, FB + 2048:FB + 2048 + 1024],
                    Mf[0:4, FB + 2048:FB + 2048 + 1024],
                    Mf[0:4, FB + 3072:FB + 3072 + 1024], ALU.max),
                    writes=[b_gq])
                p.add("vector", lambda v: v.tensor_tensor(
                    Mf[0:4, FB + 2048:FB + 2048 + 512],
                    Mf[0:4, FB + 2048:FB + 2048 + 512],
                    Mf[0:4, FB + 2560:FB + 2560 + 512], ALU.max),
                    writes=[b_gq])
                p.dma("sync", Mf[0:1, FB:FB + 2048],
                      Mf[0:4, FB + 2048:FB + 2048 + 512],
                      reads=[b_gq], writes=[b_gq])
                p.add("vector", lambda v: v.tensor_tensor(
                    Mf[0:1, FB:FB + 1024], Mf[0:1, FB:FB + 1024],
                    Mf[0:1, FB + 1024:FB + 2048], ALU.max), writes=[b_gq])
                p.add("vector", lambda v: v.tensor_tensor(
                    Mf[0:1, FB:FB + 512], Mf[0:1, FB:FB + 512],
                    Mf[0:1, FB + 512:FB + 1024], ALU.max), writes=[b_gq])
                p.add("vector", lambda v: v.tensor_scalar_max(
                    Mf[0:1, FB:FB + 512], Mf[0:1, FB:FB + 512], 1e-5),
                    writes=[b_gq], force=True)
                p.add("vector", lambda v: v.reciprocal(
                    Mf[0:1, FB + 1024:FB + 1024 + 512],
                    Mf[0:1, FB:FB + 512]), writes=[b_gq], force=True)
                p.add("vector", lambda v: v.tensor_scalar_mul(
                    Mf[0:1, FB + 1024:FB + 1024 + 512],
                    Mf[0:1, FB + 1024:FB + 1024 + 512], 127.0),
                    writes=[b_gq], force=True)
                p.add("vector", lambda v: v.tensor_scalar(
                    Mf[0:1, FB + 2048:FB + 2048 + 512], Mf[0:1, FB:FB + 512],
                    1.0 / 127.0, sb_cols[0:1, C_SWD:C_SWD + 1],
                    ALU.mult, ALU.mult),
                    reads=[B("wd_sw")], writes=[b_gq], force=True)
                bcast_rows(lambda p0, n: sb_s2b[p0:p0 + n, :],
                           Mf[0:1, FB + 1024:FB + 1024 + 512], 128,
                           [b_gq], "s2b")
                bcast_rows(lambda p0, n: sb_fdb[p0:p0 + n, :],
                           Mf[0:1, FB + 2048:FB + 2048 + 512], 128,
                           [b_gq], "fdb")
                b_s2b = B("s2b")
                b_fdb2 = B("fdb")
                for j in range(32):
                    gsl = g_all[:, j * 512:(j + 1) * 512]
                    p.add("vector", lambda v, gsl=gsl: v.tensor_tensor(
                        qi32, gsl, sb_s2b[:], ALU.mult),
                        reads=[b_g, b_s2b], writes=[b_qi])
                    p.add("vector", lambda v, j=j: v.tensor_copy(
                        gq[:, j * 512:(j + 1) * 512], qi32),
                        reads=[b_qi], writes=[b_gqk[j], b_gq])
                for j2 in range(8):
                    wdt_ = mb(FWD[j2 % 2], 32 * 128)
                    wrt = ([b_fwd[0], b_fw[0], b_fw[1]] if j2 % 2 == 0 else
                           [b_fwd[1], b_gts[0], b_gt2s[0], b_uts[0]])
                    p.dma("sync" if j2 % 2 else "gpsimd",
                          wdt_.rearrange("p (k c) -> p k c", c=128),
                          h_wqd.ap()[j2 * 4096:(j2 + 1) * 4096, :].rearrange(
                              "(k p) c -> p k c", p=128),
                          reads=[B("wdpark")], writes=wrt)
                    bank = 4 + (j2 % 2)
                    kgrp = 8 if j2 == 0 else (16 if j2 == 1 else 32)
                    for k0 in range(0, 32, kgrp):
                        def em_md(t, wdt_=wdt_, bank=bank, k0=k0, kgrp=kgrp):
                            r = None
                            for k in range(k0, k0 + kgrp):
                                r = t.matmul(
                                    psum[bank][:],
                                    wdt_.rearrange("p (k c) -> p k c",
                                                   c=128)[:, k, :],
                                    gq[:, k * 512:(k + 1) * 512],
                                    start=(k == 0), stop=(k == 31))
                            return r
                        p.add("tensor", em_md,
                              reads=wrt + [b_gqk[k0 + kgrp - 1]],
                              writes=[pbuf[bank]])
                    p.add("vector", lambda v, bank=bank: v.tensor_tensor(
                        qtmp, psum[bank][:], sb_fdb[:], ALU.mult),
                        reads=[pbuf[bank], b_fdb2], writes=[b_qt])

                    def em_a(v, j2=j2, sub=sub):
                        sl = acc[:, j2 * 1024 + sub * 512:
                                 j2 * 1024 + sub * 512 + 512]
                        return v.tensor_tensor(sl, sl, qtmp, ALU.add)
                    p.add("vector", em_a, reads=[b_qt], writes=[b_acc])

            bar4 = bar5
            p5_production(0)
            ytile = mb(65536, 16 * 1024)
            b_y = newB("yres", bar4)
            b_yg = [newB("yg%d" % g, bar4) for g in range(4)]
            for g in range(4):
                p.dma("sync",
                      ytile.rearrange("p (k t) -> p k t", k=16)[:, g * 4:
                                                                (g + 1) * 4, :],
                      h_y.ap()[g * 512:(g + 1) * 512, T0:T0 + TC].rearrange(
                          "(k p) t -> p k t", p=128),
                      reads=[b_ypark], writes=[b_yg[g], b_y])
            wo_ = [newB("woa", bar4), newB("wob", bar4)]
            for j in range(8):
                wt = sb_wo4[:, (j % 2) * 2048:(j % 2) * 2048 + 2048]
                p.dma("gpsimd", wt.rearrange("p (k c) -> p k c", c=128),
                      i_wo.ap()[:, j * 128:(j + 1) * 128].rearrange(
                          "(k p) c -> p k c", p=128),
                      writes=[wo_[j % 2]])
                for th in range(2):
                    bank = 6 + (j * 2 + th) % 2
                    for g in range(4):
                        def em_m(t, wt=wt, th=th, bank=bank, g=g):
                            r = None
                            for k in range(g * 4, g * 4 + 4):
                                r = t.matmul(
                                    psum[bank][:],
                                    wt.rearrange("p (k c) -> p k c",
                                                 c=128)[:, k, :],
                                    ytile[:, k * 1024 + th * 512:
                                          k * 1024 + th * 512 + 512],
                                    start=(k == 0), stop=(k == 15))
                            return r
                        p.add("tensor", em_m,
                              reads=[wo_[j % 2], b_yg[g], b_gq],
                              writes=[pbuf[bank]])

                    def em_e(v, j=j, th=th, bank=bank):
                        sl = acc[:, j * 1024 + th * 512:
                                 j * 1024 + th * 512 + 512]
                        return v.tensor_tensor(sl, sl, psum[bank][:], ALU.add)
                    p.add("vector", em_e, reads=[pbuf[bank]], writes=[b_acc])

            p5_rest(0)
            p5_production(1)
            p5_rest(1)

            bar6 = p.barrier()
            otok = mf(163840, 1024)
            b_ot = newB("otok", bar6)
            for ttl in range(8):
                for half in range(2):
                    bank = 6 + half

                    def em_t(t, ttl=ttl, half=half, bank=bank):
                        r = None
                        for q in range(4):
                            j = half * 4 + q
                            r = t.transpose(
                                psum[bank][:, q * 128:(q + 1) * 128],
                                acc[:, j * 1024 + ttl * 128:
                                    j * 1024 + ttl * 128 + 128], sb_idf[:])
                        return r
                    p.add("tensor", em_t, reads=[b_acc, B("idf")],
                          writes=[pbuf[bank]])
                    for q in range(4):
                        j = half * 4 + q
                        p.add("scalar", lambda sc, j=j, bank=bank, q=q:
                              sc.copy(otok[:, j * 128:(j + 1) * 128],
                                      psum[bank][:, q * 128:(q + 1) * 128]),
                              reads=[pbuf[bank]], writes=[b_ot])
                p.dma("sync",
                      o_y.ap()[T0 + ttl * 128:T0 + (ttl + 1) * 128, :],
                      otok, reads=[b_ot], writes=[B("outdone")])

        p.finalize(es)
    return nc


_NC_CACHE = {}


def kernel(L0_attn_x, L0_attn_norm_w, L0_attn_norm_b, L0_attn_in_proj_w,
           L0_attn_conv_w, L0_attn_conv_b, L0_attn_x_proj_w, L0_attn_dt_proj_w,
           L0_attn_dt_proj_b, L0_attn_A_log, L0_attn_D_param,
           L0_attn_out_proj_w, L0_ffn_w_gate, L0_ffn_w_up, L0_ffn_w_down):
    def f(a):
        return np.ascontiguousarray(np.asarray(a, dtype=np.float32))
    x = f(L0_attn_x)
    wmap = {
        "nw": f(L0_attn_norm_w).reshape(1, DM),
        "nb": f(L0_attn_norm_b).reshape(1, DM),
        "w_in": f(L0_attn_in_proj_w),
        "conv_w": f(L0_attn_conv_w).reshape(DI, 4),
        "conv_b": f(L0_attn_conv_b).reshape(1, DI),
        "w_xp": f(L0_attn_x_proj_w),
        "w_dt": f(L0_attn_dt_proj_w),
        "dt_b": f(L0_attn_dt_proj_b).reshape(1, DI),
        "a_log": f(L0_attn_A_log),
        "d_par": f(L0_attn_D_param).reshape(1, DI),
        "w_out": f(L0_attn_out_proj_w),
        "w_g": f(L0_ffn_w_gate),
        "w_u": f(L0_ffn_w_up),
        "w_d": f(L0_ffn_w_down),
    }
    if "nc" not in _NC_CACHE:
        _NC_CACHE["nc"] = build_nc()
    nc = _NC_CACHE["nc"]
    in_maps = [dict(wmap, xb=np.ascontiguousarray(x[b])) for b in range(8)]
    res = run_bass_kernel_spmd(nc, in_maps, core_ids=list(range(8)))
    out = np.stack([np.asarray(res.results[b]["y_out"]) for b in range(8)],
                   axis=0)
    return out.astype(np.float32)
